# version 18
# baseline (speedup 1.0000x reference)
import numpy as np
from contextlib import ExitStack

import concourse.bass as bass
import concourse.mybir as mybir
from concourse.bass_utils import run_bass_kernel_spmd

F32 = mybir.dt.float32
AF = mybir.ActivationFunctionType
OP = mybir.AluOpType
AX = mybir.AxisListType

NCORE = 8
DBG = False
B, S, D = 8, 2048, 1024
TOK = 2048
NT = TOK // 128
DC = D // 128
H, DH, PM = 16, 64, 256
FF = 4096
FC = FF // 128
HARM = 32
LN_EPS = 1e-5
KEPS = 1e-3
NSTEP = 5
BETA = 0.9

ENGS = ("sync", "tensor", "vector", "scalar", "gpsimd")



class Prog:

    def __init__(self, nc, stack):
        self.nc = nc
        self.stack = stack
        self.cmds = {e: [] for e in ENGS}
        self.sems = {}
        self.cnt = {}
        for s in ("t", "v", "s", "g", "ld", "ld2", "st"):
            self.sem(s)

    def sem(self, name):
        if name not in self.sems:
            self.sems[name] = self.stack.enter_context(self.nc.semaphore(name))
            self.cnt[name] = 0
        return name

    OWN = {"vector": "v", "scalar": "s", "gpsimd": "g"}

    def emit(self, eng, fn, inc=None, n=1, raw=False):
        own = None if raw else self.OWN.get(eng)
        if eng == "tensor":
            self.tensor_emits = getattr(self, "tensor_emits", 0) + 1
            cell = self.__dict__.setdefault("tcell", [None])
            if inc is not None:
                self.sem(inc)
                self.cnt[inc] += n
                s = self.sems[inc]
                def runt(e, fn=fn, s=s, n=n, cell=cell):
                    cell[0] = fn(e).then_inc(s, n)
                self.cmds[eng].append(runt)
                self.last_tensor_inc = (inc, self.cnt[inc])
            else:
                def runt2(e, fn=fn, cell=cell):
                    cell[0] = fn(e)
                self.cmds[eng].append(runt2)
                self.last_tensor_inc = None
            return
        if own is not None:
            assert inc in (None, own), (eng, inc)
            prev = self.cnt[own]
            self.cnt[own] += 1
            s = self.sems[own]
            def run(e, fn=fn, s=s, prev=prev):
                if prev:
                    e.wait_ge(s, prev)
                fn(e).then_inc(s, 1)
            self.cmds[eng].append(run)
        elif inc is not None:
            self.sem(inc)
            self.cnt[inc] += n
            s = self.sems[inc]
            self.cmds[eng].append(lambda e, fn=fn, s=s, n=n: fn(e).then_inc(s, n))
        else:
            self.cmds[eng].append(fn)

    def wait(self, eng, sem, val=None):
        val = self.cnt[sem] if val is None else val
        s = self.sems[sem]
        self.cmds[eng].append(lambda e, s=s, v=val: e.wait_ge(s, v))

    def mark(self, sem):
        return self.cnt[sem]

    def barrier(self, engs=ENGS):
        marks = {}
        for e in engs:
            if e in self.OWN:
                self.emit(e, lambda x: x.nop())
                marks[e] = (self.OWN[e], self.cnt[self.OWN[e]])
            elif e == "tensor":
                emitted = getattr(self, "tensor_emits", 0)
                self.sem("bt")
                if emitted > getattr(self, "_bar_tensor_seen", 0):
                    self._bar_tensor_seen = emitted
                    lti = getattr(self, "last_tensor_inc", None)
                    if lti is not None:
                        marks[e] = lti
                    else:
                        self.cnt["bt"] += 1
                        cell = self.__dict__.setdefault("tcell", [None])
                        s = self.sems["bt"]
                        self.cmds["tensor"].append(
                            lambda _e, cell=cell, s=s: cell[0].then_inc(s, 1))
                        marks[e] = ("bt", self.cnt["bt"])
                else:
                    marks[e] = ("bt", self.cnt["bt"])
            else:
                marks[e] = ("ld", self.cnt["ld"])
        stv = self.cnt["st"]
        for e in engs:
            for o in engs:
                if o != e:
                    sem, val = marks[o]
                    self.wait(e, sem, val)
            if stv:
                self.wait(e, "st", stv)

    def build(self):
        blk = self.stack.enter_context(self.nc.Block())
        for e in ENGS:
            def run_all(eng, e=e):
                for c in self.cmds[e]:
                    c(eng)
            getattr(blk, e)(run_all)


class PsumPool:
    def __init__(self, p, banks):
        self.p = p
        self.banks = banks
        self.busy = [None] * len(banks)
        self.rr = 0

    def acquire(self):
        i = self.rr % len(self.banks)
        self.rr += 1
        if self.busy[i] is not None:
            sem, val = self.busy[i]
            self.p.wait("tensor", sem, val)
            self.busy[i] = None
        return i, self.banks[i]

    def release(self, i, sem):
        self.busy[i] = (sem, self.p.cnt[sem])


class DBuf:

    def __init__(self, p, bufs, queue="sync", ld="ld"):
        self.p = p
        self.bufs = bufs
        self.queue = queue
        self.ld = ld
        self.tmarks = [None] * len(bufs)
        self.i = 0

    def load(self, fn):
        k = self.i % len(self.bufs)
        self.i += 1
        buf = self.bufs[k]
        if self.tmarks[k] is not None:
            self.p.wait(self.queue, "t", self.tmarks[k])
        self.p.emit(self.queue, lambda e, buf=buf, fn=fn: fn(e, buf),
                    inc=self.ld, n=16)
        return buf, self.p.mark(self.ld)

    def consumed(self):
        k = (self.i - 1) % len(self.bufs)
        self.tmarks[k] = self.p.cnt["t"]



BIAS_CONSTS = (1e-5, 1e-3, 1.5707963267948966,
               -3.141592653589793)


def setup_consts(nc, p, ec):
    arr = np.tile(np.array([BIAS_CONSTS], np.float32), (128, 1))
    cd = nc.inline_tensor(arr, "biasconsts").ap()
    cs = ec(nc.sbuf_tensor("consts_sb", [128, len(BIAS_CONSTS)], F32))
    p.emit("sync", lambda e: e.dma_start(out=cs[:], in_=cd), inc="ld", n=16)
    for j, v in enumerate(BIAS_CONSTS):
        nc.const_aps.register(F32, float(v), cs[:, j:j + 1])
    return cs


def ln_stats(p, x_sb, mean_sb, rstd_sb, sq_sb, col0, tiles):
    for n_, i in enumerate(tiles):
        xs = x_sb[:, i * D:(i + 1) * D]
        p.emit("vector", lambda e, xs=xs, i=i: e.tensor_reduce(
            out=mean_sb[:, i:i + 1], in_=xs, axis=AX.X, op=OP.add), inc="v")
        if n_ > 0:
            p.wait("scalar", "s", p.mark("s"))
        p.emit("scalar", lambda e, xs=xs, i=i: e.activation(
            out=sq_sb[:], in_=xs, func=AF.Square,
            accum_out=rstd_sb[:, i:i + 1]), inc="s")
    p.wait("vector", "s", p.mark("s"))
    for i in tiles:
        p.emit("vector", lambda e, i=i: e.tensor_scalar(
            out=mean_sb[:, i:i + 1], in0=mean_sb[:, i:i + 1],
            scalar1=1.0 / D, scalar2=None, op0=OP.mult), inc="v")
        p.emit("vector", lambda e, i=i: e.tensor_tensor(
            out=col0[:], in0=mean_sb[:, i:i + 1], in1=mean_sb[:, i:i + 1],
            op=OP.mult), inc="v")
        p.emit("vector", lambda e, i=i: e.tensor_scalar(
            out=rstd_sb[:, i:i + 1], in0=rstd_sb[:, i:i + 1],
            scalar1=1.0 / D, scalar2=None, op0=OP.mult), inc="v")
        p.emit("vector", lambda e, i=i: e.tensor_tensor(
            out=rstd_sb[:, i:i + 1], in0=rstd_sb[:, i:i + 1], in1=col0[:],
            op=OP.subtract), inc="v")
        p.wait("scalar", "v", p.mark("v"))
        p.emit("scalar", lambda e, i=i: e.activation(
            out=rstd_sb[:, i:i + 1], in_=rstd_sb[:, i:i + 1], func=AF.Ln,
            bias=LN_EPS), inc="s")
        p.emit("scalar", lambda e, i=i: e.activation(
            out=rstd_sb[:, i:i + 1], in_=rstd_sb[:, i:i + 1], func=AF.Exp,
            scale=-0.5), inc="s")
    p.wait("vector", "s", p.mark("s"))


def make_jit_xhat(p, pool, mean_sb, rstd_sb, eye_sb, xc_sb, diag_sb,
                  depth=1):
    last = {d: 0 for d in range(depth)}

    def jit(tiles, x_sb, xhT, stats=None):
        n = len(tiles) * 128
        nt = len(tiles)
        vstage = {}

        def stage(k):
            i = tiles[k]
            si = stats[k] if stats is not None else i
            sl, dsl = (k % depth) * D, (k % depth) * 128
            xs = x_sb[:, i * D:(i + 1) * D]
            p.wait("vector", "t", last[k % depth])
            p.emit("vector", lambda e, xs=xs, si=si, sl=sl: e.tensor_scalar(
                out=xc_sb[:, sl:sl + D], in0=xs,
                scalar1=mean_sb[:, si:si + 1],
                scalar2=None, op0=OP.subtract), inc="v")
            p.emit("vector", lambda e, si=si, dsl=dsl: e.tensor_scalar(
                out=diag_sb[:, dsl:dsl + 128], in0=eye_sb[:],
                scalar1=rstd_sb[:, si:si + 1],
                scalar2=None, op0=OP.mult), inc="v")
            vstage[k] = p.mark("v")

        for k in range(min(depth, nt)):
            stage(k)
        for k, i in enumerate(tiles):
            sl, dsl = (k % depth) * D, (k % depth) * 128
            p.wait("tensor", "v", vstage[k])
            for half in range(2):
                bi, bank = pool.acquire()
                for q in range(4):
                    f = half * 4 + q
                    p.emit("tensor", lambda e, f=f, q=q, sl=sl, dsl=dsl,
                           bank=bank: e.matmul(
                        bank[:, q * 128:(q + 1) * 128],
                        xc_sb[:, sl + f * 128:sl + (f + 1) * 128],
                        diag_sb[:, dsl:dsl + 128], start=True, stop=True),
                        inc="t" if q == 3 else None)
                if half == 1:
                    last[k % depth] = p.mark("t")
                p.wait("scalar", "t", p.mark("t"))
                for q in range(4):
                    f = half * 4 + q
                    p.emit("scalar", lambda e, f=f, q=q, k=k, bank=bank:
                           e.activation(
                               out=xhT[:, f * n + k * 128:f * n + (k + 1) * 128],
                               in_=bank[:, q * 128:(q + 1) * 128],
                               func=AF.Copy), inc="s")
                pool.release(bi, "s")
            if k + depth < nt:
                stage(k + depth)
    return jit


def spike_residual(p, z_sb, mem_sb, spk_sb, acc_sb, x_tile):
    p.emit("vector", lambda e: e.tensor_scalar(
        out=acc_sb[:], in0=z_sb[:], scalar1=1.0, scalar2=None,
        op0=OP.is_gt), inc="v")
    first = True
    for _ in range(2, NSTEP + 1):
        src = z_sb if first else mem_sb
        prev_spk = acc_sb if first else spk_sb
        p.emit("vector", lambda e, src=src: e.scalar_tensor_tensor(
            out=mem_sb[:], in0=src[:], scalar=BETA, in1=z_sb[:],
            op0=OP.mult, op1=OP.add), inc="v")
        p.emit("vector", lambda e, ps=prev_spk: e.tensor_tensor(
            out=mem_sb[:], in0=mem_sb[:], in1=ps[:], op=OP.subtract), inc="v")
        p.emit("vector", lambda e: e.tensor_scalar(
            out=spk_sb[:], in0=mem_sb[:], scalar1=1.0, scalar2=None,
            op0=OP.is_gt), inc="v")
        p.emit("vector", lambda e: e.tensor_tensor(
            out=acc_sb[:], in0=acc_sb[:], in1=spk_sb[:], op=OP.add), inc="v")
        first = False
    p.emit("vector", lambda e, xt=x_tile: e.scalar_tensor_tensor(
        out=xt, in0=acc_sb[:], scalar=0.1, in1=xt, op0=OP.mult, op1=OP.add),
        inc="v")


def emit_ffn0(p, pool, jit, x_sb, wfc_dram, wproj_dram, sb, dbg=None):
    xhT, h_sb, r_sb, z_sb = sb["xhT"], sb["h"], sb["r"], sb["z"]
    wfc = DBuf(p, sb["wfc"], "sync", "ld")
    wproj = DBuf(p, sb["wproj"], "sync", "ld")
    jit(list(range(4)), x_sb, xhT)
    xh_ready = p.mark("s")
    sq_mark = 0
    spike_mark = 0
    pre_wfc = []
    for tb in range(4):
        fcmode = ("fc" if "fc" in FFN0_PARTS else
                  "fcmm" if "fcmm" in FFN0_PARTS else
                  "fcload" if "fcload" in FFN0_PARTS else "off")
        for hc in range(FC if fcmode != "off" else 0):
            if pre_wfc:
                buf, ldm = pre_wfc.pop(0)
            else:
                buf, ldm = wfc.load(lambda e, buf, hc=hc: e.dma_start(
                    out=buf[:], in_=wfc_dram[hc]))
            p.wait("tensor", "ld", ldm)
            if fcmode == "fcload":
                continue
            if hc == 0:
                p.wait("tensor", "s", xh_ready)
            bi, bank = pool.acquire()
            for f in range(DC):
                p.emit("tensor", lambda e, f=f, buf=buf, bank=bank: e.matmul(
                    bank[:], buf[:, f * 128:(f + 1) * 128],
                    xhT[:, f * 512:(f + 1) * 512],
                    start=(f == 0), stop=(f == DC - 1)),
                    inc="t" if f == DC - 1 else None)
            wfc.consumed()
            if fcmode == "fcmm":
                pool.release(bi, "t")
                continue
            p.wait("scalar", "t", p.mark("t"))
            p.wait("scalar", "v", sq_mark)
            p.emit("scalar", lambda e, bank=bank: e.activation(
                out=r_sb[:], in_=bank[:], func=AF.Relu), inc="s")
            pool.release(bi, "s")
            p.wait("vector", "s", p.mark("s"))
            p.emit("vector", lambda e, hc=hc: e.tensor_mul(
                out=h_sb[:, hc * 512:(hc + 1) * 512], in0=r_sb[:],
                in1=r_sb[:]), inc="v")
            sq_mark = p.mark("v")
        h_ready = p.mark("v")
        if tb < 3:
            jit(list(range(tb * 4 + 4, tb * 4 + 8)), x_sb, xhT)
            next_xh = p.mark("s")
        if "proj" not in FFN0_PARTS:
            continue
        for k in range(8):
            bi, bank = pool.acquire()
            assert bi == k % len(pool.banks)
        banks = pool.banks
        p.wait("tensor", "v", h_ready)
        for hc in range(FC):
            buf, ldm = wproj.load(lambda e, buf, hc=hc: e.dma_start(
                out=buf[:], in_=wproj_dram[hc]))
            p.wait("tensor", "ld", ldm)
            for sub in range(4):
                for od in range(2):
                    bank = banks[sub * 2 + od]
                    p.emit("tensor",
                           lambda e, hc=hc, sub=sub, od=od, bank=bank,
                           buf=buf: e.matmul(
                               bank[:],
                               h_sb[:, hc * 512 + sub * 128:
                                    hc * 512 + (sub + 1) * 128],
                               buf[:, od * 512:(od + 1) * 512],
                               start=(hc == 0), stop=(hc == FC - 1)),
                           inc="t" if (sub == 3 and od == 1) else None)
            wproj.consumed()
        if dbg is not None and tb == 0:
            p.wait("gpsimd", "s", p.mark("s"))
            p.wait("gpsimd", "v", p.mark("v"))
            p.emit("gpsimd", lambda e: e.dma_start(out=dbg[0], in_=xhT),
                   inc="st", n=16, raw=True)
            p.emit("gpsimd", lambda e: e.dma_start(out=dbg[1], in_=h_sb),
                   inc="st", n=16, raw=True)
        proj_done = p.mark("t")
        p.wait("scalar", "t", proj_done)
        for sub in range(4):
            tile = tb * 4 + sub
            p.wait("scalar", "v", spike_mark if sub == 0 else p.mark("v"))
            for od in range(2):
                bank = banks[sub * 2 + od]
                p.emit("scalar", lambda e, od=od, bank=bank: e.activation(
                    out=z_sb[:, od * 512:(od + 1) * 512], in_=bank[:],
                    func=AF.Copy), inc="s")
            p.wait("vector", "s", p.mark("s"))
            if "spike" in FFN0_PARTS:
                spike_residual(p, z_sb, sb["mem"], sb["spk"], sb["acc"],
                               x_sb[:, tile * D:(tile + 1) * D])
        spike_mark = p.mark("v")
        for k in range(8):
            pool.release(k, "s")
        if tb < 3:
            xh_ready = next_xh
    p.barrier()



L1_UPTO = "all"
FFN0_PARTS = {"fc", "proj", "spike"}


def build_l1():
    nc = bass.Bass("TRN2", target_bir_lowering=False,
                   detect_race_conditions=False)
    x0 = nc.dram_tensor("x0", [TOK, D], F32, kind="ExternalInput").ap()
    wfc = nc.dram_tensor("wfc", [FC, 128, D], F32, kind="ExternalInput").ap()
    wproj = nc.dram_tensor("wproj", [FC, 128, D], F32,
                           kind="ExternalInput").ap()
    wkT = nc.dram_tensor("wkT", [DC, 128, D], F32, kind="ExternalInput").ap()
    bk = nc.dram_tensor("bk", [D], F32, kind="ExternalInput").ap()
    x1_o = nc.dram_tensor("x1", [TOK, D], F32, kind="ExternalOutput").ap()
    kT_o = nc.dram_tensor("kT", [D, TOK], F32, kind="ExternalOutput").ap()
    diag_o = nc.dram_tensor("diag", [TOK, H], F32, kind="ExternalOutput").ap()
    lmax_o = nc.dram_tensor("lmax", [1, 1], F32, kind="ExternalOutput").ap()
    if DBG:
        dbg_mean = nc.dram_tensor("dbg_mean", [128, NT], F32,
                                  kind="ExternalOutput").ap()
        dbg_rstd = nc.dram_tensor("dbg_rstd", [128, NT], F32,
                                  kind="ExternalOutput").ap()
        dbg_xh = nc.dram_tensor("dbg_xh", [128, DC * 512], F32,
                                kind="ExternalOutput").ap()
        dbg_h = nc.dram_tensor("dbg_h", [128, FC * 512], F32,
                               kind="ExternalOutput").ap()
    eye_d = nc.inline_tensor(np.eye(128, dtype=np.float32), "eye").ap()
    hmask = np.zeros((128, 2), dtype=np.float32)
    hmask[:64, 0] = 1.0
    hmask[64:, 1] = 1.0
    ones_d = nc.inline_tensor(hmask, "hmask").ap()

    with ExitStack() as stack:
        ec = stack.enter_context
        p = Prog(nc, stack)
        x_sb = ec(nc.sbuf_tensor("x_sb", [128, NT * D], F32))
        mean_sb = ec(nc.sbuf_tensor("mean_sb", [128, NT], F32))
        rstd_sb = ec(nc.sbuf_tensor("rstd_sb", [128, NT], F32))
        sq_sb = ec(nc.sbuf_tensor("sq_sb", [128, D], F32))
        col0 = ec(nc.sbuf_tensor("col0", [128, 1], F32))
        eye_sb = ec(nc.sbuf_tensor("eye_sb", [128, 128], F32))
        setup_consts(nc, p, ec)
        ones_sb = ec(nc.sbuf_tensor("ones_sb", [128, 2], F32))
        xc_sb = ec(nc.sbuf_tensor("xc_sb", [128, 2 * D], F32))
        dg_sb = ec(nc.sbuf_tensor("dg_sb", [128, 2 * 128], F32))
        diag_sb = ec(nc.sbuf_tensor("diag_sb", [128, NT * H], F32))
        bk_sb = ec(nc.sbuf_tensor("bk_sb", [128, DC], F32))
        banks = [ec(nc.psum_tensor(f"ps{i}", [128, 512], F32))
                 for i in range(8)]
        pool = PsumPool(p, [b[:] for b in banks])
        jit = make_jit_xhat(p, pool, mean_sb[:], rstd_sb[:], eye_sb[:],
                            xc_sb[:], dg_sb[:], depth=2)

        p.emit("sync", lambda e: e.dma_start(
            out=x_sb[:].rearrange("p (i d) -> p i d", i=NT),
            in_=x0.rearrange("(i p) d -> p i d", p=128)), inc="ld", n=16)
        p.emit("sync", lambda e: e.dma_start(out=eye_sb[:], in_=eye_d),
               inc="ld", n=16)
        p.emit("sync", lambda e: e.dma_start(out=ones_sb[:], in_=ones_d),
               inc="ld", n=16)
        p.emit("sync", lambda e: e.dma_start(
            out=bk_sb[:], in_=bk.rearrange("(c p) -> p c", p=128),
            allow_slow_non_contiguous=True), inc="ld", n=16)
        ldm = p.mark("ld")
        for e in ("tensor", "vector", "scalar"):
            p.wait(e, "ld", ldm)

        ln_stats(p, x_sb[:], mean_sb[:], rstd_sb[:], sq_sb[:], col0[:],
                 range(NT))
        p.barrier()
        if DBG:
            p.emit("gpsimd", lambda e: e.dma_start(out=dbg_mean,
                                                   in_=mean_sb[:]),
                   inc="st", n=16, raw=True)
            p.emit("gpsimd", lambda e: e.dma_start(out=dbg_rstd,
                                                   in_=rstd_sb[:]),
                   inc="st", n=16, raw=True)
        with nc.sbuf_tensor("xhT", [128, DC * 512], F32) as xhT, \
                nc.sbuf_tensor("h_sb", [128, FC * 512], F32) as h_sb, \
                nc.sbuf_tensor("r_sb", [128, 512], F32) as r_sb, \
                nc.sbuf_tensor("z_sb", [128, D], F32) as z_sb, \
                nc.sbuf_tensor("mem_sb", [128, D], F32) as mem_sb, \
                nc.sbuf_tensor("spk_sb", [128, D], F32) as spk_sb, \
                nc.sbuf_tensor("acc_sb", [128, D], F32) as acc_sb, \
                nc.sbuf_tensor("wfc0", [128, D], F32) as wfc0, \
                nc.sbuf_tensor("wfc1", [128, D], F32) as wfc1, \
                nc.sbuf_tensor("wp0", [128, D], F32) as wp0, \
                nc.sbuf_tensor("wp1", [128, D], F32) as wp1:
            sb = dict(xhT=xhT[:], h=h_sb[:], r=r_sb[:], z=z_sb[:],
                      mem=mem_sb[:], spk=spk_sb[:], acc=acc_sb[:],
                      wfc=[wfc0[:], wfc1[:]], wproj=[wp0[:], wp1[:]])
            emit_ffn0(p, pool, jit, x_sb[:], wfc, wproj, sb,
                      dbg=(dbg_xh, dbg_h) if DBG else None)

        p.emit("gpsimd", lambda e: e.dma_start(
            out=x1_o.rearrange("(i p) d -> p i d", p=128),
            in_=x_sb[:].rearrange("p (i d) -> p i d", i=NT)), inc="st", n=16, raw=True)

        if L1_UPTO == "all":
            emit_l1_kphase(nc, p, pool, jit, x_sb, mean_sb, rstd_sb, sq_sb,
                           col0, eye_sb, ones_sb, bk_sb, diag_sb, wkT,
                           kT_o, diag_o, lmax_o)
        p.barrier()
        p.wait("gpsimd", "st", p.cnt["st"])
        p.build()
    return nc


def emit_l1_kphase(nc, p, pool, jit, x_sb, mean_sb, rstd_sb, sq_sb, col0,
                   eye_sb, ones_sb, bk_sb, diag_sb, wkT, kT_o, diag_o,
                   lmax_o):
        ln_stats(p, x_sb[:], mean_sb[:], rstd_sb[:], sq_sb[:], col0[:],
                 range(NT))
        p.barrier()
        with nc.sbuf_tensor("xhT2", [128, DC * 512], F32) as xhT2, \
                nc.sbuf_tensor("kblk", [128, DC * 512], F32) as kblk, \
                nc.sbuf_tensor("k2", [128, DC * 512], F32) as k2, \
                nc.sbuf_tensor("wk0", [128, D], F32) as wk0, \
                nc.sbuf_tensor("wk1", [128, D], F32) as wk1, \
                nc.sbuf_tensor("mrow", [1, 129], F32) as mrow:
            wk = DBuf(p, [wk0[:], wk1[:]], "sync", "ld")
            st_prev = 0
            for tb in range(4):
                tiles = list(range(tb * 4, tb * 4 + 4))
                jit(tiles, x_sb[:], xhT2[:])
                xh_ready = p.mark("s")
                p.wait("scalar", "st", st_prev)
                p.wait("scalar", "t", p.mark("t"))
                for oc in range(DC):
                    buf, ldm = wk.load(lambda e, buf, oc=oc: e.dma_start(
                        out=buf[:], in_=wkT[oc]))
                    p.wait("tensor", "ld", ldm)
                    if oc == 0:
                        p.wait("tensor", "s", xh_ready)
                    bi, bank = pool.acquire()
                    for f in range(DC):
                        p.emit("tensor",
                               lambda e, f=f, buf=buf, bank=bank: e.matmul(
                                   bank[:], buf[:, f * 128:(f + 1) * 128],
                                   xhT2[:, f * 512:(f + 1) * 512],
                                   start=(f == 0), stop=(f == DC - 1)),
                               inc="t" if f == DC - 1 else None)
                    wk.consumed()
                    p.wait("scalar", "t", p.mark("t"))
                    p.emit("scalar", lambda e, oc=oc, bank=bank: e.activation(
                        out=kblk[:, oc * 512:(oc + 1) * 512], in_=bank[:],
                        func=AF.Identity, bias=bk_sb[:, oc:oc + 1]), inc="s")
                    pool.release(bi, "s")
                p.wait("gpsimd", "s", p.mark("s"))
                p.emit("gpsimd", lambda e, tb=tb: e.dma_start(
                    out=kT_o.rearrange("(c p) t -> p c t", p=128)
                        [:, :, tb * 512:(tb + 1) * 512],
                    in_=kblk[:].rearrange("p (c q) -> p c q", c=DC)),
                    inc="st", n=16, raw=True)
                st_prev = p.mark("st")
                p.emit("scalar", lambda e: e.activation(
                    out=k2[:], in_=kblk[:], func=AF.Square), inc="s")
                p.wait("tensor", "s", p.mark("s"))
                for sub in range(4):
                    tile = tb * 4 + sub
                    bi, bank = pool.acquire()
                    for h in range(H):
                        oc, half = h // 2, (h % 2)
                        p.emit("tensor",
                               lambda e, h=h, oc=oc, half=half, sub=sub,
                               bank=bank: e.matmul(
                                   bank[:, h:h + 1],
                                   k2[:, oc * 512 + sub * 128:
                                      oc * 512 + (sub + 1) * 128],
                                   ones_sb[:, half:half + 1],
                                   start=True, stop=True),
                               inc="t" if h == H - 1 else None)
                    p.wait("vector", "t", p.mark("t"))
                    p.emit("vector", lambda e, tile=tile, bank=bank:
                           e.tensor_scalar(
                               out=diag_sb[:, tile * H:(tile + 1) * H],
                               in0=bank[:, 0:H], scalar1=-0.5, scalar2=None,
                               op0=OP.mult), inc="v")
                    pool.release(bi, "v")
            p.barrier()
            p.emit("vector", lambda e: e.tensor_reduce(
                out=col0[:], in_=diag_sb[:], axis=AX.X, op=OP.max), inc="v")
            p.wait("tensor", "v", p.mark("v"))
            bi, bank = pool.acquire()
            p.emit("tensor", lambda e, bank=bank: e.matmul(
                bank[0:1, 0:128], col0[:], eye_sb[:], start=True, stop=True),
                inc="t")
            p.wait("scalar", "t", p.mark("t"))
            p.emit("scalar", lambda e, bank=bank: e.activation(
                out=mrow[0:1, 0:128], in_=bank[0:1, 0:128], func=AF.Copy),
                inc="s")
            pool.release(bi, "s")
            p.wait("vector", "s", p.mark("s"))
            p.emit("vector", lambda e: e.tensor_reduce(
                out=mrow[0:1, 128:129], in_=mrow[0:1, 0:128], axis=AX.X,
                op=OP.max), inc="v")
            p.wait("gpsimd", "v", p.mark("v"))
            p.emit("gpsimd", lambda e: e.dma_start(
                out=lmax_o, in_=mrow[0:1, 128:129]), inc="st", n=16, raw=True)
            p.emit("gpsimd", lambda e: e.dma_start(
                out=diag_o.rearrange("(i p) h -> p i h", p=128),
                in_=diag_sb[:].rearrange("p (i h) -> p i h", i=NT)),
                inc="st", n=16, raw=True)



def pack_lhsT(w):
    o, i = w.shape
    return np.ascontiguousarray(
        w.reshape(o // 128, 128, i // 128, 128).transpose(0, 3, 2, 1)
        .reshape(o // 128, 128, i))


def pack_rhs(w):
    o, i = w.shape
    return np.ascontiguousarray(w.T.reshape(i // 128, 128, o))


def stage_l1(inputs):
    hs = np.ascontiguousarray(inputs["L0_attn_hidden_states"],
                              dtype=np.float32)
    com = {
        "wfc": pack_lhsT(np.asarray(inputs["L0_ffn_W_fc"], np.float32)),
        "wproj": pack_rhs(np.asarray(inputs["L0_ffn_W_proj"], np.float32)),
        "wkT": pack_lhsT(np.asarray(inputs["L0_attn_Wk"], np.float32)),
        "bk": np.asarray(inputs["L0_attn_bk"], np.float32),
    }
    return [dict(com, x0=np.ascontiguousarray(hs[b])) for b in range(NCORE)]


def run_l1(inputs):
    nc = build_l1()
    res = run_bass_kernel_spmd(nc, stage_l1(inputs),
                               core_ids=list(range(NCORE)))
    return res.results


def _run(build, maps):
    nc = build()
    return run_bass_kernel_spmd(nc, maps, core_ids=list(range(NCORE))).results


def kernel(**inputs):
    r1 = _run(build_l1, stage_l1(inputs))
    r2 = _run(build_l2, stage_l2(inputs, r1))
    r3 = _run(build_l3, stage_l3(inputs, r2))
    r4 = _run(build_l4, stage_l4(inputs, r3))
    out = np.empty((B, S, D), np.float32)
    for j in range(NCORE):
        out[:, j * 256:(j + 1) * 256, :] = \
            np.asarray(r4[j]["x6"]).reshape(B, 256, D)
    return out



def build_l2():
    nc = bass.Bass("TRN2", target_bir_lowering=False,
                   detect_race_conditions=False)
    x1 = nc.dram_tensor("x1", [TOK, D], F32, kind="ExternalInput").ap()
    kT_i = nc.dram_tensor("kT", [D, TOK], F32, kind="ExternalInput").ap()
    diag_i = nc.dram_tensor("diag", [TOK, H], F32, kind="ExternalInput").ap()
    maxes = nc.dram_tensor("maxes", [1, NCORE], F32, kind="ExternalInput").ap()
    wqT = nc.dram_tensor("wqT", [DC, 128, D], F32, kind="ExternalInput").ap()
    wvT = nc.dram_tensor("wvT", [DC, 128, D], F32, kind="ExternalInput").ap()
    woT = nc.dram_tensor("woT", [DC, 128, D], F32, kind="ExternalInput").ap()
    projT = nc.dram_tensor("projT", [DH, PM], F32, kind="ExternalInput").ap()
    bq = nc.dram_tensor("bq", [D], F32, kind="ExternalInput").ap()
    bv = nc.dram_tensor("bv", [D], F32, kind="ExternalInput").ap()
    bo = nc.dram_tensor("bo", [D], F32, kind="ExternalInput").ap()
    x2_o = nc.dram_tensor("x2", [TOK, D], F32, kind="ExternalOutput").ap()
    if DBG:
        dbg_ctx = nc.dram_tensor("dbg_ctx", [128, H * 2 * 65], F32,
                                 kind="ExternalOutput").ap()
        dbg_q = nc.dram_tensor("dbg_q", [128, DC * 512], F32,
                               kind="ExternalOutput").ap()
        dbg_v = nc.dram_tensor("dbg_v", [128, 4 * (H * 65 + 1)], F32,
                               kind="ExternalOutput").ap()
        dbg_kf = nc.dram_tensor("dbg_kf", [128, PM], F32,
                                kind="ExternalOutput").ap()
    eye_d = nc.inline_tensor(np.eye(128, dtype=np.float32), "eye").ap()
    onesrow_d = nc.inline_tensor(np.ones((1, 128), dtype=np.float32),
                                 "onesrow").ap()

    with ExitStack() as stack:
        ec = stack.enter_context
        p = Prog(nc, stack)
        x_sb = ec(nc.sbuf_tensor("x_sb", [128, NT * D], F32))
        mean_sb = ec(nc.sbuf_tensor("mean_sb", [128, NT], F32))
        rstd_sb = ec(nc.sbuf_tensor("rstd_sb", [128, NT], F32))
        sq_sb = ec(nc.sbuf_tensor("sq_sb", [128, D], F32))
        col0 = ec(nc.sbuf_tensor("col0", [128, 1], F32))
        eye_sb = ec(nc.sbuf_tensor("eye_sb", [128, 128], F32))
        ones_row = ec(nc.sbuf_tensor("ones_row", [1, 128], F32))
        setup_consts(nc, p, ec)
        xc_sb = ec(nc.sbuf_tensor("xc_sb", [128, D], F32))
        dg_sb = ec(nc.sbuf_tensor("dg_sb", [128, 128], F32))
        bias_sb = ec(nc.sbuf_tensor("bias_sb", [128, NT * H], F32))
        stab_sb = ec(nc.sbuf_tensor("stab_sb", [128, 1], F32))
        mrow = ec(nc.sbuf_tensor("mrow", [1, NCORE + 1], F32))
        projc = ec(nc.sbuf_tensor("projc", [128, 2 * PM], F32))
        bqc = ec(nc.sbuf_tensor("bqc", [128, DC], F32))
        bv_row = ec(nc.sbuf_tensor("bv_row", [1, D], F32))
        bo_row = ec(nc.sbuf_tensor("bo_row", [1, D], F32))
        diag_sb = ec(nc.sbuf_tensor("diag_sb", [128, NT * H], F32))
        ctx_sb = ec(nc.sbuf_tensor("ctx_sb", [128, H * 2 * 65], F32))
        xhT = ec(nc.sbuf_tensor("xhT", [128, DC * 512], F32))
        banks = [ec(nc.psum_tensor(f"ps{i}", [128, 512], F32))
                 for i in range(8)]
        pool = PsumPool(p, [b[:] for b in banks])
        jit = make_jit_xhat(p, pool, mean_sb[:], rstd_sb[:], eye_sb[:],
                            xc_sb[:], dg_sb[:])

        p.emit("sync", lambda e: e.dma_start(
            out=x_sb[:].rearrange("p (i d) -> p i d", i=NT),
            in_=x1.rearrange("(i p) d -> p i d", p=128)), inc="ld", n=16)
        p.emit("sync", lambda e: e.dma_start(out=eye_sb[:], in_=eye_d),
               inc="ld", n=16)
        p.emit("sync", lambda e: e.dma_start(out=ones_row[:], in_=onesrow_d),
               inc="ld", n=16)
        p.emit("sync", lambda e: e.dma_start(
            out=diag_sb[:].rearrange("p (i h) -> p i h", i=NT),
            in_=diag_i.rearrange("(i p) h -> p i h", p=128)), inc="ld", n=16)
        p.emit("sync", lambda e: e.dma_start(out=mrow[0:1, 0:NCORE],
                                             in_=maxes), inc="ld", n=16)
        for v in range(2):
            for mc in range(2):
                p.emit("sync", lambda e, v=v, mc=mc: e.dma_start(
                    out=projc[v * 64:v * 64 + 64,
                              v * PM + mc * 128:v * PM + (mc + 1) * 128],
                    in_=projT[:, mc * 128:(mc + 1) * 128]), inc="ld", n=16)
        p.emit("sync", lambda e: e.dma_start(
            out=bqc[:], in_=bq.rearrange("(c p) -> p c", p=128),
            allow_slow_non_contiguous=True), inc="ld", n=16)
        p.emit("sync", lambda e: e.dma_start(
            out=bv_row[:], in_=bv.rearrange("(a d) -> a d", a=1)),
            inc="ld", n=16)
        p.emit("sync", lambda e: e.dma_start(
            out=bo_row[:], in_=bo.rearrange("(a d) -> a d", a=1)),
            inc="ld", n=16)
        ldm = p.mark("ld")
        for e in ("tensor", "vector", "scalar", "gpsimd"):
            p.wait(e, "ld", ldm)

        DNORM = float(DH) ** -0.25
        p.emit("vector", lambda e: e.memset(projc[64:128, 0:PM], 0.0))
        p.emit("vector", lambda e: e.memset(projc[0:64, PM:2 * PM], 0.0))
        p.emit("scalar", lambda e: e.activation(
            out=projc[:], in_=projc[:], func=AF.Copy, scale=DNORM))
        p.wait("scalar", "v", p.mark("v"))
        p.emit("vector", lambda e: e.tensor_reduce(
            out=mrow[0:1, NCORE:NCORE + 1], in_=mrow[0:1, 0:NCORE],
            axis=AX.X, op=OP.max))
        p.wait("tensor", "v", p.mark("v"))
        bi0, bank0 = pool.acquire()
        p.emit("tensor", lambda e, bank=bank0: e.matmul(
            bank[0:128, 0:1], ones_row[:], mrow[0:1, NCORE:NCORE + 1],
            start=True, stop=True), inc="t")
        p.wait("scalar", "t", p.mark("t"))
        p.emit("scalar", lambda e, bank=bank0: e.activation(
            out=stab_sb[:], in_=bank[0:128, 0:1], func=AF.Copy))
        pool.release(bi0, "s")
        p.wait("vector", "s", p.mark("s"))
        p.emit("vector", lambda e: e.tensor_scalar(
            out=bias_sb[:], in0=diag_sb[:], scalar1=stab_sb[:, 0:1],
            scalar2=None, op0=OP.subtract))
        p.emit("vector", lambda e: e.memset(ctx_sb[:], 0.0))
        ln_stats(p, x_sb[:], mean_sb[:], rstd_sb[:], sq_sb[:], col0[:],
                 range(NT))
        p.barrier()

        with nc.sbuf_tensor("wvs", [128, DC * D], F32) as wvs, \
                nc.sbuf_tensor("vblk", [128, 4 * H * 65 + 4], F32) as vblk, \
                nc.sbuf_tensor("kTb", [128, DC * 512], F32) as kTb, \
                nc.sbuf_tensor("kf_sb", [128, PM], F32) as kf_sb:
            VW = H * 65 + 1
            p.emit("sync", lambda e: e.dma_start(
                out=wvs[:].rearrange("p (c d) -> p c d", c=DC),
                in_=wvT.rearrange("c p d -> p c d")), inc="ld", n=16)
            wv_ld = p.mark("ld")
            for tb in range(4):
                tiles = list(range(tb * 4, tb * 4 + 4))
                if tb > 0:
                    p.wait("sync", "t", p.mark("t"))
                p.emit("sync", lambda e, tb=tb: e.dma_start(
                    out=kTb[:].rearrange("p (c q) -> p c q", c=DC),
                    in_=kT_i.rearrange("(c p) t -> p c t", p=128)
                        [:, :, tb * 512:(tb + 1) * 512]), inc="ld", n=16)
                ktb_ld = p.mark("ld")
                jit(tiles, x_sb[:], xhT[:])
                xh_ready = p.mark("s")
                p.wait("vector", "t", p.mark("t"))
                for sub in range(4):
                    p.emit("vector", lambda e, sub=sub: e.memset(
                        vblk[:, sub * VW + 64:sub * VW + H * 65:65], 1.0))
                ones_mark = p.mark("v")
                p.wait("tensor", "ld", wv_ld)
                p.wait("tensor", "s", xh_ready)
                for sub in range(4):
                    for ob in range(2):
                        bi, bank = pool.acquire()
                        for f in range(DC):
                            p.emit("tensor", lambda e, f=f, sub=sub, ob=ob,
                                   bank=bank: e.matmul(
                                bank[:],
                                xhT[:, f * 512 + sub * 128:
                                    f * 512 + (sub + 1) * 128],
                                wvs[:, f * D + ob * 512:
                                    f * D + (ob + 1) * 512],
                                start=(f == 0), stop=False))
                        p.emit("tensor", lambda e, ob=ob, bank=bank: e.matmul(
                            bank[:], ones_row[:],
                            bv_row[0:1, ob * 512:(ob + 1) * 512],
                            start=False, stop=True), inc="t")
                        p.wait("scalar", "t", p.mark("t"))
                        if sub == 0 and ob == 0:
                            p.wait("scalar", "v", ones_mark)
                        p.emit("scalar", lambda e, sub=sub, ob=ob, bank=bank:
                               e.activation(
                            out=vblk[:, sub * VW + ob * 8 * 65:
                                     sub * VW + ob * 8 * 65 + 8 * 65]
                            .rearrange("p (h e) -> p h e", h=8)
                            [:, :, 0:64],
                            in_=bank[:].rearrange("p (h e) -> p h e", h=8),
                            func=AF.Copy), inc="s")
                        pool.release(bi, "s")
                v_ready = p.mark("s")
                p.wait("tensor", "ld", ktb_ld)
                for h in range(H):
                    hp, hv = h // 2, h % 2
                    for sub in range(4):
                        bi, bank = pool.acquire()
                        p.emit("tensor", lambda e, hp=hp, hv=hv, sub=sub,
                               bank=bank: e.matmul(
                            bank[:, 0:PM],
                            kTb[:, hp * 512 + sub * 128:
                                hp * 512 + (sub + 1) * 128],
                            projc[:, hv * PM:(hv + 1) * PM],
                            start=True, stop=True), inc="t")
                        p.wait("scalar", "t", p.mark("t"))
                        tile = tb * 4 + sub
                        p.emit("scalar", lambda e, tile=tile, h=h,
                               bank=bank: e.activation(
                            out=kf_sb[:, 0:PM], in_=bank[:, 0:PM],
                            func=AF.Exp,
                            bias=bias_sb[:, tile * H + h:tile * H + h + 1]),
                            inc="s")
                        pool.release(bi, "s")
                        p.wait("vector", "s", p.mark("s"))
                        p.emit("vector", lambda e: e.tensor_scalar(
                            out=kf_sb[:, 0:PM], in0=kf_sb[:, 0:PM],
                            scalar1=KEPS, scalar2=None, op0=OP.add))
                        kf_ready = p.mark("v")
                        p.wait("tensor", "v", kf_ready)
                        p.wait("tensor", "s", v_ready)
                        for mc in range(2):
                            bi2, bank2 = pool.acquire()
                            p.emit("tensor", lambda e, mc=mc, h=h, sub=sub,
                                   bank2=bank2: e.matmul(
                                bank2[:, 0:65],
                                kf_sb[:, mc * 128:(mc + 1) * 128],
                                vblk[:, sub * VW + h * 65:
                                     sub * VW + (h + 1) * 65],
                                start=True, stop=True), inc="t")
                            p.wait("vector", "t", p.mark("t"))
                            p.emit("vector", lambda e, h=h, mc=mc,
                                   bank2=bank2: e.tensor_tensor(
                                out=ctx_sb[:, (h * 2 + mc) * 65:
                                           (h * 2 + mc + 1) * 65],
                                in0=ctx_sb[:, (h * 2 + mc) * 65:
                                           (h * 2 + mc + 1) * 65],
                                in1=bank2[:, 0:65], op=OP.add))
                            pool.release(bi2, "v")
            if DBG:
                p.wait("gpsimd", "v", p.mark("v"))
                p.wait("gpsimd", "s", p.mark("s"))
                p.emit("gpsimd", lambda e: e.dma_start(out=dbg_v,
                                                       in_=vblk[:]),
                       inc="st", n=16, raw=True)
                p.emit("gpsimd", lambda e: e.dma_start(out=dbg_kf,
                                                       in_=kf_sb[:]),
                       inc="st", n=16, raw=True)
                p.emit("gpsimd", lambda e: e.dma_start(out=dbg_ctx,
                                                       in_=ctx_sb[:]),
                       inc="st", n=16, raw=True)
                p.wait("gpsimd", "st", p.cnt["st"])
        p.barrier()

        with nc.sbuf_tensor("wqs", [128, DC * D], F32) as wqs, \
                nc.sbuf_tensor("qTb", [128, DC * 512], F32) as qTb, \
                nc.sbuf_tensor("qf_sb", [128, 2 * 512], F32) as qf_sb, \
                nc.sbuf_tensor("on_sb", [128, 128], F32) as on_sb, \
                nc.sbuf_tensor("den_sb", [128, 512], F32) as den_sb, \
                nc.sbuf_tensor("thi", [128, DC * 512], F32) as thi, \
                nc.sbuf_tensor("outT", [128, DC * 512], F32) as outT, \
                nc.sbuf_tensor("wo0", [128, D], F32) as wo0, \
                nc.sbuf_tensor("wo1", [128, D], F32) as wo1:
            p.emit("sync", lambda e: e.dma_start(
                out=wqs[:].rearrange("p (c d) -> p c d", c=DC),
                in_=wqT.rearrange("c p d -> p c d")), inc="ld", n=16)
            wq_ld = p.mark("ld")
            wo = DBuf(p, [wo0[:], wo1[:]], "sync", "ld")
            p.emit("vector", lambda e: e.memset(on_sb[:], 1.0))
            for tb in range(4):
                tiles = list(range(tb * 4, tb * 4 + 4))
                jit(tiles, x_sb[:], xhT[:])
                xh_ready = p.mark("s")
                p.wait("tensor", "ld", wq_ld)
                p.wait("tensor", "s", xh_ready)
                for hp in range(DC):
                    bi, bank = pool.acquire()
                    for f in range(DC):
                        p.emit("tensor", lambda e, f=f, hp=hp, bank=bank:
                               e.matmul(
                            bank[:], wqs[:, hp * D + f * 128:
                                         hp * D + (f + 1) * 128],
                            xhT[:, f * 512:(f + 1) * 512],
                            start=(f == 0), stop=(f == DC - 1)), inc="t")
                    p.wait("scalar", "t", p.mark("t"))
                    p.emit("scalar", lambda e, hp=hp, bank=bank: e.activation(
                        out=qTb[:, hp * 512:(hp + 1) * 512], in_=bank[:],
                        func=AF.Identity, bias=bqc[:, hp:hp + 1]), inc="s")
                    pool.release(bi, "s")
                q_ready = p.mark("s")

                def emit_dash(i):
                    hp, hv = divmod(i, 2)
                    sbanks = []
                    p.wait("tensor", "s", q_ready)
                    for mc in range(2):
                        bi, bank = pool.acquire()
                        sbanks.append((bi, bank))
                        p.emit("tensor", lambda e, hp=hp, hv=hv, mc=mc,
                               bank=bank: e.matmul(
                            bank[:],
                            projc[:, hv * PM + mc * 128:
                                  hv * PM + (mc + 1) * 128],
                            qTb[:, hp * 512:(hp + 1) * 512],
                            start=True, stop=True),
                            inc="t" if mc == 1 else None)
                    return sbanks, p.mark("t")

                pend = emit_dash(0)
                num_mark = pend[1]
                mult_mark = 0
                for i in range(2 * DC):
                    hp, hv = divmod(i, 2)
                    h = hp * 2 + hv
                    qf = qf_sb[:]
                    sbanks, dmark = pend
                    if i + 1 < 2 * DC:
                        pend = emit_dash(i + 1)
                    p.wait("scalar", "t", num_mark)
                    for mc in range(2):
                        bi, bank = sbanks[mc]
                        p.emit("scalar", lambda e, mc=mc, qf=qf, bank=bank:
                               e.activation(
                            out=qf[:, mc * 512:(mc + 1) * 512],
                            in_=bank[:], func=AF.Exp), inc="s")
                        pool.release(bi, "s")
                    p.wait("vector", "s", p.mark("s"))
                    p.emit("vector", lambda e, qf=qf: e.tensor_scalar(
                        out=qf, in0=qf, scalar1=KEPS,
                        scalar2=None, op0=OP.add))
                    qf_ready = p.mark("v")
                    p.wait("tensor", "v", qf_ready)
                    bi, bank = pool.acquire()
                    for mc in range(2):
                        p.emit("tensor", lambda e, mc=mc, h=h, qf=qf,
                               bank=bank: e.matmul(
                            bank[0:65, :],
                            ctx_sb[:, (h * 2 + mc) * 65:
                                   (h * 2 + mc + 1) * 65],
                            qf[:, mc * 512:(mc + 1) * 512],
                            start=(mc == 0), stop=(mc == 1)),
                            inc="t" if mc == 1 else None)
                    num_mark = p.mark("t")
                    p.wait("vector", "t", num_mark)
                    p.emit("vector", lambda e, bank=bank: e.reciprocal(
                        out=den_sb[64:65, :], in_=bank[64:65, :]))
                    p.wait("tensor", "v", p.mark("v"))
                    bi2, bank2 = pool.acquire()
                    p.emit("tensor", lambda e, bank2=bank2: e.matmul(
                        bank2[0:64, :], on_sb[64:65, 0:64],
                        den_sb[64:65, :], start=True, stop=True), inc="t")
                    p.wait("scalar", "t", p.mark("t"))
                    p.wait("scalar", "v", mult_mark)
                    p.emit("scalar", lambda e, bank2=bank2: e.activation(
                        out=den_sb[0:64, :], in_=bank2[0:64, :],
                        func=AF.Copy), inc="s")
                    pool.release(bi2, "s")
                    p.wait("vector", "s", p.mark("s"))
                    acc = outT if hv == 0 else thi
                    p.emit("vector", lambda e, hp=hp, acc=acc, bank=bank:
                           e.tensor_tensor(
                        out=acc[0:64, hp * 512:(hp + 1) * 512],
                        in0=bank[0:64, :], in1=den_sb[0:64, :],
                        op=OP.mult))
                    mult_mark = p.mark("v")
                    pool.release(bi, "v")
                p.wait("sync", "v", p.mark("v"))
                p.emit("sync", lambda e: e.dma_start(
                    out=outT[64:128, :], in_=thi[0:64, :]), inc="ld", n=16)
                p.wait("tensor", "ld", p.mark("ld"))
                out_ready = p.mark("v")
                if DBG and tb == 3:
                    p.wait("gpsimd", "s", p.mark("s"))
                    p.emit("gpsimd", lambda e: e.dma_start(out=dbg_q,
                                                           in_=qTb[:]),
                           inc="st", n=16, raw=True)
                p.wait("tensor", "v", out_ready)
                for k in range(8):
                    pool.acquire()
                zb = pool.banks
                for ic in range(DC):
                    buf, ldm2 = wo.load(lambda e, buf, ic=ic: e.dma_start(
                        out=buf[:], in_=woT[ic]))
                    p.wait("tensor", "ld", ldm2)
                    for sub in range(4):
                        for ob in range(2):
                            p.emit("tensor", lambda e, ic=ic, sub=sub, ob=ob,
                                   buf=buf: e.matmul(
                                zb[sub * 2 + ob][:],
                                outT[:, ic * 512 + sub * 128:
                                     ic * 512 + (sub + 1) * 128],
                                buf[:, ob * 512:(ob + 1) * 512],
                                start=(ic == 0), stop=False),
                                inc="t" if (sub == 3 and ob == 1) else None)
                    wo.consumed()
                for sub in range(4):
                    for ob in range(2):
                        p.emit("tensor", lambda e, sub=sub, ob=ob: e.matmul(
                            zb[sub * 2 + ob][:], ones_row[:],
                            bo_row[0:1, ob * 512:(ob + 1) * 512],
                            start=False, stop=True),
                            inc="t" if (sub == 3 and ob == 1) else None)
                wo_done = p.mark("t")
                p.wait("vector", "t", wo_done)
                for sub in range(4):
                    tile = tb * 4 + sub
                    for ob in range(2):
                        p.emit("vector", lambda e, tile=tile, sub=sub, ob=ob:
                               e.scalar_tensor_tensor(
                            out=x_sb[:, tile * D + ob * 512:
                                     tile * D + (ob + 1) * 512],
                            in0=zb[sub * 2 + ob][:], scalar=1.0,
                            in1=x_sb[:, tile * D + ob * 512:
                                     tile * D + (ob + 1) * 512],
                            op0=OP.mult, op1=OP.add))
                for k in range(8):
                    pool.release(k, "v")
        p.barrier()
        p.emit("gpsimd", lambda e: e.dma_start(
            out=x2_o.rearrange("(i p) d -> p i d", p=128),
            in_=x_sb[:].rearrange("p (i d) -> p i d", i=NT)),
            inc="st", n=16, raw=True)
        p.wait("gpsimd", "st", p.cnt["st"])
        p.build()
    return nc


def stage_l2(inputs, r1):
    mx = np.ascontiguousarray(
        np.array([[float(r1[b]["lmax"][0, 0]) for b in range(NCORE)]],
                 np.float32))
    com = {
        "wqT": pack_lhsT(np.asarray(inputs["L0_attn_Wq"], np.float32)),
        "wvT": pack_rhs(np.asarray(inputs["L0_attn_Wv"], np.float32)),
        "woT": pack_rhs(np.asarray(inputs["L0_attn_Wo"], np.float32)),
        "projT": np.ascontiguousarray(
            np.asarray(inputs["L0_attn_proj"], np.float32).T),
        "bq": np.asarray(inputs["L0_attn_bq"], np.float32),
        "bv": np.asarray(inputs["L0_attn_bv"], np.float32),
        "bo": np.asarray(inputs["L0_attn_bo"], np.float32),
        "maxes": mx,
    }
    return [dict(com, x1=np.ascontiguousarray(r1[b]["x1"]),
                 kT=np.ascontiguousarray(r1[b]["kT"]),
                 diag=np.ascontiguousarray(r1[b]["diag"]))
            for b in range(NCORE)]





def emit_ffn1_chain(nc, p, pool, jit, ugT, uoT, woR, x_src_d,
                    x_out_o, zx_d, eye_sb, boc, incc, dbg=None):
    NSEQ = 256
    with nc.sbuf_tensor("c_fm", [128, 2 * DC * NSEQ], F32) as c_fm, \
            nc.sbuf_tensor("cuo_tm", [128, 2 * D], F32) as cuo_tm, \
            nc.sbuf_tensor("gt_sb", [128, 3 * NSEQ], F32) as gt_sb, \
            nc.sbuf_tensor("ones2", [128, NSEQ], F32) as ones2:
        _ug_cm = nc.sbuf_tensor("ug_sb", [128, 3 * DC * D], F32)
        ug_sb = _ug_cm.__enter__()
        ZW = 3 * DC * NSEQ
        _zxs_cm = nc.sbuf_tensor("zxs", [128, 2 * ZW], F32)
        zxs = _zxs_cm.__enter__()
        CP = DC * NSEQ
        p.emit("sync", lambda e: e.dma_start(
            out=ug_sb[:].rearrange("p (g c d) -> p g c d", g=3, c=DC),
            in_=ugT.rearrange("g c p d -> p g c d")), inc="ld", n=16)
        ug_ld = p.mark("ld")
        p.emit("vector", lambda e: e.memset(ones2[:], 1.0))
        for cfc in range(DC):
            p.emit("vector", lambda e, cfc=cfc: e.tensor_scalar(
                out=c_fm[:, cfc * NSEQ:(cfc + 1) * NSEQ], in0=ones2[:],
                scalar1=incc[:, cfc:cfc + 1], scalar2=None, op0=OP.mult))
        p.wait("tensor", "ld", ug_ld)

        def zx_load(t):
            p.wait("sync", "v", p.mark("v"))
            p.emit("sync", lambda e, t=t: e.dma_start(
                out=zxs[:, (t % 2) * ZW:(t % 2 + 1) * ZW]
                .rearrange("p (g c q) -> p g c q", g=3, c=DC),
                in_=zx_d.rearrange("g c p t -> p g c t")
                    [:, :, :, t * NSEQ:(t + 1) * NSEQ]), inc="ld", n=16)
            return p.mark("ld")

        zx_marks = {0: zx_load(0)}
        for t in range(8):
            par, npar = (t % 2) * CP, ((t + 1) % 2) * CP
            zo = (t % 2) * ZW
            if t + 1 < 8:
                zx_marks[t + 1] = zx_load(t + 1)
            p.wait("vector", "ld", zx_marks[t])
            p.wait("tensor", "v", p.mark("v"))
            for mc in range(DC):
                gbanks = []
                for g in range(3):
                    bi, bank = pool.acquire()
                    gbanks.append((bi, bank))
                    for f in range(DC):
                        p.emit("tensor", lambda e, g=g, f=f, mc=mc,
                               par=par, bank=bank: e.matmul(
                            bank[:, 0:NSEQ],
                            ug_sb[:, g * DC * D + mc * D + f * 128:
                                  g * DC * D + mc * D + (f + 1) * 128],
                            c_fm[:, par + f * NSEQ:par + (f + 1) * NSEQ],
                            start=(f == 0), stop=(f == DC - 1)),
                            inc="t")
                p.wait("vector", "t", p.mark("t"))
                for g in range(3):
                    bi, bank = gbanks[g]
                    p.emit("vector", lambda e, g=g, mc=mc, zo=zo, bank=bank:
                           e.tensor_tensor(
                        out=gt_sb[:, g * NSEQ:(g + 1) * NSEQ],
                        in0=bank[:, 0:NSEQ],
                        in1=zxs[:, zo + (g * DC + mc) * NSEQ:
                                zo + (g * DC + mc + 1) * NSEQ], op=OP.add))
                    pool.release(bi, "v")
                p.wait("scalar", "v", p.mark("v"))
                for g in range(3):
                    fn = AF.Sigmoid if g < 2 else AF.Tanh
                    p.emit("scalar", lambda e, g=g, fn=fn: e.activation(
                        out=gt_sb[:, g * NSEQ:(g + 1) * NSEQ],
                        in_=gt_sb[:, g * NSEQ:(g + 1) * NSEQ], func=fn),
                        inc="s")
                p.wait("vector", "s", p.mark("s"))
                p.emit("vector", lambda e, mc=mc, par=par, npar=npar:
                       e.tensor_mul(
                    out=c_fm[:, npar + mc * NSEQ:npar + (mc + 1) * NSEQ],
                    in0=c_fm[:, par + mc * NSEQ:par + (mc + 1) * NSEQ],
                    in1=gt_sb[:, 0:NSEQ]))
                p.emit("vector", lambda e: e.tensor_mul(
                    out=gt_sb[:, NSEQ:2 * NSEQ],
                    in0=gt_sb[:, NSEQ:2 * NSEQ],
                    in1=gt_sb[:, 2 * NSEQ:3 * NSEQ]))
                p.emit("vector", lambda e, mc=mc, npar=npar: e.tensor_add(
                    out=c_fm[:, npar + mc * NSEQ:npar + (mc + 1) * NSEQ],
                    in0=c_fm[:, npar + mc * NSEQ:npar + (mc + 1) * NSEQ],
                    in1=gt_sb[:, NSEQ:2 * NSEQ]))
        cpar = (8 % 2) * CP
        p.barrier()
        if dbg is not None:
            p.emit("gpsimd", lambda e: e.dma_start(out=dbg[0], in_=c_fm[:]),
                   inc="st", n=16, raw=True)
            p.emit("gpsimd", lambda e: e.dma_start(out=dbg[1], in_=zxs[:]),
                   inc="st", n=16, raw=True)
            p.wait("gpsimd", "st", p.cnt["st"])
        _zxs_cm.__exit__(None, None, None)
        _ug_cm.__exit__(None, None, None)
        with nc.sbuf_tensor("uo_sb", [128, DC * D], F32) as uo_sb, \
                nc.sbuf_tensor("cuoT", [128, DC * NSEQ], F32) as cuoT:
            p.emit("sync", lambda e: e.dma_start(
                out=uo_sb[:].rearrange("p (c d) -> p c d", c=DC),
                in_=uoT.rearrange("c p d -> p c d")), inc="ld", n=16)
            p.wait("tensor", "ld", p.mark("ld"))
            for oc in range(DC):
                bi, bank = pool.acquire()
                for f in range(DC):
                    p.emit("tensor", lambda e, oc=oc, f=f, bank=bank:
                           e.matmul(
                        bank[:, 0:NSEQ],
                        uo_sb[:, oc * D + f * 128:oc * D + (f + 1) * 128],
                        c_fm[:, cpar + f * NSEQ:cpar + (f + 1) * NSEQ],
                        start=(f == 0), stop=(f == DC - 1)), inc="t")
                p.wait("scalar", "t", p.mark("t"))
                p.emit("scalar", lambda e, oc=oc, bank=bank: e.activation(
                    out=cuoT[:, oc * NSEQ:(oc + 1) * NSEQ],
                    in_=bank[:, 0:NSEQ], func=AF.Identity,
                    bias=boc[:, oc:oc + 1]), inc="s")
                pool.release(bi, "s")
            p.wait("tensor", "s", p.mark("s"))
            for oc in range(DC):
                for ss in range(2):
                    bi, bank = pool.acquire()
                    p.emit("tensor", lambda e, oc=oc, ss=ss, bank=bank:
                           e.matmul(
                        bank[:, 0:128],
                        cuoT[:, oc * NSEQ + ss * 128:
                             oc * NSEQ + (ss + 1) * 128],
                        eye_sb[:], start=True, stop=True), inc="t")
                    p.wait("scalar", "t", p.mark("t"))
                    p.emit("scalar", lambda e, oc=oc, ss=ss, bank=bank:
                           e.activation(
                        out=cuo_tm[:, ss * D + oc * 128:
                                   ss * D + (oc + 1) * 128],
                        in_=bank[:, 0:128], func=AF.Copy), inc="s")
                    pool.release(bi, "s")
            p.barrier()
            if dbg is not None:
                p.emit("gpsimd", lambda e: e.dma_start(out=dbg[2],
                                                       in_=cuo_tm[:]),
                       inc="st", n=16, raw=True)
            with nc.sbuf_tensor("wo_sb", [128, DC * D], F32) as wo_sb, \
                    nc.sbuf_tensor("xb_sb", [128, 2 * 2 * D], F32) as xb_sb, \
                    nc.sbuf_tensor("xhTb", [128, DC * 256], F32) as xhTb, \
                    nc.sbuf_tensor("hz_sb", [128, D], F32) as hz_sb:
                p.emit("sync", lambda e: e.dma_start(
                    out=wo_sb[:].rearrange("p (c d) -> p c d", c=DC),
                    in_=woR.rearrange("c p d -> p c d")), inc="ld", n=16)
                wo_ld = p.mark("ld")
                p.wait("tensor", "ld", wo_ld)

                def xb_load(t):
                    p.wait("sync", "v", p.mark("v"))
                    p.wait("sync", "s", p.mark("s"))
                    p.wait("sync", "st", p.mark("st"))
                    p.emit("sync", lambda e, t=t: e.dma_start(
                        out=xb_sb[:, (t % 2) * 2 * D:(t % 2 + 1) * 2 * D]
                        .rearrange("p (i d) -> p i d", i=2),
                        in_=x_src_d.rearrange("(i p) d -> p i d", p=128)
                            [:, 2 * t:2 * t + 2, :]), inc="ld", n=16)
                    return p.mark("ld")

                xb_marks = {0: xb_load(0)}
                for t in range(8):
                    xb = xb_sb[:, (t % 2) * 2 * D:(t % 2 + 1) * 2 * D]
                    if t + 1 < 8:
                        xb_marks[t + 1] = xb_load(t + 1)
                    for e in ("vector", "tensor"):
                        p.wait(e, "ld", xb_marks[t])
                    jit([0, 1], xb, xhTb[:],
                        stats=[2 * t, 2 * t + 1])
                    xh_ready = p.mark("s")
                    p.wait("tensor", "s", xh_ready)
                    for ss in range(2):
                        for ob in range(2):
                            bi, bank = pool.acquire()
                            for f in range(DC):
                                p.emit("tensor", lambda e, f=f, ss=ss,
                                       ob=ob, bank=bank: e.matmul(
                                    bank[:],
                                    xhTb[:, f * 256 + ss * 128:
                                         f * 256 + (ss + 1) * 128],
                                    wo_sb[:, f * D + ob * 512:
                                          f * D + (ob + 1) * 512],
                                    start=(f == 0), stop=(f == DC - 1)),
                                    inc="t")
                            p.wait("vector", "t", p.mark("t"))
                            p.emit("vector", lambda e, ss=ss, ob=ob,
                                   bank=bank: e.tensor_add(
                                out=hz_sb[:, ob * 512:(ob + 1) * 512],
                                in0=bank[:],
                                in1=cuo_tm[:, ss * D + ob * 512:
                                           ss * D + (ob + 1) * 512]))
                            pool.release(bi, "v")
                        p.wait("scalar", "v", p.mark("v"))
                        p.emit("scalar", lambda e: e.activation(
                            out=hz_sb[:], in_=hz_sb[:], func=AF.Tanh),
                            inc="s")
                        p.wait("vector", "s", p.mark("s"))
                        p.emit("vector", lambda e, ss=ss, xb=xb:
                               e.scalar_tensor_tensor(
                            out=xb[:, ss * D:(ss + 1) * D],
                            in0=hz_sb[:], scalar=0.5,
                            in1=xb[:, ss * D:(ss + 1) * D],
                            op0=OP.mult, op1=OP.add))
                    p.wait("gpsimd", "v", p.mark("v"))
                    p.emit("gpsimd", lambda e, t=t, xb=xb: e.dma_start(
                        out=x_out_o.rearrange("(i p) d -> p i d", p=128)
                            [:, 2 * t:2 * t + 2, :],
                        in_=xb.rearrange("p (i d) -> p i d", i=2)),
                        inc="st", n=16, raw=True)



def build_l3():
    nc = bass.Bass("TRN2", target_bir_lowering=False,
                   detect_race_conditions=False)
    x2 = nc.dram_tensor("x2", [TOK, D], F32, kind="ExternalInput").ap()
    wfc = nc.dram_tensor("wfc", [FC, 128, D], F32, kind="ExternalInput").ap()
    wproj = nc.dram_tensor("wproj", [FC, 128, D], F32,
                           kind="ExternalInput").ap()
    wgT = nc.dram_tensor("wgT", [3, DC, 128, D], F32,
                         kind="ExternalInput").ap()
    ugT = nc.dram_tensor("ugT", [3, DC, 128, D], F32,
                         kind="ExternalInput").ap()
    uoT = nc.dram_tensor("uoT", [DC, 128, D], F32, kind="ExternalInput").ap()
    woR = nc.dram_tensor("woR", [DC, 128, D], F32, kind="ExternalInput").ap()
    bg = nc.dram_tensor("bg", [3, D], F32, kind="ExternalInput").ap()
    bo_ff = nc.dram_tensor("bo_ff", [D], F32, kind="ExternalInput").ap()
    inc_ff = nc.dram_tensor("inc_ff", [D], F32, kind="ExternalInput").ap()
    x4_o = nc.dram_tensor("x4", [TOK, D], F32, kind="ExternalOutput").ap()
    x3_d = nc.dram_tensor("x3_d", [TOK, D], F32, kind="Internal").ap()
    if DBG:
        dbg_x3 = nc.dram_tensor("dbg_x3", [TOK, D], F32,
                                kind="ExternalOutput").ap()
        dbg_c = nc.dram_tensor("dbg_c", [128, 2 * DC * 256], F32,
                               kind="ExternalOutput").ap()
        dbg_cuo = nc.dram_tensor("dbg_cuo", [128, 2 * D], F32,
                                 kind="ExternalOutput").ap()
        dbg_zx = nc.dram_tensor("dbg_zx", [128, 3 * DC * 256], F32,
                                kind="ExternalOutput").ap()
    zx_d = nc.dram_tensor("zx_d", [3, DC, 128, TOK], F32, kind="Internal").ap()
    eye_d = nc.inline_tensor(np.eye(128, dtype=np.float32), "eye").ap()

    NSEQ = 256
    with ExitStack() as stack:
        ec = stack.enter_context
        p = Prog(nc, stack)
        mean_sb = ec(nc.sbuf_tensor("mean_sb", [128, NT], F32))
        rstd_sb = ec(nc.sbuf_tensor("rstd_sb", [128, NT], F32))
        sq_sb = ec(nc.sbuf_tensor("sq_sb", [128, D], F32))
        col0 = ec(nc.sbuf_tensor("col0", [128, 1], F32))
        eye_sb = ec(nc.sbuf_tensor("eye_sb", [128, 128], F32))
        setup_consts(nc, p, ec)
        xc_sb = ec(nc.sbuf_tensor("xc_sb", [128, 2 * D], F32))
        dg_sb = ec(nc.sbuf_tensor("dg_sb", [128, 2 * 128], F32))
        bgc = ec(nc.sbuf_tensor("bgc", [128, 3 * DC], F32))
        boc = ec(nc.sbuf_tensor("boc", [128, DC], F32))
        incc = ec(nc.sbuf_tensor("incc", [128, DC], F32))
        banks = [ec(nc.psum_tensor(f"ps{i}", [128, 512], F32))
                 for i in range(8)]
        pool = PsumPool(p, [b[:] for b in banks])
        jit = make_jit_xhat(p, pool, mean_sb[:], rstd_sb[:], eye_sb[:],
                            xc_sb[:], dg_sb[:], depth=2)

        p.emit("sync", lambda e: e.dma_start(out=eye_sb[:], in_=eye_d),
               inc="ld", n=16)
        p.emit("sync", lambda e: e.dma_start(
            out=bgc[:].rearrange("p (g c) -> p g c", g=3),
            in_=bg.rearrange("g (c p) -> p g c", p=128),
            allow_slow_non_contiguous=True), inc="ld", n=16)
        p.emit("sync", lambda e: e.dma_start(
            out=boc[:], in_=bo_ff.rearrange("(c p) -> p c", p=128),
            allow_slow_non_contiguous=True), inc="ld", n=16)
        p.emit("sync", lambda e: e.dma_start(
            out=incc[:], in_=inc_ff.rearrange("(c p) -> p c", p=128),
            allow_slow_non_contiguous=True), inc="ld", n=16)

        with nc.sbuf_tensor("x_sb", [128, NT * D], F32) as x_sb, \
                nc.sbuf_tensor("xhT", [128, DC * 512], F32) as xhT:
            p.emit("sync", lambda e: e.dma_start(
                out=x_sb[:].rearrange("p (i d) -> p i d", i=NT),
                in_=x2.rearrange("(i p) d -> p i d", p=128)), inc="ld", n=16)
            ldm = p.mark("ld")
            for e in ("tensor", "vector", "scalar", "gpsimd"):
                p.wait(e, "ld", ldm)
            ln_stats(p, x_sb[:], mean_sb[:], rstd_sb[:], sq_sb[:], col0[:],
                     range(NT))
            p.barrier()
            with nc.sbuf_tensor("h_sb", [128, FC * 512], F32) as h_sb, \
                    nc.sbuf_tensor("r_sb", [128, 512], F32) as r_sb, \
                    nc.sbuf_tensor("z_sb", [128, D], F32) as z_sb, \
                    nc.sbuf_tensor("mem_sb", [128, D], F32) as mem_sb, \
                    nc.sbuf_tensor("spk_sb", [128, D], F32) as spk_sb, \
                    nc.sbuf_tensor("acc_sb", [128, D], F32) as acc_sb, \
                    nc.sbuf_tensor("wfc0", [128, D], F32) as wfc0, \
                    nc.sbuf_tensor("wfc1", [128, D], F32) as wfc1, \
                    nc.sbuf_tensor("wp0", [128, D], F32) as wp0, \
                    nc.sbuf_tensor("wp1", [128, D], F32) as wp1:
                sb = dict(xhT=xhT[:], h=h_sb[:], r=r_sb[:], z=z_sb[:],
                          mem=mem_sb[:], spk=spk_sb[:], acc=acc_sb[:],
                          wfc=[wfc0[:], wfc1[:]], wproj=[wp0[:], wp1[:]])
                emit_ffn0(p, pool, jit, x_sb[:], wfc, wproj, sb)
            p.emit("gpsimd", lambda e: e.dma_start(
                out=x3_d.rearrange("(i p) d -> p i d", p=128),
                in_=x_sb[:].rearrange("p (i d) -> p i d", i=NT)),
                inc="st", n=16, raw=True)
            if DBG:
                p.emit("gpsimd", lambda e: e.dma_start(
                    out=dbg_x3.rearrange("(i p) d -> p i d", p=128),
                    in_=x_sb[:].rearrange("p (i d) -> p i d", i=NT)),
                    inc="st", n=16, raw=True)
            ln_stats(p, x_sb[:], mean_sb[:], rstd_sb[:], sq_sb[:], col0[:],
                     range(NT))
            p.barrier()
            with nc.sbuf_tensor("wg_sb", [128, DC * D], F32) as wg_sb, \
                    nc.sbuf_tensor("zx_sb", [128, DC * 512], F32) as zx_sb:
                for g in range(3):
                    p.wait("sync", "t", p.mark("t"))
                    p.emit("sync", lambda e, g=g: e.dma_start(
                        out=wg_sb[:].rearrange("p (c d) -> p c d", c=DC),
                        in_=wgT[g].rearrange("c p d -> p c d")),
                        inc="ld", n=16)
                    wld = p.mark("ld")
                    for tb in range(4):
                        tiles = list(range(tb * 4, tb * 4 + 4))
                        jit(tiles, x_sb[:], xhT[:])
                        xh_ready = p.mark("s")
                        p.wait("tensor", "ld", wld)
                        p.wait("tensor", "s", xh_ready)
                        p.wait("scalar", "st", p.mark("st"))
                        for mc in range(DC):
                            bi, bank = pool.acquire()
                            for f in range(DC):
                                p.emit("tensor", lambda e, f=f, mc=mc,
                                       bank=bank: e.matmul(
                                    bank[:],
                                    wg_sb[:, mc * D + f * 128:
                                          mc * D + (f + 1) * 128],
                                    xhT[:, f * 512:(f + 1) * 512],
                                    start=(f == 0), stop=(f == DC - 1)),
                                    inc="t")
                            p.wait("scalar", "t", p.mark("t"))
                            p.emit("scalar", lambda e, g=g, mc=mc, bank=bank:
                                   e.activation(
                                out=zx_sb[:, mc * 512:(mc + 1) * 512],
                                in_=bank[:], func=AF.Identity,
                                bias=bgc[:, g * DC + mc:g * DC + mc + 1]),
                                inc="s")
                            pool.release(bi, "s")
                        p.wait("gpsimd", "s", p.mark("s"))
                        p.emit("gpsimd", lambda e, g=g, tb=tb: e.dma_start(
                            out=zx_d[g].rearrange("c p t -> p c t")
                                [:, :, tb * 512:(tb + 1) * 512],
                            in_=zx_sb[:].rearrange("p (c q) -> p c q", c=DC)),
                            inc="st", n=16, raw=True)
        p.barrier()

        emit_ffn1_chain(nc, p, pool, jit, ugT, uoT, woR, x3_d,
                        x4_o, zx_d, eye_sb[:], boc[:], incc[:],
                        dbg=(dbg_c, dbg_zx, dbg_cuo) if DBG else None)
        p.barrier()
        p.wait("gpsimd", "st", p.cnt["st"])
        p.build()
    return nc


def stage_l3(inputs, res2):
    com = {
        "wfc": pack_lhsT(np.asarray(inputs["L0_ffn_W_fc"], np.float32)),
        "wproj": pack_rhs(np.asarray(inputs["L0_ffn_W_proj"], np.float32)),
        "wgT": np.stack([pack_lhsT(np.asarray(inputs[k], np.float32))
                         for k in ("L1_ffn_wf", "L1_ffn_wi", "L1_ffn_wc")]),
        "ugT": np.stack([pack_lhsT(np.asarray(inputs[k], np.float32))
                         for k in ("L1_ffn_uf", "L1_ffn_ui", "L1_ffn_uc")]),
        "uoT": pack_lhsT(np.asarray(inputs["L1_ffn_uo"], np.float32)),
        "woR": pack_rhs(np.asarray(inputs["L1_ffn_wo"], np.float32)),
        "bg": np.stack([np.asarray(inputs[k], np.float32)
                        for k in ("L1_ffn_bf", "L1_ffn_bi", "L1_ffn_bc")]),
        "bo_ff": np.asarray(inputs["L1_ffn_bo"], np.float32),
        "inc_ff": np.asarray(inputs["L1_ffn_in_c"], np.float32),
    }
    x2_all = np.stack([np.asarray(res2[b]["x2"]) for b in range(NCORE)])
    maps = []
    for j in range(NCORE):
        xj = np.ascontiguousarray(
            x2_all[:, j * 256:(j + 1) * 256, :].reshape(TOK, D))
        maps.append(dict(com, x2=xj))
    return maps



HALF_PI = 1.5707963267948966
NEG_PI = -3.141592653589793
TWO_PI = 6.283185307179586


def build_l4():
    nc = bass.Bass("TRN2", target_bir_lowering=False,
                   detect_race_conditions=False)
    x4f = nc.dram_tensor("x4f", [B * S, D], F32, kind="ExternalInput").ap()
    x4own = nc.dram_tensor("x4own", [TOK, D], F32, kind="ExternalInput").ap()
    basR = nc.dram_tensor("basR", [4, DC, 128, HARM], F32,
                          kind="ExternalInput").ap()
    phT = nc.dram_tensor("phT", [HARM, 4 * D], F32, kind="ExternalInput").ap()
    amT = nc.dram_tensor("amT", [HARM, 4 * D], F32, kind="ExternalInput").ap()
    wgT = nc.dram_tensor("wgT", [3, DC, 128, D], F32,
                         kind="ExternalInput").ap()
    ugT = nc.dram_tensor("ugT", [3, DC, 128, D], F32,
                         kind="ExternalInput").ap()
    uoT = nc.dram_tensor("uoT", [DC, 128, D], F32, kind="ExternalInput").ap()
    woR = nc.dram_tensor("woR", [DC, 128, D], F32, kind="ExternalInput").ap()
    bg = nc.dram_tensor("bg", [3, D], F32, kind="ExternalInput").ap()
    bo_ff = nc.dram_tensor("bo_ff", [D], F32, kind="ExternalInput").ap()
    inc_ff = nc.dram_tensor("inc_ff", [D], F32, kind="ExternalInput").ap()
    x6_o = nc.dram_tensor("x6", [TOK, D], F32, kind="ExternalOutput").ap()
    x5_d = nc.dram_tensor("x5_d", [TOK, D], F32, kind="Internal").ap()
    zx_d = nc.dram_tensor("zx_d", [3, DC, 128, TOK], F32, kind="Internal").ap()
    eye_d = nc.inline_tensor(np.eye(128, dtype=np.float32), "eye").ap()

    NSEQ = 256
    with ExitStack() as stack:
        ec = stack.enter_context
        p = Prog(nc, stack)
        mean_sb = ec(nc.sbuf_tensor("mean_sb", [128, NT], F32))
        rstd_sb = ec(nc.sbuf_tensor("rstd_sb", [128, NT], F32))
        mean8 = ec(nc.sbuf_tensor("mean8", [128, 4], F32))
        rstd8 = ec(nc.sbuf_tensor("rstd8", [128, 4], F32))
        sq_sb = ec(nc.sbuf_tensor("sq_sb", [128, D], F32))
        col0 = ec(nc.sbuf_tensor("col0", [128, 1], F32))
        eye_sb = ec(nc.sbuf_tensor("eye_sb", [128, 128], F32))
        setup_consts(nc, p, ec)
        xc_sb = ec(nc.sbuf_tensor("xc_sb", [128, 2 * D], F32))
        dg_sb = ec(nc.sbuf_tensor("dg_sb", [128, 2 * 128], F32))
        bgc = ec(nc.sbuf_tensor("bgc", [128, 3 * DC], F32))
        boc = ec(nc.sbuf_tensor("boc", [128, DC], F32))
        incc = ec(nc.sbuf_tensor("incc", [128, DC], F32))
        w4 = ec(nc.sbuf_tensor("w4", [HARM, 4 * D], F32))
        bas_sb = ec(nc.sbuf_tensor("bas_sb", [128, 4 * DC * HARM], F32))
        banks = [ec(nc.psum_tensor(f"ps{i}", [128, 512], F32))
                 for i in range(8)]
        pool = PsumPool(p, [b[:] for b in banks[:4] + banks[6:]])
        avb = [b[:] for b in banks[4:6]]
        jit = make_jit_xhat(p, pool, mean_sb[:], rstd_sb[:], eye_sb[:],
                            xc_sb[:], dg_sb[:], depth=2)

        p.emit("sync", lambda e: e.dma_start(out=eye_sb[:], in_=eye_d),
               inc="ld", n=16)
        p.emit("sync", lambda e: e.dma_start(
            out=bgc[:].rearrange("p (g c) -> p g c", g=3),
            in_=bg.rearrange("g (c p) -> p g c", p=128),
            allow_slow_non_contiguous=True), inc="ld", n=16)
        p.emit("sync", lambda e: e.dma_start(
            out=boc[:], in_=bo_ff.rearrange("(c p) -> p c", p=128),
            allow_slow_non_contiguous=True), inc="ld", n=16)
        p.emit("sync", lambda e: e.dma_start(
            out=incc[:], in_=inc_ff.rearrange("(c p) -> p c", p=128),
            allow_slow_non_contiguous=True), inc="ld", n=16)
        p.emit("sync", lambda e: e.dma_start(out=w4[:], in_=phT),
               inc="ld", n=16)
        p.emit("sync", lambda e: e.dma_start(
            out=bas_sb[:].rearrange("p (x c q) -> p x c q", x=4, c=DC),
            in_=basR.rearrange("x c p q -> p x c q")), inc="ld", n=16)
        ldm = p.mark("ld")
        for e in ("tensor", "vector", "scalar", "gpsimd"):
            p.wait(e, "ld", ldm)
        with nc.sbuf_tensor("am_sb", [HARM, 4 * D], F32) as am_sb:
            p.emit("sync", lambda e: e.dma_start(out=am_sb[:], in_=amT),
                   inc="ld", n=16)
            p.wait("vector", "ld", p.mark("ld"))
            p.emit("vector", lambda e: e.tensor_scalar(
                out=w4[:], in0=w4[:], scalar1=HALF_PI, scalar2=None,
                op0=OP.add))
            with nc.sbuf_tensor("mk_sb", [HARM, 4 * D], F32) as mk_sb:
                PI = -NEG_PI
                for sign in (1.0, -1.0):
                    for _ in range(8):
                        p.emit("vector", lambda e, sign=sign:
                               e.tensor_scalar(
                            out=mk_sb[:], in0=w4[:],
                            scalar1=sign * PI, scalar2=None,
                            op0=(OP.is_gt if sign > 0 else OP.is_lt)))
                        p.emit("vector", lambda e, sign=sign:
                               e.scalar_tensor_tensor(
                            out=w4[:], in0=mk_sb[:],
                            scalar=-sign * TWO_PI, in1=w4[:],
                            op0=OP.mult, op1=OP.add))
                p.wait("scalar", "v", p.mark("v"))
                p.emit("scalar", lambda e: e.activation(
                    out=w4[:], in_=w4[:], func=AF.Sin), inc="s")
            p.wait("vector", "s", p.mark("s"))
            p.emit("vector", lambda e: e.tensor_mul(
                out=w4[:], in0=w4[:], in1=am_sb[:]))
            p.barrier()

        VW = H * 65 + 1
        KW = DC * 512
        with nc.sbuf_tensor("xo_sb", [128, 2 * 2 * D], F32) as xo_sb, \
                nc.sbuf_tensor("qT_sb", [128, DC * NSEQ], F32) as qT_sb, \
                nc.sbuf_tensor("thi", [128, DC * NSEQ], F32) as thi, \
                nc.sbuf_tensor("outT", [128, DC * NSEQ], F32) as outT, \
                nc.sbuf_tensor("kTh", [128, 2 * KW], F32) as kTh, \
                nc.sbuf_tensor("vb", [128, 2 * 4 * VW], F32) as vb, \
                nc.sbuf_tensor("xf2", [128, 4 * D], F32) as xf2, \
                nc.sbuf_tensor("xfT2", [128, DC * 512], F32) as xfT2, \
                nc.sbuf_tensor("res_sb", [HARM, 1280], F32) as res_sb, \
                nc.sbuf_tensor("PT_sb", [128, 1024], F32) as PT_sb, \
                nc.sbuf_tensor("xc8_sb", [128, 2 * D], F32) as xc8_sb, \
                nc.sbuf_tensor("dg8_sb", [128, 2 * 128], F32) as dg8_sb, \
                nc.sbuf_tensor("on_sb", [128, 128], F32) as on_sb:
            jit8 = make_jit_xhat(p, pool, mean8[:], rstd8[:], eye_sb[:],
                                 xc8_sb[:], dg8_sb[:], depth=2)
            p.emit("vector", lambda e: e.memset(on_sb[:], 1.0))
            xf2_marks = {"v": p.mark("v"), "s": p.mark("s")}
            attn_state = {"add": 0}
            xo_marks = {"v": p.mark("v"), "s": p.mark("s"),
                        "st": p.mark("st")}

            def emit_xstats(idx):
                bb, qq = divmod(idx, 4)
                g0 = bb * 16 + qq * 4
                p.wait("sync", "v", xf2_marks["v"])
                p.wait("sync", "s", xf2_marks["s"])
                p.emit("sync", lambda e, g0=g0: e.dma_start(
                    out=xf2[:].rearrange("p (i d) -> p i d", i=4),
                    in_=x4f.rearrange("(i p) d -> p i d", p=128)
                        [:, g0:g0 + 4, :]), inc="ld", n=16)
                xl = p.mark("ld")
                for e in ("vector", "scalar"):
                    p.wait(e, "ld", xl)
                ln_stats(p, xf2[:], mean8[:], rstd8[:], sq_sb[:],
                         col0[:], range(4))

            def emit_xo(bb):
                sl = (bb % 2) * 2 * D
                p.wait("sync", "v", xo_marks["v"])
                p.wait("sync", "s", xo_marks["s"])
                p.wait("sync", "st", xo_marks["st"])
                p.emit("sync", lambda e, bb=bb, sl=sl: e.dma_start(
                    out=xo_sb[:, sl:sl + 2 * D]
                    .rearrange("p (i d) -> p i d", i=2),
                    in_=x4own.rearrange("(i p) d -> p i d", p=128)
                        [:, 2 * bb:2 * bb + 2, :]), inc="ld", n=16)
                xl = p.mark("ld")
                for e in ("vector", "scalar"):
                    p.wait(e, "ld", xl)
                ln_stats(p, xo_sb[:, sl:sl + 2 * D],
                         mean_sb[:, 2 * bb:2 * bb + 2],
                         rstd_sb[:, 2 * bb:2 * bb + 2],
                         sq_sb[:], col0[:], range(2))

            def emit_qprep(bb):
                sl = (bb % 2) * 2 * D
                jit([0, 1], xo_sb[:, sl:sl + 2 * D], xfT2[:],
                    stats=[2 * bb, 2 * bb + 1])
                p.wait("tensor", "s", p.mark("s"))
                bi, bank = pool.acquire()
                for f in range(DC):
                    p.emit("tensor", lambda e, f=f, bank=bank: e.matmul(
                        bank[0:HARM, 0:NSEQ],
                        bas_sb[:, (0 * DC + f) * HARM:
                               (0 * DC + f + 1) * HARM],
                        xfT2[:, f * NSEQ:(f + 1) * NSEQ],
                        start=(f == 0), stop=(f == DC - 1)), inc="t")
                p.wait("scalar", "t", p.mark("t"))
                p.emit("scalar", lambda e, bank=bank: e.activation(
                    out=res_sb[:, 1024:1024 + NSEQ],
                    in_=bank[0:HARM, 0:NSEQ], func=AF.Copy), inc="s")
                pool.release(bi, "s")
                p.wait("tensor", "s", p.mark("s"))
                for oc in range(DC):
                    bi, bank = pool.acquire()
                    p.emit("tensor", lambda e, oc=oc, bank=bank: e.matmul(
                        bank[:, 0:NSEQ],
                        w4[:, 0 * D + oc * 128:0 * D + (oc + 1) * 128],
                        res_sb[:, 1024:1024 + NSEQ],
                        start=True, stop=True), inc="t")
                    p.wait("scalar", "t", p.mark("t"))
                    p.emit("scalar", lambda e, oc=oc, bank=bank:
                           e.activation(
                        out=qT_sb[:, oc * NSEQ:(oc + 1) * NSEQ],
                        in_=bank[:, 0:NSEQ], func=AF.Copy), inc="s")
                    pool.release(bi, "s")
                p.wait("vector", "t", p.mark("t"))
                p.emit("vector", lambda e: e.memset(outT[0:65, :], 0.0))
                p.emit("vector", lambda e: e.memset(thi[0:65, :], 0.0))
                return p.mark("s")

            def emit_prep(idx):
                ko = (idx % 2) * KW
                vo = (idx % 2) * 4 * VW
                jit8(list(range(4)), xf2[:], xfT2[:])
                xf2_marks["v"] = p.mark("v")
                xf2_marks["s"] = p.mark("s")
                p.wait("tensor", "s", p.mark("s"))
                for pj in range(2):
                    bi, bank = pool.acquire()
                    for f in range(DC):
                        p.emit("tensor", lambda e, pj=pj, f=f,
                               bank=bank: e.matmul(
                            bank[0:HARM, 0:512],
                            bas_sb[:, ((pj + 1) * DC + f) * HARM:
                                   ((pj + 1) * DC + f + 1) * HARM],
                            xfT2[:, f * 512:(f + 1) * 512],
                            start=(f == 0), stop=(f == DC - 1)),
                            inc="t")
                    p.wait("scalar", "t", p.mark("t"))
                    p.emit("scalar", lambda e, pj=pj,
                           bank=bank: e.activation(
                        out=res_sb[:, pj * 512:(pj + 1) * 512],
                        in_=bank[0:HARM, 0:512], func=AF.Copy),
                        inc="s")
                    pool.release(bi, "s")
                res_ready = p.mark("s")
                p.wait("tensor", "s", res_ready)
                for oc in range(DC):
                    bi, bank = pool.acquire()
                    p.emit("tensor", lambda e, oc=oc, bank=bank:
                           e.matmul(
                        bank[:],
                        w4[:, 1 * D + oc * 128:1 * D + (oc + 1) * 128],
                        res_sb[:, 0:512],
                        start=True, stop=True), inc="t")
                    p.wait("scalar", "t", p.mark("t"))
                    p.emit("scalar", lambda e, oc=oc, ko=ko, bank=bank:
                           e.activation(
                        out=kTh[:, ko + oc * 512:ko + (oc + 1) * 512],
                        in_=bank[:], func=AF.Copy), inc="s")
                    pool.release(bi, "s")
                for sub in range(4):
                    p.emit("vector", lambda e, sub=sub, vo=vo: e.memset(
                        vb[:, vo + sub * VW + 64:
                           vo + sub * VW + H * 65:65], 1.0))
                    for ob in range(2):
                        bi, bank = pool.acquire()
                        p.emit("tensor", lambda e, sub=sub, ob=ob,
                               bank=bank: e.matmul(
                            bank[:],
                            res_sb[:, 512 + sub * 128:
                                   512 + (sub + 1) * 128],
                            w4[:, 2 * D + ob * 512:2 * D + (ob + 1) * 512],
                            start=True, stop=True), inc="t")
                        p.wait("scalar", "t", p.mark("t"))
                        p.wait("scalar", "v", p.mark("v"))
                        p.emit("scalar", lambda e, sub=sub, ob=ob, vo=vo,
                               bank=bank: e.activation(
                            out=vb[:, vo + sub * VW + ob * 8 * 65:
                                    vo + sub * VW + ob * 8 * 65 + 8 * 65]
                            .rearrange("p (h e) -> p h e", h=8)[:, :, 0:64],
                            in_=bank[:].rearrange("p (h e) -> p h e", h=8),
                            func=AF.Copy), inc="s")
                        pool.release(bi, "s")
                return p.mark("s")

            def emit_attn(idx, kv_ready, q_ready):
                b, qtr = divmod(idx, 4)
                ko = (idx % 2) * KW
                vo = (idx % 2) * 4 * VW
                p.wait("tensor", "s", max(kv_ready, q_ready))

                def emit_scores(hp, kc):
                    sbanks = []
                    for hv in range(2):
                        bi, bank = pool.acquire()
                        sbanks.append((bi, bank))
                        p.emit("tensor", lambda e, hp=hp, hv=hv,
                               kc=kc, bank=bank: e.matmul(
                            bank[:, 0:NSEQ],
                            kTh[hv * 64:hv * 64 + 64,
                                ko + hp * 512 + kc * 128:
                                ko + hp * 512 + (kc + 1) * 128],
                            qT_sb[hv * 64:hv * 64 + 64,
                                  hp * NSEQ:(hp + 1) * NSEQ],
                            start=True, stop=True),
                            inc="t" if hv == 1 else None)
                    return sbanks, p.mark("t")

                iters = [(hp, kc) for hp in range(DC) for kc in range(4)]
                pend = {j: emit_scores(*iters[j]) for j in range(3)}
                avt_marks = {}
                for j, (hp, kc) in enumerate(iters):
                    sbanks, tmark = pend.pop(j)
                    p.wait("scalar", "t", tmark)
                    if j - 2 in avt_marks:
                        p.wait("scalar", "t", avt_marks.pop(j - 2))
                    pt = PT_sb[:, (kc % 2) * 512:(kc % 2) * 512 + 512]
                    for hv in range(2):
                        bi, bank = sbanks[hv]
                        p.emit("scalar", lambda e, hv=hv, pt=pt,
                               bank=bank: e.activation(
                            out=pt[:, hv * NSEQ:(hv + 1) * NSEQ],
                            in_=bank[:, 0:NSEQ],
                            func=AF.Exp, scale=0.125), inc="s")
                        pool.release(bi, "s")
                    if j + 3 < len(iters):
                        pend[j + 3] = emit_scores(*iters[j + 3])
                    p.wait("tensor", "s", p.mark("s"))
                    if kc == 0:
                        p.wait("tensor", "v", attn_state["add"])
                    for hv in range(2):
                        h = hp * 2 + hv
                        av = avb[hv]
                        p.emit("tensor", lambda e, hv=hv, kc=kc,
                               h=h, av=av, pt=pt, vo=vo: e.matmul(
                            av[0:65, 0:NSEQ],
                            vb[:, vo + kc * VW + h * 65:
                               vo + kc * VW + (h + 1) * 65],
                            pt[:, hv * NSEQ:(hv + 1) * NSEQ],
                            start=(kc == 0), stop=(kc == 3)),
                            inc="t" if (kc == 3 and hv == 1)
                            else None)
                    avt_marks[j] = p.mark("t")
                    if kc == 3:
                        p.wait("vector", "t", p.mark("t"))
                        for hv, acc in ((0, outT), (1, thi)):
                            p.emit("vector", lambda e, hp=hp, hv=hv,
                                   acc=acc: e.tensor_tensor(
                                out=acc[0:65, hp * NSEQ:(hp + 1) * NSEQ],
                                in0=acc[0:65, hp * NSEQ:(hp + 1) * NSEQ],
                                in1=avb[hv][0:65, 0:NSEQ], op=OP.add))
                        attn_state["add"] = p.mark("v")
                        if hp == 0 and idx + 1 < 4 * B:
                            emit_xstats(idx + 1)
                        if hp == 4 and qtr == 2 and b + 1 < B:
                            emit_xo(b + 1)

            def emit_epilogue(b):
                sl = (b % 2) * 2 * D
                for acc in (outT, thi):
                    p.emit("vector", lambda e, acc=acc: e.reciprocal(
                        out=acc[64:65, :], in_=acc[64:65, :]))
                p.wait("tensor", "v", p.mark("v"))
                for hp in range(DC):
                    for acc in (outT, thi):
                        bi, bank = pool.acquire()
                        p.emit("tensor", lambda e, hp=hp, acc=acc,
                               bank=bank: e.matmul(
                            bank[0:64, 0:NSEQ],
                            on_sb[64:65, 0:64],
                            acc[64:65, hp * NSEQ:(hp + 1) * NSEQ],
                            start=True, stop=True), inc="t")
                        p.wait("vector", "t", p.mark("t"))
                        p.emit("vector", lambda e, hp=hp, acc=acc,
                               bank=bank: e.tensor_tensor(
                            out=acc[0:64, hp * NSEQ:(hp + 1) * NSEQ],
                            in0=acc[0:64, hp * NSEQ:(hp + 1) * NSEQ],
                            in1=bank[0:64, 0:NSEQ], op=OP.mult))
                        pool.release(bi, "v")
                p.wait("sync", "v", p.mark("v"))
                p.emit("sync", lambda e: e.dma_start(
                    out=outT[64:128, :], in_=thi[0:64, :]), inc="ld", n=16)
                p.wait("tensor", "ld", p.mark("ld"))
                p.wait("tensor", "v", p.mark("v"))
                bi, bank = pool.acquire()
                for oc in range(DC):
                    p.emit("tensor", lambda e, oc=oc, bank=bank: e.matmul(
                        bank[0:HARM, 0:NSEQ],
                        bas_sb[:, (3 * DC + oc) * HARM:
                               (3 * DC + oc + 1) * HARM],
                        outT[:, oc * NSEQ:(oc + 1) * NSEQ],
                        start=(oc == 0), stop=(oc == DC - 1)), inc="t")
                p.wait("scalar", "t", p.mark("t"))
                p.emit("scalar", lambda e, bank=bank: e.activation(
                    out=res_sb[:, 1024:1024 + NSEQ],
                    in_=bank[0:HARM, 0:NSEQ], func=AF.Copy), inc="s")
                pool.release(bi, "s")
                p.wait("tensor", "s", p.mark("s"))
                for sub in range(2):
                    for ob in range(2):
                        bi, bank = pool.acquire()
                        p.emit("tensor", lambda e, sub=sub, ob=ob,
                               bank=bank: e.matmul(
                            bank[:],
                            res_sb[:, 1024 + sub * 128:
                                   1024 + (sub + 1) * 128],
                            w4[:, 3 * D + ob * 512:3 * D + (ob + 1) * 512],
                            start=True, stop=True), inc="t")
                        p.wait("vector", "t", p.mark("t"))
                        p.emit("vector", lambda e, sub=sub, ob=ob, sl=sl,
                               bank=bank: e.scalar_tensor_tensor(
                            out=xo_sb[:, sl + sub * D + ob * 512:
                                      sl + sub * D + (ob + 1) * 512],
                            in0=bank[:], scalar=1.0,
                            in1=xo_sb[:, sl + sub * D + ob * 512:
                                      sl + sub * D + (ob + 1) * 512],
                            op0=OP.mult, op1=OP.add))
                        pool.release(bi, "v")
                p.wait("gpsimd", "v", p.mark("v"))
                p.emit("gpsimd", lambda e, b=b, sl=sl: e.dma_start(
                    out=x5_d.rearrange("(i p) d -> p i d", p=128)
                        [:, 2 * b:2 * b + 2, :],
                    in_=xo_sb[:, sl:sl + 2 * D]
                    .rearrange("p (i d) -> p i d", i=2)),
                    inc="st", n=16, raw=True)
                ln_stats(p, xo_sb[:, sl:sl + 2 * D],
                         mean_sb[:, 2 * b:2 * b + 2],
                         rstd_sb[:, 2 * b:2 * b + 2],
                         sq_sb[:], col0[:], range(2))
                xo_marks["v"] = p.mark("v")
                xo_marks["s"] = p.mark("s")
                xo_marks["st"] = p.mark("st")

            emit_xo(0)
            emit_xstats(0)
            qp_marks = {0: emit_qprep(0)}
            kv_marks = {0: emit_prep(0)}
            for idx in range(4 * B):
                b, qtr = divmod(idx, 4)
                emit_attn(idx, kv_marks[idx], qp_marks[b])
                if idx + 1 < 4 * B:
                    kv_marks[idx + 1] = emit_prep(idx + 1)
                if qtr == 3:
                    emit_epilogue(b)
                    if b + 1 < B:
                        qp_marks[b + 1] = emit_qprep(b + 1)
            p.barrier()
        with nc.sbuf_tensor("x_sb2", [128, NT * D], F32) as x_sb2, \
                nc.sbuf_tensor("xhT", [128, DC * 512], F32) as xhT, \
                nc.sbuf_tensor("wg_sb", [128, DC * D], F32) as wg_sb, \
                nc.sbuf_tensor("zx_sb", [128, DC * 512], F32) as zx_sb:
            p.emit("sync", lambda e: e.dma_start(
                out=x_sb2[:].rearrange("p (i d) -> p i d", i=NT),
                in_=x5_d.rearrange("(i p) d -> p i d", p=128)),
                inc="ld", n=16)
            x5_ld = p.mark("ld")
            for e in ("tensor", "vector", "scalar"):
                p.wait(e, "ld", x5_ld)
            for g in range(3):
                p.wait("sync", "t", p.mark("t"))
                p.emit("sync", lambda e, g=g: e.dma_start(
                    out=wg_sb[:].rearrange("p (c d) -> p c d", c=DC),
                    in_=wgT[g].rearrange("c p d -> p c d")), inc="ld", n=16)
                wld = p.mark("ld")
                for tb in range(4):
                    tiles = list(range(tb * 4, tb * 4 + 4))
                    jit(tiles, x_sb2[:], xhT[:])
                    xh_ready = p.mark("s")
                    p.wait("tensor", "ld", wld)
                    p.wait("tensor", "s", xh_ready)
                    p.wait("scalar", "st", p.mark("st"))
                    for mc in range(DC):
                        bi, bank = pool.acquire()
                        for f in range(DC):
                            p.emit("tensor", lambda e, f=f, mc=mc,
                                   bank=bank: e.matmul(
                                bank[:],
                                wg_sb[:, mc * D + f * 128:
                                      mc * D + (f + 1) * 128],
                                xhT[:, f * 512:(f + 1) * 512],
                                start=(f == 0), stop=(f == DC - 1)),
                                inc="t")
                        p.wait("scalar", "t", p.mark("t"))
                        p.emit("scalar", lambda e, g=g, mc=mc, bank=bank:
                               e.activation(
                            out=zx_sb[:, mc * 512:(mc + 1) * 512],
                            in_=bank[:], func=AF.Identity,
                            bias=bgc[:, g * DC + mc:g * DC + mc + 1]),
                            inc="s")
                        pool.release(bi, "s")
                    p.wait("gpsimd", "s", p.mark("s"))
                    p.emit("gpsimd", lambda e, g=g, tb=tb: e.dma_start(
                        out=zx_d[g].rearrange("c p t -> p c t")
                            [:, :, tb * 512:(tb + 1) * 512],
                        in_=zx_sb[:].rearrange("p (c q) -> p c q", c=DC)),
                        inc="st", n=16, raw=True)
        p.barrier()
        emit_ffn1_chain(nc, p, pool, jit, ugT, uoT, woR, x5_d, x6_o, zx_d,
                        eye_sb[:], boc[:], incc[:])
        p.barrier()
        p.wait("gpsimd", "st", p.cnt["st"])
        p.build()
    return nc


def stage_l4(inputs, res3):
    com = {
        "basR": np.stack([pack_rhs(np.asarray(inputs[f"L1_attn_basis_{k}"],
                                              np.float32))
                          for k in "qkvo"]),
        "phT": np.ascontiguousarray(np.concatenate(
            [np.asarray(inputs[f"L1_attn_phase_{k}"], np.float32).T
             for k in "qkvo"], axis=1)),
        "amT": np.ascontiguousarray(np.concatenate(
            [np.asarray(inputs[f"L1_attn_amp_{k}"], np.float32).T
             for k in "qkvo"], axis=1)),
        "wgT": np.stack([pack_lhsT(np.asarray(inputs[k], np.float32))
                         for k in ("L1_ffn_wf", "L1_ffn_wi", "L1_ffn_wc")]),
        "ugT": np.stack([pack_lhsT(np.asarray(inputs[k], np.float32))
                         for k in ("L1_ffn_uf", "L1_ffn_ui", "L1_ffn_uc")]),
        "uoT": pack_lhsT(np.asarray(inputs["L1_ffn_uo"], np.float32)),
        "woR": pack_rhs(np.asarray(inputs["L1_ffn_wo"], np.float32)),
        "bg": np.stack([np.asarray(inputs[k], np.float32)
                        for k in ("L1_ffn_bf", "L1_ffn_bi", "L1_ffn_bc")]),
        "bo_ff": np.asarray(inputs["L1_ffn_bo"], np.float32),
        "inc_ff": np.asarray(inputs["L1_ffn_in_c"], np.float32),
    }
    x4_by_batch = np.empty((B, S, D), np.float32)
    for j in range(NCORE):
        blk = np.asarray(res3[j]["x4"]).reshape(B, 256, D)
        x4_by_batch[:, j * 256:(j + 1) * 256, :] = blk
    x4_flat = np.ascontiguousarray(x4_by_batch.reshape(B * S, D))
    maps = []
    for j in range(NCORE):
        own = np.ascontiguousarray(
            x4_by_batch[:, j * 256:(j + 1) * 256, :].reshape(TOK, D))
        maps.append(dict(com, x4f=x4_flat, x4own=own))
    return maps


# version 19
# speedup vs baseline: 1.0005x; 1.0005x over previous
import numpy as np
from contextlib import ExitStack

import concourse.bass as bass
import concourse.mybir as mybir
from concourse.bass_utils import run_bass_kernel_spmd

F32 = mybir.dt.float32
AF = mybir.ActivationFunctionType
OP = mybir.AluOpType
AX = mybir.AxisListType

NCORE = 8
DBG = False
B, S, D = 8, 2048, 1024
TOK = 2048
NT = TOK // 128
DC = D // 128
H, DH, PM = 16, 64, 256
FF = 4096
FC = FF // 128
HARM = 32
LN_EPS = 1e-5
KEPS = 1e-3
NSTEP = 5
BETA = 0.9

ENGS = ("sync", "tensor", "vector", "scalar", "gpsimd")



class Prog:

    def __init__(self, nc, stack):
        self.nc = nc
        self.stack = stack
        self.cmds = {e: [] for e in ENGS}
        self.sems = {}
        self.cnt = {}
        for s in ("t", "v", "s", "g", "ld", "ld2", "st"):
            self.sem(s)

    def sem(self, name):
        if name not in self.sems:
            self.sems[name] = self.stack.enter_context(self.nc.semaphore(name))
            self.cnt[name] = 0
        return name

    OWN = {"vector": "v", "scalar": "s", "gpsimd": "g"}

    def emit(self, eng, fn, inc=None, n=1, raw=False):
        own = None if raw else self.OWN.get(eng)
        if eng == "tensor":
            self.tensor_emits = getattr(self, "tensor_emits", 0) + 1
            cell = self.__dict__.setdefault("tcell", [None])
            if inc is not None:
                self.sem(inc)
                self.cnt[inc] += n
                s = self.sems[inc]
                def runt(e, fn=fn, s=s, n=n, cell=cell):
                    cell[0] = fn(e).then_inc(s, n)
                self.cmds[eng].append(runt)
                self.last_tensor_inc = (inc, self.cnt[inc])
            else:
                def runt2(e, fn=fn, cell=cell):
                    cell[0] = fn(e)
                self.cmds[eng].append(runt2)
                self.last_tensor_inc = None
            return
        if own is not None:
            assert inc in (None, own), (eng, inc)
            prev = self.cnt[own]
            self.cnt[own] += 1
            s = self.sems[own]
            def run(e, fn=fn, s=s, prev=prev):
                if prev:
                    e.wait_ge(s, prev)
                fn(e).then_inc(s, 1)
            self.cmds[eng].append(run)
        elif inc is not None:
            self.sem(inc)
            self.cnt[inc] += n
            s = self.sems[inc]
            self.cmds[eng].append(lambda e, fn=fn, s=s, n=n: fn(e).then_inc(s, n))
        else:
            self.cmds[eng].append(fn)

    def wait(self, eng, sem, val=None):
        val = self.cnt[sem] if val is None else val
        s = self.sems[sem]
        self.cmds[eng].append(lambda e, s=s, v=val: e.wait_ge(s, v))

    def mark(self, sem):
        return self.cnt[sem]

    def barrier(self, engs=ENGS):
        marks = {}
        for e in engs:
            if e in self.OWN:
                self.emit(e, lambda x: x.nop())
                marks[e] = (self.OWN[e], self.cnt[self.OWN[e]])
            elif e == "tensor":
                emitted = getattr(self, "tensor_emits", 0)
                self.sem("bt")
                if emitted > getattr(self, "_bar_tensor_seen", 0):
                    self._bar_tensor_seen = emitted
                    lti = getattr(self, "last_tensor_inc", None)
                    if lti is not None:
                        marks[e] = lti
                    else:
                        self.cnt["bt"] += 1
                        cell = self.__dict__.setdefault("tcell", [None])
                        s = self.sems["bt"]
                        self.cmds["tensor"].append(
                            lambda _e, cell=cell, s=s: cell[0].then_inc(s, 1))
                        marks[e] = ("bt", self.cnt["bt"])
                else:
                    marks[e] = ("bt", self.cnt["bt"])
            else:
                marks[e] = ("ld", self.cnt["ld"])
        stv = self.cnt["st"]
        for e in engs:
            for o in engs:
                if o != e:
                    sem, val = marks[o]
                    self.wait(e, sem, val)
            if stv:
                self.wait(e, "st", stv)

    def build(self):
        blk = self.stack.enter_context(self.nc.Block())
        for e in ENGS:
            def run_all(eng, e=e):
                for c in self.cmds[e]:
                    c(eng)
            getattr(blk, e)(run_all)


class PsumPool:
    def __init__(self, p, banks):
        self.p = p
        self.banks = banks
        self.busy = [None] * len(banks)
        self.rr = 0

    def acquire(self):
        i = self.rr % len(self.banks)
        self.rr += 1
        if self.busy[i] is not None:
            sem, val = self.busy[i]
            self.p.wait("tensor", sem, val)
            self.busy[i] = None
        return i, self.banks[i]

    def release(self, i, sem):
        self.busy[i] = (sem, self.p.cnt[sem])


class DBuf:

    def __init__(self, p, bufs, queue="sync", ld="ld"):
        self.p = p
        self.bufs = bufs
        self.queue = queue
        self.ld = ld
        self.tmarks = [None] * len(bufs)
        self.i = 0

    def load(self, fn):
        k = self.i % len(self.bufs)
        self.i += 1
        buf = self.bufs[k]
        if self.tmarks[k] is not None:
            self.p.wait(self.queue, "t", self.tmarks[k])
        self.p.emit(self.queue, lambda e, buf=buf, fn=fn: fn(e, buf),
                    inc=self.ld, n=16)
        return buf, self.p.mark(self.ld)

    def consumed(self):
        k = (self.i - 1) % len(self.bufs)
        self.tmarks[k] = self.p.cnt["t"]



BIAS_CONSTS = (1e-5, 1e-3, 1.5707963267948966,
               -3.141592653589793)


def setup_consts(nc, p, ec):
    arr = np.tile(np.array([BIAS_CONSTS], np.float32), (128, 1))
    cd = nc.inline_tensor(arr, "biasconsts").ap()
    cs = ec(nc.sbuf_tensor("consts_sb", [128, len(BIAS_CONSTS)], F32))
    p.emit("sync", lambda e: e.dma_start(out=cs[:], in_=cd), inc="ld", n=16)
    for j, v in enumerate(BIAS_CONSTS):
        nc.const_aps.register(F32, float(v), cs[:, j:j + 1])
    return cs


def ln_stats(p, x_sb, mean_sb, rstd_sb, sq_sb, col0, tiles):
    for n_, i in enumerate(tiles):
        xs = x_sb[:, i * D:(i + 1) * D]
        p.emit("vector", lambda e, xs=xs, i=i: e.tensor_reduce(
            out=mean_sb[:, i:i + 1], in_=xs, axis=AX.X, op=OP.add), inc="v")
        if n_ > 0:
            p.wait("scalar", "s", p.mark("s"))
        p.emit("scalar", lambda e, xs=xs, i=i: e.activation(
            out=sq_sb[:], in_=xs, func=AF.Square,
            accum_out=rstd_sb[:, i:i + 1]), inc="s")
    p.wait("vector", "s", p.mark("s"))
    for i in tiles:
        p.emit("vector", lambda e, i=i: e.tensor_scalar(
            out=mean_sb[:, i:i + 1], in0=mean_sb[:, i:i + 1],
            scalar1=1.0 / D, scalar2=None, op0=OP.mult), inc="v")
        p.emit("vector", lambda e, i=i: e.tensor_tensor(
            out=col0[:], in0=mean_sb[:, i:i + 1], in1=mean_sb[:, i:i + 1],
            op=OP.mult), inc="v")
        p.emit("vector", lambda e, i=i: e.tensor_scalar(
            out=rstd_sb[:, i:i + 1], in0=rstd_sb[:, i:i + 1],
            scalar1=1.0 / D, scalar2=None, op0=OP.mult), inc="v")
        p.emit("vector", lambda e, i=i: e.tensor_tensor(
            out=rstd_sb[:, i:i + 1], in0=rstd_sb[:, i:i + 1], in1=col0[:],
            op=OP.subtract), inc="v")
        p.wait("scalar", "v", p.mark("v"))
        p.emit("scalar", lambda e, i=i: e.activation(
            out=rstd_sb[:, i:i + 1], in_=rstd_sb[:, i:i + 1], func=AF.Ln,
            bias=LN_EPS), inc="s")
        p.emit("scalar", lambda e, i=i: e.activation(
            out=rstd_sb[:, i:i + 1], in_=rstd_sb[:, i:i + 1], func=AF.Exp,
            scale=-0.5), inc="s")
    p.wait("vector", "s", p.mark("s"))


def make_jit_xhat(p, pool, mean_sb, rstd_sb, eye_sb, xc_sb, diag_sb,
                  depth=1):
    last = {d: 0 for d in range(depth)}

    def jit(tiles, x_sb, xhT, stats=None):
        n = len(tiles) * 128
        nt = len(tiles)
        vstage = {}

        def stage(k):
            i = tiles[k]
            si = stats[k] if stats is not None else i
            sl, dsl = (k % depth) * D, (k % depth) * 128
            xs = x_sb[:, i * D:(i + 1) * D]
            p.wait("vector", "t", last[k % depth])
            p.emit("vector", lambda e, xs=xs, si=si, sl=sl: e.tensor_scalar(
                out=xc_sb[:, sl:sl + D], in0=xs,
                scalar1=mean_sb[:, si:si + 1],
                scalar2=None, op0=OP.subtract), inc="v")
            p.emit("vector", lambda e, si=si, dsl=dsl: e.tensor_scalar(
                out=diag_sb[:, dsl:dsl + 128], in0=eye_sb[:],
                scalar1=rstd_sb[:, si:si + 1],
                scalar2=None, op0=OP.mult), inc="v")
            vstage[k] = p.mark("v")

        for k in range(min(depth, nt)):
            stage(k)
        for k, i in enumerate(tiles):
            sl, dsl = (k % depth) * D, (k % depth) * 128
            p.wait("tensor", "v", vstage[k])
            for half in range(2):
                bi, bank = pool.acquire()
                for q in range(4):
                    f = half * 4 + q
                    p.emit("tensor", lambda e, f=f, q=q, sl=sl, dsl=dsl,
                           bank=bank: e.matmul(
                        bank[:, q * 128:(q + 1) * 128],
                        xc_sb[:, sl + f * 128:sl + (f + 1) * 128],
                        diag_sb[:, dsl:dsl + 128], start=True, stop=True),
                        inc="t" if q == 3 else None)
                if half == 1:
                    last[k % depth] = p.mark("t")
                p.wait("scalar", "t", p.mark("t"))
                for q in range(4):
                    f = half * 4 + q
                    p.emit("scalar", lambda e, f=f, q=q, k=k, bank=bank:
                           e.activation(
                               out=xhT[:, f * n + k * 128:f * n + (k + 1) * 128],
                               in_=bank[:, q * 128:(q + 1) * 128],
                               func=AF.Copy), inc="s")
                pool.release(bi, "s")
            if k + depth < nt:
                stage(k + depth)
    return jit


def spike_residual(p, z_sb, mem_sb, spk_sb, acc_sb, x_tile):
    p.emit("vector", lambda e: e.tensor_scalar(
        out=acc_sb[:], in0=z_sb[:], scalar1=1.0, scalar2=None,
        op0=OP.is_gt), inc="v")
    first = True
    for _ in range(2, NSTEP + 1):
        src = z_sb if first else mem_sb
        prev_spk = acc_sb if first else spk_sb
        p.emit("vector", lambda e, src=src: e.scalar_tensor_tensor(
            out=mem_sb[:], in0=src[:], scalar=BETA, in1=z_sb[:],
            op0=OP.mult, op1=OP.add), inc="v")
        p.emit("vector", lambda e, ps=prev_spk: e.tensor_tensor(
            out=mem_sb[:], in0=mem_sb[:], in1=ps[:], op=OP.subtract), inc="v")
        p.emit("vector", lambda e: e.tensor_scalar(
            out=spk_sb[:], in0=mem_sb[:], scalar1=1.0, scalar2=None,
            op0=OP.is_gt), inc="v")
        p.emit("vector", lambda e: e.tensor_tensor(
            out=acc_sb[:], in0=acc_sb[:], in1=spk_sb[:], op=OP.add), inc="v")
        first = False
    p.emit("vector", lambda e, xt=x_tile: e.scalar_tensor_tensor(
        out=xt, in0=acc_sb[:], scalar=0.1, in1=xt, op0=OP.mult, op1=OP.add),
        inc="v")


def emit_ffn0(p, pool, jit, x_sb, wfc_dram, wproj_dram, sb, dbg=None):
    xhT, h_sb, r_sb, z_sb = sb["xhT"], sb["h"], sb["r"], sb["z"]
    wfc = DBuf(p, sb["wfc"], "sync", "ld")
    wproj = DBuf(p, sb["wproj"], "sync", "ld")
    jit(list(range(4)), x_sb, xhT)
    xh_ready = p.mark("s")
    sq_mark = 0
    spike_mark = 0
    pre_wfc = []
    for tb in range(4):
        fcmode = ("fc" if "fc" in FFN0_PARTS else
                  "fcmm" if "fcmm" in FFN0_PARTS else
                  "fcload" if "fcload" in FFN0_PARTS else "off")
        for hc in range(FC if fcmode != "off" else 0):
            if pre_wfc:
                buf, ldm = pre_wfc.pop(0)
            else:
                buf, ldm = wfc.load(lambda e, buf, hc=hc: e.dma_start(
                    out=buf[:], in_=wfc_dram[hc]))
            p.wait("tensor", "ld", ldm)
            if fcmode == "fcload":
                continue
            if hc == 0:
                p.wait("tensor", "s", xh_ready)
            bi, bank = pool.acquire()
            for f in range(DC):
                p.emit("tensor", lambda e, f=f, buf=buf, bank=bank: e.matmul(
                    bank[:], buf[:, f * 128:(f + 1) * 128],
                    xhT[:, f * 512:(f + 1) * 512],
                    start=(f == 0), stop=(f == DC - 1)),
                    inc="t" if f == DC - 1 else None)
            wfc.consumed()
            if fcmode == "fcmm":
                pool.release(bi, "t")
                continue
            p.wait("scalar", "t", p.mark("t"))
            p.wait("scalar", "v", sq_mark)
            p.emit("scalar", lambda e, bank=bank: e.activation(
                out=r_sb[:], in_=bank[:], func=AF.Relu), inc="s")
            pool.release(bi, "s")
            p.wait("vector", "s", p.mark("s"))
            p.emit("vector", lambda e, hc=hc: e.tensor_mul(
                out=h_sb[:, hc * 512:(hc + 1) * 512], in0=r_sb[:],
                in1=r_sb[:]), inc="v")
            sq_mark = p.mark("v")
        h_ready = p.mark("v")
        if tb < 3:
            jit(list(range(tb * 4 + 4, tb * 4 + 8)), x_sb, xhT)
            next_xh = p.mark("s")
        if "proj" not in FFN0_PARTS:
            continue
        for k in range(8):
            bi, bank = pool.acquire()
            assert bi == k % len(pool.banks)
        banks = pool.banks
        p.wait("tensor", "v", h_ready)
        for hc in range(FC):
            buf, ldm = wproj.load(lambda e, buf, hc=hc: e.dma_start(
                out=buf[:], in_=wproj_dram[hc]))
            p.wait("tensor", "ld", ldm)
            for sub in range(4):
                for od in range(2):
                    bank = banks[sub * 2 + od]
                    p.emit("tensor",
                           lambda e, hc=hc, sub=sub, od=od, bank=bank,
                           buf=buf: e.matmul(
                               bank[:],
                               h_sb[:, hc * 512 + sub * 128:
                                    hc * 512 + (sub + 1) * 128],
                               buf[:, od * 512:(od + 1) * 512],
                               start=(hc == 0), stop=(hc == FC - 1)),
                           inc="t" if (sub == 3 and od == 1) else None)
            wproj.consumed()
        if dbg is not None and tb == 0:
            p.wait("gpsimd", "s", p.mark("s"))
            p.wait("gpsimd", "v", p.mark("v"))
            p.emit("gpsimd", lambda e: e.dma_start(out=dbg[0], in_=xhT),
                   inc="st", n=16, raw=True)
            p.emit("gpsimd", lambda e: e.dma_start(out=dbg[1], in_=h_sb),
                   inc="st", n=16, raw=True)
        proj_done = p.mark("t")
        p.wait("scalar", "t", proj_done)
        for sub in range(4):
            tile = tb * 4 + sub
            p.wait("scalar", "v", spike_mark if sub == 0 else p.mark("v"))
            for od in range(2):
                bank = banks[sub * 2 + od]
                p.emit("scalar", lambda e, od=od, bank=bank: e.activation(
                    out=z_sb[:, od * 512:(od + 1) * 512], in_=bank[:],
                    func=AF.Copy), inc="s")
            p.wait("vector", "s", p.mark("s"))
            if "spike" in FFN0_PARTS:
                spike_residual(p, z_sb, sb["mem"], sb["spk"], sb["acc"],
                               x_sb[:, tile * D:(tile + 1) * D])
        spike_mark = p.mark("v")
        for k in range(8):
            pool.release(k, "s")
        if tb < 3:
            xh_ready = next_xh
    p.barrier()



L1_UPTO = "all"
FFN0_PARTS = {"fc", "proj", "spike"}


def build_l1():
    nc = bass.Bass("TRN2", target_bir_lowering=False,
                   detect_race_conditions=False)
    x0 = nc.dram_tensor("x0", [TOK, D], F32, kind="ExternalInput").ap()
    wfc = nc.dram_tensor("wfc", [FC, 128, D], F32, kind="ExternalInput").ap()
    wproj = nc.dram_tensor("wproj", [FC, 128, D], F32,
                           kind="ExternalInput").ap()
    wkT = nc.dram_tensor("wkT", [DC, 128, D], F32, kind="ExternalInput").ap()
    bk = nc.dram_tensor("bk", [D], F32, kind="ExternalInput").ap()
    x1_o = nc.dram_tensor("x1", [TOK, D], F32, kind="ExternalOutput").ap()
    kT_o = nc.dram_tensor("kT", [D, TOK], F32, kind="ExternalOutput").ap()
    diag_o = nc.dram_tensor("diag", [TOK, H], F32, kind="ExternalOutput").ap()
    lmax_o = nc.dram_tensor("lmax", [1, 1], F32, kind="ExternalOutput").ap()
    if DBG:
        dbg_mean = nc.dram_tensor("dbg_mean", [128, NT], F32,
                                  kind="ExternalOutput").ap()
        dbg_rstd = nc.dram_tensor("dbg_rstd", [128, NT], F32,
                                  kind="ExternalOutput").ap()
        dbg_xh = nc.dram_tensor("dbg_xh", [128, DC * 512], F32,
                                kind="ExternalOutput").ap()
        dbg_h = nc.dram_tensor("dbg_h", [128, FC * 512], F32,
                               kind="ExternalOutput").ap()
    eye_d = nc.inline_tensor(np.eye(128, dtype=np.float32), "eye").ap()
    hmask = np.zeros((128, 2), dtype=np.float32)
    hmask[:64, 0] = 1.0
    hmask[64:, 1] = 1.0
    ones_d = nc.inline_tensor(hmask, "hmask").ap()

    with ExitStack() as stack:
        ec = stack.enter_context
        p = Prog(nc, stack)
        x_sb = ec(nc.sbuf_tensor("x_sb", [128, NT * D], F32))
        mean_sb = ec(nc.sbuf_tensor("mean_sb", [128, NT], F32))
        rstd_sb = ec(nc.sbuf_tensor("rstd_sb", [128, NT], F32))
        sq_sb = ec(nc.sbuf_tensor("sq_sb", [128, D], F32))
        col0 = ec(nc.sbuf_tensor("col0", [128, 1], F32))
        eye_sb = ec(nc.sbuf_tensor("eye_sb", [128, 128], F32))
        setup_consts(nc, p, ec)
        ones_sb = ec(nc.sbuf_tensor("ones_sb", [128, 2], F32))
        xc_sb = ec(nc.sbuf_tensor("xc_sb", [128, 2 * D], F32))
        dg_sb = ec(nc.sbuf_tensor("dg_sb", [128, 2 * 128], F32))
        diag_sb = ec(nc.sbuf_tensor("diag_sb", [128, NT * H], F32))
        bk_sb = ec(nc.sbuf_tensor("bk_sb", [128, DC], F32))
        banks = [ec(nc.psum_tensor(f"ps{i}", [128, 512], F32))
                 for i in range(8)]
        pool = PsumPool(p, [b[:] for b in banks])
        jit = make_jit_xhat(p, pool, mean_sb[:], rstd_sb[:], eye_sb[:],
                            xc_sb[:], dg_sb[:], depth=2)

        p.emit("sync", lambda e: e.dma_start(
            out=x_sb[:].rearrange("p (i d) -> p i d", i=NT),
            in_=x0.rearrange("(i p) d -> p i d", p=128)), inc="ld", n=16)
        p.emit("sync", lambda e: e.dma_start(out=eye_sb[:], in_=eye_d),
               inc="ld", n=16)
        p.emit("sync", lambda e: e.dma_start(out=ones_sb[:], in_=ones_d),
               inc="ld", n=16)
        p.emit("sync", lambda e: e.dma_start(
            out=bk_sb[:], in_=bk.rearrange("(c p) -> p c", p=128),
            allow_slow_non_contiguous=True), inc="ld", n=16)
        ldm = p.mark("ld")
        for e in ("tensor", "vector", "scalar"):
            p.wait(e, "ld", ldm)

        ln_stats(p, x_sb[:], mean_sb[:], rstd_sb[:], sq_sb[:], col0[:],
                 range(NT))
        p.barrier()
        if DBG:
            p.emit("gpsimd", lambda e: e.dma_start(out=dbg_mean,
                                                   in_=mean_sb[:]),
                   inc="st", n=16, raw=True)
            p.emit("gpsimd", lambda e: e.dma_start(out=dbg_rstd,
                                                   in_=rstd_sb[:]),
                   inc="st", n=16, raw=True)
        with nc.sbuf_tensor("xhT", [128, DC * 512], F32) as xhT, \
                nc.sbuf_tensor("h_sb", [128, FC * 512], F32) as h_sb, \
                nc.sbuf_tensor("r_sb", [128, 512], F32) as r_sb, \
                nc.sbuf_tensor("z_sb", [128, D], F32) as z_sb, \
                nc.sbuf_tensor("mem_sb", [128, D], F32) as mem_sb, \
                nc.sbuf_tensor("spk_sb", [128, D], F32) as spk_sb, \
                nc.sbuf_tensor("acc_sb", [128, D], F32) as acc_sb, \
                nc.sbuf_tensor("wfc0", [128, D], F32) as wfc0, \
                nc.sbuf_tensor("wfc1", [128, D], F32) as wfc1, \
                nc.sbuf_tensor("wp0", [128, D], F32) as wp0, \
                nc.sbuf_tensor("wp1", [128, D], F32) as wp1:
            sb = dict(xhT=xhT[:], h=h_sb[:], r=r_sb[:], z=z_sb[:],
                      mem=mem_sb[:], spk=spk_sb[:], acc=acc_sb[:],
                      wfc=[wfc0[:], wfc1[:]], wproj=[wp0[:], wp1[:]])
            emit_ffn0(p, pool, jit, x_sb[:], wfc, wproj, sb,
                      dbg=(dbg_xh, dbg_h) if DBG else None)

        p.emit("gpsimd", lambda e: e.dma_start(
            out=x1_o.rearrange("(i p) d -> p i d", p=128),
            in_=x_sb[:].rearrange("p (i d) -> p i d", i=NT)), inc="st", n=16, raw=True)

        if L1_UPTO == "all":
            emit_l1_kphase(nc, p, pool, jit, x_sb, mean_sb, rstd_sb, sq_sb,
                           col0, eye_sb, ones_sb, bk_sb, diag_sb, wkT,
                           kT_o, diag_o, lmax_o)
        p.barrier()
        p.wait("gpsimd", "st", p.cnt["st"])
        p.build()
    return nc


def emit_l1_kphase(nc, p, pool, jit, x_sb, mean_sb, rstd_sb, sq_sb, col0,
                   eye_sb, ones_sb, bk_sb, diag_sb, wkT, kT_o, diag_o,
                   lmax_o):
        ln_stats(p, x_sb[:], mean_sb[:], rstd_sb[:], sq_sb[:], col0[:],
                 range(NT))
        p.barrier()
        with nc.sbuf_tensor("xhT2", [128, DC * 512], F32) as xhT2, \
                nc.sbuf_tensor("kblk", [128, DC * 512], F32) as kblk, \
                nc.sbuf_tensor("k2", [128, DC * 512], F32) as k2, \
                nc.sbuf_tensor("wk0", [128, D], F32) as wk0, \
                nc.sbuf_tensor("wk1", [128, D], F32) as wk1, \
                nc.sbuf_tensor("mrow", [1, 129], F32) as mrow:
            wk = DBuf(p, [wk0[:], wk1[:]], "sync", "ld")
            st_prev = 0
            for tb in range(4):
                tiles = list(range(tb * 4, tb * 4 + 4))
                jit(tiles, x_sb[:], xhT2[:])
                xh_ready = p.mark("s")
                p.wait("scalar", "st", st_prev)
                p.wait("scalar", "t", p.mark("t"))
                for oc in range(DC):
                    buf, ldm = wk.load(lambda e, buf, oc=oc: e.dma_start(
                        out=buf[:], in_=wkT[oc]))
                    p.wait("tensor", "ld", ldm)
                    if oc == 0:
                        p.wait("tensor", "s", xh_ready)
                    bi, bank = pool.acquire()
                    for f in range(DC):
                        p.emit("tensor",
                               lambda e, f=f, buf=buf, bank=bank: e.matmul(
                                   bank[:], buf[:, f * 128:(f + 1) * 128],
                                   xhT2[:, f * 512:(f + 1) * 512],
                                   start=(f == 0), stop=(f == DC - 1)),
                               inc="t" if f == DC - 1 else None)
                    wk.consumed()
                    p.wait("scalar", "t", p.mark("t"))
                    p.emit("scalar", lambda e, oc=oc, bank=bank: e.activation(
                        out=kblk[:, oc * 512:(oc + 1) * 512], in_=bank[:],
                        func=AF.Identity, bias=bk_sb[:, oc:oc + 1]), inc="s")
                    pool.release(bi, "s")
                p.wait("gpsimd", "s", p.mark("s"))
                p.emit("gpsimd", lambda e, tb=tb: e.dma_start(
                    out=kT_o.rearrange("(c p) t -> p c t", p=128)
                        [:, :, tb * 512:(tb + 1) * 512],
                    in_=kblk[:].rearrange("p (c q) -> p c q", c=DC)),
                    inc="st", n=16, raw=True)
                st_prev = p.mark("st")
                p.emit("scalar", lambda e: e.activation(
                    out=k2[:], in_=kblk[:], func=AF.Square), inc="s")
                p.wait("tensor", "s", p.mark("s"))
                for sub in range(4):
                    tile = tb * 4 + sub
                    bi, bank = pool.acquire()
                    for h in range(H):
                        oc, half = h // 2, (h % 2)
                        p.emit("tensor",
                               lambda e, h=h, oc=oc, half=half, sub=sub,
                               bank=bank: e.matmul(
                                   bank[:, h:h + 1],
                                   k2[:, oc * 512 + sub * 128:
                                      oc * 512 + (sub + 1) * 128],
                                   ones_sb[:, half:half + 1],
                                   start=True, stop=True),
                               inc="t" if h == H - 1 else None)
                    p.wait("vector", "t", p.mark("t"))
                    p.emit("vector", lambda e, tile=tile, bank=bank:
                           e.tensor_scalar(
                               out=diag_sb[:, tile * H:(tile + 1) * H],
                               in0=bank[:, 0:H], scalar1=-0.5, scalar2=None,
                               op0=OP.mult), inc="v")
                    pool.release(bi, "v")
            p.barrier()
            p.emit("vector", lambda e: e.tensor_reduce(
                out=col0[:], in_=diag_sb[:], axis=AX.X, op=OP.max), inc="v")
            p.wait("tensor", "v", p.mark("v"))
            bi, bank = pool.acquire()
            p.emit("tensor", lambda e, bank=bank: e.matmul(
                bank[0:1, 0:128], col0[:], eye_sb[:], start=True, stop=True),
                inc="t")
            p.wait("scalar", "t", p.mark("t"))
            p.emit("scalar", lambda e, bank=bank: e.activation(
                out=mrow[0:1, 0:128], in_=bank[0:1, 0:128], func=AF.Copy),
                inc="s")
            pool.release(bi, "s")
            p.wait("vector", "s", p.mark("s"))
            p.emit("vector", lambda e: e.tensor_reduce(
                out=mrow[0:1, 128:129], in_=mrow[0:1, 0:128], axis=AX.X,
                op=OP.max), inc="v")
            p.wait("gpsimd", "v", p.mark("v"))
            p.emit("gpsimd", lambda e: e.dma_start(
                out=lmax_o, in_=mrow[0:1, 128:129]), inc="st", n=16, raw=True)
            p.emit("gpsimd", lambda e: e.dma_start(
                out=diag_o.rearrange("(i p) h -> p i h", p=128),
                in_=diag_sb[:].rearrange("p (i h) -> p i h", i=NT)),
                inc="st", n=16, raw=True)



def pack_lhsT(w):
    o, i = w.shape
    return np.ascontiguousarray(
        w.reshape(o // 128, 128, i // 128, 128).transpose(0, 3, 2, 1)
        .reshape(o // 128, 128, i))


def pack_rhs(w):
    o, i = w.shape
    return np.ascontiguousarray(w.T.reshape(i // 128, 128, o))


def stage_l1(inputs):
    hs = np.ascontiguousarray(inputs["L0_attn_hidden_states"],
                              dtype=np.float32)
    com = {
        "wfc": pack_lhsT(np.asarray(inputs["L0_ffn_W_fc"], np.float32)),
        "wproj": pack_rhs(np.asarray(inputs["L0_ffn_W_proj"], np.float32)),
        "wkT": pack_lhsT(np.asarray(inputs["L0_attn_Wk"], np.float32)),
        "bk": np.asarray(inputs["L0_attn_bk"], np.float32),
    }
    return [dict(com, x0=np.ascontiguousarray(hs[b])) for b in range(NCORE)]


def run_l1(inputs):
    nc = build_l1()
    res = run_bass_kernel_spmd(nc, stage_l1(inputs),
                               core_ids=list(range(NCORE)))
    return res.results


def _run(build, maps):
    nc = build()
    return run_bass_kernel_spmd(nc, maps, core_ids=list(range(NCORE))).results


def kernel(**inputs):
    r1 = _run(build_l1, stage_l1(inputs))
    r2 = _run(build_l2, stage_l2(inputs, r1))
    r3 = _run(build_l3, stage_l3(inputs, r2))
    r4 = _run(build_l4, stage_l4(inputs, r3))
    out = np.empty((B, S, D), np.float32)
    for j in range(NCORE):
        out[:, j * 256:(j + 1) * 256, :] = \
            np.asarray(r4[j]["x6"]).reshape(B, 256, D)
    return out



def build_l2():
    nc = bass.Bass("TRN2", target_bir_lowering=False,
                   detect_race_conditions=False)
    x1 = nc.dram_tensor("x1", [TOK, D], F32, kind="ExternalInput").ap()
    kT_i = nc.dram_tensor("kT", [D, TOK], F32, kind="ExternalInput").ap()
    diag_i = nc.dram_tensor("diag", [TOK, H], F32, kind="ExternalInput").ap()
    maxes = nc.dram_tensor("maxes", [1, NCORE], F32, kind="ExternalInput").ap()
    wqT = nc.dram_tensor("wqT", [DC, 128, D], F32, kind="ExternalInput").ap()
    wvT = nc.dram_tensor("wvT", [DC, 128, D], F32, kind="ExternalInput").ap()
    woT = nc.dram_tensor("woT", [DC, 128, D], F32, kind="ExternalInput").ap()
    projT = nc.dram_tensor("projT", [DH, PM], F32, kind="ExternalInput").ap()
    bq = nc.dram_tensor("bq", [D], F32, kind="ExternalInput").ap()
    bv = nc.dram_tensor("bv", [D], F32, kind="ExternalInput").ap()
    bo = nc.dram_tensor("bo", [D], F32, kind="ExternalInput").ap()
    x2_o = nc.dram_tensor("x2", [TOK, D], F32, kind="ExternalOutput").ap()
    if DBG:
        dbg_ctx = nc.dram_tensor("dbg_ctx", [128, H * 2 * 65], F32,
                                 kind="ExternalOutput").ap()
        dbg_q = nc.dram_tensor("dbg_q", [128, DC * 512], F32,
                               kind="ExternalOutput").ap()
        dbg_v = nc.dram_tensor("dbg_v", [128, 4 * (H * 65 + 1)], F32,
                               kind="ExternalOutput").ap()
        dbg_kf = nc.dram_tensor("dbg_kf", [128, PM], F32,
                                kind="ExternalOutput").ap()
    eye_d = nc.inline_tensor(np.eye(128, dtype=np.float32), "eye").ap()
    onesrow_d = nc.inline_tensor(np.ones((1, 128), dtype=np.float32),
                                 "onesrow").ap()

    with ExitStack() as stack:
        ec = stack.enter_context
        p = Prog(nc, stack)
        x_sb = ec(nc.sbuf_tensor("x_sb", [128, NT * D], F32))
        mean_sb = ec(nc.sbuf_tensor("mean_sb", [128, NT], F32))
        rstd_sb = ec(nc.sbuf_tensor("rstd_sb", [128, NT], F32))
        sq_sb = ec(nc.sbuf_tensor("sq_sb", [128, D], F32))
        col0 = ec(nc.sbuf_tensor("col0", [128, 1], F32))
        eye_sb = ec(nc.sbuf_tensor("eye_sb", [128, 128], F32))
        ones_row = ec(nc.sbuf_tensor("ones_row", [1, 128], F32))
        setup_consts(nc, p, ec)
        xc_sb = ec(nc.sbuf_tensor("xc_sb", [128, D], F32))
        dg_sb = ec(nc.sbuf_tensor("dg_sb", [128, 128], F32))
        bias_sb = ec(nc.sbuf_tensor("bias_sb", [128, NT * H], F32))
        stab_sb = ec(nc.sbuf_tensor("stab_sb", [128, 1], F32))
        mrow = ec(nc.sbuf_tensor("mrow", [1, NCORE + 1], F32))
        projc = ec(nc.sbuf_tensor("projc", [128, 2 * PM], F32))
        bqc = ec(nc.sbuf_tensor("bqc", [128, DC], F32))
        bv_row = ec(nc.sbuf_tensor("bv_row", [1, D], F32))
        bo_row = ec(nc.sbuf_tensor("bo_row", [1, D], F32))
        diag_sb = ec(nc.sbuf_tensor("diag_sb", [128, NT * H], F32))
        ctx_sb = ec(nc.sbuf_tensor("ctx_sb", [128, H * 2 * 65], F32))
        xhT = ec(nc.sbuf_tensor("xhT", [128, DC * 512], F32))
        banks = [ec(nc.psum_tensor(f"ps{i}", [128, 512], F32))
                 for i in range(8)]
        pool = PsumPool(p, [b[:] for b in banks])
        jit = make_jit_xhat(p, pool, mean_sb[:], rstd_sb[:], eye_sb[:],
                            xc_sb[:], dg_sb[:])

        p.emit("sync", lambda e: e.dma_start(
            out=x_sb[:].rearrange("p (i d) -> p i d", i=NT),
            in_=x1.rearrange("(i p) d -> p i d", p=128)), inc="ld", n=16)
        p.emit("sync", lambda e: e.dma_start(out=eye_sb[:], in_=eye_d),
               inc="ld", n=16)
        p.emit("sync", lambda e: e.dma_start(out=ones_row[:], in_=onesrow_d),
               inc="ld", n=16)
        p.emit("sync", lambda e: e.dma_start(
            out=diag_sb[:].rearrange("p (i h) -> p i h", i=NT),
            in_=diag_i.rearrange("(i p) h -> p i h", p=128)), inc="ld", n=16)
        p.emit("sync", lambda e: e.dma_start(out=mrow[0:1, 0:NCORE],
                                             in_=maxes), inc="ld", n=16)
        for v in range(2):
            for mc in range(2):
                p.emit("sync", lambda e, v=v, mc=mc: e.dma_start(
                    out=projc[v * 64:v * 64 + 64,
                              v * PM + mc * 128:v * PM + (mc + 1) * 128],
                    in_=projT[:, mc * 128:(mc + 1) * 128]), inc="ld", n=16)
        p.emit("sync", lambda e: e.dma_start(
            out=bqc[:], in_=bq.rearrange("(c p) -> p c", p=128),
            allow_slow_non_contiguous=True), inc="ld", n=16)
        p.emit("sync", lambda e: e.dma_start(
            out=bv_row[:], in_=bv.rearrange("(a d) -> a d", a=1)),
            inc="ld", n=16)
        p.emit("sync", lambda e: e.dma_start(
            out=bo_row[:], in_=bo.rearrange("(a d) -> a d", a=1)),
            inc="ld", n=16)
        ldm = p.mark("ld")
        for e in ("tensor", "vector", "scalar", "gpsimd"):
            p.wait(e, "ld", ldm)

        DNORM = float(DH) ** -0.25
        p.emit("vector", lambda e: e.memset(projc[64:128, 0:PM], 0.0))
        p.emit("vector", lambda e: e.memset(projc[0:64, PM:2 * PM], 0.0))
        p.emit("scalar", lambda e: e.activation(
            out=projc[:], in_=projc[:], func=AF.Copy, scale=DNORM))
        p.wait("scalar", "v", p.mark("v"))
        p.emit("vector", lambda e: e.tensor_reduce(
            out=mrow[0:1, NCORE:NCORE + 1], in_=mrow[0:1, 0:NCORE],
            axis=AX.X, op=OP.max))
        p.wait("tensor", "v", p.mark("v"))
        bi0, bank0 = pool.acquire()
        p.emit("tensor", lambda e, bank=bank0: e.matmul(
            bank[0:128, 0:1], ones_row[:], mrow[0:1, NCORE:NCORE + 1],
            start=True, stop=True), inc="t")
        p.wait("scalar", "t", p.mark("t"))
        p.emit("scalar", lambda e, bank=bank0: e.activation(
            out=stab_sb[:], in_=bank[0:128, 0:1], func=AF.Copy))
        pool.release(bi0, "s")
        p.wait("vector", "s", p.mark("s"))
        p.emit("vector", lambda e: e.tensor_scalar(
            out=bias_sb[:], in0=diag_sb[:], scalar1=stab_sb[:, 0:1],
            scalar2=None, op0=OP.subtract))
        p.emit("vector", lambda e: e.memset(ctx_sb[:], 0.0))
        ln_stats(p, x_sb[:], mean_sb[:], rstd_sb[:], sq_sb[:], col0[:],
                 range(NT))
        p.barrier()

        with nc.sbuf_tensor("wvs", [128, DC * D], F32) as wvs, \
                nc.sbuf_tensor("vblk", [128, 4 * H * 65 + 4], F32) as vblk, \
                nc.sbuf_tensor("kTb", [128, DC * 512], F32) as kTb, \
                nc.sbuf_tensor("kf_sb", [128, PM], F32) as kf_sb:
            VW = H * 65 + 1
            p.emit("sync", lambda e: e.dma_start(
                out=wvs[:].rearrange("p (c d) -> p c d", c=DC),
                in_=wvT.rearrange("c p d -> p c d")), inc="ld", n=16)
            wv_ld = p.mark("ld")
            for tb in range(4):
                tiles = list(range(tb * 4, tb * 4 + 4))
                if tb > 0:
                    p.wait("sync", "t", p.mark("t"))
                p.emit("sync", lambda e, tb=tb: e.dma_start(
                    out=kTb[:].rearrange("p (c q) -> p c q", c=DC),
                    in_=kT_i.rearrange("(c p) t -> p c t", p=128)
                        [:, :, tb * 512:(tb + 1) * 512]), inc="ld", n=16)
                ktb_ld = p.mark("ld")
                jit(tiles, x_sb[:], xhT[:])
                xh_ready = p.mark("s")
                p.wait("vector", "t", p.mark("t"))
                for sub in range(4):
                    p.emit("vector", lambda e, sub=sub: e.memset(
                        vblk[:, sub * VW + 64:sub * VW + H * 65:65], 1.0))
                ones_mark = p.mark("v")
                p.wait("tensor", "ld", wv_ld)
                p.wait("tensor", "s", xh_ready)
                for sub in range(4):
                    for ob in range(2):
                        bi, bank = pool.acquire()
                        for f in range(DC):
                            p.emit("tensor", lambda e, f=f, sub=sub, ob=ob,
                                   bank=bank: e.matmul(
                                bank[:],
                                xhT[:, f * 512 + sub * 128:
                                    f * 512 + (sub + 1) * 128],
                                wvs[:, f * D + ob * 512:
                                    f * D + (ob + 1) * 512],
                                start=(f == 0), stop=False))
                        p.emit("tensor", lambda e, ob=ob, bank=bank: e.matmul(
                            bank[:], ones_row[:],
                            bv_row[0:1, ob * 512:(ob + 1) * 512],
                            start=False, stop=True), inc="t")
                        p.wait("scalar", "t", p.mark("t"))
                        if sub == 0 and ob == 0:
                            p.wait("scalar", "v", ones_mark)
                        p.emit("scalar", lambda e, sub=sub, ob=ob, bank=bank:
                               e.activation(
                            out=vblk[:, sub * VW + ob * 8 * 65:
                                     sub * VW + ob * 8 * 65 + 8 * 65]
                            .rearrange("p (h e) -> p h e", h=8)
                            [:, :, 0:64],
                            in_=bank[:].rearrange("p (h e) -> p h e", h=8),
                            func=AF.Copy), inc="s")
                        pool.release(bi, "s")
                v_ready = p.mark("s")
                p.wait("tensor", "ld", ktb_ld)
                for h in range(H):
                    hp, hv = h // 2, h % 2
                    for sub in range(4):
                        bi, bank = pool.acquire()
                        p.emit("tensor", lambda e, hp=hp, hv=hv, sub=sub,
                               bank=bank: e.matmul(
                            bank[:, 0:PM],
                            kTb[:, hp * 512 + sub * 128:
                                hp * 512 + (sub + 1) * 128],
                            projc[:, hv * PM:(hv + 1) * PM],
                            start=True, stop=True), inc="t")
                        p.wait("scalar", "t", p.mark("t"))
                        tile = tb * 4 + sub
                        p.emit("scalar", lambda e, tile=tile, h=h,
                               bank=bank: e.activation(
                            out=kf_sb[:, 0:PM], in_=bank[:, 0:PM],
                            func=AF.Exp,
                            bias=bias_sb[:, tile * H + h:tile * H + h + 1]),
                            inc="s")
                        pool.release(bi, "s")
                        p.wait("vector", "s", p.mark("s"))
                        p.emit("vector", lambda e: e.tensor_scalar(
                            out=kf_sb[:, 0:PM], in0=kf_sb[:, 0:PM],
                            scalar1=KEPS, scalar2=None, op0=OP.add))
                        kf_ready = p.mark("v")
                        p.wait("tensor", "v", kf_ready)
                        p.wait("tensor", "s", v_ready)
                        for mc in range(2):
                            bi2, bank2 = pool.acquire()
                            p.emit("tensor", lambda e, mc=mc, h=h, sub=sub,
                                   bank2=bank2: e.matmul(
                                bank2[:, 0:65],
                                kf_sb[:, mc * 128:(mc + 1) * 128],
                                vblk[:, sub * VW + h * 65:
                                     sub * VW + (h + 1) * 65],
                                start=True, stop=True), inc="t")
                            p.wait("vector", "t", p.mark("t"))
                            p.emit("vector", lambda e, h=h, mc=mc,
                                   bank2=bank2: e.tensor_tensor(
                                out=ctx_sb[:, (h * 2 + mc) * 65:
                                           (h * 2 + mc + 1) * 65],
                                in0=ctx_sb[:, (h * 2 + mc) * 65:
                                           (h * 2 + mc + 1) * 65],
                                in1=bank2[:, 0:65], op=OP.add))
                            pool.release(bi2, "v")
            if DBG:
                p.wait("gpsimd", "v", p.mark("v"))
                p.wait("gpsimd", "s", p.mark("s"))
                p.emit("gpsimd", lambda e: e.dma_start(out=dbg_v,
                                                       in_=vblk[:]),
                       inc="st", n=16, raw=True)
                p.emit("gpsimd", lambda e: e.dma_start(out=dbg_kf,
                                                       in_=kf_sb[:]),
                       inc="st", n=16, raw=True)
                p.emit("gpsimd", lambda e: e.dma_start(out=dbg_ctx,
                                                       in_=ctx_sb[:]),
                       inc="st", n=16, raw=True)
                p.wait("gpsimd", "st", p.cnt["st"])
        p.barrier()

        with nc.sbuf_tensor("wqs", [128, DC * D], F32) as wqs, \
                nc.sbuf_tensor("qTb", [128, DC * 512], F32) as qTb, \
                nc.sbuf_tensor("qf_sb", [128, 2 * 512], F32) as qf_sb, \
                nc.sbuf_tensor("on_sb", [128, 128], F32) as on_sb, \
                nc.sbuf_tensor("den_sb", [128, 512], F32) as den_sb, \
                nc.sbuf_tensor("thi", [128, DC * 512], F32) as thi, \
                nc.sbuf_tensor("outT", [128, DC * 512], F32) as outT, \
                nc.sbuf_tensor("wo0", [128, D], F32) as wo0, \
                nc.sbuf_tensor("wo1", [128, D], F32) as wo1:
            p.emit("sync", lambda e: e.dma_start(
                out=wqs[:].rearrange("p (c d) -> p c d", c=DC),
                in_=wqT.rearrange("c p d -> p c d")), inc="ld", n=16)
            wq_ld = p.mark("ld")
            wo = DBuf(p, [wo0[:], wo1[:]], "sync", "ld")
            p.emit("vector", lambda e: e.memset(on_sb[:], 1.0))
            for tb in range(4):
                tiles = list(range(tb * 4, tb * 4 + 4))
                jit(tiles, x_sb[:], xhT[:])
                xh_ready = p.mark("s")
                p.wait("tensor", "ld", wq_ld)
                p.wait("tensor", "s", xh_ready)
                for hp in range(DC):
                    bi, bank = pool.acquire()
                    for f in range(DC):
                        p.emit("tensor", lambda e, f=f, hp=hp, bank=bank:
                               e.matmul(
                            bank[:], wqs[:, hp * D + f * 128:
                                         hp * D + (f + 1) * 128],
                            xhT[:, f * 512:(f + 1) * 512],
                            start=(f == 0), stop=(f == DC - 1)), inc="t")
                    p.wait("scalar", "t", p.mark("t"))
                    p.emit("scalar", lambda e, hp=hp, bank=bank: e.activation(
                        out=qTb[:, hp * 512:(hp + 1) * 512], in_=bank[:],
                        func=AF.Identity, bias=bqc[:, hp:hp + 1]), inc="s")
                    pool.release(bi, "s")
                q_ready = p.mark("s")

                def emit_dash(i):
                    hp, hv = divmod(i, 2)
                    sbanks = []
                    p.wait("tensor", "s", q_ready)
                    for mc in range(2):
                        bi, bank = pool.acquire()
                        sbanks.append((bi, bank))
                        p.emit("tensor", lambda e, hp=hp, hv=hv, mc=mc,
                               bank=bank: e.matmul(
                            bank[:],
                            projc[:, hv * PM + mc * 128:
                                  hv * PM + (mc + 1) * 128],
                            qTb[:, hp * 512:(hp + 1) * 512],
                            start=True, stop=True),
                            inc="t" if mc == 1 else None)
                    return sbanks, p.mark("t")

                pend = emit_dash(0)
                num_mark = pend[1]
                mult_mark = 0
                for i in range(2 * DC):
                    hp, hv = divmod(i, 2)
                    h = hp * 2 + hv
                    qf = qf_sb[:]
                    sbanks, dmark = pend
                    if i + 1 < 2 * DC:
                        pend = emit_dash(i + 1)
                    p.wait("scalar", "t", num_mark)
                    for mc in range(2):
                        bi, bank = sbanks[mc]
                        p.emit("scalar", lambda e, mc=mc, qf=qf, bank=bank:
                               e.activation(
                            out=qf[:, mc * 512:(mc + 1) * 512],
                            in_=bank[:], func=AF.Exp), inc="s")
                        pool.release(bi, "s")
                    p.wait("vector", "s", p.mark("s"))
                    p.emit("vector", lambda e, qf=qf: e.tensor_scalar(
                        out=qf, in0=qf, scalar1=KEPS,
                        scalar2=None, op0=OP.add))
                    qf_ready = p.mark("v")
                    p.wait("tensor", "v", qf_ready)
                    bi, bank = pool.acquire()
                    for mc in range(2):
                        p.emit("tensor", lambda e, mc=mc, h=h, qf=qf,
                               bank=bank: e.matmul(
                            bank[0:65, :],
                            ctx_sb[:, (h * 2 + mc) * 65:
                                   (h * 2 + mc + 1) * 65],
                            qf[:, mc * 512:(mc + 1) * 512],
                            start=(mc == 0), stop=(mc == 1)),
                            inc="t" if mc == 1 else None)
                    num_mark = p.mark("t")
                    p.wait("vector", "t", num_mark)
                    p.emit("vector", lambda e, bank=bank: e.reciprocal(
                        out=den_sb[64:65, :], in_=bank[64:65, :]))
                    p.wait("tensor", "v", p.mark("v"))
                    bi2, bank2 = pool.acquire()
                    p.emit("tensor", lambda e, bank2=bank2: e.matmul(
                        bank2[0:64, :], on_sb[64:65, 0:64],
                        den_sb[64:65, :], start=True, stop=True), inc="t")
                    p.wait("scalar", "t", p.mark("t"))
                    p.wait("scalar", "v", mult_mark)
                    p.emit("scalar", lambda e, bank2=bank2: e.activation(
                        out=den_sb[0:64, :], in_=bank2[0:64, :],
                        func=AF.Copy), inc="s")
                    pool.release(bi2, "s")
                    p.wait("vector", "s", p.mark("s"))
                    acc = outT if hv == 0 else thi
                    p.emit("vector", lambda e, hp=hp, acc=acc, bank=bank:
                           e.tensor_tensor(
                        out=acc[0:64, hp * 512:(hp + 1) * 512],
                        in0=bank[0:64, :], in1=den_sb[0:64, :],
                        op=OP.mult))
                    mult_mark = p.mark("v")
                    pool.release(bi, "v")
                p.wait("sync", "v", p.mark("v"))
                p.emit("sync", lambda e: e.dma_start(
                    out=outT[64:128, :], in_=thi[0:64, :]), inc="ld", n=16)
                p.wait("tensor", "ld", p.mark("ld"))
                out_ready = p.mark("v")
                if DBG and tb == 3:
                    p.wait("gpsimd", "s", p.mark("s"))
                    p.emit("gpsimd", lambda e: e.dma_start(out=dbg_q,
                                                           in_=qTb[:]),
                           inc="st", n=16, raw=True)
                p.wait("tensor", "v", out_ready)
                for k in range(8):
                    pool.acquire()
                zb = pool.banks
                for ic in range(DC):
                    buf, ldm2 = wo.load(lambda e, buf, ic=ic: e.dma_start(
                        out=buf[:], in_=woT[ic]))
                    p.wait("tensor", "ld", ldm2)
                    for sub in range(4):
                        for ob in range(2):
                            p.emit("tensor", lambda e, ic=ic, sub=sub, ob=ob,
                                   buf=buf: e.matmul(
                                zb[sub * 2 + ob][:],
                                outT[:, ic * 512 + sub * 128:
                                     ic * 512 + (sub + 1) * 128],
                                buf[:, ob * 512:(ob + 1) * 512],
                                start=(ic == 0), stop=False),
                                inc="t" if (sub == 3 and ob == 1) else None)
                    wo.consumed()
                for sub in range(4):
                    for ob in range(2):
                        p.emit("tensor", lambda e, sub=sub, ob=ob: e.matmul(
                            zb[sub * 2 + ob][:], ones_row[:],
                            bo_row[0:1, ob * 512:(ob + 1) * 512],
                            start=False, stop=True),
                            inc="t" if (sub == 3 and ob == 1) else None)
                wo_done = p.mark("t")
                p.wait("vector", "t", wo_done)
                for sub in range(4):
                    tile = tb * 4 + sub
                    for ob in range(2):
                        p.emit("vector", lambda e, tile=tile, sub=sub, ob=ob:
                               e.scalar_tensor_tensor(
                            out=x_sb[:, tile * D + ob * 512:
                                     tile * D + (ob + 1) * 512],
                            in0=zb[sub * 2 + ob][:], scalar=1.0,
                            in1=x_sb[:, tile * D + ob * 512:
                                     tile * D + (ob + 1) * 512],
                            op0=OP.mult, op1=OP.add))
                for k in range(8):
                    pool.release(k, "v")
        p.barrier()
        p.emit("gpsimd", lambda e: e.dma_start(
            out=x2_o.rearrange("(i p) d -> p i d", p=128),
            in_=x_sb[:].rearrange("p (i d) -> p i d", i=NT)),
            inc="st", n=16, raw=True)
        p.wait("gpsimd", "st", p.cnt["st"])
        p.build()
    return nc


def stage_l2(inputs, r1):
    mx = np.ascontiguousarray(
        np.array([[float(r1[b]["lmax"][0, 0]) for b in range(NCORE)]],
                 np.float32))
    com = {
        "wqT": pack_lhsT(np.asarray(inputs["L0_attn_Wq"], np.float32)),
        "wvT": pack_rhs(np.asarray(inputs["L0_attn_Wv"], np.float32)),
        "woT": pack_rhs(np.asarray(inputs["L0_attn_Wo"], np.float32)),
        "projT": np.ascontiguousarray(
            np.asarray(inputs["L0_attn_proj"], np.float32).T),
        "bq": np.asarray(inputs["L0_attn_bq"], np.float32),
        "bv": np.asarray(inputs["L0_attn_bv"], np.float32),
        "bo": np.asarray(inputs["L0_attn_bo"], np.float32),
        "maxes": mx,
    }
    return [dict(com, x1=np.ascontiguousarray(r1[b]["x1"]),
                 kT=np.ascontiguousarray(r1[b]["kT"]),
                 diag=np.ascontiguousarray(r1[b]["diag"]))
            for b in range(NCORE)]





def emit_ffn1_chain(nc, p, pool, jit, ugT, uoT, woR, x_src_d,
                    x_out_o, zx_d, eye_sb, boc, incc, dbg=None):
    NSEQ = 256
    with nc.sbuf_tensor("c_fm", [128, 2 * DC * NSEQ], F32) as c_fm, \
            nc.sbuf_tensor("cuo_tm", [128, 2 * D], F32) as cuo_tm, \
            nc.sbuf_tensor("gt_sb", [128, 3 * NSEQ], F32) as gt_sb, \
            nc.sbuf_tensor("ones2", [128, NSEQ], F32) as ones2:
        _ug_cm = nc.sbuf_tensor("ug_sb", [128, 3 * DC * D], F32)
        ug_sb = _ug_cm.__enter__()
        ZW = 3 * DC * NSEQ
        _zxs_cm = nc.sbuf_tensor("zxs", [128, 2 * ZW], F32)
        zxs = _zxs_cm.__enter__()
        CP = DC * NSEQ
        p.emit("sync", lambda e: e.dma_start(
            out=ug_sb[:].rearrange("p (g c d) -> p g c d", g=3, c=DC),
            in_=ugT.rearrange("g c p d -> p g c d")), inc="ld", n=16)
        ug_ld = p.mark("ld")
        p.emit("vector", lambda e: e.memset(ones2[:], 1.0))
        for cfc in range(DC):
            p.emit("vector", lambda e, cfc=cfc: e.tensor_scalar(
                out=c_fm[:, cfc * NSEQ:(cfc + 1) * NSEQ], in0=ones2[:],
                scalar1=incc[:, cfc:cfc + 1], scalar2=None, op0=OP.mult))
        p.wait("tensor", "ld", ug_ld)

        def zx_load(t):
            p.wait("sync", "v", p.mark("v"))
            p.emit("sync", lambda e, t=t: e.dma_start(
                out=zxs[:, (t % 2) * ZW:(t % 2 + 1) * ZW]
                .rearrange("p (g c q) -> p g c q", g=3, c=DC),
                in_=zx_d.rearrange("g c p t -> p g c t")
                    [:, :, :, t * NSEQ:(t + 1) * NSEQ]), inc="ld", n=16)
            return p.mark("ld")

        zx_marks = {0: zx_load(0)}
        for t in range(8):
            par, npar = (t % 2) * CP, ((t + 1) % 2) * CP
            zo = (t % 2) * ZW
            if t + 1 < 8:
                zx_marks[t + 1] = zx_load(t + 1)
            p.wait("vector", "ld", zx_marks[t])
            p.wait("tensor", "v", p.mark("v"))
            for mc in range(DC):
                gbanks = []
                for g in range(3):
                    bi, bank = pool.acquire()
                    gbanks.append((bi, bank))
                    for f in range(DC):
                        p.emit("tensor", lambda e, g=g, f=f, mc=mc,
                               par=par, bank=bank: e.matmul(
                            bank[:, 0:NSEQ],
                            ug_sb[:, g * DC * D + mc * D + f * 128:
                                  g * DC * D + mc * D + (f + 1) * 128],
                            c_fm[:, par + f * NSEQ:par + (f + 1) * NSEQ],
                            start=(f == 0), stop=(f == DC - 1)),
                            inc="t")
                p.wait("vector", "t", p.mark("t"))
                for g in range(3):
                    bi, bank = gbanks[g]
                    p.emit("vector", lambda e, g=g, mc=mc, zo=zo, bank=bank:
                           e.tensor_tensor(
                        out=gt_sb[:, g * NSEQ:(g + 1) * NSEQ],
                        in0=bank[:, 0:NSEQ],
                        in1=zxs[:, zo + (g * DC + mc) * NSEQ:
                                zo + (g * DC + mc + 1) * NSEQ], op=OP.add))
                    pool.release(bi, "v")
                p.wait("scalar", "v", p.mark("v"))
                for g in range(3):
                    fn = AF.Sigmoid if g < 2 else AF.Tanh
                    p.emit("scalar", lambda e, g=g, fn=fn: e.activation(
                        out=gt_sb[:, g * NSEQ:(g + 1) * NSEQ],
                        in_=gt_sb[:, g * NSEQ:(g + 1) * NSEQ], func=fn),
                        inc="s")
                p.wait("vector", "s", p.mark("s"))
                p.emit("vector", lambda e, mc=mc, par=par, npar=npar:
                       e.tensor_mul(
                    out=c_fm[:, npar + mc * NSEQ:npar + (mc + 1) * NSEQ],
                    in0=c_fm[:, par + mc * NSEQ:par + (mc + 1) * NSEQ],
                    in1=gt_sb[:, 0:NSEQ]))
                p.emit("vector", lambda e: e.tensor_mul(
                    out=gt_sb[:, NSEQ:2 * NSEQ],
                    in0=gt_sb[:, NSEQ:2 * NSEQ],
                    in1=gt_sb[:, 2 * NSEQ:3 * NSEQ]))
                p.emit("vector", lambda e, mc=mc, npar=npar: e.tensor_add(
                    out=c_fm[:, npar + mc * NSEQ:npar + (mc + 1) * NSEQ],
                    in0=c_fm[:, npar + mc * NSEQ:npar + (mc + 1) * NSEQ],
                    in1=gt_sb[:, NSEQ:2 * NSEQ]))
        cpar = (8 % 2) * CP
        p.barrier()
        if dbg is not None:
            p.emit("gpsimd", lambda e: e.dma_start(out=dbg[0], in_=c_fm[:]),
                   inc="st", n=16, raw=True)
            p.emit("gpsimd", lambda e: e.dma_start(out=dbg[1], in_=zxs[:]),
                   inc="st", n=16, raw=True)
            p.wait("gpsimd", "st", p.cnt["st"])
        _zxs_cm.__exit__(None, None, None)
        _ug_cm.__exit__(None, None, None)
        with nc.sbuf_tensor("uo_sb", [128, DC * D], F32) as uo_sb, \
                nc.sbuf_tensor("cuoT", [128, DC * NSEQ], F32) as cuoT:
            p.emit("sync", lambda e: e.dma_start(
                out=uo_sb[:].rearrange("p (c d) -> p c d", c=DC),
                in_=uoT.rearrange("c p d -> p c d")), inc="ld", n=16)
            p.wait("tensor", "ld", p.mark("ld"))
            for oc in range(DC):
                bi, bank = pool.acquire()
                for f in range(DC):
                    p.emit("tensor", lambda e, oc=oc, f=f, bank=bank:
                           e.matmul(
                        bank[:, 0:NSEQ],
                        uo_sb[:, oc * D + f * 128:oc * D + (f + 1) * 128],
                        c_fm[:, cpar + f * NSEQ:cpar + (f + 1) * NSEQ],
                        start=(f == 0), stop=(f == DC - 1)), inc="t")
                p.wait("scalar", "t", p.mark("t"))
                p.emit("scalar", lambda e, oc=oc, bank=bank: e.activation(
                    out=cuoT[:, oc * NSEQ:(oc + 1) * NSEQ],
                    in_=bank[:, 0:NSEQ], func=AF.Identity,
                    bias=boc[:, oc:oc + 1]), inc="s")
                pool.release(bi, "s")
            p.wait("tensor", "s", p.mark("s"))
            for oc in range(DC):
                for ss in range(2):
                    bi, bank = pool.acquire()
                    p.emit("tensor", lambda e, oc=oc, ss=ss, bank=bank:
                           e.matmul(
                        bank[:, 0:128],
                        cuoT[:, oc * NSEQ + ss * 128:
                             oc * NSEQ + (ss + 1) * 128],
                        eye_sb[:], start=True, stop=True), inc="t")
                    p.wait("scalar", "t", p.mark("t"))
                    p.emit("scalar", lambda e, oc=oc, ss=ss, bank=bank:
                           e.activation(
                        out=cuo_tm[:, ss * D + oc * 128:
                                   ss * D + (oc + 1) * 128],
                        in_=bank[:, 0:128], func=AF.Copy), inc="s")
                    pool.release(bi, "s")
            p.barrier()
            if dbg is not None:
                p.emit("gpsimd", lambda e: e.dma_start(out=dbg[2],
                                                       in_=cuo_tm[:]),
                       inc="st", n=16, raw=True)
            with nc.sbuf_tensor("wo_sb", [128, DC * D], F32) as wo_sb, \
                    nc.sbuf_tensor("xb_sb", [128, 2 * 2 * D], F32) as xb_sb, \
                    nc.sbuf_tensor("xhTb", [128, DC * 256], F32) as xhTb, \
                    nc.sbuf_tensor("hz_sb", [128, D], F32) as hz_sb:
                p.emit("sync", lambda e: e.dma_start(
                    out=wo_sb[:].rearrange("p (c d) -> p c d", c=DC),
                    in_=woR.rearrange("c p d -> p c d")), inc="ld", n=16)
                wo_ld = p.mark("ld")
                p.wait("tensor", "ld", wo_ld)

                def xb_load(t):
                    p.wait("sync", "v", p.mark("v"))
                    p.wait("sync", "s", p.mark("s"))
                    p.wait("sync", "st", p.mark("st"))
                    p.emit("sync", lambda e, t=t: e.dma_start(
                        out=xb_sb[:, (t % 2) * 2 * D:(t % 2 + 1) * 2 * D]
                        .rearrange("p (i d) -> p i d", i=2),
                        in_=x_src_d.rearrange("(i p) d -> p i d", p=128)
                            [:, 2 * t:2 * t + 2, :]), inc="ld", n=16)
                    return p.mark("ld")

                xb_marks = {0: xb_load(0)}
                for t in range(8):
                    xb = xb_sb[:, (t % 2) * 2 * D:(t % 2 + 1) * 2 * D]
                    if t + 1 < 8:
                        xb_marks[t + 1] = xb_load(t + 1)
                    for e in ("vector", "tensor"):
                        p.wait(e, "ld", xb_marks[t])
                    jit([0, 1], xb, xhTb[:],
                        stats=[2 * t, 2 * t + 1])
                    xh_ready = p.mark("s")
                    p.wait("tensor", "s", xh_ready)
                    for ss in range(2):
                        for ob in range(2):
                            bi, bank = pool.acquire()
                            for f in range(DC):
                                p.emit("tensor", lambda e, f=f, ss=ss,
                                       ob=ob, bank=bank: e.matmul(
                                    bank[:],
                                    xhTb[:, f * 256 + ss * 128:
                                         f * 256 + (ss + 1) * 128],
                                    wo_sb[:, f * D + ob * 512:
                                          f * D + (ob + 1) * 512],
                                    start=(f == 0), stop=(f == DC - 1)),
                                    inc="t")
                            p.wait("vector", "t", p.mark("t"))
                            p.emit("vector", lambda e, ss=ss, ob=ob,
                                   bank=bank: e.tensor_add(
                                out=hz_sb[:, ob * 512:(ob + 1) * 512],
                                in0=bank[:],
                                in1=cuo_tm[:, ss * D + ob * 512:
                                           ss * D + (ob + 1) * 512]))
                            pool.release(bi, "v")
                        p.wait("scalar", "v", p.mark("v"))
                        p.emit("scalar", lambda e: e.activation(
                            out=hz_sb[:], in_=hz_sb[:], func=AF.Tanh),
                            inc="s")
                        p.wait("vector", "s", p.mark("s"))
                        p.emit("vector", lambda e, ss=ss, xb=xb:
                               e.scalar_tensor_tensor(
                            out=xb[:, ss * D:(ss + 1) * D],
                            in0=hz_sb[:], scalar=0.5,
                            in1=xb[:, ss * D:(ss + 1) * D],
                            op0=OP.mult, op1=OP.add))
                    p.wait("gpsimd", "v", p.mark("v"))
                    p.emit("gpsimd", lambda e, t=t, xb=xb: e.dma_start(
                        out=x_out_o.rearrange("(i p) d -> p i d", p=128)
                            [:, 2 * t:2 * t + 2, :],
                        in_=xb.rearrange("p (i d) -> p i d", i=2)),
                        inc="st", n=16, raw=True)



def build_l3():
    nc = bass.Bass("TRN2", target_bir_lowering=False,
                   detect_race_conditions=False)
    x2 = nc.dram_tensor("x2", [TOK, D], F32, kind="ExternalInput").ap()
    wfc = nc.dram_tensor("wfc", [FC, 128, D], F32, kind="ExternalInput").ap()
    wproj = nc.dram_tensor("wproj", [FC, 128, D], F32,
                           kind="ExternalInput").ap()
    wgT = nc.dram_tensor("wgT", [3, DC, 128, D], F32,
                         kind="ExternalInput").ap()
    ugT = nc.dram_tensor("ugT", [3, DC, 128, D], F32,
                         kind="ExternalInput").ap()
    uoT = nc.dram_tensor("uoT", [DC, 128, D], F32, kind="ExternalInput").ap()
    woR = nc.dram_tensor("woR", [DC, 128, D], F32, kind="ExternalInput").ap()
    bg = nc.dram_tensor("bg", [3, D], F32, kind="ExternalInput").ap()
    bo_ff = nc.dram_tensor("bo_ff", [D], F32, kind="ExternalInput").ap()
    inc_ff = nc.dram_tensor("inc_ff", [D], F32, kind="ExternalInput").ap()
    x4_o = nc.dram_tensor("x4", [TOK, D], F32, kind="ExternalOutput").ap()
    x3_d = nc.dram_tensor("x3_d", [TOK, D], F32, kind="Internal").ap()
    if DBG:
        dbg_x3 = nc.dram_tensor("dbg_x3", [TOK, D], F32,
                                kind="ExternalOutput").ap()
        dbg_c = nc.dram_tensor("dbg_c", [128, 2 * DC * 256], F32,
                               kind="ExternalOutput").ap()
        dbg_cuo = nc.dram_tensor("dbg_cuo", [128, 2 * D], F32,
                                 kind="ExternalOutput").ap()
        dbg_zx = nc.dram_tensor("dbg_zx", [128, 3 * DC * 256], F32,
                                kind="ExternalOutput").ap()
    zx_d = nc.dram_tensor("zx_d", [3, DC, 128, TOK], F32, kind="Internal").ap()
    eye_d = nc.inline_tensor(np.eye(128, dtype=np.float32), "eye").ap()

    NSEQ = 256
    with ExitStack() as stack:
        ec = stack.enter_context
        p = Prog(nc, stack)
        mean_sb = ec(nc.sbuf_tensor("mean_sb", [128, NT], F32))
        rstd_sb = ec(nc.sbuf_tensor("rstd_sb", [128, NT], F32))
        sq_sb = ec(nc.sbuf_tensor("sq_sb", [128, D], F32))
        col0 = ec(nc.sbuf_tensor("col0", [128, 1], F32))
        eye_sb = ec(nc.sbuf_tensor("eye_sb", [128, 128], F32))
        setup_consts(nc, p, ec)
        xc_sb = ec(nc.sbuf_tensor("xc_sb", [128, 2 * D], F32))
        dg_sb = ec(nc.sbuf_tensor("dg_sb", [128, 2 * 128], F32))
        bgc = ec(nc.sbuf_tensor("bgc", [128, 3 * DC], F32))
        boc = ec(nc.sbuf_tensor("boc", [128, DC], F32))
        incc = ec(nc.sbuf_tensor("incc", [128, DC], F32))
        banks = [ec(nc.psum_tensor(f"ps{i}", [128, 512], F32))
                 for i in range(8)]
        pool = PsumPool(p, [b[:] for b in banks])
        jit = make_jit_xhat(p, pool, mean_sb[:], rstd_sb[:], eye_sb[:],
                            xc_sb[:], dg_sb[:], depth=2)

        p.emit("sync", lambda e: e.dma_start(out=eye_sb[:], in_=eye_d),
               inc="ld", n=16)
        p.emit("sync", lambda e: e.dma_start(
            out=bgc[:].rearrange("p (g c) -> p g c", g=3),
            in_=bg.rearrange("g (c p) -> p g c", p=128),
            allow_slow_non_contiguous=True), inc="ld", n=16)
        p.emit("sync", lambda e: e.dma_start(
            out=boc[:], in_=bo_ff.rearrange("(c p) -> p c", p=128),
            allow_slow_non_contiguous=True), inc="ld", n=16)
        p.emit("sync", lambda e: e.dma_start(
            out=incc[:], in_=inc_ff.rearrange("(c p) -> p c", p=128),
            allow_slow_non_contiguous=True), inc="ld", n=16)

        with nc.sbuf_tensor("x_sb", [128, NT * D], F32) as x_sb, \
                nc.sbuf_tensor("xhT", [128, DC * 512], F32) as xhT:
            p.emit("sync", lambda e: e.dma_start(
                out=x_sb[:].rearrange("p (i d) -> p i d", i=NT),
                in_=x2.rearrange("(i p) d -> p i d", p=128)), inc="ld", n=16)
            ldm = p.mark("ld")
            for e in ("tensor", "vector", "scalar", "gpsimd"):
                p.wait(e, "ld", ldm)
            ln_stats(p, x_sb[:], mean_sb[:], rstd_sb[:], sq_sb[:], col0[:],
                     range(NT))
            p.barrier()
            with nc.sbuf_tensor("h_sb", [128, FC * 512], F32) as h_sb, \
                    nc.sbuf_tensor("r_sb", [128, 512], F32) as r_sb, \
                    nc.sbuf_tensor("z_sb", [128, D], F32) as z_sb, \
                    nc.sbuf_tensor("mem_sb", [128, D], F32) as mem_sb, \
                    nc.sbuf_tensor("spk_sb", [128, D], F32) as spk_sb, \
                    nc.sbuf_tensor("acc_sb", [128, D], F32) as acc_sb, \
                    nc.sbuf_tensor("wfc0", [128, D], F32) as wfc0, \
                    nc.sbuf_tensor("wfc1", [128, D], F32) as wfc1, \
                    nc.sbuf_tensor("wp0", [128, D], F32) as wp0, \
                    nc.sbuf_tensor("wp1", [128, D], F32) as wp1:
                sb = dict(xhT=xhT[:], h=h_sb[:], r=r_sb[:], z=z_sb[:],
                          mem=mem_sb[:], spk=spk_sb[:], acc=acc_sb[:],
                          wfc=[wfc0[:], wfc1[:]], wproj=[wp0[:], wp1[:]])
                emit_ffn0(p, pool, jit, x_sb[:], wfc, wproj, sb)
            p.emit("gpsimd", lambda e: e.dma_start(
                out=x3_d.rearrange("(i p) d -> p i d", p=128),
                in_=x_sb[:].rearrange("p (i d) -> p i d", i=NT)),
                inc="st", n=16, raw=True)
            if DBG:
                p.emit("gpsimd", lambda e: e.dma_start(
                    out=dbg_x3.rearrange("(i p) d -> p i d", p=128),
                    in_=x_sb[:].rearrange("p (i d) -> p i d", i=NT)),
                    inc="st", n=16, raw=True)
            ln_stats(p, x_sb[:], mean_sb[:], rstd_sb[:], sq_sb[:], col0[:],
                     range(NT))
            p.barrier()
            with nc.sbuf_tensor("wg_sb", [128, 2 * DC * D], F32) as wg_sb, \
                    nc.sbuf_tensor("zx_sb", [128, DC * 512], F32) as zx_sb:
                def wg_load(g):
                    p.wait("sync", "t", p.mark("t"))
                    p.emit("sync", lambda e, g=g: e.dma_start(
                        out=wg_sb[:, (g % 2) * DC * D:(g % 2 + 1) * DC * D]
                        .rearrange("p (c d) -> p c d", c=DC),
                        in_=wgT[g].rearrange("c p d -> p c d")),
                        inc="ld", n=16)
                    return p.mark("ld")

                wg_marks = {0: wg_load(0)}
                for g in range(3):
                    go = (g % 2) * DC * D
                    if g + 1 < 3:
                        wg_marks[g + 1] = wg_load(g + 1)
                    for tb in range(4):
                        tiles = list(range(tb * 4, tb * 4 + 4))
                        jit(tiles, x_sb[:], xhT[:])
                        xh_ready = p.mark("s")
                        p.wait("tensor", "ld", wg_marks[g])
                        p.wait("tensor", "s", xh_ready)
                        p.wait("scalar", "st", p.mark("st"))
                        for mc in range(DC):
                            bi, bank = pool.acquire()
                            for f in range(DC):
                                p.emit("tensor", lambda e, f=f, mc=mc,
                                       go=go, bank=bank: e.matmul(
                                    bank[:],
                                    wg_sb[:, go + mc * D + f * 128:
                                          go + mc * D + (f + 1) * 128],
                                    xhT[:, f * 512:(f + 1) * 512],
                                    start=(f == 0), stop=(f == DC - 1)),
                                    inc="t")
                            p.wait("scalar", "t", p.mark("t"))
                            p.emit("scalar", lambda e, g=g, mc=mc, bank=bank:
                                   e.activation(
                                out=zx_sb[:, mc * 512:(mc + 1) * 512],
                                in_=bank[:], func=AF.Identity,
                                bias=bgc[:, g * DC + mc:g * DC + mc + 1]),
                                inc="s")
                            pool.release(bi, "s")
                        p.wait("gpsimd", "s", p.mark("s"))
                        p.emit("gpsimd", lambda e, g=g, tb=tb: e.dma_start(
                            out=zx_d[g].rearrange("c p t -> p c t")
                                [:, :, tb * 512:(tb + 1) * 512],
                            in_=zx_sb[:].rearrange("p (c q) -> p c q", c=DC)),
                            inc="st", n=16, raw=True)
        p.barrier()

        emit_ffn1_chain(nc, p, pool, jit, ugT, uoT, woR, x3_d,
                        x4_o, zx_d, eye_sb[:], boc[:], incc[:],
                        dbg=(dbg_c, dbg_zx, dbg_cuo) if DBG else None)
        p.barrier()
        p.wait("gpsimd", "st", p.cnt["st"])
        p.build()
    return nc


def stage_l3(inputs, res2):
    com = {
        "wfc": pack_lhsT(np.asarray(inputs["L0_ffn_W_fc"], np.float32)),
        "wproj": pack_rhs(np.asarray(inputs["L0_ffn_W_proj"], np.float32)),
        "wgT": np.stack([pack_lhsT(np.asarray(inputs[k], np.float32))
                         for k in ("L1_ffn_wf", "L1_ffn_wi", "L1_ffn_wc")]),
        "ugT": np.stack([pack_lhsT(np.asarray(inputs[k], np.float32))
                         for k in ("L1_ffn_uf", "L1_ffn_ui", "L1_ffn_uc")]),
        "uoT": pack_lhsT(np.asarray(inputs["L1_ffn_uo"], np.float32)),
        "woR": pack_rhs(np.asarray(inputs["L1_ffn_wo"], np.float32)),
        "bg": np.stack([np.asarray(inputs[k], np.float32)
                        for k in ("L1_ffn_bf", "L1_ffn_bi", "L1_ffn_bc")]),
        "bo_ff": np.asarray(inputs["L1_ffn_bo"], np.float32),
        "inc_ff": np.asarray(inputs["L1_ffn_in_c"], np.float32),
    }
    x2_all = np.stack([np.asarray(res2[b]["x2"]) for b in range(NCORE)])
    maps = []
    for j in range(NCORE):
        xj = np.ascontiguousarray(
            x2_all[:, j * 256:(j + 1) * 256, :].reshape(TOK, D))
        maps.append(dict(com, x2=xj))
    return maps



HALF_PI = 1.5707963267948966
NEG_PI = -3.141592653589793
TWO_PI = 6.283185307179586


def build_l4():
    nc = bass.Bass("TRN2", target_bir_lowering=False,
                   detect_race_conditions=False)
    x4f = nc.dram_tensor("x4f", [B * S, D], F32, kind="ExternalInput").ap()
    x4own = nc.dram_tensor("x4own", [TOK, D], F32, kind="ExternalInput").ap()
    basR = nc.dram_tensor("basR", [4, DC, 128, HARM], F32,
                          kind="ExternalInput").ap()
    phT = nc.dram_tensor("phT", [HARM, 4 * D], F32, kind="ExternalInput").ap()
    amT = nc.dram_tensor("amT", [HARM, 4 * D], F32, kind="ExternalInput").ap()
    wgT = nc.dram_tensor("wgT", [3, DC, 128, D], F32,
                         kind="ExternalInput").ap()
    ugT = nc.dram_tensor("ugT", [3, DC, 128, D], F32,
                         kind="ExternalInput").ap()
    uoT = nc.dram_tensor("uoT", [DC, 128, D], F32, kind="ExternalInput").ap()
    woR = nc.dram_tensor("woR", [DC, 128, D], F32, kind="ExternalInput").ap()
    bg = nc.dram_tensor("bg", [3, D], F32, kind="ExternalInput").ap()
    bo_ff = nc.dram_tensor("bo_ff", [D], F32, kind="ExternalInput").ap()
    inc_ff = nc.dram_tensor("inc_ff", [D], F32, kind="ExternalInput").ap()
    x6_o = nc.dram_tensor("x6", [TOK, D], F32, kind="ExternalOutput").ap()
    x5_d = nc.dram_tensor("x5_d", [TOK, D], F32, kind="Internal").ap()
    zx_d = nc.dram_tensor("zx_d", [3, DC, 128, TOK], F32, kind="Internal").ap()
    eye_d = nc.inline_tensor(np.eye(128, dtype=np.float32), "eye").ap()

    NSEQ = 256
    with ExitStack() as stack:
        ec = stack.enter_context
        p = Prog(nc, stack)
        mean_sb = ec(nc.sbuf_tensor("mean_sb", [128, NT], F32))
        rstd_sb = ec(nc.sbuf_tensor("rstd_sb", [128, NT], F32))
        mean8 = ec(nc.sbuf_tensor("mean8", [128, 4], F32))
        rstd8 = ec(nc.sbuf_tensor("rstd8", [128, 4], F32))
        sq_sb = ec(nc.sbuf_tensor("sq_sb", [128, D], F32))
        col0 = ec(nc.sbuf_tensor("col0", [128, 1], F32))
        eye_sb = ec(nc.sbuf_tensor("eye_sb", [128, 128], F32))
        setup_consts(nc, p, ec)
        xc_sb = ec(nc.sbuf_tensor("xc_sb", [128, 2 * D], F32))
        dg_sb = ec(nc.sbuf_tensor("dg_sb", [128, 2 * 128], F32))
        bgc = ec(nc.sbuf_tensor("bgc", [128, 3 * DC], F32))
        boc = ec(nc.sbuf_tensor("boc", [128, DC], F32))
        incc = ec(nc.sbuf_tensor("incc", [128, DC], F32))
        w4 = ec(nc.sbuf_tensor("w4", [HARM, 4 * D], F32))
        bas_sb = ec(nc.sbuf_tensor("bas_sb", [128, 4 * DC * HARM], F32))
        banks = [ec(nc.psum_tensor(f"ps{i}", [128, 512], F32))
                 for i in range(8)]
        pool = PsumPool(p, [b[:] for b in banks[:4] + banks[6:]])
        avb = [b[:] for b in banks[4:6]]
        jit = make_jit_xhat(p, pool, mean_sb[:], rstd_sb[:], eye_sb[:],
                            xc_sb[:], dg_sb[:], depth=2)

        p.emit("sync", lambda e: e.dma_start(out=eye_sb[:], in_=eye_d),
               inc="ld", n=16)
        p.emit("sync", lambda e: e.dma_start(
            out=bgc[:].rearrange("p (g c) -> p g c", g=3),
            in_=bg.rearrange("g (c p) -> p g c", p=128),
            allow_slow_non_contiguous=True), inc="ld", n=16)
        p.emit("sync", lambda e: e.dma_start(
            out=boc[:], in_=bo_ff.rearrange("(c p) -> p c", p=128),
            allow_slow_non_contiguous=True), inc="ld", n=16)
        p.emit("sync", lambda e: e.dma_start(
            out=incc[:], in_=inc_ff.rearrange("(c p) -> p c", p=128),
            allow_slow_non_contiguous=True), inc="ld", n=16)
        p.emit("sync", lambda e: e.dma_start(out=w4[:], in_=phT),
               inc="ld", n=16)
        p.emit("sync", lambda e: e.dma_start(
            out=bas_sb[:].rearrange("p (x c q) -> p x c q", x=4, c=DC),
            in_=basR.rearrange("x c p q -> p x c q")), inc="ld", n=16)
        ldm = p.mark("ld")
        for e in ("tensor", "vector", "scalar", "gpsimd"):
            p.wait(e, "ld", ldm)
        with nc.sbuf_tensor("am_sb", [HARM, 4 * D], F32) as am_sb:
            p.emit("sync", lambda e: e.dma_start(out=am_sb[:], in_=amT),
                   inc="ld", n=16)
            p.wait("vector", "ld", p.mark("ld"))
            p.emit("vector", lambda e: e.tensor_scalar(
                out=w4[:], in0=w4[:], scalar1=HALF_PI, scalar2=None,
                op0=OP.add))
            with nc.sbuf_tensor("mk_sb", [HARM, 4 * D], F32) as mk_sb:
                PI = -NEG_PI
                for sign in (1.0, -1.0):
                    for _ in range(8):
                        p.emit("vector", lambda e, sign=sign:
                               e.tensor_scalar(
                            out=mk_sb[:], in0=w4[:],
                            scalar1=sign * PI, scalar2=None,
                            op0=(OP.is_gt if sign > 0 else OP.is_lt)))
                        p.emit("vector", lambda e, sign=sign:
                               e.scalar_tensor_tensor(
                            out=w4[:], in0=mk_sb[:],
                            scalar=-sign * TWO_PI, in1=w4[:],
                            op0=OP.mult, op1=OP.add))
                p.wait("scalar", "v", p.mark("v"))
                p.emit("scalar", lambda e: e.activation(
                    out=w4[:], in_=w4[:], func=AF.Sin), inc="s")
            p.wait("vector", "s", p.mark("s"))
            p.emit("vector", lambda e: e.tensor_mul(
                out=w4[:], in0=w4[:], in1=am_sb[:]))
            p.barrier()

        VW = H * 65 + 1
        KW = DC * 512
        with nc.sbuf_tensor("xo_sb", [128, 2 * 2 * D], F32) as xo_sb, \
                nc.sbuf_tensor("qT_sb", [128, DC * NSEQ], F32) as qT_sb, \
                nc.sbuf_tensor("thi", [128, DC * NSEQ], F32) as thi, \
                nc.sbuf_tensor("outT", [128, DC * NSEQ], F32) as outT, \
                nc.sbuf_tensor("kTh", [128, 2 * KW], F32) as kTh, \
                nc.sbuf_tensor("vb", [128, 2 * 4 * VW], F32) as vb, \
                nc.sbuf_tensor("xf2", [128, 4 * D], F32) as xf2, \
                nc.sbuf_tensor("xfT2", [128, DC * 512], F32) as xfT2, \
                nc.sbuf_tensor("res_sb", [HARM, 1280], F32) as res_sb, \
                nc.sbuf_tensor("PT_sb", [128, 1024], F32) as PT_sb, \
                nc.sbuf_tensor("xc8_sb", [128, 2 * D], F32) as xc8_sb, \
                nc.sbuf_tensor("dg8_sb", [128, 2 * 128], F32) as dg8_sb, \
                nc.sbuf_tensor("on_sb", [128, 128], F32) as on_sb:
            jit8 = make_jit_xhat(p, pool, mean8[:], rstd8[:], eye_sb[:],
                                 xc8_sb[:], dg8_sb[:], depth=2)
            p.emit("vector", lambda e: e.memset(on_sb[:], 1.0))
            xf2_marks = {"v": p.mark("v"), "s": p.mark("s")}
            attn_state = {"add": 0}
            xo_marks = {"v": p.mark("v"), "s": p.mark("s"),
                        "st": p.mark("st")}

            def emit_xstats(idx):
                bb, qq = divmod(idx, 4)
                g0 = bb * 16 + qq * 4
                p.wait("sync", "v", xf2_marks["v"])
                p.wait("sync", "s", xf2_marks["s"])
                p.emit("sync", lambda e, g0=g0: e.dma_start(
                    out=xf2[:].rearrange("p (i d) -> p i d", i=4),
                    in_=x4f.rearrange("(i p) d -> p i d", p=128)
                        [:, g0:g0 + 4, :]), inc="ld", n=16)
                xl = p.mark("ld")
                for e in ("vector", "scalar"):
                    p.wait(e, "ld", xl)
                ln_stats(p, xf2[:], mean8[:], rstd8[:], sq_sb[:],
                         col0[:], range(4))

            def emit_xo(bb):
                sl = (bb % 2) * 2 * D
                p.wait("sync", "v", xo_marks["v"])
                p.wait("sync", "s", xo_marks["s"])
                p.wait("sync", "st", xo_marks["st"])
                p.emit("sync", lambda e, bb=bb, sl=sl: e.dma_start(
                    out=xo_sb[:, sl:sl + 2 * D]
                    .rearrange("p (i d) -> p i d", i=2),
                    in_=x4own.rearrange("(i p) d -> p i d", p=128)
                        [:, 2 * bb:2 * bb + 2, :]), inc="ld", n=16)
                xl = p.mark("ld")
                for e in ("vector", "scalar"):
                    p.wait(e, "ld", xl)
                ln_stats(p, xo_sb[:, sl:sl + 2 * D],
                         mean_sb[:, 2 * bb:2 * bb + 2],
                         rstd_sb[:, 2 * bb:2 * bb + 2],
                         sq_sb[:], col0[:], range(2))

            def emit_qprep(bb):
                sl = (bb % 2) * 2 * D
                jit([0, 1], xo_sb[:, sl:sl + 2 * D], xfT2[:],
                    stats=[2 * bb, 2 * bb + 1])
                p.wait("tensor", "s", p.mark("s"))
                bi, bank = pool.acquire()
                for f in range(DC):
                    p.emit("tensor", lambda e, f=f, bank=bank: e.matmul(
                        bank[0:HARM, 0:NSEQ],
                        bas_sb[:, (0 * DC + f) * HARM:
                               (0 * DC + f + 1) * HARM],
                        xfT2[:, f * NSEQ:(f + 1) * NSEQ],
                        start=(f == 0), stop=(f == DC - 1)), inc="t")
                p.wait("scalar", "t", p.mark("t"))
                p.emit("scalar", lambda e, bank=bank: e.activation(
                    out=res_sb[:, 1024:1024 + NSEQ],
                    in_=bank[0:HARM, 0:NSEQ], func=AF.Copy), inc="s")
                pool.release(bi, "s")
                p.wait("tensor", "s", p.mark("s"))
                for oc in range(DC):
                    bi, bank = pool.acquire()
                    p.emit("tensor", lambda e, oc=oc, bank=bank: e.matmul(
                        bank[:, 0:NSEQ],
                        w4[:, 0 * D + oc * 128:0 * D + (oc + 1) * 128],
                        res_sb[:, 1024:1024 + NSEQ],
                        start=True, stop=True), inc="t")
                    p.wait("scalar", "t", p.mark("t"))
                    p.emit("scalar", lambda e, oc=oc, bank=bank:
                           e.activation(
                        out=qT_sb[:, oc * NSEQ:(oc + 1) * NSEQ],
                        in_=bank[:, 0:NSEQ], func=AF.Copy), inc="s")
                    pool.release(bi, "s")
                p.wait("vector", "t", p.mark("t"))
                p.emit("vector", lambda e: e.memset(outT[0:65, :], 0.0))
                p.emit("vector", lambda e: e.memset(thi[0:65, :], 0.0))
                return p.mark("s")

            def emit_prep(idx):
                ko = (idx % 2) * KW
                vo = (idx % 2) * 4 * VW
                jit8(list(range(4)), xf2[:], xfT2[:])
                xf2_marks["v"] = p.mark("v")
                xf2_marks["s"] = p.mark("s")
                p.wait("tensor", "s", p.mark("s"))
                for pj in range(2):
                    bi, bank = pool.acquire()
                    for f in range(DC):
                        p.emit("tensor", lambda e, pj=pj, f=f,
                               bank=bank: e.matmul(
                            bank[0:HARM, 0:512],
                            bas_sb[:, ((pj + 1) * DC + f) * HARM:
                                   ((pj + 1) * DC + f + 1) * HARM],
                            xfT2[:, f * 512:(f + 1) * 512],
                            start=(f == 0), stop=(f == DC - 1)),
                            inc="t")
                    p.wait("scalar", "t", p.mark("t"))
                    p.emit("scalar", lambda e, pj=pj,
                           bank=bank: e.activation(
                        out=res_sb[:, pj * 512:(pj + 1) * 512],
                        in_=bank[0:HARM, 0:512], func=AF.Copy),
                        inc="s")
                    pool.release(bi, "s")
                res_ready = p.mark("s")
                p.wait("tensor", "s", res_ready)
                for oc in range(DC):
                    bi, bank = pool.acquire()
                    p.emit("tensor", lambda e, oc=oc, bank=bank:
                           e.matmul(
                        bank[:],
                        w4[:, 1 * D + oc * 128:1 * D + (oc + 1) * 128],
                        res_sb[:, 0:512],
                        start=True, stop=True), inc="t")
                    p.wait("scalar", "t", p.mark("t"))
                    p.emit("scalar", lambda e, oc=oc, ko=ko, bank=bank:
                           e.activation(
                        out=kTh[:, ko + oc * 512:ko + (oc + 1) * 512],
                        in_=bank[:], func=AF.Copy), inc="s")
                    pool.release(bi, "s")
                for sub in range(4):
                    p.emit("vector", lambda e, sub=sub, vo=vo: e.memset(
                        vb[:, vo + sub * VW + 64:
                           vo + sub * VW + H * 65:65], 1.0))
                    for ob in range(2):
                        bi, bank = pool.acquire()
                        p.emit("tensor", lambda e, sub=sub, ob=ob,
                               bank=bank: e.matmul(
                            bank[:],
                            res_sb[:, 512 + sub * 128:
                                   512 + (sub + 1) * 128],
                            w4[:, 2 * D + ob * 512:2 * D + (ob + 1) * 512],
                            start=True, stop=True), inc="t")
                        p.wait("scalar", "t", p.mark("t"))
                        p.wait("scalar", "v", p.mark("v"))
                        p.emit("scalar", lambda e, sub=sub, ob=ob, vo=vo,
                               bank=bank: e.activation(
                            out=vb[:, vo + sub * VW + ob * 8 * 65:
                                    vo + sub * VW + ob * 8 * 65 + 8 * 65]
                            .rearrange("p (h e) -> p h e", h=8)[:, :, 0:64],
                            in_=bank[:].rearrange("p (h e) -> p h e", h=8),
                            func=AF.Copy), inc="s")
                        pool.release(bi, "s")
                return p.mark("s")

            def emit_attn(idx, kv_ready, q_ready):
                b, qtr = divmod(idx, 4)
                ko = (idx % 2) * KW
                vo = (idx % 2) * 4 * VW
                p.wait("tensor", "s", max(kv_ready, q_ready))

                def emit_scores(hp, kc):
                    sbanks = []
                    for hv in range(2):
                        bi, bank = pool.acquire()
                        sbanks.append((bi, bank))
                        p.emit("tensor", lambda e, hp=hp, hv=hv,
                               kc=kc, bank=bank: e.matmul(
                            bank[:, 0:NSEQ],
                            kTh[hv * 64:hv * 64 + 64,
                                ko + hp * 512 + kc * 128:
                                ko + hp * 512 + (kc + 1) * 128],
                            qT_sb[hv * 64:hv * 64 + 64,
                                  hp * NSEQ:(hp + 1) * NSEQ],
                            start=True, stop=True),
                            inc="t" if hv == 1 else None)
                    return sbanks, p.mark("t")

                iters = [(hp, kc) for hp in range(DC) for kc in range(4)]
                pend = {j: emit_scores(*iters[j]) for j in range(3)}
                avt_marks = {}
                for j, (hp, kc) in enumerate(iters):
                    sbanks, tmark = pend.pop(j)
                    p.wait("scalar", "t", tmark)
                    if j - 2 in avt_marks:
                        p.wait("scalar", "t", avt_marks.pop(j - 2))
                    pt = PT_sb[:, (kc % 2) * 512:(kc % 2) * 512 + 512]
                    for hv in range(2):
                        bi, bank = sbanks[hv]
                        p.emit("scalar", lambda e, hv=hv, pt=pt,
                               bank=bank: e.activation(
                            out=pt[:, hv * NSEQ:(hv + 1) * NSEQ],
                            in_=bank[:, 0:NSEQ],
                            func=AF.Exp, scale=0.125), inc="s")
                        pool.release(bi, "s")
                    if j + 3 < len(iters):
                        pend[j + 3] = emit_scores(*iters[j + 3])
                    p.wait("tensor", "s", p.mark("s"))
                    if kc == 0:
                        p.wait("tensor", "v", attn_state["add"])
                    for hv in range(2):
                        h = hp * 2 + hv
                        av = avb[hv]
                        p.emit("tensor", lambda e, hv=hv, kc=kc,
                               h=h, av=av, pt=pt, vo=vo: e.matmul(
                            av[0:65, 0:NSEQ],
                            vb[:, vo + kc * VW + h * 65:
                               vo + kc * VW + (h + 1) * 65],
                            pt[:, hv * NSEQ:(hv + 1) * NSEQ],
                            start=(kc == 0), stop=(kc == 3)),
                            inc="t" if (kc == 3 and hv == 1)
                            else None)
                    avt_marks[j] = p.mark("t")
                    if kc == 3:
                        p.wait("vector", "t", p.mark("t"))
                        for hv, acc in ((0, outT), (1, thi)):
                            p.emit("vector", lambda e, hp=hp, hv=hv,
                                   acc=acc: e.tensor_tensor(
                                out=acc[0:65, hp * NSEQ:(hp + 1) * NSEQ],
                                in0=acc[0:65, hp * NSEQ:(hp + 1) * NSEQ],
                                in1=avb[hv][0:65, 0:NSEQ], op=OP.add))
                        attn_state["add"] = p.mark("v")
                        if hp == 0 and idx + 1 < 4 * B:
                            emit_xstats(idx + 1)
                        if hp == 4 and qtr == 2 and b + 1 < B:
                            emit_xo(b + 1)

            def emit_epilogue(b):
                sl = (b % 2) * 2 * D
                for acc in (outT, thi):
                    p.emit("vector", lambda e, acc=acc: e.reciprocal(
                        out=acc[64:65, :], in_=acc[64:65, :]))
                p.wait("tensor", "v", p.mark("v"))
                for hp in range(DC):
                    for acc in (outT, thi):
                        bi, bank = pool.acquire()
                        p.emit("tensor", lambda e, hp=hp, acc=acc,
                               bank=bank: e.matmul(
                            bank[0:64, 0:NSEQ],
                            on_sb[64:65, 0:64],
                            acc[64:65, hp * NSEQ:(hp + 1) * NSEQ],
                            start=True, stop=True), inc="t")
                        p.wait("vector", "t", p.mark("t"))
                        p.emit("vector", lambda e, hp=hp, acc=acc,
                               bank=bank: e.tensor_tensor(
                            out=acc[0:64, hp * NSEQ:(hp + 1) * NSEQ],
                            in0=acc[0:64, hp * NSEQ:(hp + 1) * NSEQ],
                            in1=bank[0:64, 0:NSEQ], op=OP.mult))
                        pool.release(bi, "v")
                p.wait("sync", "v", p.mark("v"))
                p.emit("sync", lambda e: e.dma_start(
                    out=outT[64:128, :], in_=thi[0:64, :]), inc="ld", n=16)
                p.wait("tensor", "ld", p.mark("ld"))
                p.wait("tensor", "v", p.mark("v"))
                bi, bank = pool.acquire()
                for oc in range(DC):
                    p.emit("tensor", lambda e, oc=oc, bank=bank: e.matmul(
                        bank[0:HARM, 0:NSEQ],
                        bas_sb[:, (3 * DC + oc) * HARM:
                               (3 * DC + oc + 1) * HARM],
                        outT[:, oc * NSEQ:(oc + 1) * NSEQ],
                        start=(oc == 0), stop=(oc == DC - 1)), inc="t")
                p.wait("scalar", "t", p.mark("t"))
                p.emit("scalar", lambda e, bank=bank: e.activation(
                    out=res_sb[:, 1024:1024 + NSEQ],
                    in_=bank[0:HARM, 0:NSEQ], func=AF.Copy), inc="s")
                pool.release(bi, "s")
                p.wait("tensor", "s", p.mark("s"))
                for sub in range(2):
                    for ob in range(2):
                        bi, bank = pool.acquire()
                        p.emit("tensor", lambda e, sub=sub, ob=ob,
                               bank=bank: e.matmul(
                            bank[:],
                            res_sb[:, 1024 + sub * 128:
                                   1024 + (sub + 1) * 128],
                            w4[:, 3 * D + ob * 512:3 * D + (ob + 1) * 512],
                            start=True, stop=True), inc="t")
                        p.wait("vector", "t", p.mark("t"))
                        p.emit("vector", lambda e, sub=sub, ob=ob, sl=sl,
                               bank=bank: e.scalar_tensor_tensor(
                            out=xo_sb[:, sl + sub * D + ob * 512:
                                      sl + sub * D + (ob + 1) * 512],
                            in0=bank[:], scalar=1.0,
                            in1=xo_sb[:, sl + sub * D + ob * 512:
                                      sl + sub * D + (ob + 1) * 512],
                            op0=OP.mult, op1=OP.add))
                        pool.release(bi, "v")
                p.wait("gpsimd", "v", p.mark("v"))
                p.emit("gpsimd", lambda e, b=b, sl=sl: e.dma_start(
                    out=x5_d.rearrange("(i p) d -> p i d", p=128)
                        [:, 2 * b:2 * b + 2, :],
                    in_=xo_sb[:, sl:sl + 2 * D]
                    .rearrange("p (i d) -> p i d", i=2)),
                    inc="st", n=16, raw=True)
                ln_stats(p, xo_sb[:, sl:sl + 2 * D],
                         mean_sb[:, 2 * b:2 * b + 2],
                         rstd_sb[:, 2 * b:2 * b + 2],
                         sq_sb[:], col0[:], range(2))
                xo_marks["v"] = p.mark("v")
                xo_marks["s"] = p.mark("s")
                xo_marks["st"] = p.mark("st")

            emit_xo(0)
            emit_xstats(0)
            qp_marks = {0: emit_qprep(0)}
            kv_marks = {0: emit_prep(0)}
            for idx in range(4 * B):
                b, qtr = divmod(idx, 4)
                emit_attn(idx, kv_marks[idx], qp_marks[b])
                if idx + 1 < 4 * B:
                    kv_marks[idx + 1] = emit_prep(idx + 1)
                if qtr == 3:
                    emit_epilogue(b)
                    if b + 1 < B:
                        qp_marks[b + 1] = emit_qprep(b + 1)
            p.barrier()
        with nc.sbuf_tensor("x_sb2", [128, NT * D], F32) as x_sb2, \
                nc.sbuf_tensor("xhT", [128, DC * 512], F32) as xhT, \
                nc.sbuf_tensor("wg_sb", [128, 2 * DC * D], F32) as wg_sb, \
                nc.sbuf_tensor("zx_sb", [128, DC * 512], F32) as zx_sb:
            p.emit("sync", lambda e: e.dma_start(
                out=x_sb2[:].rearrange("p (i d) -> p i d", i=NT),
                in_=x5_d.rearrange("(i p) d -> p i d", p=128)),
                inc="ld", n=16)
            x5_ld = p.mark("ld")
            for e in ("tensor", "vector", "scalar"):
                p.wait(e, "ld", x5_ld)

            def wg_load(g):
                p.wait("sync", "t", p.mark("t"))
                p.emit("sync", lambda e, g=g: e.dma_start(
                    out=wg_sb[:, (g % 2) * DC * D:(g % 2 + 1) * DC * D]
                    .rearrange("p (c d) -> p c d", c=DC),
                    in_=wgT[g].rearrange("c p d -> p c d")), inc="ld", n=16)
                return p.mark("ld")

            wg_marks = {0: wg_load(0)}
            for g in range(3):
                go = (g % 2) * DC * D
                if g + 1 < 3:
                    wg_marks[g + 1] = wg_load(g + 1)
                for tb in range(4):
                    tiles = list(range(tb * 4, tb * 4 + 4))
                    jit(tiles, x_sb2[:], xhT[:])
                    xh_ready = p.mark("s")
                    p.wait("tensor", "ld", wg_marks[g])
                    p.wait("tensor", "s", xh_ready)
                    p.wait("scalar", "st", p.mark("st"))
                    for mc in range(DC):
                        bi, bank = pool.acquire()
                        for f in range(DC):
                            p.emit("tensor", lambda e, f=f, mc=mc,
                                   go=go, bank=bank: e.matmul(
                                bank[:],
                                wg_sb[:, go + mc * D + f * 128:
                                      go + mc * D + (f + 1) * 128],
                                xhT[:, f * 512:(f + 1) * 512],
                                start=(f == 0), stop=(f == DC - 1)),
                                inc="t")
                        p.wait("scalar", "t", p.mark("t"))
                        p.emit("scalar", lambda e, g=g, mc=mc, bank=bank:
                               e.activation(
                            out=zx_sb[:, mc * 512:(mc + 1) * 512],
                            in_=bank[:], func=AF.Identity,
                            bias=bgc[:, g * DC + mc:g * DC + mc + 1]),
                            inc="s")
                        pool.release(bi, "s")
                    p.wait("gpsimd", "s", p.mark("s"))
                    p.emit("gpsimd", lambda e, g=g, tb=tb: e.dma_start(
                        out=zx_d[g].rearrange("c p t -> p c t")
                            [:, :, tb * 512:(tb + 1) * 512],
                        in_=zx_sb[:].rearrange("p (c q) -> p c q", c=DC)),
                        inc="st", n=16, raw=True)
        p.barrier()
        emit_ffn1_chain(nc, p, pool, jit, ugT, uoT, woR, x5_d, x6_o, zx_d,
                        eye_sb[:], boc[:], incc[:])
        p.barrier()
        p.wait("gpsimd", "st", p.cnt["st"])
        p.build()
    return nc


def stage_l4(inputs, res3):
    com = {
        "basR": np.stack([pack_rhs(np.asarray(inputs[f"L1_attn_basis_{k}"],
                                              np.float32))
                          for k in "qkvo"]),
        "phT": np.ascontiguousarray(np.concatenate(
            [np.asarray(inputs[f"L1_attn_phase_{k}"], np.float32).T
             for k in "qkvo"], axis=1)),
        "amT": np.ascontiguousarray(np.concatenate(
            [np.asarray(inputs[f"L1_attn_amp_{k}"], np.float32).T
             for k in "qkvo"], axis=1)),
        "wgT": np.stack([pack_lhsT(np.asarray(inputs[k], np.float32))
                         for k in ("L1_ffn_wf", "L1_ffn_wi", "L1_ffn_wc")]),
        "ugT": np.stack([pack_lhsT(np.asarray(inputs[k], np.float32))
                         for k in ("L1_ffn_uf", "L1_ffn_ui", "L1_ffn_uc")]),
        "uoT": pack_lhsT(np.asarray(inputs["L1_ffn_uo"], np.float32)),
        "woR": pack_rhs(np.asarray(inputs["L1_ffn_wo"], np.float32)),
        "bg": np.stack([np.asarray(inputs[k], np.float32)
                        for k in ("L1_ffn_bf", "L1_ffn_bi", "L1_ffn_bc")]),
        "bo_ff": np.asarray(inputs["L1_ffn_bo"], np.float32),
        "inc_ff": np.asarray(inputs["L1_ffn_in_c"], np.float32),
    }
    x4_by_batch = np.empty((B, S, D), np.float32)
    for j in range(NCORE):
        blk = np.asarray(res3[j]["x4"]).reshape(B, 256, D)
        x4_by_batch[:, j * 256:(j + 1) * 256, :] = blk
    x4_flat = np.ascontiguousarray(x4_by_batch.reshape(B * S, D))
    maps = []
    for j in range(NCORE):
        own = np.ascontiguousarray(
            x4_by_batch[:, j * 256:(j + 1) * 256, :].reshape(TOK, D))
        maps.append(dict(com, x4f=x4_flat, x4own=own))
    return maps


# version 20
# speedup vs baseline: 1.0615x; 1.0610x over previous
import numpy as np
from contextlib import ExitStack

import concourse.bass as bass
import concourse.mybir as mybir
from concourse.bass_utils import run_bass_kernel_spmd

F32 = mybir.dt.float32
AF = mybir.ActivationFunctionType
OP = mybir.AluOpType
AX = mybir.AxisListType

NCORE = 8
DBG = False
B, S, D = 8, 2048, 1024
TOK = 2048
NT = TOK // 128
DC = D // 128
H, DH, PM = 16, 64, 256
FF = 4096
FC = FF // 128
HARM = 32
LN_EPS = 1e-5
KEPS = 1e-3
NSTEP = 5
BETA = 0.9

ENGS = ("sync", "tensor", "vector", "scalar", "gpsimd")



class Prog:

    def __init__(self, nc, stack):
        self.nc = nc
        self.stack = stack
        self.cmds = {e: [] for e in ENGS}
        self.sems = {}
        self.cnt = {}
        for s in ("t", "v", "s", "g", "ld", "ld2", "st"):
            self.sem(s)

    def sem(self, name):
        if name not in self.sems:
            self.sems[name] = self.stack.enter_context(self.nc.semaphore(name))
            self.cnt[name] = 0
        return name

    OWN = {"vector": "v", "scalar": "s", "gpsimd": "g"}

    def emit(self, eng, fn, inc=None, n=1, raw=False):
        own = None if raw else self.OWN.get(eng)
        if eng == "tensor":
            self.tensor_emits = getattr(self, "tensor_emits", 0) + 1
            cell = self.__dict__.setdefault("tcell", [None])
            if inc is not None:
                self.sem(inc)
                self.cnt[inc] += n
                s = self.sems[inc]
                def runt(e, fn=fn, s=s, n=n, cell=cell):
                    cell[0] = fn(e).then_inc(s, n)
                self.cmds[eng].append(runt)
                self.last_tensor_inc = (inc, self.cnt[inc])
            else:
                def runt2(e, fn=fn, cell=cell):
                    cell[0] = fn(e)
                self.cmds[eng].append(runt2)
                self.last_tensor_inc = None
            return
        if own is not None:
            assert inc in (None, own), (eng, inc)
            prev = self.cnt[own]
            self.cnt[own] += 1
            s = self.sems[own]
            def run(e, fn=fn, s=s, prev=prev):
                if prev:
                    e.wait_ge(s, prev)
                fn(e).then_inc(s, 1)
            self.cmds[eng].append(run)
        elif inc is not None:
            self.sem(inc)
            self.cnt[inc] += n
            s = self.sems[inc]
            self.cmds[eng].append(lambda e, fn=fn, s=s, n=n: fn(e).then_inc(s, n))
        else:
            self.cmds[eng].append(fn)

    def wait(self, eng, sem, val=None):
        val = self.cnt[sem] if val is None else val
        s = self.sems[sem]
        self.cmds[eng].append(lambda e, s=s, v=val: e.wait_ge(s, v))

    def mark(self, sem):
        return self.cnt[sem]

    def barrier(self, engs=ENGS):
        marks = {}
        for e in engs:
            if e in self.OWN:
                self.emit(e, lambda x: x.nop())
                marks[e] = (self.OWN[e], self.cnt[self.OWN[e]])
            elif e == "tensor":
                emitted = getattr(self, "tensor_emits", 0)
                self.sem("bt")
                if emitted > getattr(self, "_bar_tensor_seen", 0):
                    self._bar_tensor_seen = emitted
                    lti = getattr(self, "last_tensor_inc", None)
                    if lti is not None:
                        marks[e] = lti
                    else:
                        self.cnt["bt"] += 1
                        cell = self.__dict__.setdefault("tcell", [None])
                        s = self.sems["bt"]
                        self.cmds["tensor"].append(
                            lambda _e, cell=cell, s=s: cell[0].then_inc(s, 1))
                        marks[e] = ("bt", self.cnt["bt"])
                else:
                    marks[e] = ("bt", self.cnt["bt"])
            else:
                marks[e] = ("ld", self.cnt["ld"])
        stv = self.cnt["st"]
        for e in engs:
            for o in engs:
                if o != e:
                    sem, val = marks[o]
                    self.wait(e, sem, val)
            if stv:
                self.wait(e, "st", stv)

    def build(self):
        blk = self.stack.enter_context(self.nc.Block())
        for e in ENGS:
            def run_all(eng, e=e):
                for c in self.cmds[e]:
                    c(eng)
            getattr(blk, e)(run_all)


class PsumPool:
    def __init__(self, p, banks):
        self.p = p
        self.banks = banks
        self.busy = [None] * len(banks)
        self.rr = 0

    def acquire(self):
        i = self.rr % len(self.banks)
        self.rr += 1
        if self.busy[i] is not None:
            sem, val = self.busy[i]
            self.p.wait("tensor", sem, val)
            self.busy[i] = None
        return i, self.banks[i]

    def release(self, i, sem):
        self.busy[i] = (sem, self.p.cnt[sem])


class DBuf:

    def __init__(self, p, bufs, queue="sync", ld="ld"):
        self.p = p
        self.bufs = bufs
        self.queue = queue
        self.ld = ld
        self.tmarks = [None] * len(bufs)
        self.i = 0

    def load(self, fn):
        k = self.i % len(self.bufs)
        self.i += 1
        buf = self.bufs[k]
        if self.tmarks[k] is not None:
            self.p.wait(self.queue, "t", self.tmarks[k])
        self.p.emit(self.queue, lambda e, buf=buf, fn=fn: fn(e, buf),
                    inc=self.ld, n=16)
        return buf, self.p.mark(self.ld)

    def consumed(self):
        k = (self.i - 1) % len(self.bufs)
        self.tmarks[k] = self.p.cnt["t"]



BIAS_CONSTS = (1e-5, 1e-3, 1.5707963267948966,
               -3.141592653589793)


def setup_consts(nc, p, ec):
    arr = np.tile(np.array([BIAS_CONSTS], np.float32), (128, 1))
    cd = nc.inline_tensor(arr, "biasconsts").ap()
    cs = ec(nc.sbuf_tensor("consts_sb", [128, len(BIAS_CONSTS)], F32))
    p.emit("sync", lambda e: e.dma_start(out=cs[:], in_=cd), inc="ld", n=16)
    for j, v in enumerate(BIAS_CONSTS):
        nc.const_aps.register(F32, float(v), cs[:, j:j + 1])
    return cs


def ln_stats(p, x_sb, mean_sb, rstd_sb, sq_sb, col0, tiles):
    for n_, i in enumerate(tiles):
        xs = x_sb[:, i * D:(i + 1) * D]
        p.emit("vector", lambda e, xs=xs, i=i: e.tensor_reduce(
            out=mean_sb[:, i:i + 1], in_=xs, axis=AX.X, op=OP.add), inc="v")
        if n_ > 0:
            p.wait("scalar", "s", p.mark("s"))
        p.emit("scalar", lambda e, xs=xs, i=i: e.activation(
            out=sq_sb[:], in_=xs, func=AF.Square,
            accum_out=rstd_sb[:, i:i + 1]), inc="s")
    p.wait("vector", "s", p.mark("s"))
    for i in tiles:
        p.emit("vector", lambda e, i=i: e.tensor_scalar(
            out=mean_sb[:, i:i + 1], in0=mean_sb[:, i:i + 1],
            scalar1=1.0 / D, scalar2=None, op0=OP.mult), inc="v")
        p.emit("vector", lambda e, i=i: e.tensor_tensor(
            out=col0[:], in0=mean_sb[:, i:i + 1], in1=mean_sb[:, i:i + 1],
            op=OP.mult), inc="v")
        p.emit("vector", lambda e, i=i: e.tensor_scalar(
            out=rstd_sb[:, i:i + 1], in0=rstd_sb[:, i:i + 1],
            scalar1=1.0 / D, scalar2=None, op0=OP.mult), inc="v")
        p.emit("vector", lambda e, i=i: e.tensor_tensor(
            out=rstd_sb[:, i:i + 1], in0=rstd_sb[:, i:i + 1], in1=col0[:],
            op=OP.subtract), inc="v")
        p.wait("scalar", "v", p.mark("v"))
        p.emit("scalar", lambda e, i=i: e.activation(
            out=rstd_sb[:, i:i + 1], in_=rstd_sb[:, i:i + 1], func=AF.Ln,
            bias=LN_EPS), inc="s")
        p.emit("scalar", lambda e, i=i: e.activation(
            out=rstd_sb[:, i:i + 1], in_=rstd_sb[:, i:i + 1], func=AF.Exp,
            scale=-0.5), inc="s")
    p.wait("vector", "s", p.mark("s"))


def make_jit_xhat(p, pool, mean_sb, rstd_sb, eye_sb, xc_sb, diag_sb,
                  depth=1):
    last = {d: 0 for d in range(depth)}

    def jit(tiles, x_sb, xhT, stats=None):
        n = len(tiles) * 128
        nt = len(tiles)
        vstage = {}

        def stage(k):
            i = tiles[k]
            si = stats[k] if stats is not None else i
            sl, dsl = (k % depth) * D, (k % depth) * 128
            xs = x_sb[:, i * D:(i + 1) * D]
            p.wait("vector", "t", last[k % depth])
            p.emit("vector", lambda e, xs=xs, si=si, sl=sl: e.tensor_scalar(
                out=xc_sb[:, sl:sl + D], in0=xs,
                scalar1=mean_sb[:, si:si + 1],
                scalar2=None, op0=OP.subtract), inc="v")
            p.emit("vector", lambda e, si=si, dsl=dsl: e.tensor_scalar(
                out=diag_sb[:, dsl:dsl + 128], in0=eye_sb[:],
                scalar1=rstd_sb[:, si:si + 1],
                scalar2=None, op0=OP.mult), inc="v")
            vstage[k] = p.mark("v")

        for k in range(min(depth, nt)):
            stage(k)
        for k, i in enumerate(tiles):
            sl, dsl = (k % depth) * D, (k % depth) * 128
            p.wait("tensor", "v", vstage[k])
            for half in range(2):
                bi, bank = pool.acquire()
                for q in range(4):
                    f = half * 4 + q
                    p.emit("tensor", lambda e, f=f, q=q, sl=sl, dsl=dsl,
                           bank=bank: e.matmul(
                        bank[:, q * 128:(q + 1) * 128],
                        xc_sb[:, sl + f * 128:sl + (f + 1) * 128],
                        diag_sb[:, dsl:dsl + 128], start=True, stop=True),
                        inc="t" if q == 3 else None)
                if half == 1:
                    last[k % depth] = p.mark("t")
                p.wait("scalar", "t", p.mark("t"))
                for q in range(4):
                    f = half * 4 + q
                    p.emit("scalar", lambda e, f=f, q=q, k=k, bank=bank:
                           e.activation(
                               out=xhT[:, f * n + k * 128:f * n + (k + 1) * 128],
                               in_=bank[:, q * 128:(q + 1) * 128],
                               func=AF.Copy), inc="s")
                pool.release(bi, "s")
            if k + depth < nt:
                stage(k + depth)
    return jit


def spike_residual(p, z_sb, mem_sb, spk_sb, acc_sb, x_tile):
    p.emit("vector", lambda e: e.tensor_scalar(
        out=acc_sb[:], in0=z_sb[:], scalar1=1.0, scalar2=None,
        op0=OP.is_gt), inc="v")
    first = True
    for _ in range(2, NSTEP + 1):
        src = z_sb if first else mem_sb
        prev_spk = acc_sb if first else spk_sb
        p.emit("vector", lambda e, src=src: e.scalar_tensor_tensor(
            out=mem_sb[:], in0=src[:], scalar=BETA, in1=z_sb[:],
            op0=OP.mult, op1=OP.add), inc="v")
        p.emit("vector", lambda e, ps=prev_spk: e.tensor_tensor(
            out=mem_sb[:], in0=mem_sb[:], in1=ps[:], op=OP.subtract), inc="v")
        p.emit("vector", lambda e: e.tensor_scalar(
            out=spk_sb[:], in0=mem_sb[:], scalar1=1.0, scalar2=None,
            op0=OP.is_gt), inc="v")
        p.emit("vector", lambda e: e.tensor_tensor(
            out=acc_sb[:], in0=acc_sb[:], in1=spk_sb[:], op=OP.add), inc="v")
        first = False
    p.emit("vector", lambda e, xt=x_tile: e.scalar_tensor_tensor(
        out=xt, in0=acc_sb[:], scalar=0.1, in1=xt, op0=OP.mult, op1=OP.add),
        inc="v")


def emit_ffn0(p, pool, jit, x_sb, wfc_dram, wproj_dram, sb, dbg=None):
    xhT, h_sb, r_sb, z_sb = sb["xhT"], sb["h"], sb["r"], sb["z"]
    wfc = DBuf(p, sb["wfc"], "sync", "ld")
    wproj = DBuf(p, sb["wproj"], "sync", "ld")
    jit(list(range(4)), x_sb, xhT)
    xh_ready = p.mark("s")
    sq_mark = 0
    spike_mark = 0
    pre_wfc = []
    for tb in range(4):
        fcmode = ("fc" if "fc" in FFN0_PARTS else
                  "fcmm" if "fcmm" in FFN0_PARTS else
                  "fcload" if "fcload" in FFN0_PARTS else "off")
        for hc in range(FC if fcmode != "off" else 0):
            if pre_wfc:
                buf, ldm = pre_wfc.pop(0)
            else:
                buf, ldm = wfc.load(lambda e, buf, hc=hc: e.dma_start(
                    out=buf[:], in_=wfc_dram[hc]))
            p.wait("tensor", "ld", ldm)
            if fcmode == "fcload":
                continue
            if hc == 0:
                p.wait("tensor", "s", xh_ready)
            bi, bank = pool.acquire()
            for f in range(DC):
                p.emit("tensor", lambda e, f=f, buf=buf, bank=bank: e.matmul(
                    bank[:], buf[:, f * 128:(f + 1) * 128],
                    xhT[:, f * 512:(f + 1) * 512],
                    start=(f == 0), stop=(f == DC - 1)),
                    inc="t" if f == DC - 1 else None)
            wfc.consumed()
            if fcmode == "fcmm":
                pool.release(bi, "t")
                continue
            p.wait("scalar", "t", p.mark("t"))
            p.wait("scalar", "v", sq_mark)
            p.emit("scalar", lambda e, bank=bank: e.activation(
                out=r_sb[:], in_=bank[:], func=AF.Relu), inc="s")
            pool.release(bi, "s")
            p.wait("vector", "s", p.mark("s"))
            p.emit("vector", lambda e, hc=hc: e.tensor_mul(
                out=h_sb[:, hc * 512:(hc + 1) * 512], in0=r_sb[:],
                in1=r_sb[:]), inc="v")
            sq_mark = p.mark("v")
        h_ready = p.mark("v")
        if tb < 3:
            jit(list(range(tb * 4 + 4, tb * 4 + 8)), x_sb, xhT)
            next_xh = p.mark("s")
        if "proj" not in FFN0_PARTS:
            continue
        for k in range(8):
            bi, bank = pool.acquire()
            assert bi == k % len(pool.banks)
        banks = pool.banks
        p.wait("tensor", "v", h_ready)
        for hc in range(FC):
            buf, ldm = wproj.load(lambda e, buf, hc=hc: e.dma_start(
                out=buf[:], in_=wproj_dram[hc]))
            p.wait("tensor", "ld", ldm)
            for sub in range(4):
                for od in range(2):
                    bank = banks[sub * 2 + od]
                    p.emit("tensor",
                           lambda e, hc=hc, sub=sub, od=od, bank=bank,
                           buf=buf: e.matmul(
                               bank[:],
                               h_sb[:, hc * 512 + sub * 128:
                                    hc * 512 + (sub + 1) * 128],
                               buf[:, od * 512:(od + 1) * 512],
                               start=(hc == 0), stop=(hc == FC - 1)),
                           inc="t" if (sub == 3 and od == 1) else None)
            wproj.consumed()
        if dbg is not None and tb == 0:
            p.wait("gpsimd", "s", p.mark("s"))
            p.wait("gpsimd", "v", p.mark("v"))
            p.emit("gpsimd", lambda e: e.dma_start(out=dbg[0], in_=xhT),
                   inc="st", n=16, raw=True)
            p.emit("gpsimd", lambda e: e.dma_start(out=dbg[1], in_=h_sb),
                   inc="st", n=16, raw=True)
        proj_done = p.mark("t")
        p.wait("scalar", "t", proj_done)
        for sub in range(4):
            tile = tb * 4 + sub
            p.wait("scalar", "v", spike_mark if sub == 0 else p.mark("v"))
            for od in range(2):
                bank = banks[sub * 2 + od]
                p.emit("scalar", lambda e, od=od, bank=bank: e.activation(
                    out=z_sb[:, od * 512:(od + 1) * 512], in_=bank[:],
                    func=AF.Copy), inc="s")
            p.wait("vector", "s", p.mark("s"))
            if "spike" in FFN0_PARTS:
                spike_residual(p, z_sb, sb["mem"], sb["spk"], sb["acc"],
                               x_sb[:, tile * D:(tile + 1) * D])
        spike_mark = p.mark("v")
        for k in range(8):
            pool.release(k, "s")
        if tb < 3:
            xh_ready = next_xh
    p.barrier()



L1_UPTO = "all"
FFN0_PARTS = {"fc", "proj", "spike"}


def build_l1():
    nc = bass.Bass("TRN2", target_bir_lowering=False,
                   detect_race_conditions=False)
    x0 = nc.dram_tensor("x0", [TOK, D], F32, kind="ExternalInput").ap()
    wfc = nc.dram_tensor("wfc", [FC, 128, D], F32, kind="ExternalInput").ap()
    wproj = nc.dram_tensor("wproj", [FC, 128, D], F32,
                           kind="ExternalInput").ap()
    wkT = nc.dram_tensor("wkT", [DC, 128, D], F32, kind="ExternalInput").ap()
    bk = nc.dram_tensor("bk", [D], F32, kind="ExternalInput").ap()
    x1_o = nc.dram_tensor("x1", [TOK, D], F32, kind="ExternalOutput").ap()
    kT_o = nc.dram_tensor("kT", [D, TOK], F32, kind="ExternalOutput").ap()
    diag_o = nc.dram_tensor("diag", [TOK, H], F32, kind="ExternalOutput").ap()
    lmax_o = nc.dram_tensor("lmax", [1, 1], F32, kind="ExternalOutput").ap()
    if DBG:
        dbg_mean = nc.dram_tensor("dbg_mean", [128, NT], F32,
                                  kind="ExternalOutput").ap()
        dbg_rstd = nc.dram_tensor("dbg_rstd", [128, NT], F32,
                                  kind="ExternalOutput").ap()
        dbg_xh = nc.dram_tensor("dbg_xh", [128, DC * 512], F32,
                                kind="ExternalOutput").ap()
        dbg_h = nc.dram_tensor("dbg_h", [128, FC * 512], F32,
                               kind="ExternalOutput").ap()
    eye_d = nc.inline_tensor(np.eye(128, dtype=np.float32), "eye").ap()
    hmask = np.zeros((128, 2), dtype=np.float32)
    hmask[:64, 0] = 1.0
    hmask[64:, 1] = 1.0
    ones_d = nc.inline_tensor(hmask, "hmask").ap()

    with ExitStack() as stack:
        ec = stack.enter_context
        p = Prog(nc, stack)
        x_sb = ec(nc.sbuf_tensor("x_sb", [128, NT * D], F32))
        mean_sb = ec(nc.sbuf_tensor("mean_sb", [128, NT], F32))
        rstd_sb = ec(nc.sbuf_tensor("rstd_sb", [128, NT], F32))
        sq_sb = ec(nc.sbuf_tensor("sq_sb", [128, D], F32))
        col0 = ec(nc.sbuf_tensor("col0", [128, 1], F32))
        eye_sb = ec(nc.sbuf_tensor("eye_sb", [128, 128], F32))
        setup_consts(nc, p, ec)
        ones_sb = ec(nc.sbuf_tensor("ones_sb", [128, 2], F32))
        xc_sb = ec(nc.sbuf_tensor("xc_sb", [128, 2 * D], F32))
        dg_sb = ec(nc.sbuf_tensor("dg_sb", [128, 2 * 128], F32))
        diag_sb = ec(nc.sbuf_tensor("diag_sb", [128, NT * H], F32))
        bk_sb = ec(nc.sbuf_tensor("bk_sb", [128, DC], F32))
        banks = [ec(nc.psum_tensor(f"ps{i}", [128, 512], F32))
                 for i in range(8)]
        pool = PsumPool(p, [b[:] for b in banks])
        jit = make_jit_xhat(p, pool, mean_sb[:], rstd_sb[:], eye_sb[:],
                            xc_sb[:], dg_sb[:], depth=2)

        p.emit("sync", lambda e: e.dma_start(
            out=x_sb[:].rearrange("p (i d) -> p i d", i=NT),
            in_=x0.rearrange("(i p) d -> p i d", p=128)), inc="ld", n=16)
        p.emit("sync", lambda e: e.dma_start(out=eye_sb[:], in_=eye_d),
               inc="ld", n=16)
        p.emit("sync", lambda e: e.dma_start(out=ones_sb[:], in_=ones_d),
               inc="ld", n=16)
        p.emit("sync", lambda e: e.dma_start(
            out=bk_sb[:], in_=bk.rearrange("(c p) -> p c", p=128),
            allow_slow_non_contiguous=True), inc="ld", n=16)
        ldm = p.mark("ld")
        for e in ("tensor", "vector", "scalar"):
            p.wait(e, "ld", ldm)

        ln_stats(p, x_sb[:], mean_sb[:], rstd_sb[:], sq_sb[:], col0[:],
                 range(NT))
        p.barrier()
        if DBG:
            p.emit("gpsimd", lambda e: e.dma_start(out=dbg_mean,
                                                   in_=mean_sb[:]),
                   inc="st", n=16, raw=True)
            p.emit("gpsimd", lambda e: e.dma_start(out=dbg_rstd,
                                                   in_=rstd_sb[:]),
                   inc="st", n=16, raw=True)
        with nc.sbuf_tensor("xhT", [128, DC * 512], F32) as xhT, \
                nc.sbuf_tensor("h_sb", [128, FC * 512], F32) as h_sb, \
                nc.sbuf_tensor("r_sb", [128, 512], F32) as r_sb, \
                nc.sbuf_tensor("z_sb", [128, D], F32) as z_sb, \
                nc.sbuf_tensor("mem_sb", [128, D], F32) as mem_sb, \
                nc.sbuf_tensor("spk_sb", [128, D], F32) as spk_sb, \
                nc.sbuf_tensor("acc_sb", [128, D], F32) as acc_sb, \
                nc.sbuf_tensor("wfc0", [128, D], F32) as wfc0, \
                nc.sbuf_tensor("wfc1", [128, D], F32) as wfc1, \
                nc.sbuf_tensor("wp0", [128, D], F32) as wp0, \
                nc.sbuf_tensor("wp1", [128, D], F32) as wp1:
            sb = dict(xhT=xhT[:], h=h_sb[:], r=r_sb[:], z=z_sb[:],
                      mem=mem_sb[:], spk=spk_sb[:], acc=acc_sb[:],
                      wfc=[wfc0[:], wfc1[:]], wproj=[wp0[:], wp1[:]])
            emit_ffn0(p, pool, jit, x_sb[:], wfc, wproj, sb,
                      dbg=(dbg_xh, dbg_h) if DBG else None)

        p.emit("gpsimd", lambda e: e.dma_start(
            out=x1_o.rearrange("(i p) d -> p i d", p=128),
            in_=x_sb[:].rearrange("p (i d) -> p i d", i=NT)), inc="st", n=16, raw=True)

        if L1_UPTO == "all":
            emit_l1_kphase(nc, p, pool, jit, x_sb, mean_sb, rstd_sb, sq_sb,
                           col0, eye_sb, ones_sb, bk_sb, diag_sb, wkT,
                           kT_o, diag_o, lmax_o)
        p.barrier()
        p.wait("gpsimd", "st", p.cnt["st"])
        p.build()
    return nc


def emit_l1_kphase(nc, p, pool, jit, x_sb, mean_sb, rstd_sb, sq_sb, col0,
                   eye_sb, ones_sb, bk_sb, diag_sb, wkT, kT_o, diag_o,
                   lmax_o):
        ln_stats(p, x_sb[:], mean_sb[:], rstd_sb[:], sq_sb[:], col0[:],
                 range(NT))
        p.barrier()
        with nc.sbuf_tensor("xhT2", [128, DC * 512], F32) as xhT2, \
                nc.sbuf_tensor("kblk", [128, DC * 512], F32) as kblk, \
                nc.sbuf_tensor("k2", [128, DC * 512], F32) as k2, \
                nc.sbuf_tensor("wk0", [128, D], F32) as wk0, \
                nc.sbuf_tensor("wk1", [128, D], F32) as wk1, \
                nc.sbuf_tensor("mrow", [1, 129], F32) as mrow:
            wk = DBuf(p, [wk0[:], wk1[:]], "sync", "ld")
            st_prev = 0
            for tb in range(4):
                tiles = list(range(tb * 4, tb * 4 + 4))
                jit(tiles, x_sb[:], xhT2[:])
                xh_ready = p.mark("s")
                p.wait("scalar", "st", st_prev)
                p.wait("scalar", "t", p.mark("t"))
                for oc in range(DC):
                    buf, ldm = wk.load(lambda e, buf, oc=oc: e.dma_start(
                        out=buf[:], in_=wkT[oc]))
                    p.wait("tensor", "ld", ldm)
                    if oc == 0:
                        p.wait("tensor", "s", xh_ready)
                    bi, bank = pool.acquire()
                    for f in range(DC):
                        p.emit("tensor",
                               lambda e, f=f, buf=buf, bank=bank: e.matmul(
                                   bank[:], buf[:, f * 128:(f + 1) * 128],
                                   xhT2[:, f * 512:(f + 1) * 512],
                                   start=(f == 0), stop=(f == DC - 1)),
                               inc="t" if f == DC - 1 else None)
                    wk.consumed()
                    p.wait("scalar", "t", p.mark("t"))
                    p.emit("scalar", lambda e, oc=oc, bank=bank: e.activation(
                        out=kblk[:, oc * 512:(oc + 1) * 512], in_=bank[:],
                        func=AF.Identity, bias=bk_sb[:, oc:oc + 1]), inc="s")
                    pool.release(bi, "s")
                p.wait("gpsimd", "s", p.mark("s"))
                p.emit("gpsimd", lambda e, tb=tb: e.dma_start(
                    out=kT_o.rearrange("(c p) t -> p c t", p=128)
                        [:, :, tb * 512:(tb + 1) * 512],
                    in_=kblk[:].rearrange("p (c q) -> p c q", c=DC)),
                    inc="st", n=16, raw=True)
                st_prev = p.mark("st")
                p.emit("scalar", lambda e: e.activation(
                    out=k2[:], in_=kblk[:], func=AF.Square), inc="s")
                p.wait("tensor", "s", p.mark("s"))
                for sub in range(4):
                    tile = tb * 4 + sub
                    bi, bank = pool.acquire()
                    for h in range(H):
                        oc, half = h // 2, (h % 2)
                        p.emit("tensor",
                               lambda e, h=h, oc=oc, half=half, sub=sub,
                               bank=bank: e.matmul(
                                   bank[:, h:h + 1],
                                   k2[:, oc * 512 + sub * 128:
                                      oc * 512 + (sub + 1) * 128],
                                   ones_sb[:, half:half + 1],
                                   start=True, stop=True),
                               inc="t" if h == H - 1 else None)
                    p.wait("vector", "t", p.mark("t"))
                    p.emit("vector", lambda e, tile=tile, bank=bank:
                           e.tensor_scalar(
                               out=diag_sb[:, tile * H:(tile + 1) * H],
                               in0=bank[:, 0:H], scalar1=-0.5, scalar2=None,
                               op0=OP.mult), inc="v")
                    pool.release(bi, "v")
            p.barrier()
            p.emit("vector", lambda e: e.tensor_reduce(
                out=col0[:], in_=diag_sb[:], axis=AX.X, op=OP.max), inc="v")
            p.wait("tensor", "v", p.mark("v"))
            bi, bank = pool.acquire()
            p.emit("tensor", lambda e, bank=bank: e.matmul(
                bank[0:1, 0:128], col0[:], eye_sb[:], start=True, stop=True),
                inc="t")
            p.wait("scalar", "t", p.mark("t"))
            p.emit("scalar", lambda e, bank=bank: e.activation(
                out=mrow[0:1, 0:128], in_=bank[0:1, 0:128], func=AF.Copy),
                inc="s")
            pool.release(bi, "s")
            p.wait("vector", "s", p.mark("s"))
            p.emit("vector", lambda e: e.tensor_reduce(
                out=mrow[0:1, 128:129], in_=mrow[0:1, 0:128], axis=AX.X,
                op=OP.max), inc="v")
            p.wait("gpsimd", "v", p.mark("v"))
            p.emit("gpsimd", lambda e: e.dma_start(
                out=lmax_o, in_=mrow[0:1, 128:129]), inc="st", n=16, raw=True)
            p.emit("gpsimd", lambda e: e.dma_start(
                out=diag_o.rearrange("(i p) h -> p i h", p=128),
                in_=diag_sb[:].rearrange("p (i h) -> p i h", i=NT)),
                inc="st", n=16, raw=True)



def pack_lhsT(w):
    o, i = w.shape
    return np.ascontiguousarray(
        w.reshape(o // 128, 128, i // 128, 128).transpose(0, 3, 2, 1)
        .reshape(o // 128, 128, i))


def pack_rhs(w):
    o, i = w.shape
    return np.ascontiguousarray(w.T.reshape(i // 128, 128, o))


def stage_l1(inputs):
    hs = np.ascontiguousarray(inputs["L0_attn_hidden_states"],
                              dtype=np.float32)
    com = {
        "wfc": pack_lhsT(np.asarray(inputs["L0_ffn_W_fc"], np.float32)),
        "wproj": pack_rhs(np.asarray(inputs["L0_ffn_W_proj"], np.float32)),
        "wkT": pack_lhsT(np.asarray(inputs["L0_attn_Wk"], np.float32)),
        "bk": np.asarray(inputs["L0_attn_bk"], np.float32),
    }
    return [dict(com, x0=np.ascontiguousarray(hs[b])) for b in range(NCORE)]


def run_l1(inputs):
    nc = build_l1()
    res = run_bass_kernel_spmd(nc, stage_l1(inputs),
                               core_ids=list(range(NCORE)))
    return res.results


def _run(build, maps):
    nc = build()
    return run_bass_kernel_spmd(nc, maps, core_ids=list(range(NCORE))).results


def kernel(**inputs):
    r1 = _run(build_l1, stage_l1(inputs))
    r2 = _run(build_l2, stage_l2(inputs, r1))
    r3 = _run(build_l3, stage_l3(inputs, r2))
    r4 = _run(build_l4, stage_l4(inputs, r3))
    out = np.empty((B, S, D), np.float32)
    for j in range(NCORE):
        out[:, j * 256:(j + 1) * 256, :] = \
            np.asarray(r4[j]["x6"]).reshape(B, 256, D)
    return out



def build_l2():
    nc = bass.Bass("TRN2", target_bir_lowering=False,
                   detect_race_conditions=False)
    x1 = nc.dram_tensor("x1", [TOK, D], F32, kind="ExternalInput").ap()
    kT_i = nc.dram_tensor("kT", [D, TOK], F32, kind="ExternalInput").ap()
    diag_i = nc.dram_tensor("diag", [TOK, H], F32, kind="ExternalInput").ap()
    maxes = nc.dram_tensor("maxes", [1, NCORE], F32, kind="ExternalInput").ap()
    wqT = nc.dram_tensor("wqT", [DC, 128, D], F32, kind="ExternalInput").ap()
    wvT = nc.dram_tensor("wvT", [DC, 128, D], F32, kind="ExternalInput").ap()
    woT = nc.dram_tensor("woT", [DC, 128, D], F32, kind="ExternalInput").ap()
    projT = nc.dram_tensor("projT", [DH, PM], F32, kind="ExternalInput").ap()
    bq = nc.dram_tensor("bq", [D], F32, kind="ExternalInput").ap()
    bv = nc.dram_tensor("bv", [D], F32, kind="ExternalInput").ap()
    bo = nc.dram_tensor("bo", [D], F32, kind="ExternalInput").ap()
    x2_o = nc.dram_tensor("x2", [TOK, D], F32, kind="ExternalOutput").ap()
    if DBG:
        dbg_ctx = nc.dram_tensor("dbg_ctx", [128, H * 2 * 65], F32,
                                 kind="ExternalOutput").ap()
        dbg_q = nc.dram_tensor("dbg_q", [128, DC * 512], F32,
                               kind="ExternalOutput").ap()
        dbg_v = nc.dram_tensor("dbg_v", [128, 4 * (H * 65 + 1)], F32,
                               kind="ExternalOutput").ap()
        dbg_kf = nc.dram_tensor("dbg_kf", [128, PM], F32,
                                kind="ExternalOutput").ap()
    eye_d = nc.inline_tensor(np.eye(128, dtype=np.float32), "eye").ap()
    onesrow_d = nc.inline_tensor(np.ones((1, 128), dtype=np.float32),
                                 "onesrow").ap()

    with ExitStack() as stack:
        ec = stack.enter_context
        p = Prog(nc, stack)
        x_sb = ec(nc.sbuf_tensor("x_sb", [128, NT * D], F32))
        mean_sb = ec(nc.sbuf_tensor("mean_sb", [128, NT], F32))
        rstd_sb = ec(nc.sbuf_tensor("rstd_sb", [128, NT], F32))
        sq_sb = ec(nc.sbuf_tensor("sq_sb", [128, D], F32))
        col0 = ec(nc.sbuf_tensor("col0", [128, 1], F32))
        eye_sb = ec(nc.sbuf_tensor("eye_sb", [128, 128], F32))
        ones_row = ec(nc.sbuf_tensor("ones_row", [1, 128], F32))
        setup_consts(nc, p, ec)
        xc_sb = ec(nc.sbuf_tensor("xc_sb", [128, D], F32))
        dg_sb = ec(nc.sbuf_tensor("dg_sb", [128, 128], F32))
        bias_sb = ec(nc.sbuf_tensor("bias_sb", [128, NT * H], F32))
        stab_sb = ec(nc.sbuf_tensor("stab_sb", [128, 1], F32))
        mrow = ec(nc.sbuf_tensor("mrow", [1, NCORE + 1], F32))
        projc = ec(nc.sbuf_tensor("projc", [128, 2 * PM], F32))
        bqc = ec(nc.sbuf_tensor("bqc", [128, DC], F32))
        bv_row = ec(nc.sbuf_tensor("bv_row", [1, D], F32))
        bo_row = ec(nc.sbuf_tensor("bo_row", [1, D], F32))
        diag_sb = ec(nc.sbuf_tensor("diag_sb", [128, NT * H], F32))
        ctx_sb = ec(nc.sbuf_tensor("ctx_sb", [128, H * 2 * 65], F32))
        xhT = ec(nc.sbuf_tensor("xhT", [128, DC * 512], F32))
        banks = [ec(nc.psum_tensor(f"ps{i}", [128, 512], F32))
                 for i in range(8)]
        pool = PsumPool(p, [b[:] for b in banks])
        jit = make_jit_xhat(p, pool, mean_sb[:], rstd_sb[:], eye_sb[:],
                            xc_sb[:], dg_sb[:])

        p.emit("sync", lambda e: e.dma_start(
            out=x_sb[:].rearrange("p (i d) -> p i d", i=NT),
            in_=x1.rearrange("(i p) d -> p i d", p=128)), inc="ld", n=16)
        p.emit("sync", lambda e: e.dma_start(out=eye_sb[:], in_=eye_d),
               inc="ld", n=16)
        p.emit("sync", lambda e: e.dma_start(out=ones_row[:], in_=onesrow_d),
               inc="ld", n=16)
        p.emit("sync", lambda e: e.dma_start(
            out=diag_sb[:].rearrange("p (i h) -> p i h", i=NT),
            in_=diag_i.rearrange("(i p) h -> p i h", p=128)), inc="ld", n=16)
        p.emit("sync", lambda e: e.dma_start(out=mrow[0:1, 0:NCORE],
                                             in_=maxes), inc="ld", n=16)
        for v in range(2):
            for mc in range(2):
                p.emit("sync", lambda e, v=v, mc=mc: e.dma_start(
                    out=projc[v * 64:v * 64 + 64,
                              v * PM + mc * 128:v * PM + (mc + 1) * 128],
                    in_=projT[:, mc * 128:(mc + 1) * 128]), inc="ld", n=16)
        p.emit("sync", lambda e: e.dma_start(
            out=bqc[:], in_=bq.rearrange("(c p) -> p c", p=128),
            allow_slow_non_contiguous=True), inc="ld", n=16)
        p.emit("sync", lambda e: e.dma_start(
            out=bv_row[:], in_=bv.rearrange("(a d) -> a d", a=1)),
            inc="ld", n=16)
        p.emit("sync", lambda e: e.dma_start(
            out=bo_row[:], in_=bo.rearrange("(a d) -> a d", a=1)),
            inc="ld", n=16)
        ldm = p.mark("ld")
        for e in ("tensor", "vector", "scalar", "gpsimd"):
            p.wait(e, "ld", ldm)

        DNORM = float(DH) ** -0.25
        p.emit("vector", lambda e: e.memset(projc[64:128, 0:PM], 0.0))
        p.emit("vector", lambda e: e.memset(projc[0:64, PM:2 * PM], 0.0))
        p.emit("scalar", lambda e: e.activation(
            out=projc[:], in_=projc[:], func=AF.Copy, scale=DNORM))
        p.wait("scalar", "v", p.mark("v"))
        p.emit("vector", lambda e: e.tensor_reduce(
            out=mrow[0:1, NCORE:NCORE + 1], in_=mrow[0:1, 0:NCORE],
            axis=AX.X, op=OP.max))
        p.wait("tensor", "v", p.mark("v"))
        bi0, bank0 = pool.acquire()
        p.emit("tensor", lambda e, bank=bank0: e.matmul(
            bank[0:128, 0:1], ones_row[:], mrow[0:1, NCORE:NCORE + 1],
            start=True, stop=True), inc="t")
        p.wait("scalar", "t", p.mark("t"))
        p.emit("scalar", lambda e, bank=bank0: e.activation(
            out=stab_sb[:], in_=bank[0:128, 0:1], func=AF.Copy))
        pool.release(bi0, "s")
        p.wait("vector", "s", p.mark("s"))
        p.emit("vector", lambda e: e.tensor_scalar(
            out=bias_sb[:], in0=diag_sb[:], scalar1=stab_sb[:, 0:1],
            scalar2=None, op0=OP.subtract))
        p.emit("vector", lambda e: e.memset(ctx_sb[:], 0.0))
        ln_stats(p, x_sb[:], mean_sb[:], rstd_sb[:], sq_sb[:], col0[:],
                 range(NT))
        p.barrier()

        with nc.sbuf_tensor("wvs", [128, DC * D], F32) as wvs, \
                nc.sbuf_tensor("vblk", [128, 4 * H * 65 + 4], F32) as vblk, \
                nc.sbuf_tensor("kTb", [128, DC * 512], F32) as kTb, \
                nc.sbuf_tensor("kf_sb", [128, 2 * PM], F32) as kf_sb:
            VW = H * 65 + 1
            p.emit("sync", lambda e: e.dma_start(
                out=wvs[:].rearrange("p (c d) -> p c d", c=DC),
                in_=wvT.rearrange("c p d -> p c d")), inc="ld", n=16)
            wv_ld = p.mark("ld")
            for tb in range(4):
                tiles = list(range(tb * 4, tb * 4 + 4))
                if tb > 0:
                    p.wait("sync", "t", p.mark("t"))
                p.emit("sync", lambda e, tb=tb: e.dma_start(
                    out=kTb[:].rearrange("p (c q) -> p c q", c=DC),
                    in_=kT_i.rearrange("(c p) t -> p c t", p=128)
                        [:, :, tb * 512:(tb + 1) * 512]), inc="ld", n=16)
                ktb_ld = p.mark("ld")
                jit(tiles, x_sb[:], xhT[:])
                xh_ready = p.mark("s")
                p.wait("vector", "t", p.mark("t"))
                for sub in range(4):
                    p.emit("vector", lambda e, sub=sub: e.memset(
                        vblk[:, sub * VW + 64:sub * VW + H * 65:65], 1.0))
                ones_mark = p.mark("v")
                p.wait("tensor", "ld", wv_ld)
                p.wait("tensor", "s", xh_ready)
                for sub in range(4):
                    for ob in range(2):
                        bi, bank = pool.acquire()
                        for f in range(DC):
                            p.emit("tensor", lambda e, f=f, sub=sub, ob=ob,
                                   bank=bank: e.matmul(
                                bank[:],
                                xhT[:, f * 512 + sub * 128:
                                    f * 512 + (sub + 1) * 128],
                                wvs[:, f * D + ob * 512:
                                    f * D + (ob + 1) * 512],
                                start=(f == 0), stop=False))
                        p.emit("tensor", lambda e, ob=ob, bank=bank: e.matmul(
                            bank[:], ones_row[:],
                            bv_row[0:1, ob * 512:(ob + 1) * 512],
                            start=False, stop=True), inc="t")
                        p.wait("scalar", "t", p.mark("t"))
                        if sub == 0 and ob == 0:
                            p.wait("scalar", "v", ones_mark)
                        p.emit("scalar", lambda e, sub=sub, ob=ob, bank=bank:
                               e.activation(
                            out=vblk[:, sub * VW + ob * 8 * 65:
                                     sub * VW + ob * 8 * 65 + 8 * 65]
                            .rearrange("p (h e) -> p h e", h=8)
                            [:, :, 0:64],
                            in_=bank[:].rearrange("p (h e) -> p h e", h=8),
                            func=AF.Copy), inc="s")
                        pool.release(bi, "s")
                v_ready = p.mark("s")
                p.wait("tensor", "ld", ktb_ld)

                def emit_dashk(i):
                    h, sub = divmod(i, 4)
                    hp, hv = h // 2, h % 2
                    bi, bank = pool.acquire()
                    p.emit("tensor", lambda e, hp=hp, hv=hv, sub=sub,
                           bank=bank: e.matmul(
                        bank[:, 0:PM],
                        kTb[:, hp * 512 + sub * 128:
                            hp * 512 + (sub + 1) * 128],
                        projc[:, hv * PM:(hv + 1) * PM],
                        start=True, stop=True), inc="t")
                    return bi, bank, p.mark("t")

                pend = emit_dashk(0)
                ctx_marks = {}
                for i in range(4 * H):
                    h, sub = divmod(i, 4)
                    sl = (i % 2) * PM
                    bi, bank, dmark = pend
                    if i + 1 < 4 * H:
                        pend = emit_dashk(i + 1)
                    p.wait("scalar", "t", dmark)
                    if i - 2 in ctx_marks:
                        p.wait("scalar", "t", ctx_marks.pop(i - 2))
                    tile = tb * 4 + sub
                    p.emit("scalar", lambda e, tile=tile, h=h, sl=sl,
                           bank=bank: e.activation(
                        out=kf_sb[:, sl:sl + PM], in_=bank[:, 0:PM],
                        func=AF.Exp,
                        bias=bias_sb[:, tile * H + h:tile * H + h + 1]),
                        inc="s")
                    pool.release(bi, "s")
                    p.wait("vector", "s", p.mark("s"))
                    p.emit("vector", lambda e, sl=sl: e.tensor_scalar(
                        out=kf_sb[:, sl:sl + PM], in0=kf_sb[:, sl:sl + PM],
                        scalar1=KEPS, scalar2=None, op0=OP.add))
                    kf_ready = p.mark("v")
                    p.wait("tensor", "v", kf_ready)
                    p.wait("tensor", "s", v_ready)
                    for mc in range(2):
                        bi2, bank2 = pool.acquire()
                        p.emit("tensor", lambda e, mc=mc, h=h, sub=sub,
                               sl=sl, bank2=bank2: e.matmul(
                            bank2[:, 0:65],
                            kf_sb[:, sl + mc * 128:sl + (mc + 1) * 128],
                            vblk[:, sub * VW + h * 65:
                                 sub * VW + (h + 1) * 65],
                            start=True, stop=True), inc="t")
                        p.wait("vector", "t", p.mark("t"))
                        p.emit("vector", lambda e, h=h, mc=mc,
                               bank2=bank2: e.tensor_tensor(
                            out=ctx_sb[:, (h * 2 + mc) * 65:
                                       (h * 2 + mc + 1) * 65],
                            in0=ctx_sb[:, (h * 2 + mc) * 65:
                                       (h * 2 + mc + 1) * 65],
                            in1=bank2[:, 0:65], op=OP.add))
                        pool.release(bi2, "v")
                    ctx_marks[i] = p.mark("t")
            if DBG:
                p.wait("gpsimd", "v", p.mark("v"))
                p.wait("gpsimd", "s", p.mark("s"))
                p.emit("gpsimd", lambda e: e.dma_start(out=dbg_v,
                                                       in_=vblk[:]),
                       inc="st", n=16, raw=True)
                p.emit("gpsimd", lambda e: e.dma_start(out=dbg_kf,
                                                       in_=kf_sb[:]),
                       inc="st", n=16, raw=True)
                p.emit("gpsimd", lambda e: e.dma_start(out=dbg_ctx,
                                                       in_=ctx_sb[:]),
                       inc="st", n=16, raw=True)
                p.wait("gpsimd", "st", p.cnt["st"])
        p.barrier()

        with nc.sbuf_tensor("wqs", [128, DC * D], F32) as wqs, \
                nc.sbuf_tensor("qTb", [128, DC * 512], F32) as qTb, \
                nc.sbuf_tensor("qf_sb", [128, 2 * 512], F32) as qf_sb, \
                nc.sbuf_tensor("on_sb", [128, 128], F32) as on_sb, \
                nc.sbuf_tensor("den_sb", [128, 512], F32) as den_sb, \
                nc.sbuf_tensor("thi", [128, DC * 512], F32) as thi, \
                nc.sbuf_tensor("outT", [128, DC * 512], F32) as outT, \
                nc.sbuf_tensor("wo0", [128, D], F32) as wo0, \
                nc.sbuf_tensor("wo1", [128, D], F32) as wo1:
            p.emit("sync", lambda e: e.dma_start(
                out=wqs[:].rearrange("p (c d) -> p c d", c=DC),
                in_=wqT.rearrange("c p d -> p c d")), inc="ld", n=16)
            wq_ld = p.mark("ld")
            wo = DBuf(p, [wo0[:], wo1[:]], "sync", "ld")
            p.emit("vector", lambda e: e.memset(on_sb[:], 1.0))
            for tb in range(4):
                tiles = list(range(tb * 4, tb * 4 + 4))
                jit(tiles, x_sb[:], xhT[:])
                xh_ready = p.mark("s")
                p.wait("tensor", "ld", wq_ld)
                p.wait("tensor", "s", xh_ready)
                for hp in range(DC):
                    bi, bank = pool.acquire()
                    for f in range(DC):
                        p.emit("tensor", lambda e, f=f, hp=hp, bank=bank:
                               e.matmul(
                            bank[:], wqs[:, hp * D + f * 128:
                                         hp * D + (f + 1) * 128],
                            xhT[:, f * 512:(f + 1) * 512],
                            start=(f == 0), stop=(f == DC - 1)), inc="t")
                    p.wait("scalar", "t", p.mark("t"))
                    p.emit("scalar", lambda e, hp=hp, bank=bank: e.activation(
                        out=qTb[:, hp * 512:(hp + 1) * 512], in_=bank[:],
                        func=AF.Identity, bias=bqc[:, hp:hp + 1]), inc="s")
                    pool.release(bi, "s")
                q_ready = p.mark("s")

                def emit_dash(i):
                    hp, hv = divmod(i, 2)
                    sbanks = []
                    p.wait("tensor", "s", q_ready)
                    for mc in range(2):
                        bi, bank = pool.acquire()
                        sbanks.append((bi, bank))
                        p.emit("tensor", lambda e, hp=hp, hv=hv, mc=mc,
                               bank=bank: e.matmul(
                            bank[:],
                            projc[:, hv * PM + mc * 128:
                                  hv * PM + (mc + 1) * 128],
                            qTb[:, hp * 512:(hp + 1) * 512],
                            start=True, stop=True),
                            inc="t" if mc == 1 else None)
                    return sbanks, p.mark("t")

                pend = emit_dash(0)
                num_mark = pend[1]
                mult_mark = 0
                for i in range(2 * DC):
                    hp, hv = divmod(i, 2)
                    h = hp * 2 + hv
                    qf = qf_sb[:]
                    sbanks, dmark = pend
                    if i + 1 < 2 * DC:
                        pend = emit_dash(i + 1)
                    p.wait("scalar", "t", num_mark)
                    for mc in range(2):
                        bi, bank = sbanks[mc]
                        p.emit("scalar", lambda e, mc=mc, qf=qf, bank=bank:
                               e.activation(
                            out=qf[:, mc * 512:(mc + 1) * 512],
                            in_=bank[:], func=AF.Exp), inc="s")
                        pool.release(bi, "s")
                    p.wait("vector", "s", p.mark("s"))
                    p.emit("vector", lambda e, qf=qf: e.tensor_scalar(
                        out=qf, in0=qf, scalar1=KEPS,
                        scalar2=None, op0=OP.add))
                    qf_ready = p.mark("v")
                    p.wait("tensor", "v", qf_ready)
                    bi, bank = pool.acquire()
                    for mc in range(2):
                        p.emit("tensor", lambda e, mc=mc, h=h, qf=qf,
                               bank=bank: e.matmul(
                            bank[0:65, :],
                            ctx_sb[:, (h * 2 + mc) * 65:
                                   (h * 2 + mc + 1) * 65],
                            qf[:, mc * 512:(mc + 1) * 512],
                            start=(mc == 0), stop=(mc == 1)),
                            inc="t" if mc == 1 else None)
                    num_mark = p.mark("t")
                    p.wait("vector", "t", num_mark)
                    p.emit("vector", lambda e, bank=bank: e.reciprocal(
                        out=den_sb[64:65, :], in_=bank[64:65, :]))
                    p.wait("tensor", "v", p.mark("v"))
                    bi2, bank2 = pool.acquire()
                    p.emit("tensor", lambda e, bank2=bank2: e.matmul(
                        bank2[0:64, :], on_sb[64:65, 0:64],
                        den_sb[64:65, :], start=True, stop=True), inc="t")
                    p.wait("scalar", "t", p.mark("t"))
                    p.wait("scalar", "v", mult_mark)
                    p.emit("scalar", lambda e, bank2=bank2: e.activation(
                        out=den_sb[0:64, :], in_=bank2[0:64, :],
                        func=AF.Copy), inc="s")
                    pool.release(bi2, "s")
                    p.wait("vector", "s", p.mark("s"))
                    acc = outT if hv == 0 else thi
                    p.emit("vector", lambda e, hp=hp, acc=acc, bank=bank:
                           e.tensor_tensor(
                        out=acc[0:64, hp * 512:(hp + 1) * 512],
                        in0=bank[0:64, :], in1=den_sb[0:64, :],
                        op=OP.mult))
                    mult_mark = p.mark("v")
                    pool.release(bi, "v")
                p.wait("sync", "v", p.mark("v"))
                p.emit("sync", lambda e: e.dma_start(
                    out=outT[64:128, :], in_=thi[0:64, :]), inc="ld", n=16)
                p.wait("tensor", "ld", p.mark("ld"))
                out_ready = p.mark("v")
                if DBG and tb == 3:
                    p.wait("gpsimd", "s", p.mark("s"))
                    p.emit("gpsimd", lambda e: e.dma_start(out=dbg_q,
                                                           in_=qTb[:]),
                           inc="st", n=16, raw=True)
                p.wait("tensor", "v", out_ready)
                for k in range(8):
                    pool.acquire()
                zb = pool.banks
                for ic in range(DC):
                    buf, ldm2 = wo.load(lambda e, buf, ic=ic: e.dma_start(
                        out=buf[:], in_=woT[ic]))
                    p.wait("tensor", "ld", ldm2)
                    for sub in range(4):
                        for ob in range(2):
                            p.emit("tensor", lambda e, ic=ic, sub=sub, ob=ob,
                                   buf=buf: e.matmul(
                                zb[sub * 2 + ob][:],
                                outT[:, ic * 512 + sub * 128:
                                     ic * 512 + (sub + 1) * 128],
                                buf[:, ob * 512:(ob + 1) * 512],
                                start=(ic == 0), stop=False),
                                inc="t" if (sub == 3 and ob == 1) else None)
                    wo.consumed()
                for sub in range(4):
                    for ob in range(2):
                        p.emit("tensor", lambda e, sub=sub, ob=ob: e.matmul(
                            zb[sub * 2 + ob][:], ones_row[:],
                            bo_row[0:1, ob * 512:(ob + 1) * 512],
                            start=False, stop=True),
                            inc="t" if (sub == 3 and ob == 1) else None)
                wo_done = p.mark("t")
                p.wait("vector", "t", wo_done)
                for sub in range(4):
                    tile = tb * 4 + sub
                    for ob in range(2):
                        p.emit("vector", lambda e, tile=tile, sub=sub, ob=ob:
                               e.scalar_tensor_tensor(
                            out=x_sb[:, tile * D + ob * 512:
                                     tile * D + (ob + 1) * 512],
                            in0=zb[sub * 2 + ob][:], scalar=1.0,
                            in1=x_sb[:, tile * D + ob * 512:
                                     tile * D + (ob + 1) * 512],
                            op0=OP.mult, op1=OP.add))
                for k in range(8):
                    pool.release(k, "v")
        p.barrier()
        p.emit("gpsimd", lambda e: e.dma_start(
            out=x2_o.rearrange("(i p) d -> p i d", p=128),
            in_=x_sb[:].rearrange("p (i d) -> p i d", i=NT)),
            inc="st", n=16, raw=True)
        p.wait("gpsimd", "st", p.cnt["st"])
        p.build()
    return nc


def stage_l2(inputs, r1):
    mx = np.ascontiguousarray(
        np.array([[float(r1[b]["lmax"][0, 0]) for b in range(NCORE)]],
                 np.float32))
    com = {
        "wqT": pack_lhsT(np.asarray(inputs["L0_attn_Wq"], np.float32)),
        "wvT": pack_rhs(np.asarray(inputs["L0_attn_Wv"], np.float32)),
        "woT": pack_rhs(np.asarray(inputs["L0_attn_Wo"], np.float32)),
        "projT": np.ascontiguousarray(
            np.asarray(inputs["L0_attn_proj"], np.float32).T),
        "bq": np.asarray(inputs["L0_attn_bq"], np.float32),
        "bv": np.asarray(inputs["L0_attn_bv"], np.float32),
        "bo": np.asarray(inputs["L0_attn_bo"], np.float32),
        "maxes": mx,
    }
    return [dict(com, x1=np.ascontiguousarray(r1[b]["x1"]),
                 kT=np.ascontiguousarray(r1[b]["kT"]),
                 diag=np.ascontiguousarray(r1[b]["diag"]))
            for b in range(NCORE)]





def emit_ffn1_chain(nc, p, pool, jit, ugT, uoT, woR, x_src_d,
                    x_out_o, zx_d, eye_sb, boc, incc, dbg=None):
    NSEQ = 256
    with nc.sbuf_tensor("c_fm", [128, 2 * DC * NSEQ], F32) as c_fm, \
            nc.sbuf_tensor("cuo_tm", [128, 2 * D], F32) as cuo_tm, \
            nc.sbuf_tensor("gt_sb", [128, 3 * NSEQ], F32) as gt_sb, \
            nc.sbuf_tensor("ones2", [128, NSEQ], F32) as ones2:
        _ug_cm = nc.sbuf_tensor("ug_sb", [128, 3 * DC * D], F32)
        ug_sb = _ug_cm.__enter__()
        ZW = 3 * DC * NSEQ
        _zxs_cm = nc.sbuf_tensor("zxs", [128, 2 * ZW], F32)
        zxs = _zxs_cm.__enter__()
        CP = DC * NSEQ
        p.emit("sync", lambda e: e.dma_start(
            out=ug_sb[:].rearrange("p (g c d) -> p g c d", g=3, c=DC),
            in_=ugT.rearrange("g c p d -> p g c d")), inc="ld", n=16)
        ug_ld = p.mark("ld")
        p.emit("vector", lambda e: e.memset(ones2[:], 1.0))
        for cfc in range(DC):
            p.emit("vector", lambda e, cfc=cfc: e.tensor_scalar(
                out=c_fm[:, cfc * NSEQ:(cfc + 1) * NSEQ], in0=ones2[:],
                scalar1=incc[:, cfc:cfc + 1], scalar2=None, op0=OP.mult))
        p.wait("tensor", "ld", ug_ld)

        def zx_load(t):
            p.wait("sync", "v", p.mark("v"))
            p.emit("sync", lambda e, t=t: e.dma_start(
                out=zxs[:, (t % 2) * ZW:(t % 2 + 1) * ZW]
                .rearrange("p (g c q) -> p g c q", g=3, c=DC),
                in_=zx_d.rearrange("g c p t -> p g c t")
                    [:, :, :, t * NSEQ:(t + 1) * NSEQ]), inc="ld", n=16)
            return p.mark("ld")

        zx_marks = {0: zx_load(0)}
        for t in range(8):
            par, npar = (t % 2) * CP, ((t + 1) % 2) * CP
            zo = (t % 2) * ZW
            if t + 1 < 8:
                zx_marks[t + 1] = zx_load(t + 1)
            p.wait("vector", "ld", zx_marks[t])
            p.wait("tensor", "v", p.mark("v"))
            for mc in range(DC):
                gbanks = []
                for g in range(3):
                    bi, bank = pool.acquire()
                    gbanks.append((bi, bank))
                    for f in range(DC):
                        p.emit("tensor", lambda e, g=g, f=f, mc=mc,
                               par=par, bank=bank: e.matmul(
                            bank[:, 0:NSEQ],
                            ug_sb[:, g * DC * D + mc * D + f * 128:
                                  g * DC * D + mc * D + (f + 1) * 128],
                            c_fm[:, par + f * NSEQ:par + (f + 1) * NSEQ],
                            start=(f == 0), stop=(f == DC - 1)),
                            inc="t")
                p.wait("vector", "t", p.mark("t"))
                for g in range(3):
                    bi, bank = gbanks[g]
                    p.emit("vector", lambda e, g=g, mc=mc, zo=zo, bank=bank:
                           e.tensor_tensor(
                        out=gt_sb[:, g * NSEQ:(g + 1) * NSEQ],
                        in0=bank[:, 0:NSEQ],
                        in1=zxs[:, zo + (g * DC + mc) * NSEQ:
                                zo + (g * DC + mc + 1) * NSEQ], op=OP.add))
                    pool.release(bi, "v")
                p.wait("scalar", "v", p.mark("v"))
                for g in range(3):
                    fn = AF.Sigmoid if g < 2 else AF.Tanh
                    p.emit("scalar", lambda e, g=g, fn=fn: e.activation(
                        out=gt_sb[:, g * NSEQ:(g + 1) * NSEQ],
                        in_=gt_sb[:, g * NSEQ:(g + 1) * NSEQ], func=fn),
                        inc="s")
                p.wait("vector", "s", p.mark("s"))
                p.emit("vector", lambda e, mc=mc, par=par, npar=npar:
                       e.tensor_mul(
                    out=c_fm[:, npar + mc * NSEQ:npar + (mc + 1) * NSEQ],
                    in0=c_fm[:, par + mc * NSEQ:par + (mc + 1) * NSEQ],
                    in1=gt_sb[:, 0:NSEQ]))
                p.emit("vector", lambda e: e.tensor_mul(
                    out=gt_sb[:, NSEQ:2 * NSEQ],
                    in0=gt_sb[:, NSEQ:2 * NSEQ],
                    in1=gt_sb[:, 2 * NSEQ:3 * NSEQ]))
                p.emit("vector", lambda e, mc=mc, npar=npar: e.tensor_add(
                    out=c_fm[:, npar + mc * NSEQ:npar + (mc + 1) * NSEQ],
                    in0=c_fm[:, npar + mc * NSEQ:npar + (mc + 1) * NSEQ],
                    in1=gt_sb[:, NSEQ:2 * NSEQ]))
        cpar = (8 % 2) * CP
        p.barrier()
        if dbg is not None:
            p.emit("gpsimd", lambda e: e.dma_start(out=dbg[0], in_=c_fm[:]),
                   inc="st", n=16, raw=True)
            p.emit("gpsimd", lambda e: e.dma_start(out=dbg[1], in_=zxs[:]),
                   inc="st", n=16, raw=True)
            p.wait("gpsimd", "st", p.cnt["st"])
        _zxs_cm.__exit__(None, None, None)
        _ug_cm.__exit__(None, None, None)
        with nc.sbuf_tensor("uo_sb", [128, DC * D], F32) as uo_sb, \
                nc.sbuf_tensor("cuoT", [128, DC * NSEQ], F32) as cuoT:
            p.emit("sync", lambda e: e.dma_start(
                out=uo_sb[:].rearrange("p (c d) -> p c d", c=DC),
                in_=uoT.rearrange("c p d -> p c d")), inc="ld", n=16)
            p.wait("tensor", "ld", p.mark("ld"))
            for oc in range(DC):
                bi, bank = pool.acquire()
                for f in range(DC):
                    p.emit("tensor", lambda e, oc=oc, f=f, bank=bank:
                           e.matmul(
                        bank[:, 0:NSEQ],
                        uo_sb[:, oc * D + f * 128:oc * D + (f + 1) * 128],
                        c_fm[:, cpar + f * NSEQ:cpar + (f + 1) * NSEQ],
                        start=(f == 0), stop=(f == DC - 1)), inc="t")
                p.wait("scalar", "t", p.mark("t"))
                p.emit("scalar", lambda e, oc=oc, bank=bank: e.activation(
                    out=cuoT[:, oc * NSEQ:(oc + 1) * NSEQ],
                    in_=bank[:, 0:NSEQ], func=AF.Identity,
                    bias=boc[:, oc:oc + 1]), inc="s")
                pool.release(bi, "s")
            p.wait("tensor", "s", p.mark("s"))
            for oc in range(DC):
                for ss in range(2):
                    bi, bank = pool.acquire()
                    p.emit("tensor", lambda e, oc=oc, ss=ss, bank=bank:
                           e.matmul(
                        bank[:, 0:128],
                        cuoT[:, oc * NSEQ + ss * 128:
                             oc * NSEQ + (ss + 1) * 128],
                        eye_sb[:], start=True, stop=True), inc="t")
                    p.wait("scalar", "t", p.mark("t"))
                    p.emit("scalar", lambda e, oc=oc, ss=ss, bank=bank:
                           e.activation(
                        out=cuo_tm[:, ss * D + oc * 128:
                                   ss * D + (oc + 1) * 128],
                        in_=bank[:, 0:128], func=AF.Copy), inc="s")
                    pool.release(bi, "s")
            p.barrier()
            if dbg is not None:
                p.emit("gpsimd", lambda e: e.dma_start(out=dbg[2],
                                                       in_=cuo_tm[:]),
                       inc="st", n=16, raw=True)
            with nc.sbuf_tensor("wo_sb", [128, DC * D], F32) as wo_sb, \
                    nc.sbuf_tensor("xb_sb", [128, 2 * 2 * D], F32) as xb_sb, \
                    nc.sbuf_tensor("xhTb", [128, DC * 256], F32) as xhTb, \
                    nc.sbuf_tensor("hz_sb", [128, D], F32) as hz_sb:
                p.emit("sync", lambda e: e.dma_start(
                    out=wo_sb[:].rearrange("p (c d) -> p c d", c=DC),
                    in_=woR.rearrange("c p d -> p c d")), inc="ld", n=16)
                wo_ld = p.mark("ld")
                p.wait("tensor", "ld", wo_ld)

                def xb_load(t):
                    p.wait("sync", "v", p.mark("v"))
                    p.wait("sync", "s", p.mark("s"))
                    p.wait("sync", "st", p.mark("st"))
                    p.emit("sync", lambda e, t=t: e.dma_start(
                        out=xb_sb[:, (t % 2) * 2 * D:(t % 2 + 1) * 2 * D]
                        .rearrange("p (i d) -> p i d", i=2),
                        in_=x_src_d.rearrange("(i p) d -> p i d", p=128)
                            [:, 2 * t:2 * t + 2, :]), inc="ld", n=16)
                    return p.mark("ld")

                xb_marks = {0: xb_load(0)}
                for t in range(8):
                    xb = xb_sb[:, (t % 2) * 2 * D:(t % 2 + 1) * 2 * D]
                    if t + 1 < 8:
                        xb_marks[t + 1] = xb_load(t + 1)
                    for e in ("vector", "tensor"):
                        p.wait(e, "ld", xb_marks[t])
                    jit([0, 1], xb, xhTb[:],
                        stats=[2 * t, 2 * t + 1])
                    xh_ready = p.mark("s")
                    p.wait("tensor", "s", xh_ready)
                    for ss in range(2):
                        for ob in range(2):
                            bi, bank = pool.acquire()
                            for f in range(DC):
                                p.emit("tensor", lambda e, f=f, ss=ss,
                                       ob=ob, bank=bank: e.matmul(
                                    bank[:],
                                    xhTb[:, f * 256 + ss * 128:
                                         f * 256 + (ss + 1) * 128],
                                    wo_sb[:, f * D + ob * 512:
                                          f * D + (ob + 1) * 512],
                                    start=(f == 0), stop=(f == DC - 1)),
                                    inc="t")
                            p.wait("vector", "t", p.mark("t"))
                            p.emit("vector", lambda e, ss=ss, ob=ob,
                                   bank=bank: e.tensor_add(
                                out=hz_sb[:, ob * 512:(ob + 1) * 512],
                                in0=bank[:],
                                in1=cuo_tm[:, ss * D + ob * 512:
                                           ss * D + (ob + 1) * 512]))
                            pool.release(bi, "v")
                        p.wait("scalar", "v", p.mark("v"))
                        p.emit("scalar", lambda e: e.activation(
                            out=hz_sb[:], in_=hz_sb[:], func=AF.Tanh),
                            inc="s")
                        p.wait("vector", "s", p.mark("s"))
                        p.emit("vector", lambda e, ss=ss, xb=xb:
                               e.scalar_tensor_tensor(
                            out=xb[:, ss * D:(ss + 1) * D],
                            in0=hz_sb[:], scalar=0.5,
                            in1=xb[:, ss * D:(ss + 1) * D],
                            op0=OP.mult, op1=OP.add))
                    p.wait("gpsimd", "v", p.mark("v"))
                    p.emit("gpsimd", lambda e, t=t, xb=xb: e.dma_start(
                        out=x_out_o.rearrange("(i p) d -> p i d", p=128)
                            [:, 2 * t:2 * t + 2, :],
                        in_=xb.rearrange("p (i d) -> p i d", i=2)),
                        inc="st", n=16, raw=True)



def build_l3():
    nc = bass.Bass("TRN2", target_bir_lowering=False,
                   detect_race_conditions=False)
    x2 = nc.dram_tensor("x2", [TOK, D], F32, kind="ExternalInput").ap()
    wfc = nc.dram_tensor("wfc", [FC, 128, D], F32, kind="ExternalInput").ap()
    wproj = nc.dram_tensor("wproj", [FC, 128, D], F32,
                           kind="ExternalInput").ap()
    wgT = nc.dram_tensor("wgT", [3, DC, 128, D], F32,
                         kind="ExternalInput").ap()
    ugT = nc.dram_tensor("ugT", [3, DC, 128, D], F32,
                         kind="ExternalInput").ap()
    uoT = nc.dram_tensor("uoT", [DC, 128, D], F32, kind="ExternalInput").ap()
    woR = nc.dram_tensor("woR", [DC, 128, D], F32, kind="ExternalInput").ap()
    bg = nc.dram_tensor("bg", [3, D], F32, kind="ExternalInput").ap()
    bo_ff = nc.dram_tensor("bo_ff", [D], F32, kind="ExternalInput").ap()
    inc_ff = nc.dram_tensor("inc_ff", [D], F32, kind="ExternalInput").ap()
    x4_o = nc.dram_tensor("x4", [TOK, D], F32, kind="ExternalOutput").ap()
    x3_d = nc.dram_tensor("x3_d", [TOK, D], F32, kind="Internal").ap()
    if DBG:
        dbg_x3 = nc.dram_tensor("dbg_x3", [TOK, D], F32,
                                kind="ExternalOutput").ap()
        dbg_c = nc.dram_tensor("dbg_c", [128, 2 * DC * 256], F32,
                               kind="ExternalOutput").ap()
        dbg_cuo = nc.dram_tensor("dbg_cuo", [128, 2 * D], F32,
                                 kind="ExternalOutput").ap()
        dbg_zx = nc.dram_tensor("dbg_zx", [128, 3 * DC * 256], F32,
                                kind="ExternalOutput").ap()
    zx_d = nc.dram_tensor("zx_d", [3, DC, 128, TOK], F32, kind="Internal").ap()
    eye_d = nc.inline_tensor(np.eye(128, dtype=np.float32), "eye").ap()

    NSEQ = 256
    with ExitStack() as stack:
        ec = stack.enter_context
        p = Prog(nc, stack)
        mean_sb = ec(nc.sbuf_tensor("mean_sb", [128, NT], F32))
        rstd_sb = ec(nc.sbuf_tensor("rstd_sb", [128, NT], F32))
        sq_sb = ec(nc.sbuf_tensor("sq_sb", [128, D], F32))
        col0 = ec(nc.sbuf_tensor("col0", [128, 1], F32))
        eye_sb = ec(nc.sbuf_tensor("eye_sb", [128, 128], F32))
        setup_consts(nc, p, ec)
        xc_sb = ec(nc.sbuf_tensor("xc_sb", [128, 2 * D], F32))
        dg_sb = ec(nc.sbuf_tensor("dg_sb", [128, 2 * 128], F32))
        bgc = ec(nc.sbuf_tensor("bgc", [128, 3 * DC], F32))
        boc = ec(nc.sbuf_tensor("boc", [128, DC], F32))
        incc = ec(nc.sbuf_tensor("incc", [128, DC], F32))
        banks = [ec(nc.psum_tensor(f"ps{i}", [128, 512], F32))
                 for i in range(8)]
        pool = PsumPool(p, [b[:] for b in banks])
        jit = make_jit_xhat(p, pool, mean_sb[:], rstd_sb[:], eye_sb[:],
                            xc_sb[:], dg_sb[:], depth=2)

        p.emit("sync", lambda e: e.dma_start(out=eye_sb[:], in_=eye_d),
               inc="ld", n=16)
        p.emit("sync", lambda e: e.dma_start(
            out=bgc[:].rearrange("p (g c) -> p g c", g=3),
            in_=bg.rearrange("g (c p) -> p g c", p=128),
            allow_slow_non_contiguous=True), inc="ld", n=16)
        p.emit("sync", lambda e: e.dma_start(
            out=boc[:], in_=bo_ff.rearrange("(c p) -> p c", p=128),
            allow_slow_non_contiguous=True), inc="ld", n=16)
        p.emit("sync", lambda e: e.dma_start(
            out=incc[:], in_=inc_ff.rearrange("(c p) -> p c", p=128),
            allow_slow_non_contiguous=True), inc="ld", n=16)

        with nc.sbuf_tensor("x_sb", [128, NT * D], F32) as x_sb, \
                nc.sbuf_tensor("xhT", [128, DC * 512], F32) as xhT:
            p.emit("sync", lambda e: e.dma_start(
                out=x_sb[:].rearrange("p (i d) -> p i d", i=NT),
                in_=x2.rearrange("(i p) d -> p i d", p=128)), inc="ld", n=16)
            ldm = p.mark("ld")
            for e in ("tensor", "vector", "scalar", "gpsimd"):
                p.wait(e, "ld", ldm)
            ln_stats(p, x_sb[:], mean_sb[:], rstd_sb[:], sq_sb[:], col0[:],
                     range(NT))
            p.barrier()
            with nc.sbuf_tensor("h_sb", [128, FC * 512], F32) as h_sb, \
                    nc.sbuf_tensor("r_sb", [128, 512], F32) as r_sb, \
                    nc.sbuf_tensor("z_sb", [128, D], F32) as z_sb, \
                    nc.sbuf_tensor("mem_sb", [128, D], F32) as mem_sb, \
                    nc.sbuf_tensor("spk_sb", [128, D], F32) as spk_sb, \
                    nc.sbuf_tensor("acc_sb", [128, D], F32) as acc_sb, \
                    nc.sbuf_tensor("wfc0", [128, D], F32) as wfc0, \
                    nc.sbuf_tensor("wfc1", [128, D], F32) as wfc1, \
                    nc.sbuf_tensor("wp0", [128, D], F32) as wp0, \
                    nc.sbuf_tensor("wp1", [128, D], F32) as wp1:
                sb = dict(xhT=xhT[:], h=h_sb[:], r=r_sb[:], z=z_sb[:],
                          mem=mem_sb[:], spk=spk_sb[:], acc=acc_sb[:],
                          wfc=[wfc0[:], wfc1[:]], wproj=[wp0[:], wp1[:]])
                emit_ffn0(p, pool, jit, x_sb[:], wfc, wproj, sb)
            p.emit("gpsimd", lambda e: e.dma_start(
                out=x3_d.rearrange("(i p) d -> p i d", p=128),
                in_=x_sb[:].rearrange("p (i d) -> p i d", i=NT)),
                inc="st", n=16, raw=True)
            if DBG:
                p.emit("gpsimd", lambda e: e.dma_start(
                    out=dbg_x3.rearrange("(i p) d -> p i d", p=128),
                    in_=x_sb[:].rearrange("p (i d) -> p i d", i=NT)),
                    inc="st", n=16, raw=True)
            ln_stats(p, x_sb[:], mean_sb[:], rstd_sb[:], sq_sb[:], col0[:],
                     range(NT))
            p.barrier()
            with nc.sbuf_tensor("wg_sb", [128, 2 * DC * D], F32) as wg_sb, \
                    nc.sbuf_tensor("zx_sb", [128, DC * 512], F32) as zx_sb:
                def wg_load(g):
                    p.wait("sync", "t", p.mark("t"))
                    p.emit("sync", lambda e, g=g: e.dma_start(
                        out=wg_sb[:, (g % 2) * DC * D:(g % 2 + 1) * DC * D]
                        .rearrange("p (c d) -> p c d", c=DC),
                        in_=wgT[g].rearrange("c p d -> p c d")),
                        inc="ld", n=16)
                    return p.mark("ld")

                wg_marks = {0: wg_load(0)}
                for g in range(3):
                    go = (g % 2) * DC * D
                    if g + 1 < 3:
                        wg_marks[g + 1] = wg_load(g + 1)
                    for tb in range(4):
                        tiles = list(range(tb * 4, tb * 4 + 4))
                        jit(tiles, x_sb[:], xhT[:])
                        xh_ready = p.mark("s")
                        p.wait("tensor", "ld", wg_marks[g])
                        p.wait("tensor", "s", xh_ready)
                        p.wait("scalar", "st", p.mark("st"))
                        for mc in range(DC):
                            bi, bank = pool.acquire()
                            for f in range(DC):
                                p.emit("tensor", lambda e, f=f, mc=mc,
                                       go=go, bank=bank: e.matmul(
                                    bank[:],
                                    wg_sb[:, go + mc * D + f * 128:
                                          go + mc * D + (f + 1) * 128],
                                    xhT[:, f * 512:(f + 1) * 512],
                                    start=(f == 0), stop=(f == DC - 1)),
                                    inc="t")
                            p.wait("scalar", "t", p.mark("t"))
                            p.emit("scalar", lambda e, g=g, mc=mc, bank=bank:
                                   e.activation(
                                out=zx_sb[:, mc * 512:(mc + 1) * 512],
                                in_=bank[:], func=AF.Identity,
                                bias=bgc[:, g * DC + mc:g * DC + mc + 1]),
                                inc="s")
                            pool.release(bi, "s")
                        p.wait("gpsimd", "s", p.mark("s"))
                        p.emit("gpsimd", lambda e, g=g, tb=tb: e.dma_start(
                            out=zx_d[g].rearrange("c p t -> p c t")
                                [:, :, tb * 512:(tb + 1) * 512],
                            in_=zx_sb[:].rearrange("p (c q) -> p c q", c=DC)),
                            inc="st", n=16, raw=True)
        p.barrier()

        emit_ffn1_chain(nc, p, pool, jit, ugT, uoT, woR, x3_d,
                        x4_o, zx_d, eye_sb[:], boc[:], incc[:],
                        dbg=(dbg_c, dbg_zx, dbg_cuo) if DBG else None)
        p.barrier()
        p.wait("gpsimd", "st", p.cnt["st"])
        p.build()
    return nc


def stage_l3(inputs, res2):
    com = {
        "wfc": pack_lhsT(np.asarray(inputs["L0_ffn_W_fc"], np.float32)),
        "wproj": pack_rhs(np.asarray(inputs["L0_ffn_W_proj"], np.float32)),
        "wgT": np.stack([pack_lhsT(np.asarray(inputs[k], np.float32))
                         for k in ("L1_ffn_wf", "L1_ffn_wi", "L1_ffn_wc")]),
        "ugT": np.stack([pack_lhsT(np.asarray(inputs[k], np.float32))
                         for k in ("L1_ffn_uf", "L1_ffn_ui", "L1_ffn_uc")]),
        "uoT": pack_lhsT(np.asarray(inputs["L1_ffn_uo"], np.float32)),
        "woR": pack_rhs(np.asarray(inputs["L1_ffn_wo"], np.float32)),
        "bg": np.stack([np.asarray(inputs[k], np.float32)
                        for k in ("L1_ffn_bf", "L1_ffn_bi", "L1_ffn_bc")]),
        "bo_ff": np.asarray(inputs["L1_ffn_bo"], np.float32),
        "inc_ff": np.asarray(inputs["L1_ffn_in_c"], np.float32),
    }
    x2_all = np.stack([np.asarray(res2[b]["x2"]) for b in range(NCORE)])
    maps = []
    for j in range(NCORE):
        xj = np.ascontiguousarray(
            x2_all[:, j * 256:(j + 1) * 256, :].reshape(TOK, D))
        maps.append(dict(com, x2=xj))
    return maps



HALF_PI = 1.5707963267948966
NEG_PI = -3.141592653589793
TWO_PI = 6.283185307179586


def build_l4():
    nc = bass.Bass("TRN2", target_bir_lowering=False,
                   detect_race_conditions=False)
    x4f = nc.dram_tensor("x4f", [B * S, D], F32, kind="ExternalInput").ap()
    x4own = nc.dram_tensor("x4own", [TOK, D], F32, kind="ExternalInput").ap()
    basR = nc.dram_tensor("basR", [4, DC, 128, HARM], F32,
                          kind="ExternalInput").ap()
    phT = nc.dram_tensor("phT", [HARM, 4 * D], F32, kind="ExternalInput").ap()
    amT = nc.dram_tensor("amT", [HARM, 4 * D], F32, kind="ExternalInput").ap()
    wgT = nc.dram_tensor("wgT", [3, DC, 128, D], F32,
                         kind="ExternalInput").ap()
    ugT = nc.dram_tensor("ugT", [3, DC, 128, D], F32,
                         kind="ExternalInput").ap()
    uoT = nc.dram_tensor("uoT", [DC, 128, D], F32, kind="ExternalInput").ap()
    woR = nc.dram_tensor("woR", [DC, 128, D], F32, kind="ExternalInput").ap()
    bg = nc.dram_tensor("bg", [3, D], F32, kind="ExternalInput").ap()
    bo_ff = nc.dram_tensor("bo_ff", [D], F32, kind="ExternalInput").ap()
    inc_ff = nc.dram_tensor("inc_ff", [D], F32, kind="ExternalInput").ap()
    x6_o = nc.dram_tensor("x6", [TOK, D], F32, kind="ExternalOutput").ap()
    x5_d = nc.dram_tensor("x5_d", [TOK, D], F32, kind="Internal").ap()
    zx_d = nc.dram_tensor("zx_d", [3, DC, 128, TOK], F32, kind="Internal").ap()
    eye_d = nc.inline_tensor(np.eye(128, dtype=np.float32), "eye").ap()

    NSEQ = 256
    with ExitStack() as stack:
        ec = stack.enter_context
        p = Prog(nc, stack)
        mean_sb = ec(nc.sbuf_tensor("mean_sb", [128, NT], F32))
        rstd_sb = ec(nc.sbuf_tensor("rstd_sb", [128, NT], F32))
        mean8 = ec(nc.sbuf_tensor("mean8", [128, 4], F32))
        rstd8 = ec(nc.sbuf_tensor("rstd8", [128, 4], F32))
        sq_sb = ec(nc.sbuf_tensor("sq_sb", [128, D], F32))
        col0 = ec(nc.sbuf_tensor("col0", [128, 1], F32))
        eye_sb = ec(nc.sbuf_tensor("eye_sb", [128, 128], F32))
        setup_consts(nc, p, ec)
        xc_sb = ec(nc.sbuf_tensor("xc_sb", [128, 2 * D], F32))
        dg_sb = ec(nc.sbuf_tensor("dg_sb", [128, 2 * 128], F32))
        bgc = ec(nc.sbuf_tensor("bgc", [128, 3 * DC], F32))
        boc = ec(nc.sbuf_tensor("boc", [128, DC], F32))
        incc = ec(nc.sbuf_tensor("incc", [128, DC], F32))
        w4 = ec(nc.sbuf_tensor("w4", [HARM, 4 * D], F32))
        bas_sb = ec(nc.sbuf_tensor("bas_sb", [128, 4 * DC * HARM], F32))
        banks = [ec(nc.psum_tensor(f"ps{i}", [128, 512], F32))
                 for i in range(8)]
        pool = PsumPool(p, [b[:] for b in banks[:4] + banks[6:]])
        avb = [b[:] for b in banks[4:6]]
        jit = make_jit_xhat(p, pool, mean_sb[:], rstd_sb[:], eye_sb[:],
                            xc_sb[:], dg_sb[:], depth=2)

        p.emit("sync", lambda e: e.dma_start(out=eye_sb[:], in_=eye_d),
               inc="ld", n=16)
        p.emit("sync", lambda e: e.dma_start(
            out=bgc[:].rearrange("p (g c) -> p g c", g=3),
            in_=bg.rearrange("g (c p) -> p g c", p=128),
            allow_slow_non_contiguous=True), inc="ld", n=16)
        p.emit("sync", lambda e: e.dma_start(
            out=boc[:], in_=bo_ff.rearrange("(c p) -> p c", p=128),
            allow_slow_non_contiguous=True), inc="ld", n=16)
        p.emit("sync", lambda e: e.dma_start(
            out=incc[:], in_=inc_ff.rearrange("(c p) -> p c", p=128),
            allow_slow_non_contiguous=True), inc="ld", n=16)
        p.emit("sync", lambda e: e.dma_start(out=w4[:], in_=phT),
               inc="ld", n=16)
        p.emit("sync", lambda e: e.dma_start(
            out=bas_sb[:].rearrange("p (x c q) -> p x c q", x=4, c=DC),
            in_=basR.rearrange("x c p q -> p x c q")), inc="ld", n=16)
        ldm = p.mark("ld")
        for e in ("tensor", "vector", "scalar", "gpsimd"):
            p.wait(e, "ld", ldm)
        with nc.sbuf_tensor("am_sb", [HARM, 4 * D], F32) as am_sb:
            p.emit("sync", lambda e: e.dma_start(out=am_sb[:], in_=amT),
                   inc="ld", n=16)
            p.wait("vector", "ld", p.mark("ld"))
            p.emit("vector", lambda e: e.tensor_scalar(
                out=w4[:], in0=w4[:], scalar1=HALF_PI, scalar2=None,
                op0=OP.add))
            with nc.sbuf_tensor("mk_sb", [HARM, 4 * D], F32) as mk_sb:
                PI = -NEG_PI
                for sign in (1.0, -1.0):
                    for _ in range(8):
                        p.emit("vector", lambda e, sign=sign:
                               e.tensor_scalar(
                            out=mk_sb[:], in0=w4[:],
                            scalar1=sign * PI, scalar2=None,
                            op0=(OP.is_gt if sign > 0 else OP.is_lt)))
                        p.emit("vector", lambda e, sign=sign:
                               e.scalar_tensor_tensor(
                            out=w4[:], in0=mk_sb[:],
                            scalar=-sign * TWO_PI, in1=w4[:],
                            op0=OP.mult, op1=OP.add))
                p.wait("scalar", "v", p.mark("v"))
                p.emit("scalar", lambda e: e.activation(
                    out=w4[:], in_=w4[:], func=AF.Sin), inc="s")
            p.wait("vector", "s", p.mark("s"))
            p.emit("vector", lambda e: e.tensor_mul(
                out=w4[:], in0=w4[:], in1=am_sb[:]))
            p.barrier()

        VW = H * 65 + 1
        KW = DC * 512
        with nc.sbuf_tensor("xo_sb", [128, 2 * 2 * D], F32) as xo_sb, \
                nc.sbuf_tensor("qT_sb", [128, DC * NSEQ], F32) as qT_sb, \
                nc.sbuf_tensor("thi", [128, DC * NSEQ], F32) as thi, \
                nc.sbuf_tensor("outT", [128, DC * NSEQ], F32) as outT, \
                nc.sbuf_tensor("kTh", [128, 2 * KW], F32) as kTh, \
                nc.sbuf_tensor("vb", [128, 2 * 4 * VW], F32) as vb, \
                nc.sbuf_tensor("xf2", [128, 4 * D], F32) as xf2, \
                nc.sbuf_tensor("xfT2", [128, DC * 512], F32) as xfT2, \
                nc.sbuf_tensor("res_sb", [HARM, 1280], F32) as res_sb, \
                nc.sbuf_tensor("PT_sb", [128, 1024], F32) as PT_sb, \
                nc.sbuf_tensor("xc8_sb", [128, 2 * D], F32) as xc8_sb, \
                nc.sbuf_tensor("dg8_sb", [128, 2 * 128], F32) as dg8_sb, \
                nc.sbuf_tensor("on_sb", [128, 128], F32) as on_sb:
            jit8 = make_jit_xhat(p, pool, mean8[:], rstd8[:], eye_sb[:],
                                 xc8_sb[:], dg8_sb[:], depth=2)
            p.emit("vector", lambda e: e.memset(on_sb[:], 1.0))
            xf2_marks = {"v": p.mark("v"), "s": p.mark("s")}
            attn_state = {"add": 0}
            xo_marks = {"v": p.mark("v"), "s": p.mark("s"),
                        "st": p.mark("st")}

            def emit_xstats(idx):
                bb, qq = divmod(idx, 4)
                g0 = bb * 16 + qq * 4
                p.wait("sync", "v", xf2_marks["v"])
                p.wait("sync", "s", xf2_marks["s"])
                p.emit("sync", lambda e, g0=g0: e.dma_start(
                    out=xf2[:].rearrange("p (i d) -> p i d", i=4),
                    in_=x4f.rearrange("(i p) d -> p i d", p=128)
                        [:, g0:g0 + 4, :]), inc="ld", n=16)
                xl = p.mark("ld")
                for e in ("vector", "scalar"):
                    p.wait(e, "ld", xl)
                ln_stats(p, xf2[:], mean8[:], rstd8[:], sq_sb[:],
                         col0[:], range(4))

            def emit_xo(bb):
                sl = (bb % 2) * 2 * D
                p.wait("sync", "v", xo_marks["v"])
                p.wait("sync", "s", xo_marks["s"])
                p.wait("sync", "st", xo_marks["st"])
                p.emit("sync", lambda e, bb=bb, sl=sl: e.dma_start(
                    out=xo_sb[:, sl:sl + 2 * D]
                    .rearrange("p (i d) -> p i d", i=2),
                    in_=x4own.rearrange("(i p) d -> p i d", p=128)
                        [:, 2 * bb:2 * bb + 2, :]), inc="ld", n=16)
                xl = p.mark("ld")
                for e in ("vector", "scalar"):
                    p.wait(e, "ld", xl)
                ln_stats(p, xo_sb[:, sl:sl + 2 * D],
                         mean_sb[:, 2 * bb:2 * bb + 2],
                         rstd_sb[:, 2 * bb:2 * bb + 2],
                         sq_sb[:], col0[:], range(2))

            def emit_qprep(bb):
                sl = (bb % 2) * 2 * D
                jit([0, 1], xo_sb[:, sl:sl + 2 * D], xfT2[:],
                    stats=[2 * bb, 2 * bb + 1])
                p.wait("tensor", "s", p.mark("s"))
                bi, bank = pool.acquire()
                for f in range(DC):
                    p.emit("tensor", lambda e, f=f, bank=bank: e.matmul(
                        bank[0:HARM, 0:NSEQ],
                        bas_sb[:, (0 * DC + f) * HARM:
                               (0 * DC + f + 1) * HARM],
                        xfT2[:, f * NSEQ:(f + 1) * NSEQ],
                        start=(f == 0), stop=(f == DC - 1)), inc="t")
                p.wait("scalar", "t", p.mark("t"))
                p.emit("scalar", lambda e, bank=bank: e.activation(
                    out=res_sb[:, 1024:1024 + NSEQ],
                    in_=bank[0:HARM, 0:NSEQ], func=AF.Copy), inc="s")
                pool.release(bi, "s")
                p.wait("tensor", "s", p.mark("s"))
                for oc in range(DC):
                    bi, bank = pool.acquire()
                    p.emit("tensor", lambda e, oc=oc, bank=bank: e.matmul(
                        bank[:, 0:NSEQ],
                        w4[:, 0 * D + oc * 128:0 * D + (oc + 1) * 128],
                        res_sb[:, 1024:1024 + NSEQ],
                        start=True, stop=True), inc="t")
                    p.wait("scalar", "t", p.mark("t"))
                    p.emit("scalar", lambda e, oc=oc, bank=bank:
                           e.activation(
                        out=qT_sb[:, oc * NSEQ:(oc + 1) * NSEQ],
                        in_=bank[:, 0:NSEQ], func=AF.Copy), inc="s")
                    pool.release(bi, "s")
                p.wait("vector", "t", p.mark("t"))
                p.emit("vector", lambda e: e.memset(outT[0:65, :], 0.0))
                p.emit("vector", lambda e: e.memset(thi[0:65, :], 0.0))
                return p.mark("s")

            def emit_prep(idx):
                ko = (idx % 2) * KW
                vo = (idx % 2) * 4 * VW
                jit8(list(range(4)), xf2[:], xfT2[:])
                xf2_marks["v"] = p.mark("v")
                xf2_marks["s"] = p.mark("s")
                p.wait("tensor", "s", p.mark("s"))
                for pj in range(2):
                    bi, bank = pool.acquire()
                    for f in range(DC):
                        p.emit("tensor", lambda e, pj=pj, f=f,
                               bank=bank: e.matmul(
                            bank[0:HARM, 0:512],
                            bas_sb[:, ((pj + 1) * DC + f) * HARM:
                                   ((pj + 1) * DC + f + 1) * HARM],
                            xfT2[:, f * 512:(f + 1) * 512],
                            start=(f == 0), stop=(f == DC - 1)),
                            inc="t")
                    p.wait("scalar", "t", p.mark("t"))
                    p.emit("scalar", lambda e, pj=pj,
                           bank=bank: e.activation(
                        out=res_sb[:, pj * 512:(pj + 1) * 512],
                        in_=bank[0:HARM, 0:512], func=AF.Copy),
                        inc="s")
                    pool.release(bi, "s")
                res_ready = p.mark("s")
                p.wait("tensor", "s", res_ready)
                for oc in range(DC):
                    bi, bank = pool.acquire()
                    p.emit("tensor", lambda e, oc=oc, bank=bank:
                           e.matmul(
                        bank[:],
                        w4[:, 1 * D + oc * 128:1 * D + (oc + 1) * 128],
                        res_sb[:, 0:512],
                        start=True, stop=True), inc="t")
                    p.wait("scalar", "t", p.mark("t"))
                    p.emit("scalar", lambda e, oc=oc, ko=ko, bank=bank:
                           e.activation(
                        out=kTh[:, ko + oc * 512:ko + (oc + 1) * 512],
                        in_=bank[:], func=AF.Copy), inc="s")
                    pool.release(bi, "s")
                for sub in range(4):
                    p.emit("vector", lambda e, sub=sub, vo=vo: e.memset(
                        vb[:, vo + sub * VW + 64:
                           vo + sub * VW + H * 65:65], 1.0))
                    for ob in range(2):
                        bi, bank = pool.acquire()
                        p.emit("tensor", lambda e, sub=sub, ob=ob,
                               bank=bank: e.matmul(
                            bank[:],
                            res_sb[:, 512 + sub * 128:
                                   512 + (sub + 1) * 128],
                            w4[:, 2 * D + ob * 512:2 * D + (ob + 1) * 512],
                            start=True, stop=True), inc="t")
                        p.wait("scalar", "t", p.mark("t"))
                        p.wait("scalar", "v", p.mark("v"))
                        p.emit("scalar", lambda e, sub=sub, ob=ob, vo=vo,
                               bank=bank: e.activation(
                            out=vb[:, vo + sub * VW + ob * 8 * 65:
                                    vo + sub * VW + ob * 8 * 65 + 8 * 65]
                            .rearrange("p (h e) -> p h e", h=8)[:, :, 0:64],
                            in_=bank[:].rearrange("p (h e) -> p h e", h=8),
                            func=AF.Copy), inc="s")
                        pool.release(bi, "s")
                return p.mark("s")

            def emit_attn(idx, kv_ready, q_ready):
                b, qtr = divmod(idx, 4)
                ko = (idx % 2) * KW
                vo = (idx % 2) * 4 * VW
                p.wait("tensor", "s", max(kv_ready, q_ready))

                def emit_scores(hp, kc):
                    sbanks = []
                    for hv in range(2):
                        bi, bank = pool.acquire()
                        sbanks.append((bi, bank))
                        p.emit("tensor", lambda e, hp=hp, hv=hv,
                               kc=kc, bank=bank: e.matmul(
                            bank[:, 0:NSEQ],
                            kTh[hv * 64:hv * 64 + 64,
                                ko + hp * 512 + kc * 128:
                                ko + hp * 512 + (kc + 1) * 128],
                            qT_sb[hv * 64:hv * 64 + 64,
                                  hp * NSEQ:(hp + 1) * NSEQ],
                            start=True, stop=True),
                            inc="t" if hv == 1 else None)
                    return sbanks, p.mark("t")

                iters = [(hp, kc) for hp in range(DC) for kc in range(4)]
                pend = {j: emit_scores(*iters[j]) for j in range(3)}
                avt_marks = {}
                for j, (hp, kc) in enumerate(iters):
                    sbanks, tmark = pend.pop(j)
                    p.wait("scalar", "t", tmark)
                    if j - 2 in avt_marks:
                        p.wait("scalar", "t", avt_marks.pop(j - 2))
                    pt = PT_sb[:, (kc % 2) * 512:(kc % 2) * 512 + 512]
                    for hv in range(2):
                        bi, bank = sbanks[hv]
                        p.emit("scalar", lambda e, hv=hv, pt=pt,
                               bank=bank: e.activation(
                            out=pt[:, hv * NSEQ:(hv + 1) * NSEQ],
                            in_=bank[:, 0:NSEQ],
                            func=AF.Exp, scale=0.125), inc="s")
                        pool.release(bi, "s")
                    if j + 3 < len(iters):
                        pend[j + 3] = emit_scores(*iters[j + 3])
                    p.wait("tensor", "s", p.mark("s"))
                    if kc == 0:
                        p.wait("tensor", "v", attn_state["add"])
                    for hv in range(2):
                        h = hp * 2 + hv
                        av = avb[hv]
                        p.emit("tensor", lambda e, hv=hv, kc=kc,
                               h=h, av=av, pt=pt, vo=vo: e.matmul(
                            av[0:65, 0:NSEQ],
                            vb[:, vo + kc * VW + h * 65:
                               vo + kc * VW + (h + 1) * 65],
                            pt[:, hv * NSEQ:(hv + 1) * NSEQ],
                            start=(kc == 0), stop=(kc == 3)),
                            inc="t" if (kc == 3 and hv == 1)
                            else None)
                    avt_marks[j] = p.mark("t")
                    if kc == 3:
                        p.wait("vector", "t", p.mark("t"))
                        for hv, acc in ((0, outT), (1, thi)):
                            p.emit("vector", lambda e, hp=hp, hv=hv,
                                   acc=acc: e.tensor_tensor(
                                out=acc[0:65, hp * NSEQ:(hp + 1) * NSEQ],
                                in0=acc[0:65, hp * NSEQ:(hp + 1) * NSEQ],
                                in1=avb[hv][0:65, 0:NSEQ], op=OP.add))
                        attn_state["add"] = p.mark("v")
                        if hp == 0 and idx + 1 < 4 * B:
                            emit_xstats(idx + 1)
                        if hp == 4 and qtr == 2 and b + 1 < B:
                            emit_xo(b + 1)

            def emit_epilogue(b):
                sl = (b % 2) * 2 * D
                for acc in (outT, thi):
                    p.emit("vector", lambda e, acc=acc: e.reciprocal(
                        out=acc[64:65, :], in_=acc[64:65, :]))
                p.wait("tensor", "v", p.mark("v"))
                for hp in range(DC):
                    for acc in (outT, thi):
                        bi, bank = pool.acquire()
                        p.emit("tensor", lambda e, hp=hp, acc=acc,
                               bank=bank: e.matmul(
                            bank[0:64, 0:NSEQ],
                            on_sb[64:65, 0:64],
                            acc[64:65, hp * NSEQ:(hp + 1) * NSEQ],
                            start=True, stop=True), inc="t")
                        p.wait("vector", "t", p.mark("t"))
                        p.emit("vector", lambda e, hp=hp, acc=acc,
                               bank=bank: e.tensor_tensor(
                            out=acc[0:64, hp * NSEQ:(hp + 1) * NSEQ],
                            in0=acc[0:64, hp * NSEQ:(hp + 1) * NSEQ],
                            in1=bank[0:64, 0:NSEQ], op=OP.mult))
                        pool.release(bi, "v")
                p.wait("sync", "v", p.mark("v"))
                p.emit("sync", lambda e: e.dma_start(
                    out=outT[64:128, :], in_=thi[0:64, :]), inc="ld", n=16)
                p.wait("tensor", "ld", p.mark("ld"))
                p.wait("tensor", "v", p.mark("v"))
                bi, bank = pool.acquire()
                for oc in range(DC):
                    p.emit("tensor", lambda e, oc=oc, bank=bank: e.matmul(
                        bank[0:HARM, 0:NSEQ],
                        bas_sb[:, (3 * DC + oc) * HARM:
                               (3 * DC + oc + 1) * HARM],
                        outT[:, oc * NSEQ:(oc + 1) * NSEQ],
                        start=(oc == 0), stop=(oc == DC - 1)), inc="t")
                p.wait("scalar", "t", p.mark("t"))
                p.emit("scalar", lambda e, bank=bank: e.activation(
                    out=res_sb[:, 1024:1024 + NSEQ],
                    in_=bank[0:HARM, 0:NSEQ], func=AF.Copy), inc="s")
                pool.release(bi, "s")
                p.wait("tensor", "s", p.mark("s"))
                for sub in range(2):
                    for ob in range(2):
                        bi, bank = pool.acquire()
                        p.emit("tensor", lambda e, sub=sub, ob=ob,
                               bank=bank: e.matmul(
                            bank[:],
                            res_sb[:, 1024 + sub * 128:
                                   1024 + (sub + 1) * 128],
                            w4[:, 3 * D + ob * 512:3 * D + (ob + 1) * 512],
                            start=True, stop=True), inc="t")
                        p.wait("vector", "t", p.mark("t"))
                        p.emit("vector", lambda e, sub=sub, ob=ob, sl=sl,
                               bank=bank: e.scalar_tensor_tensor(
                            out=xo_sb[:, sl + sub * D + ob * 512:
                                      sl + sub * D + (ob + 1) * 512],
                            in0=bank[:], scalar=1.0,
                            in1=xo_sb[:, sl + sub * D + ob * 512:
                                      sl + sub * D + (ob + 1) * 512],
                            op0=OP.mult, op1=OP.add))
                        pool.release(bi, "v")
                p.wait("gpsimd", "v", p.mark("v"))
                p.emit("gpsimd", lambda e, b=b, sl=sl: e.dma_start(
                    out=x5_d.rearrange("(i p) d -> p i d", p=128)
                        [:, 2 * b:2 * b + 2, :],
                    in_=xo_sb[:, sl:sl + 2 * D]
                    .rearrange("p (i d) -> p i d", i=2)),
                    inc="st", n=16, raw=True)
                ln_stats(p, xo_sb[:, sl:sl + 2 * D],
                         mean_sb[:, 2 * b:2 * b + 2],
                         rstd_sb[:, 2 * b:2 * b + 2],
                         sq_sb[:], col0[:], range(2))
                xo_marks["v"] = p.mark("v")
                xo_marks["s"] = p.mark("s")
                xo_marks["st"] = p.mark("st")

            emit_xo(0)
            emit_xstats(0)
            qp_marks = {0: emit_qprep(0)}
            kv_marks = {0: emit_prep(0)}
            for idx in range(4 * B):
                b, qtr = divmod(idx, 4)
                emit_attn(idx, kv_marks[idx], qp_marks[b])
                if idx + 1 < 4 * B:
                    kv_marks[idx + 1] = emit_prep(idx + 1)
                if qtr == 3:
                    emit_epilogue(b)
                    if b + 1 < B:
                        qp_marks[b + 1] = emit_qprep(b + 1)
            p.barrier()
        with nc.sbuf_tensor("x_sb2", [128, NT * D], F32) as x_sb2, \
                nc.sbuf_tensor("xhT", [128, DC * 512], F32) as xhT, \
                nc.sbuf_tensor("wg_sb", [128, 2 * DC * D], F32) as wg_sb, \
                nc.sbuf_tensor("zx_sb", [128, DC * 512], F32) as zx_sb:
            p.emit("sync", lambda e: e.dma_start(
                out=x_sb2[:].rearrange("p (i d) -> p i d", i=NT),
                in_=x5_d.rearrange("(i p) d -> p i d", p=128)),
                inc="ld", n=16)
            x5_ld = p.mark("ld")
            for e in ("tensor", "vector", "scalar"):
                p.wait(e, "ld", x5_ld)

            def wg_load(g):
                p.wait("sync", "t", p.mark("t"))
                p.emit("sync", lambda e, g=g: e.dma_start(
                    out=wg_sb[:, (g % 2) * DC * D:(g % 2 + 1) * DC * D]
                    .rearrange("p (c d) -> p c d", c=DC),
                    in_=wgT[g].rearrange("c p d -> p c d")), inc="ld", n=16)
                return p.mark("ld")

            wg_marks = {0: wg_load(0)}
            for g in range(3):
                go = (g % 2) * DC * D
                if g + 1 < 3:
                    wg_marks[g + 1] = wg_load(g + 1)
                for tb in range(4):
                    tiles = list(range(tb * 4, tb * 4 + 4))
                    jit(tiles, x_sb2[:], xhT[:])
                    xh_ready = p.mark("s")
                    p.wait("tensor", "ld", wg_marks[g])
                    p.wait("tensor", "s", xh_ready)
                    p.wait("scalar", "st", p.mark("st"))
                    for mc in range(DC):
                        bi, bank = pool.acquire()
                        for f in range(DC):
                            p.emit("tensor", lambda e, f=f, mc=mc,
                                   go=go, bank=bank: e.matmul(
                                bank[:],
                                wg_sb[:, go + mc * D + f * 128:
                                      go + mc * D + (f + 1) * 128],
                                xhT[:, f * 512:(f + 1) * 512],
                                start=(f == 0), stop=(f == DC - 1)),
                                inc="t")
                        p.wait("scalar", "t", p.mark("t"))
                        p.emit("scalar", lambda e, g=g, mc=mc, bank=bank:
                               e.activation(
                            out=zx_sb[:, mc * 512:(mc + 1) * 512],
                            in_=bank[:], func=AF.Identity,
                            bias=bgc[:, g * DC + mc:g * DC + mc + 1]),
                            inc="s")
                        pool.release(bi, "s")
                    p.wait("gpsimd", "s", p.mark("s"))
                    p.emit("gpsimd", lambda e, g=g, tb=tb: e.dma_start(
                        out=zx_d[g].rearrange("c p t -> p c t")
                            [:, :, tb * 512:(tb + 1) * 512],
                        in_=zx_sb[:].rearrange("p (c q) -> p c q", c=DC)),
                        inc="st", n=16, raw=True)
        p.barrier()
        emit_ffn1_chain(nc, p, pool, jit, ugT, uoT, woR, x5_d, x6_o, zx_d,
                        eye_sb[:], boc[:], incc[:])
        p.barrier()
        p.wait("gpsimd", "st", p.cnt["st"])
        p.build()
    return nc


def stage_l4(inputs, res3):
    com = {
        "basR": np.stack([pack_rhs(np.asarray(inputs[f"L1_attn_basis_{k}"],
                                              np.float32))
                          for k in "qkvo"]),
        "phT": np.ascontiguousarray(np.concatenate(
            [np.asarray(inputs[f"L1_attn_phase_{k}"], np.float32).T
             for k in "qkvo"], axis=1)),
        "amT": np.ascontiguousarray(np.concatenate(
            [np.asarray(inputs[f"L1_attn_amp_{k}"], np.float32).T
             for k in "qkvo"], axis=1)),
        "wgT": np.stack([pack_lhsT(np.asarray(inputs[k], np.float32))
                         for k in ("L1_ffn_wf", "L1_ffn_wi", "L1_ffn_wc")]),
        "ugT": np.stack([pack_lhsT(np.asarray(inputs[k], np.float32))
                         for k in ("L1_ffn_uf", "L1_ffn_ui", "L1_ffn_uc")]),
        "uoT": pack_lhsT(np.asarray(inputs["L1_ffn_uo"], np.float32)),
        "woR": pack_rhs(np.asarray(inputs["L1_ffn_wo"], np.float32)),
        "bg": np.stack([np.asarray(inputs[k], np.float32)
                        for k in ("L1_ffn_bf", "L1_ffn_bi", "L1_ffn_bc")]),
        "bo_ff": np.asarray(inputs["L1_ffn_bo"], np.float32),
        "inc_ff": np.asarray(inputs["L1_ffn_in_c"], np.float32),
    }
    x4_by_batch = np.empty((B, S, D), np.float32)
    for j in range(NCORE):
        blk = np.asarray(res3[j]["x4"]).reshape(B, 256, D)
        x4_by_batch[:, j * 256:(j + 1) * 256, :] = blk
    x4_flat = np.ascontiguousarray(x4_by_batch.reshape(B * S, D))
    maps = []
    for j in range(NCORE):
        own = np.ascontiguousarray(
            x4_by_batch[:, j * 256:(j + 1) * 256, :].reshape(TOK, D))
        maps.append(dict(com, x4f=x4_flat, x4own=own))
    return maps


# version 21
# speedup vs baseline: 1.0711x; 1.0091x over previous
import numpy as np
from contextlib import ExitStack

import concourse.bass as bass
import concourse.mybir as mybir
from concourse.bass_utils import run_bass_kernel_spmd

F32 = mybir.dt.float32
AF = mybir.ActivationFunctionType
OP = mybir.AluOpType
AX = mybir.AxisListType

NCORE = 8
DBG = False
B, S, D = 8, 2048, 1024
TOK = 2048
NT = TOK // 128
DC = D // 128
H, DH, PM = 16, 64, 256
FF = 4096
FC = FF // 128
HARM = 32
LN_EPS = 1e-5
KEPS = 1e-3
NSTEP = 5
BETA = 0.9

ENGS = ("sync", "tensor", "vector", "scalar", "gpsimd")



class Prog:

    def __init__(self, nc, stack):
        self.nc = nc
        self.stack = stack
        self.cmds = {e: [] for e in ENGS}
        self.sems = {}
        self.cnt = {}
        for s in ("t", "v", "s", "g", "ld", "ld2", "st"):
            self.sem(s)

    def sem(self, name):
        if name not in self.sems:
            self.sems[name] = self.stack.enter_context(self.nc.semaphore(name))
            self.cnt[name] = 0
        return name

    OWN = {"vector": "v", "scalar": "s", "gpsimd": "g"}

    def emit(self, eng, fn, inc=None, n=1, raw=False):
        own = None if raw else self.OWN.get(eng)
        if eng == "tensor":
            self.tensor_emits = getattr(self, "tensor_emits", 0) + 1
            cell = self.__dict__.setdefault("tcell", [None])
            if inc is not None:
                self.sem(inc)
                self.cnt[inc] += n
                s = self.sems[inc]
                def runt(e, fn=fn, s=s, n=n, cell=cell):
                    cell[0] = fn(e).then_inc(s, n)
                self.cmds[eng].append(runt)
                self.last_tensor_inc = (inc, self.cnt[inc])
            else:
                def runt2(e, fn=fn, cell=cell):
                    cell[0] = fn(e)
                self.cmds[eng].append(runt2)
                self.last_tensor_inc = None
            return
        if own is not None:
            assert inc in (None, own), (eng, inc)
            prev = self.cnt[own]
            self.cnt[own] += 1
            s = self.sems[own]
            def run(e, fn=fn, s=s, prev=prev):
                if prev:
                    e.wait_ge(s, prev)
                fn(e).then_inc(s, 1)
            self.cmds[eng].append(run)
        elif inc is not None:
            self.sem(inc)
            self.cnt[inc] += n
            s = self.sems[inc]
            self.cmds[eng].append(lambda e, fn=fn, s=s, n=n: fn(e).then_inc(s, n))
        else:
            self.cmds[eng].append(fn)

    def wait(self, eng, sem, val=None):
        val = self.cnt[sem] if val is None else val
        s = self.sems[sem]
        self.cmds[eng].append(lambda e, s=s, v=val: e.wait_ge(s, v))

    def mark(self, sem):
        return self.cnt[sem]

    def barrier(self, engs=ENGS):
        marks = {}
        for e in engs:
            if e in self.OWN:
                self.emit(e, lambda x: x.nop())
                marks[e] = (self.OWN[e], self.cnt[self.OWN[e]])
            elif e == "tensor":
                emitted = getattr(self, "tensor_emits", 0)
                self.sem("bt")
                if emitted > getattr(self, "_bar_tensor_seen", 0):
                    self._bar_tensor_seen = emitted
                    lti = getattr(self, "last_tensor_inc", None)
                    if lti is not None:
                        marks[e] = lti
                    else:
                        self.cnt["bt"] += 1
                        cell = self.__dict__.setdefault("tcell", [None])
                        s = self.sems["bt"]
                        self.cmds["tensor"].append(
                            lambda _e, cell=cell, s=s: cell[0].then_inc(s, 1))
                        marks[e] = ("bt", self.cnt["bt"])
                else:
                    marks[e] = ("bt", self.cnt["bt"])
            else:
                marks[e] = ("ld", self.cnt["ld"])
        stv = self.cnt["st"]
        for e in engs:
            for o in engs:
                if o != e:
                    sem, val = marks[o]
                    self.wait(e, sem, val)
            if stv:
                self.wait(e, "st", stv)

    def build(self):
        blk = self.stack.enter_context(self.nc.Block())
        for e in ENGS:
            def run_all(eng, e=e):
                for c in self.cmds[e]:
                    c(eng)
            getattr(blk, e)(run_all)


class PsumPool:
    def __init__(self, p, banks):
        self.p = p
        self.banks = banks
        self.busy = [None] * len(banks)
        self.rr = 0

    def acquire(self):
        i = self.rr % len(self.banks)
        self.rr += 1
        if self.busy[i] is not None:
            sem, val = self.busy[i]
            self.p.wait("tensor", sem, val)
            self.busy[i] = None
        return i, self.banks[i]

    def release(self, i, sem):
        self.busy[i] = (sem, self.p.cnt[sem])


class DBuf:

    def __init__(self, p, bufs, queue="sync", ld="ld"):
        self.p = p
        self.bufs = bufs
        self.queue = queue
        self.ld = ld
        self.tmarks = [None] * len(bufs)
        self.i = 0

    def load(self, fn):
        k = self.i % len(self.bufs)
        self.i += 1
        buf = self.bufs[k]
        if self.tmarks[k] is not None:
            self.p.wait(self.queue, "t", self.tmarks[k])
        self.p.emit(self.queue, lambda e, buf=buf, fn=fn: fn(e, buf),
                    inc=self.ld, n=16)
        return buf, self.p.mark(self.ld)

    def consumed(self):
        k = (self.i - 1) % len(self.bufs)
        self.tmarks[k] = self.p.cnt["t"]



BIAS_CONSTS = (1e-5, 1e-3, 1.5707963267948966,
               -3.141592653589793)


def setup_consts(nc, p, ec):
    arr = np.tile(np.array([BIAS_CONSTS], np.float32), (128, 1))
    cd = nc.inline_tensor(arr, "biasconsts").ap()
    cs = ec(nc.sbuf_tensor("consts_sb", [128, len(BIAS_CONSTS)], F32))
    p.emit("sync", lambda e: e.dma_start(out=cs[:], in_=cd), inc="ld", n=16)
    for j, v in enumerate(BIAS_CONSTS):
        nc.const_aps.register(F32, float(v), cs[:, j:j + 1])
    return cs


def ln_stats(p, x_sb, mean_sb, rstd_sb, sq_sb, col0, tiles):
    for n_, i in enumerate(tiles):
        xs = x_sb[:, i * D:(i + 1) * D]
        p.emit("vector", lambda e, xs=xs, i=i: e.tensor_reduce(
            out=mean_sb[:, i:i + 1], in_=xs, axis=AX.X, op=OP.add), inc="v")
        if n_ > 0:
            p.wait("scalar", "s", p.mark("s"))
        p.emit("scalar", lambda e, xs=xs, i=i: e.activation(
            out=sq_sb[:], in_=xs, func=AF.Square,
            accum_out=rstd_sb[:, i:i + 1]), inc="s")
    p.wait("vector", "s", p.mark("s"))
    for i in tiles:
        p.emit("vector", lambda e, i=i: e.tensor_scalar(
            out=mean_sb[:, i:i + 1], in0=mean_sb[:, i:i + 1],
            scalar1=1.0 / D, scalar2=None, op0=OP.mult), inc="v")
        p.emit("vector", lambda e, i=i: e.tensor_tensor(
            out=col0[:], in0=mean_sb[:, i:i + 1], in1=mean_sb[:, i:i + 1],
            op=OP.mult), inc="v")
        p.emit("vector", lambda e, i=i: e.tensor_scalar(
            out=rstd_sb[:, i:i + 1], in0=rstd_sb[:, i:i + 1],
            scalar1=1.0 / D, scalar2=None, op0=OP.mult), inc="v")
        p.emit("vector", lambda e, i=i: e.tensor_tensor(
            out=rstd_sb[:, i:i + 1], in0=rstd_sb[:, i:i + 1], in1=col0[:],
            op=OP.subtract), inc="v")
        p.wait("scalar", "v", p.mark("v"))
        p.emit("scalar", lambda e, i=i: e.activation(
            out=rstd_sb[:, i:i + 1], in_=rstd_sb[:, i:i + 1], func=AF.Ln,
            bias=LN_EPS), inc="s")
        p.emit("scalar", lambda e, i=i: e.activation(
            out=rstd_sb[:, i:i + 1], in_=rstd_sb[:, i:i + 1], func=AF.Exp,
            scale=-0.5), inc="s")
    p.wait("vector", "s", p.mark("s"))


def make_jit_xhat(p, pool, mean_sb, rstd_sb, eye_sb, xc_sb, diag_sb,
                  depth=1):
    last = {d: 0 for d in range(depth)}

    def jit(tiles, x_sb, xhT, stats=None):
        n = len(tiles) * 128
        nt = len(tiles)
        vstage = {}

        def stage(k):
            i = tiles[k]
            si = stats[k] if stats is not None else i
            sl, dsl = (k % depth) * D, (k % depth) * 128
            xs = x_sb[:, i * D:(i + 1) * D]
            p.wait("vector", "t", last[k % depth])
            p.emit("vector", lambda e, xs=xs, si=si, sl=sl: e.tensor_scalar(
                out=xc_sb[:, sl:sl + D], in0=xs,
                scalar1=mean_sb[:, si:si + 1],
                scalar2=None, op0=OP.subtract), inc="v")
            p.emit("vector", lambda e, si=si, dsl=dsl: e.tensor_scalar(
                out=diag_sb[:, dsl:dsl + 128], in0=eye_sb[:],
                scalar1=rstd_sb[:, si:si + 1],
                scalar2=None, op0=OP.mult), inc="v")
            vstage[k] = p.mark("v")

        for k in range(min(depth, nt)):
            stage(k)
        for k, i in enumerate(tiles):
            sl, dsl = (k % depth) * D, (k % depth) * 128
            p.wait("tensor", "v", vstage[k])
            for half in range(2):
                bi, bank = pool.acquire()
                for q in range(4):
                    f = half * 4 + q
                    p.emit("tensor", lambda e, f=f, q=q, sl=sl, dsl=dsl,
                           bank=bank: e.matmul(
                        bank[:, q * 128:(q + 1) * 128],
                        xc_sb[:, sl + f * 128:sl + (f + 1) * 128],
                        diag_sb[:, dsl:dsl + 128], start=True, stop=True),
                        inc="t" if q == 3 else None)
                if half == 1:
                    last[k % depth] = p.mark("t")
                p.wait("scalar", "t", p.mark("t"))
                for q in range(4):
                    f = half * 4 + q
                    p.emit("scalar", lambda e, f=f, q=q, k=k, bank=bank:
                           e.activation(
                               out=xhT[:, f * n + k * 128:f * n + (k + 1) * 128],
                               in_=bank[:, q * 128:(q + 1) * 128],
                               func=AF.Copy), inc="s")
                pool.release(bi, "s")
            if k + depth < nt:
                stage(k + depth)
    return jit


def spike_residual(p, z_sb, mem_sb, spk_sb, acc_sb, x_tile):
    p.emit("vector", lambda e: e.tensor_scalar(
        out=acc_sb[:], in0=z_sb[:], scalar1=1.0, scalar2=None,
        op0=OP.is_gt), inc="v")
    first = True
    for _ in range(2, NSTEP + 1):
        src = z_sb if first else mem_sb
        prev_spk = acc_sb if first else spk_sb
        p.emit("vector", lambda e, src=src: e.scalar_tensor_tensor(
            out=mem_sb[:], in0=src[:], scalar=BETA, in1=z_sb[:],
            op0=OP.mult, op1=OP.add), inc="v")
        p.emit("vector", lambda e, ps=prev_spk: e.tensor_tensor(
            out=mem_sb[:], in0=mem_sb[:], in1=ps[:], op=OP.subtract), inc="v")
        p.emit("vector", lambda e: e.tensor_scalar(
            out=spk_sb[:], in0=mem_sb[:], scalar1=1.0, scalar2=None,
            op0=OP.is_gt), inc="v")
        p.emit("vector", lambda e: e.tensor_tensor(
            out=acc_sb[:], in0=acc_sb[:], in1=spk_sb[:], op=OP.add), inc="v")
        first = False
    p.emit("vector", lambda e, xt=x_tile: e.scalar_tensor_tensor(
        out=xt, in0=acc_sb[:], scalar=0.1, in1=xt, op0=OP.mult, op1=OP.add),
        inc="v")


def emit_ffn0(p, pool, jit, x_sb, wfc_dram, wproj_dram, sb, dbg=None):
    xhT, h_sb, r_sb, z_sb = sb["xhT"], sb["h"], sb["r"], sb["z"]
    wfc = DBuf(p, sb["wfc"], "sync", "ld")
    wproj = DBuf(p, sb["wproj"], "sync", "ld")
    jit(list(range(4)), x_sb, xhT)
    xh_ready = p.mark("s")
    sq_mark = 0
    spike_mark = 0
    pre_wfc = []
    for tb in range(4):
        fcmode = ("fc" if "fc" in FFN0_PARTS else
                  "fcmm" if "fcmm" in FFN0_PARTS else
                  "fcload" if "fcload" in FFN0_PARTS else "off")
        for hc in range(FC if fcmode != "off" else 0):
            if pre_wfc:
                buf, ldm = pre_wfc.pop(0)
            else:
                buf, ldm = wfc.load(lambda e, buf, hc=hc: e.dma_start(
                    out=buf[:], in_=wfc_dram[hc]))
            p.wait("tensor", "ld", ldm)
            if fcmode == "fcload":
                continue
            if hc == 0:
                p.wait("tensor", "s", xh_ready)
            bi, bank = pool.acquire()
            for f in range(DC):
                p.emit("tensor", lambda e, f=f, buf=buf, bank=bank: e.matmul(
                    bank[:], buf[:, f * 128:(f + 1) * 128],
                    xhT[:, f * 512:(f + 1) * 512],
                    start=(f == 0), stop=(f == DC - 1)),
                    inc="t" if f == DC - 1 else None)
            wfc.consumed()
            if fcmode == "fcmm":
                pool.release(bi, "t")
                continue
            p.wait("scalar", "t", p.mark("t"))
            p.wait("scalar", "v", sq_mark)
            p.emit("scalar", lambda e, bank=bank: e.activation(
                out=r_sb[:], in_=bank[:], func=AF.Relu), inc="s")
            pool.release(bi, "s")
            p.wait("vector", "s", p.mark("s"))
            p.emit("vector", lambda e, hc=hc: e.tensor_mul(
                out=h_sb[:, hc * 512:(hc + 1) * 512], in0=r_sb[:],
                in1=r_sb[:]), inc="v")
            sq_mark = p.mark("v")
        h_ready = p.mark("v")
        if tb < 3:
            jit(list(range(tb * 4 + 4, tb * 4 + 8)), x_sb, xhT)
            next_xh = p.mark("s")
        if "proj" not in FFN0_PARTS:
            continue
        for k in range(8):
            bi, bank = pool.acquire()
            assert bi == k % len(pool.banks)
        banks = pool.banks
        p.wait("tensor", "v", h_ready)
        for hc in range(FC):
            buf, ldm = wproj.load(lambda e, buf, hc=hc: e.dma_start(
                out=buf[:], in_=wproj_dram[hc]))
            p.wait("tensor", "ld", ldm)
            for sub in range(4):
                for od in range(2):
                    bank = banks[sub * 2 + od]
                    p.emit("tensor",
                           lambda e, hc=hc, sub=sub, od=od, bank=bank,
                           buf=buf: e.matmul(
                               bank[:],
                               h_sb[:, hc * 512 + sub * 128:
                                    hc * 512 + (sub + 1) * 128],
                               buf[:, od * 512:(od + 1) * 512],
                               start=(hc == 0), stop=(hc == FC - 1)),
                           inc="t" if (sub == 3 and od == 1) else None)
            wproj.consumed()
        if dbg is not None and tb == 0:
            p.wait("gpsimd", "s", p.mark("s"))
            p.wait("gpsimd", "v", p.mark("v"))
            p.emit("gpsimd", lambda e: e.dma_start(out=dbg[0], in_=xhT),
                   inc="st", n=16, raw=True)
            p.emit("gpsimd", lambda e: e.dma_start(out=dbg[1], in_=h_sb),
                   inc="st", n=16, raw=True)
        proj_done = p.mark("t")
        p.wait("scalar", "t", proj_done)
        for sub in range(4):
            tile = tb * 4 + sub
            p.wait("scalar", "v", spike_mark if sub == 0 else p.mark("v"))
            for od in range(2):
                bank = banks[sub * 2 + od]
                p.emit("scalar", lambda e, od=od, bank=bank: e.activation(
                    out=z_sb[:, od * 512:(od + 1) * 512], in_=bank[:],
                    func=AF.Copy), inc="s")
            p.wait("vector", "s", p.mark("s"))
            if "spike" in FFN0_PARTS:
                spike_residual(p, z_sb, sb["mem"], sb["spk"], sb["acc"],
                               x_sb[:, tile * D:(tile + 1) * D])
        spike_mark = p.mark("v")
        for k in range(8):
            pool.release(k, "s")
        if tb < 3:
            xh_ready = next_xh
    p.barrier()



L1_UPTO = "all"
FFN0_PARTS = {"fc", "proj", "spike"}


def build_l1():
    nc = bass.Bass("TRN2", target_bir_lowering=False,
                   detect_race_conditions=False)
    x0 = nc.dram_tensor("x0", [TOK, D], F32, kind="ExternalInput").ap()
    wfc = nc.dram_tensor("wfc", [FC, 128, D], F32, kind="ExternalInput").ap()
    wproj = nc.dram_tensor("wproj", [FC, 128, D], F32,
                           kind="ExternalInput").ap()
    wkT = nc.dram_tensor("wkT", [DC, 128, D], F32, kind="ExternalInput").ap()
    bk = nc.dram_tensor("bk", [D], F32, kind="ExternalInput").ap()
    x1_o = nc.dram_tensor("x1", [TOK, D], F32, kind="ExternalOutput").ap()
    kT_o = nc.dram_tensor("kT", [D, TOK], F32, kind="ExternalOutput").ap()
    diag_o = nc.dram_tensor("diag", [TOK, H], F32, kind="ExternalOutput").ap()
    lmax_o = nc.dram_tensor("lmax", [1, 1], F32, kind="ExternalOutput").ap()
    if DBG:
        dbg_mean = nc.dram_tensor("dbg_mean", [128, NT], F32,
                                  kind="ExternalOutput").ap()
        dbg_rstd = nc.dram_tensor("dbg_rstd", [128, NT], F32,
                                  kind="ExternalOutput").ap()
        dbg_xh = nc.dram_tensor("dbg_xh", [128, DC * 512], F32,
                                kind="ExternalOutput").ap()
        dbg_h = nc.dram_tensor("dbg_h", [128, FC * 512], F32,
                               kind="ExternalOutput").ap()
    eye_d = nc.inline_tensor(np.eye(128, dtype=np.float32), "eye").ap()
    hmask = np.zeros((128, 2), dtype=np.float32)
    hmask[:64, 0] = 1.0
    hmask[64:, 1] = 1.0
    ones_d = nc.inline_tensor(hmask, "hmask").ap()

    with ExitStack() as stack:
        ec = stack.enter_context
        p = Prog(nc, stack)
        x_sb = ec(nc.sbuf_tensor("x_sb", [128, NT * D], F32))
        mean_sb = ec(nc.sbuf_tensor("mean_sb", [128, NT], F32))
        rstd_sb = ec(nc.sbuf_tensor("rstd_sb", [128, NT], F32))
        sq_sb = ec(nc.sbuf_tensor("sq_sb", [128, D], F32))
        col0 = ec(nc.sbuf_tensor("col0", [128, 1], F32))
        eye_sb = ec(nc.sbuf_tensor("eye_sb", [128, 128], F32))
        setup_consts(nc, p, ec)
        ones_sb = ec(nc.sbuf_tensor("ones_sb", [128, 2], F32))
        xc_sb = ec(nc.sbuf_tensor("xc_sb", [128, 2 * D], F32))
        dg_sb = ec(nc.sbuf_tensor("dg_sb", [128, 2 * 128], F32))
        diag_sb = ec(nc.sbuf_tensor("diag_sb", [128, NT * H], F32))
        bk_sb = ec(nc.sbuf_tensor("bk_sb", [128, DC], F32))
        banks = [ec(nc.psum_tensor(f"ps{i}", [128, 512], F32))
                 for i in range(8)]
        pool = PsumPool(p, [b[:] for b in banks])
        jit = make_jit_xhat(p, pool, mean_sb[:], rstd_sb[:], eye_sb[:],
                            xc_sb[:], dg_sb[:], depth=2)

        p.emit("sync", lambda e: e.dma_start(
            out=x_sb[:].rearrange("p (i d) -> p i d", i=NT),
            in_=x0.rearrange("(i p) d -> p i d", p=128)), inc="ld", n=16)
        p.emit("sync", lambda e: e.dma_start(out=eye_sb[:], in_=eye_d),
               inc="ld", n=16)
        p.emit("sync", lambda e: e.dma_start(out=ones_sb[:], in_=ones_d),
               inc="ld", n=16)
        p.emit("sync", lambda e: e.dma_start(
            out=bk_sb[:], in_=bk.rearrange("(c p) -> p c", p=128),
            allow_slow_non_contiguous=True), inc="ld", n=16)
        ldm = p.mark("ld")
        for e in ("tensor", "vector", "scalar"):
            p.wait(e, "ld", ldm)

        ln_stats(p, x_sb[:], mean_sb[:], rstd_sb[:], sq_sb[:], col0[:],
                 range(NT))
        p.barrier()
        if DBG:
            p.emit("gpsimd", lambda e: e.dma_start(out=dbg_mean,
                                                   in_=mean_sb[:]),
                   inc="st", n=16, raw=True)
            p.emit("gpsimd", lambda e: e.dma_start(out=dbg_rstd,
                                                   in_=rstd_sb[:]),
                   inc="st", n=16, raw=True)
        with nc.sbuf_tensor("xhT", [128, DC * 512], F32) as xhT, \
                nc.sbuf_tensor("h_sb", [128, FC * 512], F32) as h_sb, \
                nc.sbuf_tensor("r_sb", [128, 512], F32) as r_sb, \
                nc.sbuf_tensor("z_sb", [128, D], F32) as z_sb, \
                nc.sbuf_tensor("mem_sb", [128, D], F32) as mem_sb, \
                nc.sbuf_tensor("spk_sb", [128, D], F32) as spk_sb, \
                nc.sbuf_tensor("acc_sb", [128, D], F32) as acc_sb, \
                nc.sbuf_tensor("wfc0", [128, D], F32) as wfc0, \
                nc.sbuf_tensor("wfc1", [128, D], F32) as wfc1, \
                nc.sbuf_tensor("wp0", [128, D], F32) as wp0, \
                nc.sbuf_tensor("wp1", [128, D], F32) as wp1:
            sb = dict(xhT=xhT[:], h=h_sb[:], r=r_sb[:], z=z_sb[:],
                      mem=mem_sb[:], spk=spk_sb[:], acc=acc_sb[:],
                      wfc=[wfc0[:], wfc1[:]], wproj=[wp0[:], wp1[:]])
            emit_ffn0(p, pool, jit, x_sb[:], wfc, wproj, sb,
                      dbg=(dbg_xh, dbg_h) if DBG else None)

        p.emit("gpsimd", lambda e: e.dma_start(
            out=x1_o.rearrange("(i p) d -> p i d", p=128),
            in_=x_sb[:].rearrange("p (i d) -> p i d", i=NT)), inc="st", n=16, raw=True)

        if L1_UPTO == "all":
            emit_l1_kphase(nc, p, pool, jit, x_sb, mean_sb, rstd_sb, sq_sb,
                           col0, eye_sb, ones_sb, bk_sb, diag_sb, wkT,
                           kT_o, diag_o, lmax_o)
        p.barrier()
        p.wait("gpsimd", "st", p.cnt["st"])
        p.build()
    return nc


def emit_l1_kphase(nc, p, pool, jit, x_sb, mean_sb, rstd_sb, sq_sb, col0,
                   eye_sb, ones_sb, bk_sb, diag_sb, wkT, kT_o, diag_o,
                   lmax_o):
        ln_stats(p, x_sb[:], mean_sb[:], rstd_sb[:], sq_sb[:], col0[:],
                 range(NT))
        p.barrier()
        with nc.sbuf_tensor("xhT2", [128, DC * 512], F32) as xhT2, \
                nc.sbuf_tensor("kblk", [128, DC * 512], F32) as kblk, \
                nc.sbuf_tensor("k2", [128, DC * 512], F32) as k2, \
                nc.sbuf_tensor("wk0", [128, D], F32) as wk0, \
                nc.sbuf_tensor("wk1", [128, D], F32) as wk1, \
                nc.sbuf_tensor("mrow", [1, 129], F32) as mrow:
            wk = DBuf(p, [wk0[:], wk1[:]], "sync", "ld")
            st_prev = 0
            for tb in range(4):
                tiles = list(range(tb * 4, tb * 4 + 4))
                jit(tiles, x_sb[:], xhT2[:])
                xh_ready = p.mark("s")
                p.wait("scalar", "st", st_prev)
                p.wait("scalar", "t", p.mark("t"))
                for oc in range(DC):
                    buf, ldm = wk.load(lambda e, buf, oc=oc: e.dma_start(
                        out=buf[:], in_=wkT[oc]))
                    p.wait("tensor", "ld", ldm)
                    if oc == 0:
                        p.wait("tensor", "s", xh_ready)
                    bi, bank = pool.acquire()
                    for f in range(DC):
                        p.emit("tensor",
                               lambda e, f=f, buf=buf, bank=bank: e.matmul(
                                   bank[:], buf[:, f * 128:(f + 1) * 128],
                                   xhT2[:, f * 512:(f + 1) * 512],
                                   start=(f == 0), stop=(f == DC - 1)),
                               inc="t" if f == DC - 1 else None)
                    wk.consumed()
                    p.wait("scalar", "t", p.mark("t"))
                    p.emit("scalar", lambda e, oc=oc, bank=bank: e.activation(
                        out=kblk[:, oc * 512:(oc + 1) * 512], in_=bank[:],
                        func=AF.Identity, bias=bk_sb[:, oc:oc + 1]), inc="s")
                    pool.release(bi, "s")
                p.wait("gpsimd", "s", p.mark("s"))
                p.emit("gpsimd", lambda e, tb=tb: e.dma_start(
                    out=kT_o.rearrange("(c p) t -> p c t", p=128)
                        [:, :, tb * 512:(tb + 1) * 512],
                    in_=kblk[:].rearrange("p (c q) -> p c q", c=DC)),
                    inc="st", n=16, raw=True)
                st_prev = p.mark("st")
                p.emit("scalar", lambda e: e.activation(
                    out=k2[:], in_=kblk[:], func=AF.Square), inc="s")
                p.wait("tensor", "s", p.mark("s"))
                for sub in range(4):
                    tile = tb * 4 + sub
                    bi, bank = pool.acquire()
                    for h in range(H):
                        oc, half = h // 2, (h % 2)
                        p.emit("tensor",
                               lambda e, h=h, oc=oc, half=half, sub=sub,
                               bank=bank: e.matmul(
                                   bank[:, h:h + 1],
                                   k2[:, oc * 512 + sub * 128:
                                      oc * 512 + (sub + 1) * 128],
                                   ones_sb[:, half:half + 1],
                                   start=True, stop=True),
                               inc="t" if h == H - 1 else None)
                    p.wait("vector", "t", p.mark("t"))
                    p.emit("vector", lambda e, tile=tile, bank=bank:
                           e.tensor_scalar(
                               out=diag_sb[:, tile * H:(tile + 1) * H],
                               in0=bank[:, 0:H], scalar1=-0.5, scalar2=None,
                               op0=OP.mult), inc="v")
                    pool.release(bi, "v")
            p.barrier()
            p.emit("vector", lambda e: e.tensor_reduce(
                out=col0[:], in_=diag_sb[:], axis=AX.X, op=OP.max), inc="v")
            p.wait("tensor", "v", p.mark("v"))
            bi, bank = pool.acquire()
            p.emit("tensor", lambda e, bank=bank: e.matmul(
                bank[0:1, 0:128], col0[:], eye_sb[:], start=True, stop=True),
                inc="t")
            p.wait("scalar", "t", p.mark("t"))
            p.emit("scalar", lambda e, bank=bank: e.activation(
                out=mrow[0:1, 0:128], in_=bank[0:1, 0:128], func=AF.Copy),
                inc="s")
            pool.release(bi, "s")
            p.wait("vector", "s", p.mark("s"))
            p.emit("vector", lambda e: e.tensor_reduce(
                out=mrow[0:1, 128:129], in_=mrow[0:1, 0:128], axis=AX.X,
                op=OP.max), inc="v")
            p.wait("gpsimd", "v", p.mark("v"))
            p.emit("gpsimd", lambda e: e.dma_start(
                out=lmax_o, in_=mrow[0:1, 128:129]), inc="st", n=16, raw=True)
            p.emit("gpsimd", lambda e: e.dma_start(
                out=diag_o.rearrange("(i p) h -> p i h", p=128),
                in_=diag_sb[:].rearrange("p (i h) -> p i h", i=NT)),
                inc="st", n=16, raw=True)



def pack_lhsT(w):
    o, i = w.shape
    return np.ascontiguousarray(
        w.reshape(o // 128, 128, i // 128, 128).transpose(0, 3, 2, 1)
        .reshape(o // 128, 128, i))


def pack_rhs(w):
    o, i = w.shape
    return np.ascontiguousarray(w.T.reshape(i // 128, 128, o))


def stage_l1(inputs):
    hs = np.ascontiguousarray(inputs["L0_attn_hidden_states"],
                              dtype=np.float32)
    com = {
        "wfc": pack_lhsT(np.asarray(inputs["L0_ffn_W_fc"], np.float32)),
        "wproj": pack_rhs(np.asarray(inputs["L0_ffn_W_proj"], np.float32)),
        "wkT": pack_lhsT(np.asarray(inputs["L0_attn_Wk"], np.float32)),
        "bk": np.asarray(inputs["L0_attn_bk"], np.float32),
    }
    return [dict(com, x0=np.ascontiguousarray(hs[b])) for b in range(NCORE)]


def run_l1(inputs):
    nc = build_l1()
    res = run_bass_kernel_spmd(nc, stage_l1(inputs),
                               core_ids=list(range(NCORE)))
    return res.results


def _run(build, maps):
    nc = build()
    return run_bass_kernel_spmd(nc, maps, core_ids=list(range(NCORE))).results


def kernel(**inputs):
    r1 = _run(build_l1, stage_l1(inputs))
    r2 = _run(build_l2, stage_l2(inputs, r1))
    r3 = _run(build_l3, stage_l3(inputs, r2))
    r4 = _run(build_l4, stage_l4(inputs, r3))
    out = np.empty((B, S, D), np.float32)
    for j in range(NCORE):
        out[:, j * 256:(j + 1) * 256, :] = \
            np.asarray(r4[j]["x6"]).reshape(B, 256, D)
    return out



def build_l2():
    nc = bass.Bass("TRN2", target_bir_lowering=False,
                   detect_race_conditions=False)
    x1 = nc.dram_tensor("x1", [TOK, D], F32, kind="ExternalInput").ap()
    kT_i = nc.dram_tensor("kT", [D, TOK], F32, kind="ExternalInput").ap()
    diag_i = nc.dram_tensor("diag", [TOK, H], F32, kind="ExternalInput").ap()
    maxes = nc.dram_tensor("maxes", [1, NCORE], F32, kind="ExternalInput").ap()
    wqT = nc.dram_tensor("wqT", [DC, 128, D], F32, kind="ExternalInput").ap()
    wvT = nc.dram_tensor("wvT", [DC, 128, D], F32, kind="ExternalInput").ap()
    woT = nc.dram_tensor("woT", [DC, 128, D], F32, kind="ExternalInput").ap()
    projT = nc.dram_tensor("projT", [DH, PM], F32, kind="ExternalInput").ap()
    bq = nc.dram_tensor("bq", [D], F32, kind="ExternalInput").ap()
    bv = nc.dram_tensor("bv", [D], F32, kind="ExternalInput").ap()
    bo = nc.dram_tensor("bo", [D], F32, kind="ExternalInput").ap()
    x2_o = nc.dram_tensor("x2", [TOK, D], F32, kind="ExternalOutput").ap()
    if DBG:
        dbg_ctx = nc.dram_tensor("dbg_ctx", [128, H * 2 * 65], F32,
                                 kind="ExternalOutput").ap()
        dbg_q = nc.dram_tensor("dbg_q", [128, DC * 512], F32,
                               kind="ExternalOutput").ap()
        dbg_v = nc.dram_tensor("dbg_v", [128, 4 * (H * 65 + 1)], F32,
                               kind="ExternalOutput").ap()
        dbg_kf = nc.dram_tensor("dbg_kf", [128, PM], F32,
                                kind="ExternalOutput").ap()
    eye_d = nc.inline_tensor(np.eye(128, dtype=np.float32), "eye").ap()
    onesrow_d = nc.inline_tensor(np.ones((1, 128), dtype=np.float32),
                                 "onesrow").ap()

    with ExitStack() as stack:
        ec = stack.enter_context
        p = Prog(nc, stack)
        x_sb = ec(nc.sbuf_tensor("x_sb", [128, NT * D], F32))
        mean_sb = ec(nc.sbuf_tensor("mean_sb", [128, NT], F32))
        rstd_sb = ec(nc.sbuf_tensor("rstd_sb", [128, NT], F32))
        sq_sb = ec(nc.sbuf_tensor("sq_sb", [128, D], F32))
        col0 = ec(nc.sbuf_tensor("col0", [128, 1], F32))
        eye_sb = ec(nc.sbuf_tensor("eye_sb", [128, 128], F32))
        ones_row = ec(nc.sbuf_tensor("ones_row", [1, 128], F32))
        setup_consts(nc, p, ec)
        xc_sb = ec(nc.sbuf_tensor("xc_sb", [128, D], F32))
        dg_sb = ec(nc.sbuf_tensor("dg_sb", [128, 128], F32))
        bias_sb = ec(nc.sbuf_tensor("bias_sb", [128, NT * H], F32))
        stab_sb = ec(nc.sbuf_tensor("stab_sb", [128, 1], F32))
        mrow = ec(nc.sbuf_tensor("mrow", [1, NCORE + 1], F32))
        projc = ec(nc.sbuf_tensor("projc", [128, 2 * PM], F32))
        bqc = ec(nc.sbuf_tensor("bqc", [128, DC], F32))
        bv_row = ec(nc.sbuf_tensor("bv_row", [1, D], F32))
        bo_row = ec(nc.sbuf_tensor("bo_row", [1, D], F32))
        diag_sb = ec(nc.sbuf_tensor("diag_sb", [128, NT * H], F32))
        ctx_sb = ec(nc.sbuf_tensor("ctx_sb", [128, H * 2 * 65], F32))
        xhT = ec(nc.sbuf_tensor("xhT", [128, DC * 512], F32))
        banks = [ec(nc.psum_tensor(f"ps{i}", [128, 512], F32))
                 for i in range(8)]
        pool = PsumPool(p, [b[:] for b in banks])
        jit = make_jit_xhat(p, pool, mean_sb[:], rstd_sb[:], eye_sb[:],
                            xc_sb[:], dg_sb[:])

        p.emit("sync", lambda e: e.dma_start(
            out=x_sb[:].rearrange("p (i d) -> p i d", i=NT),
            in_=x1.rearrange("(i p) d -> p i d", p=128)), inc="ld", n=16)
        p.emit("sync", lambda e: e.dma_start(out=eye_sb[:], in_=eye_d),
               inc="ld", n=16)
        p.emit("sync", lambda e: e.dma_start(out=ones_row[:], in_=onesrow_d),
               inc="ld", n=16)
        p.emit("sync", lambda e: e.dma_start(
            out=diag_sb[:].rearrange("p (i h) -> p i h", i=NT),
            in_=diag_i.rearrange("(i p) h -> p i h", p=128)), inc="ld", n=16)
        p.emit("sync", lambda e: e.dma_start(out=mrow[0:1, 0:NCORE],
                                             in_=maxes), inc="ld", n=16)
        for v in range(2):
            for mc in range(2):
                p.emit("sync", lambda e, v=v, mc=mc: e.dma_start(
                    out=projc[v * 64:v * 64 + 64,
                              v * PM + mc * 128:v * PM + (mc + 1) * 128],
                    in_=projT[:, mc * 128:(mc + 1) * 128]), inc="ld", n=16)
        p.emit("sync", lambda e: e.dma_start(
            out=bqc[:], in_=bq.rearrange("(c p) -> p c", p=128),
            allow_slow_non_contiguous=True), inc="ld", n=16)
        p.emit("sync", lambda e: e.dma_start(
            out=bv_row[:], in_=bv.rearrange("(a d) -> a d", a=1)),
            inc="ld", n=16)
        p.emit("sync", lambda e: e.dma_start(
            out=bo_row[:], in_=bo.rearrange("(a d) -> a d", a=1)),
            inc="ld", n=16)
        ldm = p.mark("ld")
        for e in ("tensor", "vector", "scalar", "gpsimd"):
            p.wait(e, "ld", ldm)

        DNORM = float(DH) ** -0.25
        p.emit("vector", lambda e: e.memset(projc[64:128, 0:PM], 0.0))
        p.emit("vector", lambda e: e.memset(projc[0:64, PM:2 * PM], 0.0))
        p.emit("scalar", lambda e: e.activation(
            out=projc[:], in_=projc[:], func=AF.Copy, scale=DNORM))
        p.wait("scalar", "v", p.mark("v"))
        p.emit("vector", lambda e: e.tensor_reduce(
            out=mrow[0:1, NCORE:NCORE + 1], in_=mrow[0:1, 0:NCORE],
            axis=AX.X, op=OP.max))
        p.wait("tensor", "v", p.mark("v"))
        bi0, bank0 = pool.acquire()
        p.emit("tensor", lambda e, bank=bank0: e.matmul(
            bank[0:128, 0:1], ones_row[:], mrow[0:1, NCORE:NCORE + 1],
            start=True, stop=True), inc="t")
        p.wait("scalar", "t", p.mark("t"))
        p.emit("scalar", lambda e, bank=bank0: e.activation(
            out=stab_sb[:], in_=bank[0:128, 0:1], func=AF.Copy))
        pool.release(bi0, "s")
        p.wait("vector", "s", p.mark("s"))
        p.emit("vector", lambda e: e.tensor_scalar(
            out=bias_sb[:], in0=diag_sb[:], scalar1=stab_sb[:, 0:1],
            scalar2=None, op0=OP.subtract))
        p.emit("vector", lambda e: e.memset(ctx_sb[:], 0.0))
        ln_stats(p, x_sb[:], mean_sb[:], rstd_sb[:], sq_sb[:], col0[:],
                 range(NT))
        p.barrier()

        with nc.sbuf_tensor("wvs", [128, DC * D], F32) as wvs, \
                nc.sbuf_tensor("vblk", [128, 4 * H * 65 + 4], F32) as vblk, \
                nc.sbuf_tensor("kTb", [128, DC * 512], F32) as kTb, \
                nc.sbuf_tensor("kf_sb", [128, 2 * PM], F32) as kf_sb:
            VW = H * 65 + 1
            p.emit("sync", lambda e: e.dma_start(
                out=wvs[:].rearrange("p (c d) -> p c d", c=DC),
                in_=wvT.rearrange("c p d -> p c d")), inc="ld", n=16)
            wv_ld = p.mark("ld")
            for tb in range(4):
                tiles = list(range(tb * 4, tb * 4 + 4))
                if tb > 0:
                    p.wait("sync", "t", p.mark("t"))
                p.emit("sync", lambda e, tb=tb: e.dma_start(
                    out=kTb[:].rearrange("p (c q) -> p c q", c=DC),
                    in_=kT_i.rearrange("(c p) t -> p c t", p=128)
                        [:, :, tb * 512:(tb + 1) * 512]), inc="ld", n=16)
                ktb_ld = p.mark("ld")
                jit(tiles, x_sb[:], xhT[:])
                xh_ready = p.mark("s")
                p.wait("vector", "t", p.mark("t"))
                for sub in range(4):
                    p.emit("vector", lambda e, sub=sub: e.memset(
                        vblk[:, sub * VW + 64:sub * VW + H * 65:65], 1.0))
                ones_mark = p.mark("v")
                p.wait("tensor", "ld", wv_ld)
                p.wait("tensor", "s", xh_ready)
                for sub in range(4):
                    for ob in range(2):
                        bi, bank = pool.acquire()
                        for f in range(DC):
                            p.emit("tensor", lambda e, f=f, sub=sub, ob=ob,
                                   bank=bank: e.matmul(
                                bank[:],
                                xhT[:, f * 512 + sub * 128:
                                    f * 512 + (sub + 1) * 128],
                                wvs[:, f * D + ob * 512:
                                    f * D + (ob + 1) * 512],
                                start=(f == 0), stop=False))
                        p.emit("tensor", lambda e, ob=ob, bank=bank: e.matmul(
                            bank[:], ones_row[:],
                            bv_row[0:1, ob * 512:(ob + 1) * 512],
                            start=False, stop=True), inc="t")
                        p.wait("scalar", "t", p.mark("t"))
                        if sub == 0 and ob == 0:
                            p.wait("scalar", "v", ones_mark)
                        p.emit("scalar", lambda e, sub=sub, ob=ob, bank=bank:
                               e.activation(
                            out=vblk[:, sub * VW + ob * 8 * 65:
                                     sub * VW + ob * 8 * 65 + 8 * 65]
                            .rearrange("p (h e) -> p h e", h=8)
                            [:, :, 0:64],
                            in_=bank[:].rearrange("p (h e) -> p h e", h=8),
                            func=AF.Copy), inc="s")
                        pool.release(bi, "s")
                v_ready = p.mark("s")
                p.wait("tensor", "ld", ktb_ld)

                def emit_dashk(i):
                    h, sub = divmod(i, 4)
                    hp, hv = h // 2, h % 2
                    bi, bank = pool.acquire()
                    p.emit("tensor", lambda e, hp=hp, hv=hv, sub=sub,
                           bank=bank: e.matmul(
                        bank[:, 0:PM],
                        kTb[:, hp * 512 + sub * 128:
                            hp * 512 + (sub + 1) * 128],
                        projc[:, hv * PM:(hv + 1) * PM],
                        start=True, stop=True), inc="t")
                    return bi, bank, p.mark("t")

                pend = emit_dashk(0)
                ctx_marks = {}
                for i in range(4 * H):
                    h, sub = divmod(i, 4)
                    sl = (i % 2) * PM
                    bi, bank, dmark = pend
                    if i + 1 < 4 * H:
                        pend = emit_dashk(i + 1)
                    p.wait("scalar", "t", dmark)
                    if i - 2 in ctx_marks:
                        p.wait("scalar", "t", ctx_marks.pop(i - 2))
                    tile = tb * 4 + sub
                    p.emit("scalar", lambda e, tile=tile, h=h, sl=sl,
                           bank=bank: e.activation(
                        out=kf_sb[:, sl:sl + PM], in_=bank[:, 0:PM],
                        func=AF.Exp,
                        bias=bias_sb[:, tile * H + h:tile * H + h + 1]),
                        inc="s")
                    pool.release(bi, "s")
                    p.wait("vector", "s", p.mark("s"))
                    p.emit("vector", lambda e, sl=sl: e.tensor_scalar(
                        out=kf_sb[:, sl:sl + PM], in0=kf_sb[:, sl:sl + PM],
                        scalar1=KEPS, scalar2=None, op0=OP.add))
                    kf_ready = p.mark("v")
                    p.wait("tensor", "v", kf_ready)
                    p.wait("tensor", "s", v_ready)
                    for mc in range(2):
                        bi2, bank2 = pool.acquire()
                        p.emit("tensor", lambda e, mc=mc, h=h, sub=sub,
                               sl=sl, bank2=bank2: e.matmul(
                            bank2[:, 0:65],
                            kf_sb[:, sl + mc * 128:sl + (mc + 1) * 128],
                            vblk[:, sub * VW + h * 65:
                                 sub * VW + (h + 1) * 65],
                            start=True, stop=True), inc="t")
                        p.wait("vector", "t", p.mark("t"))
                        p.emit("vector", lambda e, h=h, mc=mc,
                               bank2=bank2: e.tensor_tensor(
                            out=ctx_sb[:, (h * 2 + mc) * 65:
                                       (h * 2 + mc + 1) * 65],
                            in0=ctx_sb[:, (h * 2 + mc) * 65:
                                       (h * 2 + mc + 1) * 65],
                            in1=bank2[:, 0:65], op=OP.add))
                        pool.release(bi2, "v")
                    ctx_marks[i] = p.mark("t")
            if DBG:
                p.wait("gpsimd", "v", p.mark("v"))
                p.wait("gpsimd", "s", p.mark("s"))
                p.emit("gpsimd", lambda e: e.dma_start(out=dbg_v,
                                                       in_=vblk[:]),
                       inc="st", n=16, raw=True)
                p.emit("gpsimd", lambda e: e.dma_start(out=dbg_kf,
                                                       in_=kf_sb[:]),
                       inc="st", n=16, raw=True)
                p.emit("gpsimd", lambda e: e.dma_start(out=dbg_ctx,
                                                       in_=ctx_sb[:]),
                       inc="st", n=16, raw=True)
                p.wait("gpsimd", "st", p.cnt["st"])
        p.barrier()

        with nc.sbuf_tensor("wqs", [128, DC * D], F32) as wqs, \
                nc.sbuf_tensor("qTb", [128, DC * 512], F32) as qTb, \
                nc.sbuf_tensor("qf_sb", [128, 2 * 512], F32) as qf_sb, \
                nc.sbuf_tensor("on_sb", [128, 128], F32) as on_sb, \
                nc.sbuf_tensor("den_sb", [128, 512], F32) as den_sb, \
                nc.sbuf_tensor("thi", [128, DC * 512], F32) as thi, \
                nc.sbuf_tensor("outT", [128, DC * 512], F32) as outT, \
                nc.sbuf_tensor("wo0", [128, D], F32) as wo0, \
                nc.sbuf_tensor("wo1", [128, D], F32) as wo1:
            p.emit("sync", lambda e: e.dma_start(
                out=wqs[:].rearrange("p (c d) -> p c d", c=DC),
                in_=wqT.rearrange("c p d -> p c d")), inc="ld", n=16)
            wq_ld = p.mark("ld")
            wo = DBuf(p, [wo0[:], wo1[:]], "sync", "ld")
            p.emit("vector", lambda e: e.memset(on_sb[:], 1.0))
            for tb in range(4):
                tiles = list(range(tb * 4, tb * 4 + 4))
                jit(tiles, x_sb[:], xhT[:])
                xh_ready = p.mark("s")
                p.wait("tensor", "ld", wq_ld)
                p.wait("tensor", "s", xh_ready)
                for hp in range(DC):
                    bi, bank = pool.acquire()
                    for f in range(DC):
                        p.emit("tensor", lambda e, f=f, hp=hp, bank=bank:
                               e.matmul(
                            bank[:], wqs[:, hp * D + f * 128:
                                         hp * D + (f + 1) * 128],
                            xhT[:, f * 512:(f + 1) * 512],
                            start=(f == 0), stop=(f == DC - 1)), inc="t")
                    p.wait("scalar", "t", p.mark("t"))
                    p.emit("scalar", lambda e, hp=hp, bank=bank: e.activation(
                        out=qTb[:, hp * 512:(hp + 1) * 512], in_=bank[:],
                        func=AF.Identity, bias=bqc[:, hp:hp + 1]), inc="s")
                    pool.release(bi, "s")
                q_ready = p.mark("s")

                def emit_dash(i):
                    hp, hv = divmod(i, 2)
                    sbanks = []
                    p.wait("tensor", "s", q_ready)
                    for mc in range(2):
                        bi, bank = pool.acquire()
                        sbanks.append((bi, bank))
                        p.emit("tensor", lambda e, hp=hp, hv=hv, mc=mc,
                               bank=bank: e.matmul(
                            bank[:],
                            projc[:, hv * PM + mc * 128:
                                  hv * PM + (mc + 1) * 128],
                            qTb[:, hp * 512:(hp + 1) * 512],
                            start=True, stop=True),
                            inc="t" if mc == 1 else None)
                    return sbanks, p.mark("t")

                pend = emit_dash(0)
                num_mark = pend[1]
                state = {"mult": 0, "bcast": 0}

                def do_normalize(bi, bank, hp, hv, nmark):
                    p.wait("vector", "t", max(nmark, state["bcast"]))
                    p.emit("vector", lambda e, bank=bank: e.reciprocal(
                        out=den_sb[64:65, :], in_=bank[64:65, :]))
                    p.wait("tensor", "v", p.mark("v"))
                    bi2, bank2 = pool.acquire()
                    p.emit("tensor", lambda e, bank2=bank2: e.matmul(
                        bank2[0:64, :], on_sb[64:65, 0:64],
                        den_sb[64:65, :], start=True, stop=True), inc="t")
                    state["bcast"] = p.mark("t")
                    p.wait("scalar", "t", state["bcast"])
                    p.wait("scalar", "v", state["mult"])
                    p.emit("scalar", lambda e, bank2=bank2: e.activation(
                        out=den_sb[0:64, :], in_=bank2[0:64, :],
                        func=AF.Copy), inc="s")
                    pool.release(bi2, "s")
                    p.wait("vector", "s", p.mark("s"))
                    acc = outT if hv == 0 else thi
                    p.emit("vector", lambda e, hp=hp, acc=acc, bank=bank:
                           e.tensor_tensor(
                        out=acc[0:64, hp * 512:(hp + 1) * 512],
                        in0=bank[0:64, :], in1=den_sb[0:64, :],
                        op=OP.mult))
                    state["mult"] = p.mark("v")
                    pool.release(bi, "v")

                norm_pend = None
                for i in range(2 * DC):
                    hp, hv = divmod(i, 2)
                    h = hp * 2 + hv
                    qf = qf_sb[:]
                    sbanks, dmark = pend
                    if i + 1 < 2 * DC:
                        pend = emit_dash(i + 1)
                    p.wait("scalar", "t", num_mark)
                    for mc in range(2):
                        bi, bank = sbanks[mc]
                        p.emit("scalar", lambda e, mc=mc, qf=qf, bank=bank:
                               e.activation(
                            out=qf[:, mc * 512:(mc + 1) * 512],
                            in_=bank[:], func=AF.Exp), inc="s")
                        pool.release(bi, "s")
                    p.wait("vector", "s", p.mark("s"))
                    p.emit("vector", lambda e, qf=qf: e.tensor_scalar(
                        out=qf, in0=qf, scalar1=KEPS,
                        scalar2=None, op0=OP.add))
                    qf_ready = p.mark("v")
                    p.wait("tensor", "v", qf_ready)
                    bi, bank = pool.acquire()
                    for mc in range(2):
                        p.emit("tensor", lambda e, mc=mc, h=h, qf=qf,
                               bank=bank: e.matmul(
                            bank[0:65, :],
                            ctx_sb[:, (h * 2 + mc) * 65:
                                   (h * 2 + mc + 1) * 65],
                            qf[:, mc * 512:(mc + 1) * 512],
                            start=(mc == 0), stop=(mc == 1)),
                            inc="t" if mc == 1 else None)
                    num_mark = p.mark("t")
                    if norm_pend is not None:
                        do_normalize(*norm_pend)
                    norm_pend = (bi, bank, hp, hv, num_mark)
                do_normalize(*norm_pend)
                p.wait("sync", "v", p.mark("v"))
                p.emit("sync", lambda e: e.dma_start(
                    out=outT[64:128, :], in_=thi[0:64, :]), inc="ld", n=16)
                p.wait("tensor", "ld", p.mark("ld"))
                out_ready = p.mark("v")
                if DBG and tb == 3:
                    p.wait("gpsimd", "s", p.mark("s"))
                    p.emit("gpsimd", lambda e: e.dma_start(out=dbg_q,
                                                           in_=qTb[:]),
                           inc="st", n=16, raw=True)
                p.wait("tensor", "v", out_ready)
                for k in range(8):
                    pool.acquire()
                zb = pool.banks
                for ic in range(DC):
                    buf, ldm2 = wo.load(lambda e, buf, ic=ic: e.dma_start(
                        out=buf[:], in_=woT[ic]))
                    p.wait("tensor", "ld", ldm2)
                    for sub in range(4):
                        for ob in range(2):
                            p.emit("tensor", lambda e, ic=ic, sub=sub, ob=ob,
                                   buf=buf: e.matmul(
                                zb[sub * 2 + ob][:],
                                outT[:, ic * 512 + sub * 128:
                                     ic * 512 + (sub + 1) * 128],
                                buf[:, ob * 512:(ob + 1) * 512],
                                start=(ic == 0), stop=False),
                                inc="t" if (sub == 3 and ob == 1) else None)
                    wo.consumed()
                for sub in range(4):
                    for ob in range(2):
                        p.emit("tensor", lambda e, sub=sub, ob=ob: e.matmul(
                            zb[sub * 2 + ob][:], ones_row[:],
                            bo_row[0:1, ob * 512:(ob + 1) * 512],
                            start=False, stop=True),
                            inc="t" if (sub == 3 and ob == 1) else None)
                wo_done = p.mark("t")
                p.wait("vector", "t", wo_done)
                for sub in range(4):
                    tile = tb * 4 + sub
                    for ob in range(2):
                        p.emit("vector", lambda e, tile=tile, sub=sub, ob=ob:
                               e.scalar_tensor_tensor(
                            out=x_sb[:, tile * D + ob * 512:
                                     tile * D + (ob + 1) * 512],
                            in0=zb[sub * 2 + ob][:], scalar=1.0,
                            in1=x_sb[:, tile * D + ob * 512:
                                     tile * D + (ob + 1) * 512],
                            op0=OP.mult, op1=OP.add))
                for k in range(8):
                    pool.release(k, "v")
        p.barrier()
        p.emit("gpsimd", lambda e: e.dma_start(
            out=x2_o.rearrange("(i p) d -> p i d", p=128),
            in_=x_sb[:].rearrange("p (i d) -> p i d", i=NT)),
            inc="st", n=16, raw=True)
        p.wait("gpsimd", "st", p.cnt["st"])
        p.build()
    return nc


def stage_l2(inputs, r1):
    mx = np.ascontiguousarray(
        np.array([[float(r1[b]["lmax"][0, 0]) for b in range(NCORE)]],
                 np.float32))
    com = {
        "wqT": pack_lhsT(np.asarray(inputs["L0_attn_Wq"], np.float32)),
        "wvT": pack_rhs(np.asarray(inputs["L0_attn_Wv"], np.float32)),
        "woT": pack_rhs(np.asarray(inputs["L0_attn_Wo"], np.float32)),
        "projT": np.ascontiguousarray(
            np.asarray(inputs["L0_attn_proj"], np.float32).T),
        "bq": np.asarray(inputs["L0_attn_bq"], np.float32),
        "bv": np.asarray(inputs["L0_attn_bv"], np.float32),
        "bo": np.asarray(inputs["L0_attn_bo"], np.float32),
        "maxes": mx,
    }
    return [dict(com, x1=np.ascontiguousarray(r1[b]["x1"]),
                 kT=np.ascontiguousarray(r1[b]["kT"]),
                 diag=np.ascontiguousarray(r1[b]["diag"]))
            for b in range(NCORE)]





def emit_ffn1_chain(nc, p, pool, jit, ugT, uoT, woR, x_src_d,
                    x_out_o, zx_d, eye_sb, boc, incc, dbg=None):
    NSEQ = 256
    with nc.sbuf_tensor("c_fm", [128, 2 * DC * NSEQ], F32) as c_fm, \
            nc.sbuf_tensor("cuo_tm", [128, 2 * D], F32) as cuo_tm, \
            nc.sbuf_tensor("gt_sb", [128, 3 * NSEQ], F32) as gt_sb, \
            nc.sbuf_tensor("ones2", [128, NSEQ], F32) as ones2:
        _ug_cm = nc.sbuf_tensor("ug_sb", [128, 3 * DC * D], F32)
        ug_sb = _ug_cm.__enter__()
        ZW = 3 * DC * NSEQ
        _zxs_cm = nc.sbuf_tensor("zxs", [128, 2 * ZW], F32)
        zxs = _zxs_cm.__enter__()
        CP = DC * NSEQ
        p.emit("sync", lambda e: e.dma_start(
            out=ug_sb[:].rearrange("p (g c d) -> p g c d", g=3, c=DC),
            in_=ugT.rearrange("g c p d -> p g c d")), inc="ld", n=16)
        ug_ld = p.mark("ld")
        p.emit("vector", lambda e: e.memset(ones2[:], 1.0))
        for cfc in range(DC):
            p.emit("vector", lambda e, cfc=cfc: e.tensor_scalar(
                out=c_fm[:, cfc * NSEQ:(cfc + 1) * NSEQ], in0=ones2[:],
                scalar1=incc[:, cfc:cfc + 1], scalar2=None, op0=OP.mult))
        p.wait("tensor", "ld", ug_ld)

        def zx_load(t):
            p.wait("sync", "v", p.mark("v"))
            p.emit("sync", lambda e, t=t: e.dma_start(
                out=zxs[:, (t % 2) * ZW:(t % 2 + 1) * ZW]
                .rearrange("p (g c q) -> p g c q", g=3, c=DC),
                in_=zx_d.rearrange("g c p t -> p g c t")
                    [:, :, :, t * NSEQ:(t + 1) * NSEQ]), inc="ld", n=16)
            return p.mark("ld")

        zx_marks = {0: zx_load(0)}
        for t in range(8):
            par, npar = (t % 2) * CP, ((t + 1) % 2) * CP
            zo = (t % 2) * ZW
            if t + 1 < 8:
                zx_marks[t + 1] = zx_load(t + 1)
            p.wait("vector", "ld", zx_marks[t])
            p.wait("tensor", "v", p.mark("v"))
            for mc in range(DC):
                gbanks = []
                for g in range(3):
                    bi, bank = pool.acquire()
                    gbanks.append((bi, bank))
                    for f in range(DC):
                        p.emit("tensor", lambda e, g=g, f=f, mc=mc,
                               par=par, bank=bank: e.matmul(
                            bank[:, 0:NSEQ],
                            ug_sb[:, g * DC * D + mc * D + f * 128:
                                  g * DC * D + mc * D + (f + 1) * 128],
                            c_fm[:, par + f * NSEQ:par + (f + 1) * NSEQ],
                            start=(f == 0), stop=(f == DC - 1)),
                            inc="t")
                p.wait("vector", "t", p.mark("t"))
                for g in range(3):
                    bi, bank = gbanks[g]
                    p.emit("vector", lambda e, g=g, mc=mc, zo=zo, bank=bank:
                           e.tensor_tensor(
                        out=gt_sb[:, g * NSEQ:(g + 1) * NSEQ],
                        in0=bank[:, 0:NSEQ],
                        in1=zxs[:, zo + (g * DC + mc) * NSEQ:
                                zo + (g * DC + mc + 1) * NSEQ], op=OP.add))
                    pool.release(bi, "v")
                p.wait("scalar", "v", p.mark("v"))
                for g in range(3):
                    fn = AF.Sigmoid if g < 2 else AF.Tanh
                    p.emit("scalar", lambda e, g=g, fn=fn: e.activation(
                        out=gt_sb[:, g * NSEQ:(g + 1) * NSEQ],
                        in_=gt_sb[:, g * NSEQ:(g + 1) * NSEQ], func=fn),
                        inc="s")
                p.wait("vector", "s", p.mark("s"))
                p.emit("vector", lambda e, mc=mc, par=par, npar=npar:
                       e.tensor_mul(
                    out=c_fm[:, npar + mc * NSEQ:npar + (mc + 1) * NSEQ],
                    in0=c_fm[:, par + mc * NSEQ:par + (mc + 1) * NSEQ],
                    in1=gt_sb[:, 0:NSEQ]))
                p.emit("vector", lambda e: e.tensor_mul(
                    out=gt_sb[:, NSEQ:2 * NSEQ],
                    in0=gt_sb[:, NSEQ:2 * NSEQ],
                    in1=gt_sb[:, 2 * NSEQ:3 * NSEQ]))
                p.emit("vector", lambda e, mc=mc, npar=npar: e.tensor_add(
                    out=c_fm[:, npar + mc * NSEQ:npar + (mc + 1) * NSEQ],
                    in0=c_fm[:, npar + mc * NSEQ:npar + (mc + 1) * NSEQ],
                    in1=gt_sb[:, NSEQ:2 * NSEQ]))
        cpar = (8 % 2) * CP
        p.barrier()
        if dbg is not None:
            p.emit("gpsimd", lambda e: e.dma_start(out=dbg[0], in_=c_fm[:]),
                   inc="st", n=16, raw=True)
            p.emit("gpsimd", lambda e: e.dma_start(out=dbg[1], in_=zxs[:]),
                   inc="st", n=16, raw=True)
            p.wait("gpsimd", "st", p.cnt["st"])
        _zxs_cm.__exit__(None, None, None)
        _ug_cm.__exit__(None, None, None)
        with nc.sbuf_tensor("uo_sb", [128, DC * D], F32) as uo_sb, \
                nc.sbuf_tensor("cuoT", [128, DC * NSEQ], F32) as cuoT:
            p.emit("sync", lambda e: e.dma_start(
                out=uo_sb[:].rearrange("p (c d) -> p c d", c=DC),
                in_=uoT.rearrange("c p d -> p c d")), inc="ld", n=16)
            p.wait("tensor", "ld", p.mark("ld"))
            for oc in range(DC):
                bi, bank = pool.acquire()
                for f in range(DC):
                    p.emit("tensor", lambda e, oc=oc, f=f, bank=bank:
                           e.matmul(
                        bank[:, 0:NSEQ],
                        uo_sb[:, oc * D + f * 128:oc * D + (f + 1) * 128],
                        c_fm[:, cpar + f * NSEQ:cpar + (f + 1) * NSEQ],
                        start=(f == 0), stop=(f == DC - 1)), inc="t")
                p.wait("scalar", "t", p.mark("t"))
                p.emit("scalar", lambda e, oc=oc, bank=bank: e.activation(
                    out=cuoT[:, oc * NSEQ:(oc + 1) * NSEQ],
                    in_=bank[:, 0:NSEQ], func=AF.Identity,
                    bias=boc[:, oc:oc + 1]), inc="s")
                pool.release(bi, "s")
            p.wait("tensor", "s", p.mark("s"))
            for oc in range(DC):
                for ss in range(2):
                    bi, bank = pool.acquire()
                    p.emit("tensor", lambda e, oc=oc, ss=ss, bank=bank:
                           e.matmul(
                        bank[:, 0:128],
                        cuoT[:, oc * NSEQ + ss * 128:
                             oc * NSEQ + (ss + 1) * 128],
                        eye_sb[:], start=True, stop=True), inc="t")
                    p.wait("scalar", "t", p.mark("t"))
                    p.emit("scalar", lambda e, oc=oc, ss=ss, bank=bank:
                           e.activation(
                        out=cuo_tm[:, ss * D + oc * 128:
                                   ss * D + (oc + 1) * 128],
                        in_=bank[:, 0:128], func=AF.Copy), inc="s")
                    pool.release(bi, "s")
            p.barrier()
            if dbg is not None:
                p.emit("gpsimd", lambda e: e.dma_start(out=dbg[2],
                                                       in_=cuo_tm[:]),
                       inc="st", n=16, raw=True)
            with nc.sbuf_tensor("wo_sb", [128, DC * D], F32) as wo_sb, \
                    nc.sbuf_tensor("xb_sb", [128, 2 * 2 * D], F32) as xb_sb, \
                    nc.sbuf_tensor("xhTb", [128, DC * 256], F32) as xhTb, \
                    nc.sbuf_tensor("hz_sb", [128, D], F32) as hz_sb:
                p.emit("sync", lambda e: e.dma_start(
                    out=wo_sb[:].rearrange("p (c d) -> p c d", c=DC),
                    in_=woR.rearrange("c p d -> p c d")), inc="ld", n=16)
                wo_ld = p.mark("ld")
                p.wait("tensor", "ld", wo_ld)

                def xb_load(t):
                    p.wait("sync", "v", p.mark("v"))
                    p.wait("sync", "s", p.mark("s"))
                    p.wait("sync", "st", p.mark("st"))
                    p.emit("sync", lambda e, t=t: e.dma_start(
                        out=xb_sb[:, (t % 2) * 2 * D:(t % 2 + 1) * 2 * D]
                        .rearrange("p (i d) -> p i d", i=2),
                        in_=x_src_d.rearrange("(i p) d -> p i d", p=128)
                            [:, 2 * t:2 * t + 2, :]), inc="ld", n=16)
                    return p.mark("ld")

                xb_marks = {0: xb_load(0)}
                for t in range(8):
                    xb = xb_sb[:, (t % 2) * 2 * D:(t % 2 + 1) * 2 * D]
                    if t + 1 < 8:
                        xb_marks[t + 1] = xb_load(t + 1)
                    for e in ("vector", "tensor"):
                        p.wait(e, "ld", xb_marks[t])
                    jit([0, 1], xb, xhTb[:],
                        stats=[2 * t, 2 * t + 1])
                    xh_ready = p.mark("s")
                    p.wait("tensor", "s", xh_ready)
                    for ss in range(2):
                        for ob in range(2):
                            bi, bank = pool.acquire()
                            for f in range(DC):
                                p.emit("tensor", lambda e, f=f, ss=ss,
                                       ob=ob, bank=bank: e.matmul(
                                    bank[:],
                                    xhTb[:, f * 256 + ss * 128:
                                         f * 256 + (ss + 1) * 128],
                                    wo_sb[:, f * D + ob * 512:
                                          f * D + (ob + 1) * 512],
                                    start=(f == 0), stop=(f == DC - 1)),
                                    inc="t")
                            p.wait("vector", "t", p.mark("t"))
                            p.emit("vector", lambda e, ss=ss, ob=ob,
                                   bank=bank: e.tensor_add(
                                out=hz_sb[:, ob * 512:(ob + 1) * 512],
                                in0=bank[:],
                                in1=cuo_tm[:, ss * D + ob * 512:
                                           ss * D + (ob + 1) * 512]))
                            pool.release(bi, "v")
                        p.wait("scalar", "v", p.mark("v"))
                        p.emit("scalar", lambda e: e.activation(
                            out=hz_sb[:], in_=hz_sb[:], func=AF.Tanh),
                            inc="s")
                        p.wait("vector", "s", p.mark("s"))
                        p.emit("vector", lambda e, ss=ss, xb=xb:
                               e.scalar_tensor_tensor(
                            out=xb[:, ss * D:(ss + 1) * D],
                            in0=hz_sb[:], scalar=0.5,
                            in1=xb[:, ss * D:(ss + 1) * D],
                            op0=OP.mult, op1=OP.add))
                    p.wait("gpsimd", "v", p.mark("v"))
                    p.emit("gpsimd", lambda e, t=t, xb=xb: e.dma_start(
                        out=x_out_o.rearrange("(i p) d -> p i d", p=128)
                            [:, 2 * t:2 * t + 2, :],
                        in_=xb.rearrange("p (i d) -> p i d", i=2)),
                        inc="st", n=16, raw=True)



def build_l3():
    nc = bass.Bass("TRN2", target_bir_lowering=False,
                   detect_race_conditions=False)
    x2 = nc.dram_tensor("x2", [TOK, D], F32, kind="ExternalInput").ap()
    wfc = nc.dram_tensor("wfc", [FC, 128, D], F32, kind="ExternalInput").ap()
    wproj = nc.dram_tensor("wproj", [FC, 128, D], F32,
                           kind="ExternalInput").ap()
    wgT = nc.dram_tensor("wgT", [3, DC, 128, D], F32,
                         kind="ExternalInput").ap()
    ugT = nc.dram_tensor("ugT", [3, DC, 128, D], F32,
                         kind="ExternalInput").ap()
    uoT = nc.dram_tensor("uoT", [DC, 128, D], F32, kind="ExternalInput").ap()
    woR = nc.dram_tensor("woR", [DC, 128, D], F32, kind="ExternalInput").ap()
    bg = nc.dram_tensor("bg", [3, D], F32, kind="ExternalInput").ap()
    bo_ff = nc.dram_tensor("bo_ff", [D], F32, kind="ExternalInput").ap()
    inc_ff = nc.dram_tensor("inc_ff", [D], F32, kind="ExternalInput").ap()
    x4_o = nc.dram_tensor("x4", [TOK, D], F32, kind="ExternalOutput").ap()
    x3_d = nc.dram_tensor("x3_d", [TOK, D], F32, kind="Internal").ap()
    if DBG:
        dbg_x3 = nc.dram_tensor("dbg_x3", [TOK, D], F32,
                                kind="ExternalOutput").ap()
        dbg_c = nc.dram_tensor("dbg_c", [128, 2 * DC * 256], F32,
                               kind="ExternalOutput").ap()
        dbg_cuo = nc.dram_tensor("dbg_cuo", [128, 2 * D], F32,
                                 kind="ExternalOutput").ap()
        dbg_zx = nc.dram_tensor("dbg_zx", [128, 3 * DC * 256], F32,
                                kind="ExternalOutput").ap()
    zx_d = nc.dram_tensor("zx_d", [3, DC, 128, TOK], F32, kind="Internal").ap()
    eye_d = nc.inline_tensor(np.eye(128, dtype=np.float32), "eye").ap()

    NSEQ = 256
    with ExitStack() as stack:
        ec = stack.enter_context
        p = Prog(nc, stack)
        mean_sb = ec(nc.sbuf_tensor("mean_sb", [128, NT], F32))
        rstd_sb = ec(nc.sbuf_tensor("rstd_sb", [128, NT], F32))
        sq_sb = ec(nc.sbuf_tensor("sq_sb", [128, D], F32))
        col0 = ec(nc.sbuf_tensor("col0", [128, 1], F32))
        eye_sb = ec(nc.sbuf_tensor("eye_sb", [128, 128], F32))
        setup_consts(nc, p, ec)
        xc_sb = ec(nc.sbuf_tensor("xc_sb", [128, 2 * D], F32))
        dg_sb = ec(nc.sbuf_tensor("dg_sb", [128, 2 * 128], F32))
        bgc = ec(nc.sbuf_tensor("bgc", [128, 3 * DC], F32))
        boc = ec(nc.sbuf_tensor("boc", [128, DC], F32))
        incc = ec(nc.sbuf_tensor("incc", [128, DC], F32))
        banks = [ec(nc.psum_tensor(f"ps{i}", [128, 512], F32))
                 for i in range(8)]
        pool = PsumPool(p, [b[:] for b in banks])
        jit = make_jit_xhat(p, pool, mean_sb[:], rstd_sb[:], eye_sb[:],
                            xc_sb[:], dg_sb[:], depth=2)

        p.emit("sync", lambda e: e.dma_start(out=eye_sb[:], in_=eye_d),
               inc="ld", n=16)
        p.emit("sync", lambda e: e.dma_start(
            out=bgc[:].rearrange("p (g c) -> p g c", g=3),
            in_=bg.rearrange("g (c p) -> p g c", p=128),
            allow_slow_non_contiguous=True), inc="ld", n=16)
        p.emit("sync", lambda e: e.dma_start(
            out=boc[:], in_=bo_ff.rearrange("(c p) -> p c", p=128),
            allow_slow_non_contiguous=True), inc="ld", n=16)
        p.emit("sync", lambda e: e.dma_start(
            out=incc[:], in_=inc_ff.rearrange("(c p) -> p c", p=128),
            allow_slow_non_contiguous=True), inc="ld", n=16)

        with nc.sbuf_tensor("x_sb", [128, NT * D], F32) as x_sb, \
                nc.sbuf_tensor("xhT", [128, DC * 512], F32) as xhT:
            p.emit("sync", lambda e: e.dma_start(
                out=x_sb[:].rearrange("p (i d) -> p i d", i=NT),
                in_=x2.rearrange("(i p) d -> p i d", p=128)), inc="ld", n=16)
            ldm = p.mark("ld")
            for e in ("tensor", "vector", "scalar", "gpsimd"):
                p.wait(e, "ld", ldm)
            ln_stats(p, x_sb[:], mean_sb[:], rstd_sb[:], sq_sb[:], col0[:],
                     range(NT))
            p.barrier()
            with nc.sbuf_tensor("h_sb", [128, FC * 512], F32) as h_sb, \
                    nc.sbuf_tensor("r_sb", [128, 512], F32) as r_sb, \
                    nc.sbuf_tensor("z_sb", [128, D], F32) as z_sb, \
                    nc.sbuf_tensor("mem_sb", [128, D], F32) as mem_sb, \
                    nc.sbuf_tensor("spk_sb", [128, D], F32) as spk_sb, \
                    nc.sbuf_tensor("acc_sb", [128, D], F32) as acc_sb, \
                    nc.sbuf_tensor("wfc0", [128, D], F32) as wfc0, \
                    nc.sbuf_tensor("wfc1", [128, D], F32) as wfc1, \
                    nc.sbuf_tensor("wp0", [128, D], F32) as wp0, \
                    nc.sbuf_tensor("wp1", [128, D], F32) as wp1:
                sb = dict(xhT=xhT[:], h=h_sb[:], r=r_sb[:], z=z_sb[:],
                          mem=mem_sb[:], spk=spk_sb[:], acc=acc_sb[:],
                          wfc=[wfc0[:], wfc1[:]], wproj=[wp0[:], wp1[:]])
                emit_ffn0(p, pool, jit, x_sb[:], wfc, wproj, sb)
            p.emit("gpsimd", lambda e: e.dma_start(
                out=x3_d.rearrange("(i p) d -> p i d", p=128),
                in_=x_sb[:].rearrange("p (i d) -> p i d", i=NT)),
                inc="st", n=16, raw=True)
            if DBG:
                p.emit("gpsimd", lambda e: e.dma_start(
                    out=dbg_x3.rearrange("(i p) d -> p i d", p=128),
                    in_=x_sb[:].rearrange("p (i d) -> p i d", i=NT)),
                    inc="st", n=16, raw=True)
            ln_stats(p, x_sb[:], mean_sb[:], rstd_sb[:], sq_sb[:], col0[:],
                     range(NT))
            p.barrier()
            with nc.sbuf_tensor("wg_sb", [128, 2 * DC * D], F32) as wg_sb, \
                    nc.sbuf_tensor("zx_sb", [128, DC * 512], F32) as zx_sb:
                def wg_load(g):
                    p.wait("sync", "t", p.mark("t"))
                    p.emit("sync", lambda e, g=g: e.dma_start(
                        out=wg_sb[:, (g % 2) * DC * D:(g % 2 + 1) * DC * D]
                        .rearrange("p (c d) -> p c d", c=DC),
                        in_=wgT[g].rearrange("c p d -> p c d")),
                        inc="ld", n=16)
                    return p.mark("ld")

                wg_marks = {0: wg_load(0)}
                for g in range(3):
                    go = (g % 2) * DC * D
                    if g + 1 < 3:
                        wg_marks[g + 1] = wg_load(g + 1)
                    for tb in range(4):
                        tiles = list(range(tb * 4, tb * 4 + 4))
                        jit(tiles, x_sb[:], xhT[:])
                        xh_ready = p.mark("s")
                        p.wait("tensor", "ld", wg_marks[g])
                        p.wait("tensor", "s", xh_ready)
                        p.wait("scalar", "st", p.mark("st"))
                        for mc in range(DC):
                            bi, bank = pool.acquire()
                            for f in range(DC):
                                p.emit("tensor", lambda e, f=f, mc=mc,
                                       go=go, bank=bank: e.matmul(
                                    bank[:],
                                    wg_sb[:, go + mc * D + f * 128:
                                          go + mc * D + (f + 1) * 128],
                                    xhT[:, f * 512:(f + 1) * 512],
                                    start=(f == 0), stop=(f == DC - 1)),
                                    inc="t")
                            p.wait("scalar", "t", p.mark("t"))
                            p.emit("scalar", lambda e, g=g, mc=mc, bank=bank:
                                   e.activation(
                                out=zx_sb[:, mc * 512:(mc + 1) * 512],
                                in_=bank[:], func=AF.Identity,
                                bias=bgc[:, g * DC + mc:g * DC + mc + 1]),
                                inc="s")
                            pool.release(bi, "s")
                        p.wait("gpsimd", "s", p.mark("s"))
                        p.emit("gpsimd", lambda e, g=g, tb=tb: e.dma_start(
                            out=zx_d[g].rearrange("c p t -> p c t")
                                [:, :, tb * 512:(tb + 1) * 512],
                            in_=zx_sb[:].rearrange("p (c q) -> p c q", c=DC)),
                            inc="st", n=16, raw=True)
        p.barrier()

        emit_ffn1_chain(nc, p, pool, jit, ugT, uoT, woR, x3_d,
                        x4_o, zx_d, eye_sb[:], boc[:], incc[:],
                        dbg=(dbg_c, dbg_zx, dbg_cuo) if DBG else None)
        p.barrier()
        p.wait("gpsimd", "st", p.cnt["st"])
        p.build()
    return nc


def stage_l3(inputs, res2):
    com = {
        "wfc": pack_lhsT(np.asarray(inputs["L0_ffn_W_fc"], np.float32)),
        "wproj": pack_rhs(np.asarray(inputs["L0_ffn_W_proj"], np.float32)),
        "wgT": np.stack([pack_lhsT(np.asarray(inputs[k], np.float32))
                         for k in ("L1_ffn_wf", "L1_ffn_wi", "L1_ffn_wc")]),
        "ugT": np.stack([pack_lhsT(np.asarray(inputs[k], np.float32))
                         for k in ("L1_ffn_uf", "L1_ffn_ui", "L1_ffn_uc")]),
        "uoT": pack_lhsT(np.asarray(inputs["L1_ffn_uo"], np.float32)),
        "woR": pack_rhs(np.asarray(inputs["L1_ffn_wo"], np.float32)),
        "bg": np.stack([np.asarray(inputs[k], np.float32)
                        for k in ("L1_ffn_bf", "L1_ffn_bi", "L1_ffn_bc")]),
        "bo_ff": np.asarray(inputs["L1_ffn_bo"], np.float32),
        "inc_ff": np.asarray(inputs["L1_ffn_in_c"], np.float32),
    }
    x2_all = np.stack([np.asarray(res2[b]["x2"]) for b in range(NCORE)])
    maps = []
    for j in range(NCORE):
        xj = np.ascontiguousarray(
            x2_all[:, j * 256:(j + 1) * 256, :].reshape(TOK, D))
        maps.append(dict(com, x2=xj))
    return maps



HALF_PI = 1.5707963267948966
NEG_PI = -3.141592653589793
TWO_PI = 6.283185307179586


def build_l4():
    nc = bass.Bass("TRN2", target_bir_lowering=False,
                   detect_race_conditions=False)
    x4f = nc.dram_tensor("x4f", [B * S, D], F32, kind="ExternalInput").ap()
    x4own = nc.dram_tensor("x4own", [TOK, D], F32, kind="ExternalInput").ap()
    basR = nc.dram_tensor("basR", [4, DC, 128, HARM], F32,
                          kind="ExternalInput").ap()
    phT = nc.dram_tensor("phT", [HARM, 4 * D], F32, kind="ExternalInput").ap()
    amT = nc.dram_tensor("amT", [HARM, 4 * D], F32, kind="ExternalInput").ap()
    wgT = nc.dram_tensor("wgT", [3, DC, 128, D], F32,
                         kind="ExternalInput").ap()
    ugT = nc.dram_tensor("ugT", [3, DC, 128, D], F32,
                         kind="ExternalInput").ap()
    uoT = nc.dram_tensor("uoT", [DC, 128, D], F32, kind="ExternalInput").ap()
    woR = nc.dram_tensor("woR", [DC, 128, D], F32, kind="ExternalInput").ap()
    bg = nc.dram_tensor("bg", [3, D], F32, kind="ExternalInput").ap()
    bo_ff = nc.dram_tensor("bo_ff", [D], F32, kind="ExternalInput").ap()
    inc_ff = nc.dram_tensor("inc_ff", [D], F32, kind="ExternalInput").ap()
    x6_o = nc.dram_tensor("x6", [TOK, D], F32, kind="ExternalOutput").ap()
    x5_d = nc.dram_tensor("x5_d", [TOK, D], F32, kind="Internal").ap()
    zx_d = nc.dram_tensor("zx_d", [3, DC, 128, TOK], F32, kind="Internal").ap()
    eye_d = nc.inline_tensor(np.eye(128, dtype=np.float32), "eye").ap()

    NSEQ = 256
    with ExitStack() as stack:
        ec = stack.enter_context
        p = Prog(nc, stack)
        mean_sb = ec(nc.sbuf_tensor("mean_sb", [128, NT], F32))
        rstd_sb = ec(nc.sbuf_tensor("rstd_sb", [128, NT], F32))
        mean8 = ec(nc.sbuf_tensor("mean8", [128, 4], F32))
        rstd8 = ec(nc.sbuf_tensor("rstd8", [128, 4], F32))
        sq_sb = ec(nc.sbuf_tensor("sq_sb", [128, D], F32))
        col0 = ec(nc.sbuf_tensor("col0", [128, 1], F32))
        eye_sb = ec(nc.sbuf_tensor("eye_sb", [128, 128], F32))
        setup_consts(nc, p, ec)
        xc_sb = ec(nc.sbuf_tensor("xc_sb", [128, 2 * D], F32))
        dg_sb = ec(nc.sbuf_tensor("dg_sb", [128, 2 * 128], F32))
        bgc = ec(nc.sbuf_tensor("bgc", [128, 3 * DC], F32))
        boc = ec(nc.sbuf_tensor("boc", [128, DC], F32))
        incc = ec(nc.sbuf_tensor("incc", [128, DC], F32))
        w4 = ec(nc.sbuf_tensor("w4", [HARM, 4 * D], F32))
        bas_sb = ec(nc.sbuf_tensor("bas_sb", [128, 4 * DC * HARM], F32))
        banks = [ec(nc.psum_tensor(f"ps{i}", [128, 512], F32))
                 for i in range(8)]
        pool = PsumPool(p, [b[:] for b in banks[:4] + banks[6:]])
        avb = [b[:] for b in banks[4:6]]
        jit = make_jit_xhat(p, pool, mean_sb[:], rstd_sb[:], eye_sb[:],
                            xc_sb[:], dg_sb[:], depth=2)

        p.emit("sync", lambda e: e.dma_start(out=eye_sb[:], in_=eye_d),
               inc="ld", n=16)
        p.emit("sync", lambda e: e.dma_start(
            out=bgc[:].rearrange("p (g c) -> p g c", g=3),
            in_=bg.rearrange("g (c p) -> p g c", p=128),
            allow_slow_non_contiguous=True), inc="ld", n=16)
        p.emit("sync", lambda e: e.dma_start(
            out=boc[:], in_=bo_ff.rearrange("(c p) -> p c", p=128),
            allow_slow_non_contiguous=True), inc="ld", n=16)
        p.emit("sync", lambda e: e.dma_start(
            out=incc[:], in_=inc_ff.rearrange("(c p) -> p c", p=128),
            allow_slow_non_contiguous=True), inc="ld", n=16)
        p.emit("sync", lambda e: e.dma_start(out=w4[:], in_=phT),
               inc="ld", n=16)
        p.emit("sync", lambda e: e.dma_start(
            out=bas_sb[:].rearrange("p (x c q) -> p x c q", x=4, c=DC),
            in_=basR.rearrange("x c p q -> p x c q")), inc="ld", n=16)
        ldm = p.mark("ld")
        for e in ("tensor", "vector", "scalar", "gpsimd"):
            p.wait(e, "ld", ldm)
        with nc.sbuf_tensor("am_sb", [HARM, 4 * D], F32) as am_sb:
            p.emit("sync", lambda e: e.dma_start(out=am_sb[:], in_=amT),
                   inc="ld", n=16)
            p.wait("vector", "ld", p.mark("ld"))
            p.emit("vector", lambda e: e.tensor_scalar(
                out=w4[:], in0=w4[:], scalar1=HALF_PI, scalar2=None,
                op0=OP.add))
            with nc.sbuf_tensor("mk_sb", [HARM, 4 * D], F32) as mk_sb:
                PI = -NEG_PI
                for sign in (1.0, -1.0):
                    for _ in range(8):
                        p.emit("vector", lambda e, sign=sign:
                               e.tensor_scalar(
                            out=mk_sb[:], in0=w4[:],
                            scalar1=sign * PI, scalar2=None,
                            op0=(OP.is_gt if sign > 0 else OP.is_lt)))
                        p.emit("vector", lambda e, sign=sign:
                               e.scalar_tensor_tensor(
                            out=w4[:], in0=mk_sb[:],
                            scalar=-sign * TWO_PI, in1=w4[:],
                            op0=OP.mult, op1=OP.add))
                p.wait("scalar", "v", p.mark("v"))
                p.emit("scalar", lambda e: e.activation(
                    out=w4[:], in_=w4[:], func=AF.Sin), inc="s")
            p.wait("vector", "s", p.mark("s"))
            p.emit("vector", lambda e: e.tensor_mul(
                out=w4[:], in0=w4[:], in1=am_sb[:]))
            p.barrier()

        VW = H * 65 + 1
        KW = DC * 512
        with nc.sbuf_tensor("xo_sb", [128, 2 * 2 * D], F32) as xo_sb, \
                nc.sbuf_tensor("qT_sb", [128, DC * NSEQ], F32) as qT_sb, \
                nc.sbuf_tensor("thi", [128, DC * NSEQ], F32) as thi, \
                nc.sbuf_tensor("outT", [128, DC * NSEQ], F32) as outT, \
                nc.sbuf_tensor("kTh", [128, 2 * KW], F32) as kTh, \
                nc.sbuf_tensor("vb", [128, 2 * 4 * VW], F32) as vb, \
                nc.sbuf_tensor("xf2", [128, 4 * D], F32) as xf2, \
                nc.sbuf_tensor("xfT2", [128, DC * 512], F32) as xfT2, \
                nc.sbuf_tensor("res_sb", [HARM, 1280], F32) as res_sb, \
                nc.sbuf_tensor("PT_sb", [128, 1024], F32) as PT_sb, \
                nc.sbuf_tensor("xc8_sb", [128, 2 * D], F32) as xc8_sb, \
                nc.sbuf_tensor("dg8_sb", [128, 2 * 128], F32) as dg8_sb, \
                nc.sbuf_tensor("on_sb", [128, 128], F32) as on_sb:
            jit8 = make_jit_xhat(p, pool, mean8[:], rstd8[:], eye_sb[:],
                                 xc8_sb[:], dg8_sb[:], depth=2)
            p.emit("vector", lambda e: e.memset(on_sb[:], 1.0))
            xf2_marks = {"v": p.mark("v"), "s": p.mark("s")}
            attn_state = {"add": 0}
            xo_marks = {"v": p.mark("v"), "s": p.mark("s"),
                        "st": p.mark("st")}

            def emit_xstats(idx):
                bb, qq = divmod(idx, 4)
                g0 = bb * 16 + qq * 4
                p.wait("sync", "v", xf2_marks["v"])
                p.wait("sync", "s", xf2_marks["s"])
                p.emit("sync", lambda e, g0=g0: e.dma_start(
                    out=xf2[:].rearrange("p (i d) -> p i d", i=4),
                    in_=x4f.rearrange("(i p) d -> p i d", p=128)
                        [:, g0:g0 + 4, :]), inc="ld", n=16)
                xl = p.mark("ld")
                for e in ("vector", "scalar"):
                    p.wait(e, "ld", xl)
                ln_stats(p, xf2[:], mean8[:], rstd8[:], sq_sb[:],
                         col0[:], range(4))

            def emit_xo(bb):
                sl = (bb % 2) * 2 * D
                p.wait("sync", "v", xo_marks["v"])
                p.wait("sync", "s", xo_marks["s"])
                p.wait("sync", "st", xo_marks["st"])
                p.emit("sync", lambda e, bb=bb, sl=sl: e.dma_start(
                    out=xo_sb[:, sl:sl + 2 * D]
                    .rearrange("p (i d) -> p i d", i=2),
                    in_=x4own.rearrange("(i p) d -> p i d", p=128)
                        [:, 2 * bb:2 * bb + 2, :]), inc="ld", n=16)
                xl = p.mark("ld")
                for e in ("vector", "scalar"):
                    p.wait(e, "ld", xl)
                ln_stats(p, xo_sb[:, sl:sl + 2 * D],
                         mean_sb[:, 2 * bb:2 * bb + 2],
                         rstd_sb[:, 2 * bb:2 * bb + 2],
                         sq_sb[:], col0[:], range(2))

            def emit_qprep(bb):
                sl = (bb % 2) * 2 * D
                jit([0, 1], xo_sb[:, sl:sl + 2 * D], xfT2[:],
                    stats=[2 * bb, 2 * bb + 1])
                p.wait("tensor", "s", p.mark("s"))
                bi, bank = pool.acquire()
                for f in range(DC):
                    p.emit("tensor", lambda e, f=f, bank=bank: e.matmul(
                        bank[0:HARM, 0:NSEQ],
                        bas_sb[:, (0 * DC + f) * HARM:
                               (0 * DC + f + 1) * HARM],
                        xfT2[:, f * NSEQ:(f + 1) * NSEQ],
                        start=(f == 0), stop=(f == DC - 1)), inc="t")
                p.wait("scalar", "t", p.mark("t"))
                p.emit("scalar", lambda e, bank=bank: e.activation(
                    out=res_sb[:, 1024:1024 + NSEQ],
                    in_=bank[0:HARM, 0:NSEQ], func=AF.Copy), inc="s")
                pool.release(bi, "s")
                p.wait("tensor", "s", p.mark("s"))
                for oc in range(DC):
                    bi, bank = pool.acquire()
                    p.emit("tensor", lambda e, oc=oc, bank=bank: e.matmul(
                        bank[:, 0:NSEQ],
                        w4[:, 0 * D + oc * 128:0 * D + (oc + 1) * 128],
                        res_sb[:, 1024:1024 + NSEQ],
                        start=True, stop=True), inc="t")
                    p.wait("scalar", "t", p.mark("t"))
                    p.emit("scalar", lambda e, oc=oc, bank=bank:
                           e.activation(
                        out=qT_sb[:, oc * NSEQ:(oc + 1) * NSEQ],
                        in_=bank[:, 0:NSEQ], func=AF.Copy), inc="s")
                    pool.release(bi, "s")
                p.wait("vector", "t", p.mark("t"))
                p.emit("vector", lambda e: e.memset(outT[0:65, :], 0.0))
                p.emit("vector", lambda e: e.memset(thi[0:65, :], 0.0))
                return p.mark("s")

            def emit_prep(idx):
                ko = (idx % 2) * KW
                vo = (idx % 2) * 4 * VW
                jit8(list(range(4)), xf2[:], xfT2[:])
                xf2_marks["v"] = p.mark("v")
                xf2_marks["s"] = p.mark("s")
                p.wait("tensor", "s", p.mark("s"))
                for pj in range(2):
                    bi, bank = pool.acquire()
                    for f in range(DC):
                        p.emit("tensor", lambda e, pj=pj, f=f,
                               bank=bank: e.matmul(
                            bank[0:HARM, 0:512],
                            bas_sb[:, ((pj + 1) * DC + f) * HARM:
                                   ((pj + 1) * DC + f + 1) * HARM],
                            xfT2[:, f * 512:(f + 1) * 512],
                            start=(f == 0), stop=(f == DC - 1)),
                            inc="t")
                    p.wait("scalar", "t", p.mark("t"))
                    p.emit("scalar", lambda e, pj=pj,
                           bank=bank: e.activation(
                        out=res_sb[:, pj * 512:(pj + 1) * 512],
                        in_=bank[0:HARM, 0:512], func=AF.Copy),
                        inc="s")
                    pool.release(bi, "s")
                res_ready = p.mark("s")
                p.wait("tensor", "s", res_ready)
                for oc in range(DC):
                    bi, bank = pool.acquire()
                    p.emit("tensor", lambda e, oc=oc, bank=bank:
                           e.matmul(
                        bank[:],
                        w4[:, 1 * D + oc * 128:1 * D + (oc + 1) * 128],
                        res_sb[:, 0:512],
                        start=True, stop=True), inc="t")
                    p.wait("scalar", "t", p.mark("t"))
                    p.emit("scalar", lambda e, oc=oc, ko=ko, bank=bank:
                           e.activation(
                        out=kTh[:, ko + oc * 512:ko + (oc + 1) * 512],
                        in_=bank[:], func=AF.Copy), inc="s")
                    pool.release(bi, "s")
                for sub in range(4):
                    p.emit("vector", lambda e, sub=sub, vo=vo: e.memset(
                        vb[:, vo + sub * VW + 64:
                           vo + sub * VW + H * 65:65], 1.0))
                    for ob in range(2):
                        bi, bank = pool.acquire()
                        p.emit("tensor", lambda e, sub=sub, ob=ob,
                               bank=bank: e.matmul(
                            bank[:],
                            res_sb[:, 512 + sub * 128:
                                   512 + (sub + 1) * 128],
                            w4[:, 2 * D + ob * 512:2 * D + (ob + 1) * 512],
                            start=True, stop=True), inc="t")
                        p.wait("scalar", "t", p.mark("t"))
                        p.wait("scalar", "v", p.mark("v"))
                        p.emit("scalar", lambda e, sub=sub, ob=ob, vo=vo,
                               bank=bank: e.activation(
                            out=vb[:, vo + sub * VW + ob * 8 * 65:
                                    vo + sub * VW + ob * 8 * 65 + 8 * 65]
                            .rearrange("p (h e) -> p h e", h=8)[:, :, 0:64],
                            in_=bank[:].rearrange("p (h e) -> p h e", h=8),
                            func=AF.Copy), inc="s")
                        pool.release(bi, "s")
                return p.mark("s")

            def emit_attn(idx, kv_ready, q_ready):
                b, qtr = divmod(idx, 4)
                ko = (idx % 2) * KW
                vo = (idx % 2) * 4 * VW
                p.wait("tensor", "s", max(kv_ready, q_ready))

                def emit_scores(hp, kc):
                    sbanks = []
                    for hv in range(2):
                        bi, bank = pool.acquire()
                        sbanks.append((bi, bank))
                        p.emit("tensor", lambda e, hp=hp, hv=hv,
                               kc=kc, bank=bank: e.matmul(
                            bank[:, 0:NSEQ],
                            kTh[hv * 64:hv * 64 + 64,
                                ko + hp * 512 + kc * 128:
                                ko + hp * 512 + (kc + 1) * 128],
                            qT_sb[hv * 64:hv * 64 + 64,
                                  hp * NSEQ:(hp + 1) * NSEQ],
                            start=True, stop=True),
                            inc="t" if hv == 1 else None)
                    return sbanks, p.mark("t")

                iters = [(hp, kc) for hp in range(DC) for kc in range(4)]
                pend = {j: emit_scores(*iters[j]) for j in range(3)}
                avt_marks = {}
                for j, (hp, kc) in enumerate(iters):
                    sbanks, tmark = pend.pop(j)
                    p.wait("scalar", "t", tmark)
                    if j - 2 in avt_marks:
                        p.wait("scalar", "t", avt_marks.pop(j - 2))
                    pt = PT_sb[:, (kc % 2) * 512:(kc % 2) * 512 + 512]
                    for hv in range(2):
                        bi, bank = sbanks[hv]
                        p.emit("scalar", lambda e, hv=hv, pt=pt,
                               bank=bank: e.activation(
                            out=pt[:, hv * NSEQ:(hv + 1) * NSEQ],
                            in_=bank[:, 0:NSEQ],
                            func=AF.Exp, scale=0.125), inc="s")
                        pool.release(bi, "s")
                    if j + 3 < len(iters):
                        pend[j + 3] = emit_scores(*iters[j + 3])
                    p.wait("tensor", "s", p.mark("s"))
                    if kc == 0:
                        p.wait("tensor", "v", attn_state["add"])
                    for hv in range(2):
                        h = hp * 2 + hv
                        av = avb[hv]
                        p.emit("tensor", lambda e, hv=hv, kc=kc,
                               h=h, av=av, pt=pt, vo=vo: e.matmul(
                            av[0:65, 0:NSEQ],
                            vb[:, vo + kc * VW + h * 65:
                               vo + kc * VW + (h + 1) * 65],
                            pt[:, hv * NSEQ:(hv + 1) * NSEQ],
                            start=(kc == 0), stop=(kc == 3)),
                            inc="t" if (kc == 3 and hv == 1)
                            else None)
                    avt_marks[j] = p.mark("t")
                    if kc == 3:
                        p.wait("vector", "t", p.mark("t"))
                        for hv, acc in ((0, outT), (1, thi)):
                            p.emit("vector", lambda e, hp=hp, hv=hv,
                                   acc=acc: e.tensor_tensor(
                                out=acc[0:65, hp * NSEQ:(hp + 1) * NSEQ],
                                in0=acc[0:65, hp * NSEQ:(hp + 1) * NSEQ],
                                in1=avb[hv][0:65, 0:NSEQ], op=OP.add))
                        attn_state["add"] = p.mark("v")
                        if hp == 0 and idx + 1 < 4 * B:
                            emit_xstats(idx + 1)
                        if hp == 4 and qtr == 2 and b + 1 < B:
                            emit_xo(b + 1)

            def emit_epilogue(b):
                sl = (b % 2) * 2 * D
                for acc in (outT, thi):
                    p.emit("vector", lambda e, acc=acc: e.reciprocal(
                        out=acc[64:65, :], in_=acc[64:65, :]))
                p.wait("tensor", "v", p.mark("v"))
                for hp in range(DC):
                    for acc in (outT, thi):
                        bi, bank = pool.acquire()
                        p.emit("tensor", lambda e, hp=hp, acc=acc,
                               bank=bank: e.matmul(
                            bank[0:64, 0:NSEQ],
                            on_sb[64:65, 0:64],
                            acc[64:65, hp * NSEQ:(hp + 1) * NSEQ],
                            start=True, stop=True), inc="t")
                        p.wait("vector", "t", p.mark("t"))
                        p.emit("vector", lambda e, hp=hp, acc=acc,
                               bank=bank: e.tensor_tensor(
                            out=acc[0:64, hp * NSEQ:(hp + 1) * NSEQ],
                            in0=acc[0:64, hp * NSEQ:(hp + 1) * NSEQ],
                            in1=bank[0:64, 0:NSEQ], op=OP.mult))
                        pool.release(bi, "v")
                p.wait("sync", "v", p.mark("v"))
                p.emit("sync", lambda e: e.dma_start(
                    out=outT[64:128, :], in_=thi[0:64, :]), inc="ld", n=16)
                p.wait("tensor", "ld", p.mark("ld"))
                p.wait("tensor", "v", p.mark("v"))
                bi, bank = pool.acquire()
                for oc in range(DC):
                    p.emit("tensor", lambda e, oc=oc, bank=bank: e.matmul(
                        bank[0:HARM, 0:NSEQ],
                        bas_sb[:, (3 * DC + oc) * HARM:
                               (3 * DC + oc + 1) * HARM],
                        outT[:, oc * NSEQ:(oc + 1) * NSEQ],
                        start=(oc == 0), stop=(oc == DC - 1)), inc="t")
                p.wait("scalar", "t", p.mark("t"))
                p.emit("scalar", lambda e, bank=bank: e.activation(
                    out=res_sb[:, 1024:1024 + NSEQ],
                    in_=bank[0:HARM, 0:NSEQ], func=AF.Copy), inc="s")
                pool.release(bi, "s")
                p.wait("tensor", "s", p.mark("s"))
                for sub in range(2):
                    for ob in range(2):
                        bi, bank = pool.acquire()
                        p.emit("tensor", lambda e, sub=sub, ob=ob,
                               bank=bank: e.matmul(
                            bank[:],
                            res_sb[:, 1024 + sub * 128:
                                   1024 + (sub + 1) * 128],
                            w4[:, 3 * D + ob * 512:3 * D + (ob + 1) * 512],
                            start=True, stop=True), inc="t")
                        p.wait("vector", "t", p.mark("t"))
                        p.emit("vector", lambda e, sub=sub, ob=ob, sl=sl,
                               bank=bank: e.scalar_tensor_tensor(
                            out=xo_sb[:, sl + sub * D + ob * 512:
                                      sl + sub * D + (ob + 1) * 512],
                            in0=bank[:], scalar=1.0,
                            in1=xo_sb[:, sl + sub * D + ob * 512:
                                      sl + sub * D + (ob + 1) * 512],
                            op0=OP.mult, op1=OP.add))
                        pool.release(bi, "v")
                p.wait("gpsimd", "v", p.mark("v"))
                p.emit("gpsimd", lambda e, b=b, sl=sl: e.dma_start(
                    out=x5_d.rearrange("(i p) d -> p i d", p=128)
                        [:, 2 * b:2 * b + 2, :],
                    in_=xo_sb[:, sl:sl + 2 * D]
                    .rearrange("p (i d) -> p i d", i=2)),
                    inc="st", n=16, raw=True)
                ln_stats(p, xo_sb[:, sl:sl + 2 * D],
                         mean_sb[:, 2 * b:2 * b + 2],
                         rstd_sb[:, 2 * b:2 * b + 2],
                         sq_sb[:], col0[:], range(2))
                xo_marks["v"] = p.mark("v")
                xo_marks["s"] = p.mark("s")
                xo_marks["st"] = p.mark("st")

            emit_xo(0)
            emit_xstats(0)
            qp_marks = {0: emit_qprep(0)}
            kv_marks = {0: emit_prep(0)}
            for idx in range(4 * B):
                b, qtr = divmod(idx, 4)
                emit_attn(idx, kv_marks[idx], qp_marks[b])
                if idx + 1 < 4 * B:
                    kv_marks[idx + 1] = emit_prep(idx + 1)
                if qtr == 3:
                    emit_epilogue(b)
                    if b + 1 < B:
                        qp_marks[b + 1] = emit_qprep(b + 1)
            p.barrier()
        with nc.sbuf_tensor("x_sb2", [128, NT * D], F32) as x_sb2, \
                nc.sbuf_tensor("xhT", [128, DC * 512], F32) as xhT, \
                nc.sbuf_tensor("wg_sb", [128, 2 * DC * D], F32) as wg_sb, \
                nc.sbuf_tensor("zx_sb", [128, DC * 512], F32) as zx_sb:
            p.emit("sync", lambda e: e.dma_start(
                out=x_sb2[:].rearrange("p (i d) -> p i d", i=NT),
                in_=x5_d.rearrange("(i p) d -> p i d", p=128)),
                inc="ld", n=16)
            x5_ld = p.mark("ld")
            for e in ("tensor", "vector", "scalar"):
                p.wait(e, "ld", x5_ld)

            def wg_load(g):
                p.wait("sync", "t", p.mark("t"))
                p.emit("sync", lambda e, g=g: e.dma_start(
                    out=wg_sb[:, (g % 2) * DC * D:(g % 2 + 1) * DC * D]
                    .rearrange("p (c d) -> p c d", c=DC),
                    in_=wgT[g].rearrange("c p d -> p c d")), inc="ld", n=16)
                return p.mark("ld")

            wg_marks = {0: wg_load(0)}
            for g in range(3):
                go = (g % 2) * DC * D
                if g + 1 < 3:
                    wg_marks[g + 1] = wg_load(g + 1)
                for tb in range(4):
                    tiles = list(range(tb * 4, tb * 4 + 4))
                    jit(tiles, x_sb2[:], xhT[:])
                    xh_ready = p.mark("s")
                    p.wait("tensor", "ld", wg_marks[g])
                    p.wait("tensor", "s", xh_ready)
                    p.wait("scalar", "st", p.mark("st"))
                    for mc in range(DC):
                        bi, bank = pool.acquire()
                        for f in range(DC):
                            p.emit("tensor", lambda e, f=f, mc=mc,
                                   go=go, bank=bank: e.matmul(
                                bank[:],
                                wg_sb[:, go + mc * D + f * 128:
                                      go + mc * D + (f + 1) * 128],
                                xhT[:, f * 512:(f + 1) * 512],
                                start=(f == 0), stop=(f == DC - 1)),
                                inc="t")
                        p.wait("scalar", "t", p.mark("t"))
                        p.emit("scalar", lambda e, g=g, mc=mc, bank=bank:
                               e.activation(
                            out=zx_sb[:, mc * 512:(mc + 1) * 512],
                            in_=bank[:], func=AF.Identity,
                            bias=bgc[:, g * DC + mc:g * DC + mc + 1]),
                            inc="s")
                        pool.release(bi, "s")
                    p.wait("gpsimd", "s", p.mark("s"))
                    p.emit("gpsimd", lambda e, g=g, tb=tb: e.dma_start(
                        out=zx_d[g].rearrange("c p t -> p c t")
                            [:, :, tb * 512:(tb + 1) * 512],
                        in_=zx_sb[:].rearrange("p (c q) -> p c q", c=DC)),
                        inc="st", n=16, raw=True)
        p.barrier()
        emit_ffn1_chain(nc, p, pool, jit, ugT, uoT, woR, x5_d, x6_o, zx_d,
                        eye_sb[:], boc[:], incc[:])
        p.barrier()
        p.wait("gpsimd", "st", p.cnt["st"])
        p.build()
    return nc


def stage_l4(inputs, res3):
    com = {
        "basR": np.stack([pack_rhs(np.asarray(inputs[f"L1_attn_basis_{k}"],
                                              np.float32))
                          for k in "qkvo"]),
        "phT": np.ascontiguousarray(np.concatenate(
            [np.asarray(inputs[f"L1_attn_phase_{k}"], np.float32).T
             for k in "qkvo"], axis=1)),
        "amT": np.ascontiguousarray(np.concatenate(
            [np.asarray(inputs[f"L1_attn_amp_{k}"], np.float32).T
             for k in "qkvo"], axis=1)),
        "wgT": np.stack([pack_lhsT(np.asarray(inputs[k], np.float32))
                         for k in ("L1_ffn_wf", "L1_ffn_wi", "L1_ffn_wc")]),
        "ugT": np.stack([pack_lhsT(np.asarray(inputs[k], np.float32))
                         for k in ("L1_ffn_uf", "L1_ffn_ui", "L1_ffn_uc")]),
        "uoT": pack_lhsT(np.asarray(inputs["L1_ffn_uo"], np.float32)),
        "woR": pack_rhs(np.asarray(inputs["L1_ffn_wo"], np.float32)),
        "bg": np.stack([np.asarray(inputs[k], np.float32)
                        for k in ("L1_ffn_bf", "L1_ffn_bi", "L1_ffn_bc")]),
        "bo_ff": np.asarray(inputs["L1_ffn_bo"], np.float32),
        "inc_ff": np.asarray(inputs["L1_ffn_in_c"], np.float32),
    }
    x4_by_batch = np.empty((B, S, D), np.float32)
    for j in range(NCORE):
        blk = np.asarray(res3[j]["x4"]).reshape(B, 256, D)
        x4_by_batch[:, j * 256:(j + 1) * 256, :] = blk
    x4_flat = np.ascontiguousarray(x4_by_batch.reshape(B * S, D))
    maps = []
    for j in range(NCORE):
        own = np.ascontiguousarray(
            x4_by_batch[:, j * 256:(j + 1) * 256, :].reshape(TOK, D))
        maps.append(dict(com, x4f=x4_flat, x4own=own))
    return maps
